# version 02
# baseline (speedup 1.0000x reference)
import contextlib
import numpy as np
import concourse.bass as bass
import concourse.mybir as mybir
from concourse.bass_utils import run_bass_kernel_spmd

F32 = mybir.dt.float32
BF16 = mybir.dt.bfloat16
AF = mybir.ActivationFunctionType
ALU = mybir.AluOpType
AX = mybir.AxisListType

B, S, DM, H, D = 4, 4096, 1024, 16, 64
TPC, FF, NE, QT = 2048, 2048, 8, 512
NQT, NKC, NDC, NFC, NXT = TPC // QT, S // 128, DM // 128, FF // 128, S // 128

ENGS = ("t", "s", "v", "g", "y")


class Prog:

    def __init__(self):
        self.recs = {e: [] for e in ENGS}
        self.cnt = {}

    def emit(self, eng, fn, waits=(), inc=True, sem=None):
        semn = sem or eng
        inc_k = 16 if eng in ("g", "y") else 1
        ws = [(e, v) for (e, v) in waits if v and v > 0]
        self.recs[eng].append((fn, ws, semn if inc else None, inc_k))
        if inc:
            self.cnt[semn] = self.cnt.get(semn, 0) + inc_k
        return self.cnt.get(semn, 0)

    def finalize(self, nc, sems):
        def run(eng_obj, name):
            lw = {}
            for fn, ws, semn, inc_k in self.recs[name]:
                for se, val in ws:
                    if val > lw.get(se, 0):
                        eng_obj.wait_ge(sems[se], val)
                        lw[se] = val
                inst = fn(eng_obj)
                if semn is not None:
                    inst.then_inc(sems[semn], inc_k)

        with nc.Block() as block:
            block.tensor(lambda e: run(e, "t"))
            block.scalar(lambda e: run(e, "s"))
            block.vector(lambda e: run(e, "v"))
            block.gpsimd(lambda e: run(e, "g"))
            block.sync(lambda e: run(e, "y"))


def build_full():
    nc = bass.Bass("TRN2", target_bir_lowering=False)

    dt_ = nc.dram_tensor
    xr = dt_("xr", [S, DM], F32, kind="ExternalInput")
    jv_d = dt_("jv", [1, S], F32, kind="ExternalInput")
    eye_d = dt_("eye", [128, 128], F32, kind="ExternalInput")
    wq_d = dt_("wq", [DM, DM], F32, kind="ExternalInput")
    wk_d = dt_("wk", [DM, DM], F32, kind="ExternalInput")
    wv_d = dt_("wv", [DM, DM], F32, kind="ExternalInput")
    wo_d = dt_("wo", [DM, DM], F32, kind="ExternalInput")
    wfq_d = dt_("wfq", [DM, DM], F32, kind="ExternalInput")
    wfg_d = dt_("wfg", [DM, DM], F32, kind="ExternalInput")
    bq_d = dt_("bq_m", [128, 8], F32, kind="ExternalInput")
    bk_d = dt_("bk_m", [128, 8], F32, kind="ExternalInput")
    bo_d = dt_("bo_m", [128, 8], F32, kind="ExternalInput")
    bfq_d = dt_("bfq_m", [128, 8], F32, kind="ExternalInput")
    bfg_d = dt_("bfg_m", [128, 8], F32, kind="ExternalInput")
    bv_d = dt_("bv_r", [1, DM], F32, kind="ExternalInput")
    upz_d = dt_("upz_m", [128, 512], F32, kind="ExternalInput")
    wg_d = dt_("wg_m", [128, 64], F32, kind="ExternalInput")
    ws1_d = dt_("ws1", [DM, FF], F32, kind="ExternalInput")
    ws3_d = dt_("ws3", [DM, FF], F32, kind="ExternalInput")
    ws2_d = dt_("ws2", [FF, DM], F32, kind="ExternalInput")
    we1_d = dt_("we1", [NE, DM, FF], F32, kind="ExternalInput")
    we3_d = dt_("we3", [NE, DM, FF], F32, kind="ExternalInput")
    we2_d = dt_("we2", [NE, FF, DM], F32, kind="ExternalInput")
    out_d = dt_("out", [TPC, DM], F32, kind="ExternalOutput")

    al = nc.alloc_sbuf_tensor
    eye_f = al("eye_f", [128, 128], F32)
    eye_b = al("eye_b", [128, 128], BF16)
    ones1_b = al("ones1_b", [1, 128], BF16)
    ones128_b = al("ones128_b", [128, 1], BF16)
    cpool = al("cpool", [128, 2], F32)
    bq_sb = al("bq_sb", [128, 8], F32)
    bk_sb = al("bk_sb", [128, 8], F32)
    bo_sb = al("bo_sb", [128, 8], F32)
    bfq_sb = al("bfq_sb", [128, 8], F32)
    bfg_sb = al("bfg_sb", [128, 8], F32)
    cbias = al("cbias", [128, 8], F32)
    bvb = al("bvb", [128, 128], F32)
    wg_b = al("wg_b", [128, 64], BF16)
    xm_sb = al("xm_sb", [128, 8], F32)
    xm_b = al("xm_b", [128, 8], BF16)
    glo_sb = al("glo_sb", [128, 8], F32)
    glo_b = al("glo_b", [128, 8], BF16)
    rowsF = al("rowsF", [1, 1024], F32)
    recr_f = [rowsF[0:1, 0:512], rowsF[0:1, 512:1024]]
    rowsB = al("rowsB", [1, 2048], BF16)
    mrow_b, irow_b = rowsB[0:1, 0:512], rowsB[0:1, 512:1024]
    recr = [rowsB[0:1, 1024:1536], rowsB[0:1, 1536:2048]]
    gsig = al("gsig", [128, 8], F32)
    gt = al("gt", [128, 40], F32)
    wtok = al("wtok", [128, 8], F32)
    wr8 = al("wr8", [8, TPC], BF16)
    wstg = al("wstg", [1, 1024], BF16)
    mq = [al(f"mq{c}", [128, TPC], BF16) for c in range(NDC)]
    y1 = mq

    ap_ = nc.alloc_psum_tensor
    ps_s = [ap_(f"ps_s{p}", [128, 2 * QT], F32) for p in range(2)]
    ps_c = [ap_(f"ps_c{i}", [128, QT], F32) for i in range(2)]
    ps_tr = [ap_(f"ps_tr{p}", [128, 1024], BF16) for p in range(2)]

    stk = contextlib.ExitStack()
    STREAMS = ["t", "s", "v", "y_jv", "y_eye", "y_x0", "y_x1", "y_st0",
               "y_st1", "g_sm", "g_j", "g_wb", "g_upz", "g_bv", "g_kv0",
               "g_kv1", "g_c30", "g_c31", "g_wr8", "g_st0", "g_st1",
               "g_we13", "g_we2"]
    sems = {e: stk.enter_context(nc.semaphore(f"sem_{e}")) for e in STREAMS}
    P = Prog()
    nc.const_aps.register(F32, 1e-5, cpool[:, 0:1])

    def chunked(dr, c=8):
        return dr.rearrange("(c p) n -> p c n", p=128)

    def wsl(buf, di, dc):
        return buf[:, di * DM + dc * 128: di * DM + (dc + 1) * 128]

    sc_x = contextlib.ExitStack()
    x_fm = [sc_x.enter_context(nc.sbuf_tensor(f"x_fm{c}", [128, S], BF16))
            for c in range(NDC)]
    sc_ctx = contextlib.ExitStack()
    ctx_fm = [sc_ctx.enter_context(nc.sbuf_tensor(f"ctx{c}", [128, TPC], BF16))
              for c in range(NDC)]
    sc_kaug = contextlib.ExitStack()
    k_augA = sc_kaug.enter_context(nc.sbuf_tensor("k_augA", [128, S], BF16))
    k_augB = sc_kaug.enter_context(nc.sbuf_tensor("k_augB", [128, S], BF16))

    sc_p0 = contextlib.ExitStack()
    poolF = sc_p0.enter_context(nc.sbuf_tensor("poolF", [1, S], F32))
    poolB = sc_p0.enter_context(nc.sbuf_tensor("poolB", [65, S], BF16))
    stage_a = [sc_p0.enter_context(nc.sbuf_tensor(f"stg{p}", [128, DM], F32))
               for p in range(2)]
    xb_t = [sc_p0.enter_context(nc.sbuf_tensor(f"xb{p}", [128, DM], BF16))
            for p in range(2)]
    jv_sb = poolF[0:1, :]
    jv_b = poolB[0:1, :]
    jsq_b = poolB[32:33, :]
    onesrow_b = poolB[64:65, :]

    ld_jv = P.emit("y", lambda y: y.dma_start(jv_sb, jv_d[:, :]), sem="y_jv")
    ld_eye = P.emit("y", lambda y: y.dma_start(eye_f[:, :], eye_d[:, :]), sem="y_eye")
    gw_small = 0
    for dst, src in [
        (eye_b[:, :], eye_d[:, :]), (bq_sb[:, :], bq_d[:, :]),
        (bk_sb[:, :], bk_d[:, :]), (bo_sb[:, :], bo_d[:, :]),
        (bfq_sb[:, :], bfq_d[:, :]), (bfg_sb[:, :], bfg_d[:, :]),
        (wg_b[:, :], wg_d[:, :]),
    ]:
        gw_small = P.emit("g", lambda g, d=dst, s=src: g.dma_start(d, s), sem="g_sm")

    def memsets(v):
        v.memset(cpool[:, 0:1], 1e-5)
        v.memset(ones1_b[:, :], 1.0)
        v.memset(ones128_b[:, :], 1.0)
        return v.memset(onesrow_b, 1.0)
    v_init = P.emit("v", memsets)
    v_jsq = P.emit("v", lambda v: v.tensor_tensor(jsq_b, jv_sb, jv_sb,
                                                  ALU.mult),
                   waits=[("y_jv", ld_jv)])
    v_jb = P.emit("v", lambda v: v.tensor_copy(jv_b, jv_sb),
                  waits=[("v", v_jsq)])
    gw_j = 0
    for ka in (k_augA, k_augB):
        P.emit("g", lambda g, ka=ka: g.dma_start(ka[64:65, :], jsq_b),
               waits=[("v", v_jb), ("v", v_init)], sem="g_j")
        P.emit("g", lambda g, ka=ka: g.dma_start(ka[65:66, :], jv_b),
               sem="g_j")
        gw_j = P.emit("g", lambda g, ka=ka: g.dma_start(ka[66:67, :],
                                                        onesrow_b), sem="g_j")

    x_ld, x_cv, x_tr, x_ev = {}, {}, {}, {}
    for t in range(NXT):
        par = t % 2
        x_ld[t] = P.emit("y", lambda y, t=t, par=par: y.dma_start(
            stage_a[par][:, :], xr[t * 128:(t + 1) * 128, :]),
            waits=[("v", x_cv.get(t - 2, 0))], sem=f"y_x{par}")
        x_cv[t] = P.emit("v", lambda v, par=par: v.tensor_copy(
            xb_t[par][:, :], stage_a[par][:, :]),
            waits=[(f"y_x{par}", x_ld[t]), ("t", x_tr.get(t - 2, 0)),
                   ("v", v_init)])

        def trf(t_, par=par):
            for dc in range(NDC):
                ii = t_.transpose(ps_tr[par][:, dc * 128:(dc + 1) * 128],
                                  xb_t[par][:, dc * 128:(dc + 1) * 128],
                                  eye_b[:, :])
            return ii
        x_tr[t] = P.emit("t", trf, waits=[("v", x_cv[t]),
                                          ("g_sm", gw_small),
                                          ("v", x_ev.get(t - 2, 0))])

        def evf(v, t=t, par=par):
            for dc in range(NDC):
                ii = v.tensor_copy(x_fm[dc][:, t * 128:(t + 1) * 128],
                                   ps_tr[par][:, dc * 128:(dc + 1) * 128])
            return ii
        x_ev[t] = P.emit("v", evf, waits=[("t", x_tr[t])])
    x_done = x_ev[NXT - 1]
    xtr_last = x_tr[NXT - 1]

    sc_p0.close()

    sc_c3 = contextlib.ExitStack()
    C3 = sc_c3.enter_context(nc.sbuf_tensor("C3", [16, 3 * TPC], BF16))
    sc_w = contextlib.ExitStack()
    w_big1 = sc_w.enter_context(nc.sbuf_tensor("w_big1", [128, 8 * DM], BF16))
    w_big2 = w_big1

    gw_wq = P.emit("g", lambda g: g.dma_start(
        w_big1[:, :].rearrange("p (c n) -> p c n", c=8), chunked(wq_d[:, :])),
        waits=[("t", xtr_last), ("v", x_done), ("g_j", gw_j)], sem="g_wb")

    mq_mm, mq_ev = {}, {}
    for qc in range(NQT):
        for dc in range(NDC):
            i = qc * NDC + dc
            par = i % 2

            def mmf(t_, qc=qc, dc=dc, par=par):
                for di in range(NDC):
                    ii = t_.matmul(ps_s[par][:, 0:QT], wsl(w_big1, di, dc),
                                   x_fm[di][:, qc * QT:(qc + 1) * QT],
                                   start=(di == 0), stop=(di == NDC - 1))
                return ii
            mq_mm[i] = P.emit("t", mmf, waits=[("v", x_done), ("g_wb", gw_wq),
                                               ("v", mq_ev.get(i - 2, 0))])
            mq_ev[i] = P.emit("v", lambda v, qc=qc, dc=dc, par=par:
                              v.tensor_scalar(mq[dc][:, qc * QT:(qc + 1) * QT],
                                              ps_s[par][:, 0:QT],
                                              bq_sb[:, dc:dc + 1], None,
                                              ALU.add),
                              waits=[("t", mq_mm[i])])
    mq_done = mq_ev[NQT * NDC - 1]

    def xmeanf(v):
        for dc in range(NDC):
            ii = v.reduce_sum(xm_sb[:, dc:dc + 1], x_fm[dc][:, :], axis=AX.X)
        return ii
    v_xm = P.emit("v", xmeanf, waits=[("v", x_done)])
    v_xmb = P.emit("v", lambda v: v.tensor_scalar(
        xm_b[:, :], xm_sb[:, :], 1.0 / S, None, ALU.mult), waits=[("v", v_xm)])

    def glof(t_):
        for dc in range(NDC):
            for di in range(NDC):
                ii = t_.matmul(ps_c[0][:, dc:dc + 1], wsl(w_big1, di, dc),
                               xm_b[:, di:di + 1], start=(di == 0),
                               stop=(di == NDC - 1))
        return ii
    t_glo = P.emit("t", glof, waits=[("v", v_xmb), ("g_wb", gw_wq)])
    v_glo = P.emit("v", lambda v: v.tensor_tensor(
        glo_sb[:, :], ps_c[0][:, 0:8], bq_sb[:, :], ALU.add),
        waits=[("t", t_glo)])
    v_glob = P.emit("v", lambda v: v.tensor_copy(glo_b[:, :], glo_sb[:, :]),
                    waits=[("v", v_glo)])

    def gff(t_):
        for dc in range(NDC):
            for di in range(NDC):
                ii = t_.matmul(ps_c[1][:, dc:dc + 1], wsl(w_big2, di, dc),
                               glo_b[:, di:di + 1], start=(di == 0),
                               stop=(di == NDC - 1))
        return ii
    gw_wfg = P.emit("g", lambda g: g.dma_start(
        w_big1[:, :].rearrange("p (c n) -> p c n", c=8), chunked(wfg_d[:, :])),
        waits=[("t", t_glo), ("t", mq_mm[NQT * NDC - 1])], sem="g_wb")
    t_gf = P.emit("t", gff, waits=[("v", v_glob), ("g_wb", gw_wfg)])
    v_cb1 = P.emit("v", lambda v: v.tensor_tensor(
        cbias[:, :], ps_c[1][:, 0:8], bfq_sb[:, :], ALU.add),
        waits=[("t", t_gf)])
    v_cb = P.emit("v", lambda v: v.tensor_tensor(
        cbias[:, :], cbias[:, :], bfg_sb[:, :], ALU.add), waits=[("v", v_cb1)])

    gw_wfq = P.emit("g", lambda g: g.dma_start(
        w_big2[:, :].rearrange("p (c n) -> p c n", c=8), chunked(wfq_d[:, :])),
        waits=[("t", t_gf)], sem="g_wb")

    sc_p2 = contextlib.ExitStack()
    upz_b = sc_p2.enter_context(nc.sbuf_tensor("upz_b", [128, 512], BF16))
    gw_upz = P.emit("g", lambda g: g.dma_start(upz_b[:, :], upz_d[:, :]),
                    waits=[("t", xtr_last), ("v", x_done), ("g_j", gw_j)],
                    sem="g_upz")
    spz = sc_p2.enter_context(nc.sbuf_tensor("spz", [128, 4096], F32))
    Sp, Sz = spz[0:16, 0:TPC], spz[32:48, 0:TPC]
    cA, cB = spz[64:80, 0:TPC], spz[64:80, TPC:2 * TPC]
    ctp = sc_p2.enter_context(nc.sbuf_tensor("ctp", [128, 1024], BF16))
    c_t_sb = [ctp[:, 0:QT], ctp[:, QT:2 * QT]]

    c_mm, c_tanh, pz_mm, sig = {}, {}, {}, {}
    for qc in range(NQT):
        for dc in range(NDC):
            i = qc * NDC + dc
            par = i % 2

            def cmf(t_, qc=qc, dc=dc, par=par):
                for di in range(NDC):
                    ii = t_.matmul(ps_s[par][:, 0:QT], wsl(w_big2, di, dc),
                                   mq[di][:, qc * QT:(qc + 1) * QT],
                                   start=(di == 0), stop=(di == NDC - 1))
                return ii
            c_mm[i] = P.emit("t", cmf,
                             waits=[("v", v_cb), ("g_wb", gw_wfq), ("v", mq_done),
                                    ("s", c_tanh.get(i - 2, 0))])
            c_tanh[i] = P.emit("s", lambda s, dc=dc, par=par: s.activation(
                c_t_sb[par], ps_s[par][:, 0:QT], AF.Tanh,
                bias=cbias[:, dc:dc + 1]), waits=[("t", c_mm[i])])
            pz_mm[i] = P.emit("t", lambda t_, dc=dc, par=par: t_.matmul(
                ps_c[0][0:64, 0:QT], upz_b[:, dc * 64:(dc + 1) * 64],
                c_t_sb[par], start=(dc == 0), stop=(dc == NDC - 1)),
                waits=[("s", c_tanh[i]), ("s", sig.get(qc - 1, 0)),
                       ("v", v_glo if i == 0 else 0),
                       ("g_upz", gw_upz if i == 0 else 0)])
        sig[qc] = P.emit("s", lambda s, qc=qc: (
            s.activation(Sp[:, qc * QT:(qc + 1) * QT], ps_c[0][0:16, 0:QT],
                         AF.Sigmoid),
            s.activation(Sz[:, qc * QT:(qc + 1) * QT], ps_c[0][32:48, 0:QT],
                         AF.Sigmoid))[-1],
            waits=[("t", pz_mm[qc * NDC + NDC - 1])])
    sig_done = sig[NQT - 1]
    pz_last = pz_mm[NQT * NDC - 1]

    steps = [
        lambda v: v.reciprocal(cA, Sz),
        lambda v: v.tensor_tensor(cA, cA, cA, ALU.mult),
        lambda v: v.tensor_scalar(cA, cA, -1.1920928955078125e-07,
                                  None, ALU.mult),
        lambda v: v.tensor_scalar(cB, Sp, float(S), None, ALU.mult),
        lambda v: v.tensor_copy(C3[:, 0:TPC], cA),
        lambda v: v.tensor_tensor(cA, cA, cB, ALU.mult),
        lambda v: v.tensor_scalar(C3[:, TPC:2 * TPC], cA, -2.0, None,
                                  ALU.mult),
        lambda v: v.tensor_tensor(cA, cA, cB, ALU.mult),
        lambda v: v.tensor_scalar(C3[:, 2 * TPC:3 * TPC], cA, -4.0, None,
                                  ALU.add),
    ]
    vprev = P.emit("v", steps[0], waits=[("s", sig_done)])
    for f in steps[1:]:
        vprev = P.emit("v", f, waits=[("v", vprev)])
    coeff_done = vprev

    sc_p2.close()
    sc_w.close()

    sc_att2 = contextlib.ExitStack()
    v_aug = sc_att2.enter_context(nc.sbuf_tensor("v_aug", [128, NKC * 130],
                                                 BF16))
    kv_w = [sc_att2.enter_context(nc.sbuf_tensor(f"kv{p}", [128, 2048], BF16))
            for p in range(2)]
    e_t = [sc_att2.enter_context(nc.sbuf_tensor(f"e_t{p}", [128, 2 * QT],
                                                BF16)) for p in range(2)]
    rhs_aug = [sc_att2.enter_context(nc.sbuf_tensor(f"rhs{p}", [128, QT],
                                                    BF16)) for p in range(4)]
    rbp = sc_att2.enter_context(nc.sbuf_tensor("rbp", [128, QT], F32))
    rb_sb = [rbp[0:64, :], rbp[64:128, :]]
    bv_rb = sc_att2.enter_context(nc.sbuf_tensor("bv_rb", [1, DM], BF16))
    gw_bv = P.emit("g", lambda g: g.dma_start(bv_rb[:, :], bv_d[:, :]),
                   waits=[("t", pz_last), ("v", coeff_done)], sem="g_bv")

    def vones(v):
        va = v_aug[:, :].rearrange("p (t c) -> p t c", c=130)
        v.memset(va[:, :, 64:65], 1.0)
        return v.memset(va[:, :, 129:130], 1.0)
    v_vones = P.emit("v", vones, waits=[("t", pz_last), ("v", coeff_done)])

    AT = {}
    pair_t_end = {}
    for p in range(NE):
        wpar = p % 2
        cols = slice(p * 128, (p + 1) * 128)
        P.emit("g", lambda g, wpar=wpar, cols=cols: g.dma_start(
            kv_w[wpar][:, 0:1024].rearrange("p (c n) -> p c n", c=8),
            chunked(wk_d[:, cols])),
            waits=[("t", pair_t_end.get(p - 2, pz_last)), ("g_j", gw_j),
                   ("v", coeff_done if p == 0 else 0),
                   ("s", sig_done if p == 0 else 0)], sem=f"g_kv{wpar}")
        kv_gw = P.emit("g", lambda g, wpar=wpar, cols=cols: g.dma_start(
            kv_w[wpar][:, 1024:2048].rearrange("p (c n) -> p c n", c=8),
            chunked(wv_d[:, cols])), sem=f"g_kv{wpar}")

        kev = {}
        for kt in range(8):
            par = kt % 2

            def kmf(t_, kt=kt, par=par, wpar=wpar):
                for di in range(NDC):
                    ii = t_.matmul(ps_s[par][:, 0:QT],
                                   kv_w[wpar][:, di * 128:(di + 1) * 128],
                                   x_fm[di][:, kt * QT:(kt + 1) * QT],
                                   start=(di == 0), stop=(di == NDC - 1))
                return ii
            w = [(f"g_kv{wpar}", kv_gw), ("v", kev.get(kt - 2, 0))]
            if kt == 0:
                w += [("s", AT.get((p - 1, NQT - 1, "rbs"), sig_done)),
                      ("v", AT.get((p - 1, NQT - 1, "ctxe"),
                                   max(coeff_done, mq_done)))]
            tk = P.emit("t", kmf, waits=w)
            kev[kt] = P.emit("v", lambda v, kt=kt, par=par, p=p: (
                v.tensor_scalar(k_augA[0:64, kt * QT:(kt + 1) * QT],
                                ps_s[par][0:64, 0:QT], bk_sb[0:64, p:p + 1],
                                0.125, ALU.add, ALU.mult),
                v.tensor_scalar(k_augB[0:64, kt * QT:(kt + 1) * QT],
                                ps_s[par][64:128, 0:QT], bk_sb[64:128, p:p + 1],
                                0.125, ALU.add, ALU.mult))[-1],
                waits=[("t", tk),
                       ("t", AT.get((p - 1, NQT - 1, "last_sc"), 0)
                        if kt == 0 else 0)])

        t_bv = P.emit("t", lambda t_, p=p: t_.matmul(
            ps_c[0][:, 0:128], ones1_b[:, :], bv_rb[:, p * 128:(p + 1) * 128],
            start=True, stop=True),
            waits=[("v", AT.get((p - 1, NQT - 1, "ctxe"), 0)),
                   ("s", sig_done if p == 0 else 0), ("v", v_init),
                   ("g_bv", gw_bv)])
        v_bv = P.emit("v", lambda v: v.tensor_copy(bvb[:, :],
                                                   ps_c[0][:, 0:128]),
                      waits=[("t", t_bv)])
        vev = {}
        for tt in range(NKC):
            par = tt % 2

            def vmf(t_, tt=tt, par=par, wpar=wpar):
                for di in range(NDC):
                    ii = t_.matmul(ps_s[par][:, QT:QT + 128],
                                   x_fm[di][:, tt * 128:(tt + 1) * 128],
                                   kv_w[wpar][:, 1024 + di * 128:
                                               1024 + (di + 1) * 128],
                                   start=(di == 0), stop=(di == NDC - 1))
                return ii
            tv = P.emit("t", vmf, waits=[("v", vev.get(tt - 2, 0))])
            base = tt * 130
            vev[tt] = P.emit("v", lambda v, base=base, par=par: (
                v.tensor_tensor(v_aug[:, base:base + 64],
                                ps_s[par][:, QT:QT + 64], bvb[:, 0:64],
                                ALU.add),
                v.tensor_tensor(v_aug[:, base + 65:base + 129],
                                ps_s[par][:, QT + 64:QT + 128], bvb[:, 64:128],
                                ALU.add))[-1],
                waits=[("t", tv), ("v", v_bv), ("v", v_vones if tt == 0 else 0),
                       ("t", AT.get((p - 1, NQT - 1, NKC - 1, "ctx"), 0)
                        if tt == 0 else 0)])
        kv_done_v = max(vev[NKC - 1], kev[7])

        for qc in range(NQT):
            for hi in range(2):
                bi = 2 * (qc % 2) + hi
                h = 2 * p + hi
                guard = AT.get((p, qc - 2, "last_sc"),
                               AT.get((p - 1, NQT - 1, "last_sc"), pz_last))
                AT[(p, qc, hi, "q")] = P.emit(
                    "v", lambda v, bi=bi, hi=hi, p=p, qc=qc: v.tensor_copy(
                        rhs_aug[bi][0:64, :],
                        mq[p][hi * 64:(hi + 1) * 64, qc * QT:(qc + 1) * QT]),
                    waits=[("t", guard),
                           ("v", kv_done_v if qc == 0 and hi == 0 else 0)])
                AT[(p, qc, hi, "c3")] = P.emit(
                    "g", lambda g, bi=bi, h=h, qc=qc: g.dma_start(
                        rhs_aug[bi][64:67, 0:QT],
                        C3[h:h + 1, :].rearrange("o (c n) -> o c n", c=3)
                        [:, :, qc * QT:(qc + 1) * QT]),
                    waits=[("v", AT[(p, qc, hi, "q")])], sem=f"g_c3{qc % 2}")
            for kc in range(NKC):
                par = kc % 2
                wsc = [("s", AT.get((p, qc, kc - 2, "exp"), 0)),
                       (f"g_c3{qc % 2}", AT[(p, qc, 1, "c3")]),
                       ("v", kv_done_v)]
                if kc == 0:
                    wsc += [("s", AT.get((p, qc - 1, "rbs"),
                                         AT.get((p - 1, NQT - 1, "rbs"), 0))),
                            ("g_j", gw_j if p == 0 and qc == 0 else 0)]

                def scf(t_, kc=kc, par=par, biA=2 * (qc % 2),
                        biB=2 * (qc % 2) + 1):
                    t_.matmul(ps_s[par][:, 0:QT],
                              k_augA[0:67, kc * 128:(kc + 1) * 128],
                              rhs_aug[biA][0:67, :], start=True, stop=True)
                    return t_.matmul(ps_s[par][:, QT:2 * QT],
                                     k_augB[0:67, kc * 128:(kc + 1) * 128],
                                     rhs_aug[biB][0:67, :], start=True,
                                     stop=True)
                AT[(p, qc, kc, "sc")] = P.emit("t", scf, waits=wsc)
                AT[(p, qc, kc, "exp")] = P.emit(
                    "s", lambda s, kc=kc, par=par: s.activation(
                        e_t[kc % 2][:, :], ps_s[par][:, :], AF.Exp),
                    waits=[("t", AT[(p, qc, kc, "sc")]),
                           ("t", AT.get((p, qc, kc - 2, "ctx"), 0))])

                def ctf(t_, kc=kc):
                    base = kc * 130
                    t_.matmul(ps_c[0][0:65, :], v_aug[:, base:base + 65],
                              e_t[kc % 2][:, 0:QT], start=(kc == 0),
                              stop=(kc == NKC - 1))
                    return t_.matmul(ps_c[1][0:65, :],
                                     v_aug[:, base + 65:base + 130],
                                     e_t[kc % 2][:, QT:2 * QT],
                                     start=(kc == 0), stop=(kc == NKC - 1))
                wct = [("s", AT[(p, qc, kc, "exp")])]
                if kc == 0:
                    wct += [("v", AT.get((p, qc - 1, "ctxe"),
                                         AT.get((p - 1, NQT - 1, "ctxe"), 0))),
                            ("v", v_bv)]
                AT[(p, qc, kc, "ctx")] = P.emit("t", ctf, waits=wct)
            AT[(p, qc, "last_sc")] = AT[(p, qc, NKC - 1, "sc")]
            AT[(p, qc, "rec")] = P.emit("v", lambda v: (
                v.reciprocal(recr_f[0], ps_c[0][64:65, :]),
                v.reciprocal(recr_f[1], ps_c[1][64:65, :]))[-1],
                waits=[("t", AT[(p, qc, NKC - 1, "ctx")])])
            AT[(p, qc, "recb")] = P.emit("v", lambda v: (
                v.tensor_copy(recr[0], recr_f[0]),
                v.tensor_copy(recr[1], recr_f[1]))[-1],
                waits=[("v", AT[(p, qc, "rec")])])
            AT[(p, qc, "rbt")] = P.emit("t", lambda t_: (
                t_.matmul(ps_s[0][0:64, 0:QT], ones1_b[:, 0:64], recr[0],
                          start=True, stop=True),
                t_.matmul(ps_s[0][0:64, QT:2 * QT], ones1_b[:, 0:64], recr[1],
                          start=True, stop=True))[-1],
                waits=[("v", AT[(p, qc, "recb")]),
                       ("s", AT[(p, qc, NKC - 1, "exp")])])
            AT[(p, qc, "rbs")] = P.emit("s", lambda s: (
                s.copy(rb_sb[0], ps_s[0][0:64, 0:QT]),
                s.copy(rb_sb[1], ps_s[0][0:64, QT:2 * QT]))[-1],
                waits=[("t", AT[(p, qc, "rbt")])])
            AT[(p, qc, "ctxe")] = P.emit("v", lambda v, p=p, qc=qc: (
                v.tensor_tensor(ctx_fm[p][0:64, qc * QT:(qc + 1) * QT],
                                ps_c[0][0:64, :], rb_sb[0], ALU.mult),
                v.tensor_tensor(ctx_fm[p][64:128, qc * QT:(qc + 1) * QT],
                                ps_c[1][0:64, :], rb_sb[1], ALU.mult))[-1],
                waits=[("s", AT[(p, qc, "rbs")])])
        pair_t_end[p] = P.cnt["t"]
    att_v = AT[(NE - 1, NQT - 1, "ctxe")]
    att_rbs = AT[(NE - 1, NQT - 1, "rbs")]
    att_t = P.cnt["t"]

    sc_att2.close()
    sc_c3.close()
    sc_kaug.close()
    sc_p4 = contextlib.ExitStack()
    wo_b = sc_p4.enter_context(nc.sbuf_tensor("wo_b", [128, 8 * DM], BF16))
    zbuf_a = sc_p4.enter_context(nc.sbuf_tensor("zbuf_a", [128, 8 * QT], F32))
    mb_a = sc_p4.enter_context(nc.sbuf_tensor("mb_a", [128, QT], F32))
    ivb_a = sc_p4.enter_context(nc.sbuf_tensor("ivb_a", [128, QT], F32))
    rows_a = sc_p4.enter_context(nc.sbuf_tensor("rows_a", [1, 2048], F32))
    zbp_a = [sc_p4.enter_context(nc.sbuf_tensor(f"zbp_a{p}", [128, QT], BF16))
             for p in range(2)]
    zsp_a = [sc_p4.enter_context(nc.sbuf_tensor(f"zsp_a{p}", [128, QT], BF16))
             for p in range(2)]

    gw_wo = P.emit("g", lambda g: g.dma_start(
        wo_b[:, :].rearrange("p (c n) -> p c n", c=8), chunked(wo_d[:, :])),
        waits=[("t", att_t), ("s", att_rbs), ("v", att_v)], sem="g_wb")

    def ln_phase(qc, mode, y_dst, st_, zbuf, mb_sb, invb_sb, rows4,
                 zb_t, zsq_t):
        L = {}
        mrow, srow = rows4[0:1, 0:512], rows4[0:1, 512:1024]
        vrow, vrow2 = rows4[0:1, 1024:1536], rows4[0:1, 1536:2048]
        zev = st_.setdefault("zev", {})
        for dc in range(NDC):
            gi = qc * NDC + dc
            par = gi % 2
            if mode == "attn":
                wmm = [("v", att_v), ("g_wb", gw_wo),
                       ("v", zev.get(gi - 2, 0)),
                       ("s", att_rbs if gi < 2 else 0),
                       ("s", st_.get("prev_bcs", 0) if dc < 2 else 0)]

                def mmf(t_, qc=qc, dc=dc, par=par):
                    for di in range(NDC):
                        ii = t_.matmul(ps_s[par][:, 0:QT], wsl(wo_b, di, dc),
                                       ctx_fm[di][:, qc * QT:(qc + 1) * QT],
                                       start=(di == 0), stop=(di == NDC - 1))
                    return ii
                L[(dc, "mm")] = P.emit("t", mmf, waits=wmm)
                zev[gi] = L[(dc, "z")] = P.emit(
                    "v", lambda v, qc=qc, dc=dc, par=par:
                    v.scalar_tensor_tensor(
                        zbuf[:, dc * QT:(dc + 1) * QT], ps_s[par][:, 0:QT],
                        bo_sb[:, dc:dc + 1],
                        x_fm[dc][:, qc * QT:(qc + 1) * QT], ALU.add, ALU.add),
                    waits=[("t", L[(dc, "mm")])])
            else:
                L[(dc, "z")] = P.emit(
                    "v", lambda v, qc=qc, dc=dc: v.scalar_tensor_tensor(
                        zbuf[:, dc * QT:(dc + 1) * QT],
                        st_["routed"][dc][:, qc * QT:(qc + 1) * QT], 1.0 / 3.0,
                        y1[dc][:, qc * QT:(qc + 1) * QT], ALU.mult, ALU.add),
                    waits=[("v", st_.get("src_v", 0)),
                           ("t", st_.get("prev_ttr", 0))])
            zg = (L.get((dc - 2, "sums"), st_.get("prev_sums_last", 0)
                        if dc < 2 else 0))
            L[(dc, "zb")] = P.emit("v", lambda v, dc=dc: v.tensor_copy(
                zb_t[dc % 2][:, :], zbuf[:, dc * QT:(dc + 1) * QT]),
                waits=[("v", L[(dc, "z")]), ("t", zg)])
            L[(dc, "zsq")] = P.emit("s", lambda s, dc=dc: s.activation(
                zsq_t[dc % 2][:, :], zbuf[:, dc * QT:(dc + 1) * QT],
                AF.Square), waits=[("v", L[(dc, "z")]), ("t", zg)])
            wsum = [("v", L[(dc, "zb")]), ("s", L[(dc, "zsq")])]
            if dc == 0:
                wsum += [("s", st_.get("psc_guard_s", 0)),
                         ("v", st_.get("psc_guard_v", 0))]
            L[(dc, "sums")] = P.emit("t", lambda t_, dc=dc: (
                t_.matmul(ps_c[0][0:1, 0:QT], ones128_b[:, :],
                          zb_t[dc % 2][:, :], start=(dc == 0),
                          stop=(dc == NDC - 1)),
                t_.matmul(ps_c[1][0:1, 0:QT], ones128_b[:, :],
                          zsq_t[dc % 2][:, :], start=(dc == 0),
                          stop=(dc == NDC - 1)))[-1], waits=wsum)
        r0 = P.emit("v", lambda v: v.tensor_scalar(
            mrow, ps_c[0][0:1, 0:QT], 1.0 / DM, None, ALU.mult),
            waits=[("t", L[(NDC - 1, "sums")])])
        r1 = P.emit("v", lambda v: v.tensor_scalar(
            vrow, ps_c[1][0:1, 0:QT], 1.0 / DM, None, ALU.mult),
            waits=[("v", r0)])
        r2 = P.emit("v", lambda v: v.tensor_tensor(vrow2, mrow, mrow,
                                                   ALU.mult), waits=[("v", r1)])
        r3 = P.emit("v", lambda v: v.tensor_tensor(vrow, vrow, vrow2,
                                                   ALU.subtract),
                    waits=[("v", r2)])
        r4 = P.emit("s", lambda s: s.activation(srow, vrow, AF.Sqrt,
                                                bias=1e-5), waits=[("v", r3)])
        r5 = P.emit("v", lambda v: v.reciprocal(vrow2, srow),
                    waits=[("s", r4)])
        r6 = P.emit("v", lambda v: (v.tensor_copy(mrow_b, mrow),
                                    v.tensor_copy(irow_b, vrow2))[-1],
                    waits=[("v", r5)])
        wbc = [("v", r6), ("v", L[(NDC - 1, "z")]),
               ("s", st_.get("pss_guard_s", 0)),
               ("v", st_.get("pss_guard_v", 0))]
        bct = P.emit("t", lambda t_: (
            t_.matmul(ps_s[0][:, 0:QT], ones1_b[:, :], mrow_b, start=True,
                      stop=True),
            t_.matmul(ps_s[1][:, 0:QT], ones1_b[:, :], irow_b, start=True,
                      stop=True))[-1], waits=wbc)
        bcs = P.emit("s", lambda s: (s.copy(mb_sb[:, :], ps_s[0][:, 0:QT]),
                                     s.copy(invb_sb[:, :],
                                            ps_s[1][:, 0:QT]))[-1],
                     waits=[("t", bct)])
        subv = P.emit("v", lambda v: [v.tensor_tensor(
            zbuf[:, dc * QT:(dc + 1) * QT], zbuf[:, dc * QT:(dc + 1) * QT],
            mb_sb[:, :], ALU.subtract) for dc in range(NDC)][-1],
            waits=[("s", bcs)])
        if y_dst is not None:
            fin = P.emit("v", lambda v, qc=qc: [v.tensor_tensor(
                y_dst[dc][:, qc * QT:(qc + 1) * QT],
                zbuf[:, dc * QT:(dc + 1) * QT], invb_sb[:, :], ALU.mult)
                for dc in range(NDC)][-1], waits=[("v", subv)])
        else:
            fin = P.emit("v", lambda v: [v.tensor_tensor(
                zbuf[:, dc * QT:(dc + 1) * QT], zbuf[:, dc * QT:(dc + 1) * QT],
                invb_sb[:, :], ALU.mult) for dc in range(NDC)][-1],
                waits=[("v", subv)])
        st_["prev_sums_last"] = L[(NDC - 1, "sums")]
        st_["prev_bcs"] = bcs
        st_["pss_guard_s"] = bcs
        st_["psc_guard_v"] = r1
        L["fin"], L["bcs"], L["r1"] = fin, bcs, r1
        return L

    st1 = {"psc_guard_s": sig_done}
    gate_ws = {}
    gw_wr8 = 0
    for qc in range(NQT):
        L1 = ln_phase(qc, "attn", y1, st1, zbuf_a, mb_a, ivb_a, rows_a,
                      zbp_a, zsp_a)
        for tt in range(4):
            tok = qc * QT + tt * 128

            def gmf(t_, qc=qc, tt=tt, tok=tok):
                for di in range(NDC):
                    ii = t_.matmul(ps_c[0][:, 8 * tt:8 * tt + 8],
                                   y1[di][:, tok:tok + 128],
                                   wg_b[:, di * 8:(di + 1) * 8],
                                   start=(di == 0), stop=(di == NDC - 1))
                return ii
            gt_t = P.emit("t", gmf, waits=[("v", L1["fin"]),
                                           ("s", gate_ws.get((qc, tt - 1), 0))])
            gs = P.emit("s", lambda s, tt=tt: s.activation(
                gsig[:, :], ps_c[0][:, 8 * tt:8 * tt + 8], AF.Sigmoid),
                waits=[("t", gt_t)])
            st1["psc_guard_s"] = gs
            chain = [
                lambda v: v.reduce_max(gt[:, 0:1], gsig[:, :], axis=AX.X),
                lambda v: v.tensor_scalar(gt[:, 8:16], gsig[:, :], gt[:, 0:1],
                                          None, ALU.is_equal),
                lambda v: v.scalar_tensor_tensor(gt[:, 16:24], gt[:, 8:16],
                                                 -1e30, gsig[:, :], ALU.mult,
                                                 ALU.add),
                lambda v: v.reduce_max(gt[:, 1:2], gt[:, 16:24], axis=AX.X),
                lambda v: v.tensor_scalar(gt[:, 24:32], gt[:, 16:24],
                                          gt[:, 1:2], None, ALU.is_equal),
                lambda v: v.tensor_tensor(gt[:, 24:32], gt[:, 24:32],
                                          gt[:, 8:16], ALU.add),
                lambda v: v.tensor_tensor(gt[:, 2:3], gt[:, 0:1], gt[:, 1:2],
                                          ALU.add),
                lambda v: v.reciprocal(gt[:, 3:4], gt[:, 2:3]),
                lambda v: v.tensor_tensor(gt[:, 32:40], gsig[:, :],
                                          gt[:, 24:32], ALU.mult),
                lambda v: v.tensor_scalar(wtok[:, :], gt[:, 32:40], gt[:, 3:4],
                                          None, ALU.mult),
            ]
            vv = P.emit("v", chain[0], waits=[("s", gs)])
            for f in chain[1:]:
                vv = P.emit("v", f, waits=[("v", vv)])
            wts = 0
            for rnd in range(2):
                def wtf(t_, rnd=rnd):
                    for e4 in range(4):
                        ii = t_.transpose(
                            ps_s[0][0:1, QT + e4 * 128:QT + (e4 + 1) * 128],
                            wtok[:, rnd * 4 + e4:rnd * 4 + e4 + 1],
                            eye_f[:, :])
                    return ii
                wt_t = P.emit("t", wtf, waits=[("v", vv), ("s", wts),
                                               ("y_eye", ld_eye)])
                wts = P.emit("s", lambda s, rnd=rnd: s.copy(
                    wstg[0:1, 0:512], ps_s[0][0:1, QT:QT + 512]),
                    waits=[("t", wt_t), ("g_wr8", gw_wr8)])
                for e4 in range(4):
                    gw_wr8 = P.emit("g", lambda g, rnd=rnd, tok=tok, e4=e4:
                                    g.dma_start(
                        wr8[rnd * 4 + e4:rnd * 4 + e4 + 1, tok:tok + 128],
                        wstg[0:1, e4 * 128:(e4 + 1) * 128]),
                        waits=[("s", wts)] if e4 == 0 else [], sem="g_wr8")
            gate_ws[(qc, tt)] = wts
    wr8_done = gw_wr8
    p4_v = P.cnt["v"]
    p4_s = P.cnt["s"]
    p4_t = P.cnt["t"]

    sc_p4.close()
    sc_ctx.close()
    sc_x.close()

    sc_r = contextlib.ExitStack()
    routed = [sc_r.enter_context(nc.sbuf_tensor(f"rt{c}", [128, TPC], BF16))
              for c in range(NDC)]
    sc_ffn = contextlib.ExitStack()
    we1f = sc_ffn.enter_context(nc.sbuf_tensor("we1f", [128, 8 * FF], BF16))
    we3f = sc_ffn.enter_context(nc.sbuf_tensor("we3f", [128, 8 * FF], BF16))
    we2f = sc_ffn.enter_context(nc.sbuf_tensor("we2f", [128, NFC * DM], BF16))
    h_sb = sc_ffn.enter_context(nc.sbuf_tensor("h_sb", [128, NFC * QT], BF16))
    sa_sb = [sc_ffn.enter_context(nc.sbuf_tensor(f"sa{p}", [128, QT], BF16))
             for p in range(2)]
    h1_sb = [sc_ffn.enter_context(nc.sbuf_tensor(f"h1{p}", [128, QT], F32))
             for p in range(2)]
    wbb = sc_ffn.enter_context(nc.sbuf_tensor("wbb", [128, QT], BF16))

    FW = {}
    ab_last, o_last, o_racc_last = {}, {}, {}
    ffn_s_last, ffn_h1_last = 0, 0
    space_guard = [("v", p4_v), ("t", p4_t), ("s", p4_s)]
    stg_dma = {}
    for ei in range(NE + 1):
        e = ei - 1
        s1d = ws1_d[:, :] if e < 0 else we1_d[e, :, :]
        s3d = ws3_d[:, :] if e < 0 else we3_d[e, :, :]
        s2d = ws2_d[:, :] if e < 0 else we2_d[e, :, :]
        FW[(ei, "w1")] = P.emit("g", lambda g, s1d=s1d: g.dma_start(
            we1f[:, :].rearrange("p (c n) -> p c n", c=8), chunked(s1d)),
            waits=[("t", ab_last.get(ei - 1, 0))] +
                  (space_guard if ei == 0 else []), sem="g_we13")
        FW[(ei, "w3")] = P.emit("g", lambda g, s3d=s3d: g.dma_start(
            we3f[:, :].rearrange("p (c n) -> p c n", c=8), chunked(s3d)),
            sem="g_we13")
        FW[(ei, "w2")] = P.emit("g", lambda g, s2d=s2d: g.dma_start(
            we2f[:, :].rearrange("p (c n) -> p c n", c=16),
            s2d.rearrange("(c p) n -> p c n", p=128)),
            waits=[("t", o_last.get(ei - 1, 0))], sem="g_we2")
        ht, ot, sls = {}, {}, {}
        for tt in range(NQT):
            if e >= 0:
                sd = P.emit("g", lambda g, e=e, tt=tt: g.dma_start(
                    wstg[0:1, (tt % 2) * 512:(tt % 2) * 512 + 512],
                    wr8[e:e + 1, tt * QT:(tt + 1) * QT]),
                    waits=[("g_wr8", wr8_done),
                           ("t", stg_dma.get(tt % 2, 0))],
                    sem=f"g_st{tt % 2}")
                bw = P.emit("t", lambda t_, tt=tt: t_.matmul(
                    ps_c[0][:, 0:QT], ones1_b[:, :],
                    wstg[0:1, (tt % 2) * 512:(tt % 2) * 512 + 512],
                    start=True, stop=True),
                    waits=[(f"g_st{tt % 2}", sd),
                           ("v", ot.get((tt - 1, "racc"),
                                        o_racc_last.get(ei - 1, 0)))])
                stg_dma[tt % 2] = bw
                bws = P.emit("s", lambda s: s.copy(wbb[:, :],
                                                   ps_c[0][:, 0:QT]),
                             waits=[("t", bw)])
            else:
                bws = 0
            for fc in range(NFC):
                par = fc % 2

                def abf(t_, fc=fc, par=par, tt=tt):
                    for di in range(NDC):
                        t_.matmul(ps_s[par][:, 0:QT],
                                  we1f[:, di * FF + fc * 128:
                                       di * FF + (fc + 1) * 128],
                                  y1[di][:, tt * QT:(tt + 1) * QT],
                                  start=(di == 0), stop=(di == NDC - 1))
                    for di in range(NDC):
                        ii = t_.matmul(ps_s[par][:, QT:2 * QT],
                                       we3f[:, di * FF + fc * 128:
                                            di * FF + (fc + 1) * 128],
                                       y1[di][:, tt * QT:(tt + 1) * QT],
                                       start=(di == 0), stop=(di == NDC - 1))
                    return ii
                w = [("g_we13", FW[(ei, "w3")]),
                     ("v", ht.get((tt, fc - 2), 0)),
                     ("s", sls.get((tt, fc - 2), 0))]
                ab = P.emit("t", abf, waits=w)
                sls[(tt, fc)] = sl = P.emit(
                    "s", lambda s, fc=fc, par=par: s.activation(
                        sa_sb[fc % 2][:, :], ps_s[par][:, 0:QT], AF.Silu),
                    waits=[("t", ab)])
                ffn_s_last = sl
                h1 = P.emit("v", lambda v, fc=fc, par=par: v.tensor_tensor(
                    h1_sb[fc % 2][:, :], sa_sb[fc % 2][:, :],
                    ps_s[par][:, QT:2 * QT], ALU.mult), waits=[("s", sl)])
                ffn_h1_last = h1
                wh = [("v", h1), ("t", ot.get((tt - 1, "last"),
                                              o_last.get(ei - 1, 0)))]
                if e >= 0:
                    ht[(tt, fc)] = P.emit(
                        "v", lambda v, fc=fc: v.tensor_tensor(
                            h_sb[:, fc * QT:(fc + 1) * QT],
                            h1_sb[fc % 2][:, :], wbb[:, :], ALU.mult),
                        waits=wh + [("s", bws)])
                else:
                    ht[(tt, fc)] = P.emit(
                        "v", lambda v, fc=fc: v.tensor_copy(
                            h_sb[:, fc * QT:(fc + 1) * QT],
                            h1_sb[fc % 2][:, :]), waits=wh)
            if tt == NQT - 1:
                ab_last[ei] = P.cnt["t"]
            for dc in range(NDC):
                par = dc % 2

                def omf(t_, dc=dc, par=par):
                    for fc in range(NFC):
                        ii = t_.matmul(ps_c[par][:, 0:QT],
                                       we2f[:, fc * DM + dc * 128:
                                            fc * DM + (dc + 1) * 128],
                                       h_sb[:, fc * QT:(fc + 1) * QT],
                                       start=(fc == 0), stop=(fc == NFC - 1))
                    return ii
                w = [("v", ht[(tt, NFC - 1)]), ("g_we2", FW[(ei, "w2")]),
                     ("v", ot.get((tt, dc - 2, "racc"), 0))]
                if dc == 0 and e >= 0:
                    w.append(("s", bws))
                if dc < 2:
                    w.append(("v", ot.get((tt - 1, "racc"),
                                          o_racc_last.get(ei - 1, 0))))
                ot[(tt, dc, "t")] = P.emit("t", omf, waits=w)
                if ei == 0:
                    ot[(tt, dc, "racc")] = P.emit(
                        "v", lambda v, dc=dc, par=par, tt=tt: v.tensor_copy(
                            routed[dc][:, tt * QT:(tt + 1) * QT],
                            ps_c[par][:, 0:QT]),
                        waits=[("t", ot[(tt, dc, "t")])])
                else:
                    ot[(tt, dc, "racc")] = P.emit(
                        "v", lambda v, dc=dc, par=par, tt=tt: v.tensor_tensor(
                            routed[dc][:, tt * QT:(tt + 1) * QT],
                            routed[dc][:, tt * QT:(tt + 1) * QT],
                            ps_c[par][:, 0:QT], ALU.add),
                        waits=[("t", ot[(tt, dc, "t")])])
            ot[(tt, "last")] = ot[(tt, NDC - 1, "t")]
            ot[(tt, "racc")] = ot[(tt, NDC - 1, "racc")]
        o_last[ei] = ot[(NQT - 1, "last")]
        o_racc_last[ei] = ot[(NQT - 1, "racc")]
    ffn_done_v = o_racc_last[NE]
    p5_t = P.cnt["t"]

    sc_ffn.close()

    sc_p6 = contextlib.ExitStack()
    zbuf_b = sc_p6.enter_context(nc.sbuf_tensor("zbuf_b", [128, 8 * QT], F32))
    mb_b = sc_p6.enter_context(nc.sbuf_tensor("mb_b", [128, QT], F32))
    ivb_b = sc_p6.enter_context(nc.sbuf_tensor("ivb_b", [128, QT], F32))
    rows_b = sc_p6.enter_context(nc.sbuf_tensor("rows_b", [1, 2048], F32))
    zbp_b = [sc_p6.enter_context(nc.sbuf_tensor(f"zbp_b{p}", [128, QT], BF16))
             for p in range(2)]
    zsp_b = [sc_p6.enter_context(nc.sbuf_tensor(f"zsp_b{p}", [128, QT], BF16))
             for p in range(2)]
    stage_b = [sc_p6.enter_context(nc.sbuf_tensor(f"ost{p}", [128, DM], F32))
               for p in range(2)]

    st2 = {"src_v": ffn_done_v, "psc_guard_v": ffn_done_v,
           "pss_guard_s": ffn_s_last, "pss_guard_v": ffn_h1_last,
           "prev_ttr": p5_t, "routed": routed}
    ST = {}
    prev_st = {0: 0, 1: 0}
    for qc in range(NQT):
        L2 = ln_phase(qc, "ffn", None, st2, zbuf_b, mb_b, ivb_b, rows_b,
                      zbp_b, zsp_b)
        last_ttr = 0
        for tt in range(4):
            par = tt % 2
            tok = qc * QT + tt * 128

            def trf(t_, tt=tt):
                for dc in range(NDC):
                    ii = t_.transpose(
                        ps_c[dc // 4][:, (dc % 4) * 128:(dc % 4 + 1) * 128],
                        zbuf_b[:, dc * QT + tt * 128:dc * QT + (tt + 1) * 128],
                        eye_f[:, :])
                return ii
            ttr = P.emit("t", trf, waits=[("v", L2["fin"]),
                                          ("v", ST.get((qc, tt - 1, "ev"), 0)),
                                          ("v", L2["r1"] if tt == 0 else 0)])
            last_ttr = ttr
            ST[(qc, tt, "ev")] = P.emit("v", lambda v, par=par: [
                v.tensor_copy(stage_b[par][:, dc * 128:(dc + 1) * 128],
                              ps_c[dc // 4][:, (dc % 4) * 128:
                                            (dc % 4 + 1) * 128])
                for dc in range(NDC)][-1],
                waits=[("t", ttr), (f"y_st{par}", prev_st[par])])
            ST[(qc, tt, "st")] = P.emit(
                "y", lambda y, par=par, tok=tok: y.dma_start(
                    out_d[tok:tok + 128, :], stage_b[par][:, :]),
                waits=[("v", ST[(qc, tt, "ev")])], sem=f"y_st{par}")
            prev_st[par] = ST[(qc, tt, "st")]
        st2["prev_ttr"] = last_ttr
        st2["psc_guard_v"] = ST[(qc, 3, "ev")]
    P.emit("y", lambda y: y.nop(),
           waits=[("y_st0", ST[(NQT - 1, 2, "st")]),
                  ("y_st1", ST[(NQT - 1, 3, "st")])], inc=False)

    P.finalize(nc, sems)
    sc_p6.close()
    sc_r.close()
    stk.close()
    return nc


def _host_inputs(inputs):
    x = np.ascontiguousarray(np.asarray(inputs["L0_attn_hidden_states"],
                                        np.float32))
    up = np.asarray(inputs["L0_attn_up"], np.float32).reshape(-1)
    uz = np.asarray(inputs["L0_attn_uz"], np.float32).reshape(-1)
    upz = np.zeros((128, 512), np.float32)
    r = np.arange(128)
    for dc in range(8):
        h = 2 * dc + r // 64
        upz[r, dc * 64 + h] = up[dc * 128 + r]
        upz[r, dc * 64 + 32 + h] = uz[dc * 128 + r]
    wg = np.asarray(inputs["L0_ffn_Wg"], np.float32)
    wg_m = np.ascontiguousarray(wg.reshape(8, 128, 8).transpose(1, 0, 2)
                                .reshape(128, 64))

    def colmat(name):
        a = np.asarray(inputs[name], np.float32).reshape(-1)
        return np.ascontiguousarray(a.reshape(8, 128).T)

    def arr(name):
        return np.ascontiguousarray(np.asarray(inputs[name], np.float32))

    shared = {
        "eye": np.eye(128, dtype=np.float32),
        "wq": arr("L0_attn_wq"), "wk": arr("L0_attn_wk"),
        "wv": arr("L0_attn_wv"), "wo": arr("L0_attn_wo"),
        "wfq": arr("L0_attn_wfq"), "wfg": arr("L0_attn_wfg"),
        "bq_m": colmat("L0_attn_bq"), "bk_m": colmat("L0_attn_bk"),
        "bo_m": colmat("L0_attn_bo"), "bfq_m": colmat("L0_attn_bfq"),
        "bfg_m": colmat("L0_attn_bfg"),
        "bv_r": np.ascontiguousarray(
            np.asarray(inputs["L0_attn_bv"], np.float32).reshape(1, -1)),
        "upz_m": upz, "wg_m": wg_m,
        "ws1": arr("L0_ffn_Ws1"), "ws3": arr("L0_ffn_Ws3"),
        "ws2": arr("L0_ffn_Ws2"), "we1": arr("L0_ffn_We1"),
        "we3": arr("L0_ffn_We3"), "we2": arr("L0_ffn_We2"),
    }
    in_maps = []
    for c in range(8):
        b, off = c // 2, (c % 2) * TPC
        m = dict(shared)
        m["xr"] = np.ascontiguousarray(
            np.concatenate([x[b, off:], x[b, :off]], axis=0))
        m["jv"] = np.ascontiguousarray(
            np.concatenate([np.arange(off, S), np.arange(0, off)])
            .astype(np.float32).reshape(1, S))
        in_maps.append(m)
    return in_maps


_NC_CACHE = {}


def kernel(**inputs):
    if "nc" not in _NC_CACHE:
        _NC_CACHE["nc"] = build_full()
    nc = _NC_CACHE["nc"]
    in_maps = _host_inputs(inputs)
    res = run_bass_kernel_spmd(nc, in_maps, core_ids=list(range(8)))
    out = np.empty((B, S, DM), np.float32)
    for c in range(8):
        b, off = c // 2, (c % 2) * TPC
        out[b, off:off + TPC] = res.results[c]["out"]
    return out


# version 03
# speedup vs baseline: 1.2823x; 1.2823x over previous
import contextlib
import numpy as np
import concourse.bass as bass
import concourse.mybir as mybir
from concourse.bass_utils import run_bass_kernel_spmd

F32 = mybir.dt.float32
BF16 = mybir.dt.bfloat16
AF = mybir.ActivationFunctionType
ALU = mybir.AluOpType
AX = mybir.AxisListType

B, S, DM, H, D = 4, 4096, 1024, 16, 64
TPC, FF, NE, QT = 2048, 2048, 8, 512
NQT, NKC, NDC, NFC, NXT = TPC // QT, S // 128, DM // 128, FF // 128, S // 128

ENGS = ("t", "s", "v", "g", "y")


class Prog:

    def __init__(self):
        self.recs = {e: [] for e in ENGS}
        self.cnt = {}

    def emit(self, eng, fn, waits=(), inc=True, sem=None):
        semn = sem or eng
        inc_k = 16 if eng in ("g", "y") else 1
        ws = [(e, v) for (e, v) in waits if v and v > 0]
        self.recs[eng].append((fn, ws, semn if inc else None, inc_k))
        if inc:
            self.cnt[semn] = self.cnt.get(semn, 0) + inc_k
        return self.cnt.get(semn, 0)

    def finalize(self, nc, sems):
        def run(eng_obj, name):
            lw = {}
            for fn, ws, semn, inc_k in self.recs[name]:
                for se, val in ws:
                    if val > lw.get(se, 0):
                        eng_obj.wait_ge(sems[se], val)
                        lw[se] = val
                inst = fn(eng_obj)
                if semn is not None:
                    inst.then_inc(sems[semn], inc_k)

        with nc.Block() as block:
            block.tensor(lambda e: run(e, "t"))
            block.scalar(lambda e: run(e, "s"))
            block.vector(lambda e: run(e, "v"))
            block.gpsimd(lambda e: run(e, "g"))
            block.sync(lambda e: run(e, "y"))


def build_full():
    nc = bass.Bass("TRN2", target_bir_lowering=False)

    dt_ = nc.dram_tensor
    xr = dt_("xr", [S, DM], F32, kind="ExternalInput")
    jv_d = dt_("jv", [1, S], F32, kind="ExternalInput")
    eye_d = dt_("eye", [128, 128], F32, kind="ExternalInput")
    wq_d = dt_("wq", [DM, DM], F32, kind="ExternalInput")
    wk_d = dt_("wk", [DM, DM], F32, kind="ExternalInput")
    wv_d = dt_("wv", [DM, DM], F32, kind="ExternalInput")
    wo_d = dt_("wo", [DM, DM], F32, kind="ExternalInput")
    wfq_d = dt_("wfq", [DM, DM], F32, kind="ExternalInput")
    wfg_d = dt_("wfg", [DM, DM], F32, kind="ExternalInput")
    bq_d = dt_("bq_m", [128, 8], F32, kind="ExternalInput")
    bk_d = dt_("bk_m", [128, 8], F32, kind="ExternalInput")
    bo_d = dt_("bo_m", [128, 8], F32, kind="ExternalInput")
    bfq_d = dt_("bfq_m", [128, 8], F32, kind="ExternalInput")
    bfg_d = dt_("bfg_m", [128, 8], F32, kind="ExternalInput")
    bv_d = dt_("bv_r", [1, DM], F32, kind="ExternalInput")
    upz_d = dt_("upz_m", [128, 512], F32, kind="ExternalInput")
    wg_d = dt_("wg_m", [128, 64], F32, kind="ExternalInput")
    ws1_d = dt_("ws1", [DM, FF], F32, kind="ExternalInput")
    ws3_d = dt_("ws3", [DM, FF], F32, kind="ExternalInput")
    ws2_d = dt_("ws2", [FF, DM], F32, kind="ExternalInput")
    we1_d = dt_("we1", [NE, DM, FF], F32, kind="ExternalInput")
    we3_d = dt_("we3", [NE, DM, FF], F32, kind="ExternalInput")
    we2_d = dt_("we2", [NE, FF, DM], F32, kind="ExternalInput")
    out_d = dt_("out", [TPC, DM], F32, kind="ExternalOutput")

    al = nc.alloc_sbuf_tensor
    eye_f = al("eye_f", [128, 128], F32)
    eye_b = al("eye_b", [128, 128], BF16)
    ones1_b = al("ones1_b", [1, 128], BF16)
    ones128_b = al("ones128_b", [128, 1], BF16)
    cpool = al("cpool", [128, 2], F32)
    bq_sb = al("bq_sb", [128, 8], F32)
    bk_sb = al("bk_sb", [128, 8], F32)
    bo_sb = al("bo_sb", [128, 8], F32)
    bfq_sb = al("bfq_sb", [128, 8], F32)
    bfg_sb = al("bfg_sb", [128, 8], F32)
    cbias = al("cbias", [128, 8], F32)
    bvb = al("bvb", [128, 128], F32)
    wg_b = al("wg_b", [128, 64], BF16)
    xm_sb = al("xm_sb", [128, 8], F32)
    xm_b = al("xm_b", [128, 8], BF16)
    glo_sb = al("glo_sb", [128, 8], F32)
    glo_b = al("glo_b", [128, 8], BF16)
    rowsF = al("rowsF", [1, 1024], F32)
    recr_f = [rowsF[0:1, 0:512], rowsF[0:1, 512:1024]]
    rowsB = al("rowsB", [1, 2048], BF16)
    mrow_b, irow_b = rowsB[0:1, 0:512], rowsB[0:1, 512:1024]
    recr = [rowsB[0:1, 1024:1536], rowsB[0:1, 1536:2048]]
    gsig = al("gsig", [128, 8], F32)
    gt = al("gt", [128, 40], F32)
    wtok = al("wtok", [128, 8], F32)
    wr8 = al("wr8", [8, TPC], BF16)
    wstg = al("wstg", [1, 1024], BF16)
    mq = [al(f"mq{c}", [128, TPC], BF16) for c in range(NDC)]
    y1 = mq

    ap_ = nc.alloc_psum_tensor
    ps_s = [ap_(f"ps_s{p}", [128, 2 * QT], F32) for p in range(2)]
    ps_c = [ap_(f"ps_c{i}", [128, QT], F32) for i in range(2)]
    ps_tr = [ap_(f"ps_tr{p}", [128, 1024], BF16) for p in range(2)]

    stk = contextlib.ExitStack()
    STREAMS = ["t", "s", "v", "y_jv", "y_eye", "y_x0", "y_x1", "y_st0",
               "y_st1", "g_sm", "g_j", "g_wb", "g_upz", "g_bv", "g_kv0",
               "g_kv1", "g_c30", "g_c31", "g_wr8", "g_st0", "g_st1",
               "g_we13", "g_we2"]
    sems = {e: stk.enter_context(nc.semaphore(f"sem_{e}")) for e in STREAMS}
    P = Prog()
    nc.const_aps.register(F32, 1e-5, cpool[:, 0:1])

    def chunked(dr, c=8):
        return dr.rearrange("(c p) n -> p c n", p=128)

    def wsl(buf, di, dc):
        return buf[:, di * DM + dc * 128: di * DM + (dc + 1) * 128]

    sc_x = contextlib.ExitStack()
    x_fm = [sc_x.enter_context(nc.sbuf_tensor(f"x_fm{c}", [128, S], BF16))
            for c in range(NDC)]
    sc_ctx = contextlib.ExitStack()
    ctx_fm = [sc_ctx.enter_context(nc.sbuf_tensor(f"ctx{c}", [128, TPC], BF16))
              for c in range(NDC)]
    sc_kaug = contextlib.ExitStack()
    k_augA = sc_kaug.enter_context(nc.sbuf_tensor("k_augA", [128, S], BF16))
    k_augB = sc_kaug.enter_context(nc.sbuf_tensor("k_augB", [128, S], BF16))

    sc_p0 = contextlib.ExitStack()
    poolF = sc_p0.enter_context(nc.sbuf_tensor("poolF", [1, S], F32))
    poolB = sc_p0.enter_context(nc.sbuf_tensor("poolB", [65, S], BF16))
    stage_a = [sc_p0.enter_context(nc.sbuf_tensor(f"stg{p}", [128, DM], F32))
               for p in range(2)]
    xb_t = [sc_p0.enter_context(nc.sbuf_tensor(f"xb{p}", [128, DM], BF16))
            for p in range(2)]
    jv_sb = poolF[0:1, :]
    jv_b = poolB[0:1, :]
    jsq_b = poolB[32:33, :]
    onesrow_b = poolB[64:65, :]

    ld_jv = P.emit("y", lambda y: y.dma_start(jv_sb, jv_d[:, :]), sem="y_jv")
    ld_eye = P.emit("y", lambda y: y.dma_start(eye_f[:, :], eye_d[:, :]), sem="y_eye")
    gw_small = 0
    for dst, src in [
        (eye_b[:, :], eye_d[:, :]), (bq_sb[:, :], bq_d[:, :]),
        (bk_sb[:, :], bk_d[:, :]), (bo_sb[:, :], bo_d[:, :]),
        (bfq_sb[:, :], bfq_d[:, :]), (bfg_sb[:, :], bfg_d[:, :]),
        (wg_b[:, :], wg_d[:, :]),
    ]:
        gw_small = P.emit("g", lambda g, d=dst, s=src: g.dma_start(d, s), sem="g_sm")

    def memsets(v):
        v.memset(cpool[:, 0:1], 1e-5)
        v.memset(ones1_b[:, :], 1.0)
        v.memset(ones128_b[:, :], 1.0)
        return v.memset(onesrow_b, 1.0)
    v_init = P.emit("v", memsets)
    v_jsq = P.emit("v", lambda v: v.tensor_tensor(jsq_b, jv_sb, jv_sb,
                                                  ALU.mult),
                   waits=[("y_jv", ld_jv)])
    v_jb = P.emit("v", lambda v: v.tensor_copy(jv_b, jv_sb),
                  waits=[("v", v_jsq)])
    gw_j = 0
    for ka in (k_augA, k_augB):
        P.emit("g", lambda g, ka=ka: g.dma_start(ka[64:65, :], jsq_b),
               waits=[("v", v_jb), ("v", v_init)], sem="g_j")
        P.emit("g", lambda g, ka=ka: g.dma_start(ka[65:66, :], jv_b),
               sem="g_j")
        gw_j = P.emit("g", lambda g, ka=ka: g.dma_start(ka[66:67, :],
                                                        onesrow_b), sem="g_j")

    x_ld, x_cv, x_tr, x_ev = {}, {}, {}, {}
    for t in range(NXT):
        par = t % 2
        x_ld[t] = P.emit("y", lambda y, t=t, par=par: y.dma_start(
            stage_a[par][:, :], xr[t * 128:(t + 1) * 128, :]),
            waits=[("v", x_cv.get(t - 2, 0))], sem=f"y_x{par}")
        x_cv[t] = P.emit("v", lambda v, par=par: v.tensor_copy(
            xb_t[par][:, :], stage_a[par][:, :]),
            waits=[(f"y_x{par}", x_ld[t]), ("t", x_tr.get(t - 2, 0)),
                   ("v", v_init)])

        def trf(t_, par=par):
            for dc in range(NDC):
                ii = t_.transpose(ps_tr[par][:, dc * 128:(dc + 1) * 128],
                                  xb_t[par][:, dc * 128:(dc + 1) * 128],
                                  eye_b[:, :])
            return ii
        x_tr[t] = P.emit("t", trf, waits=[("v", x_cv[t]),
                                          ("g_sm", gw_small),
                                          ("v", x_ev.get(t - 2, 0))])

        def evf(v, t=t, par=par):
            for dc in range(NDC):
                ii = v.tensor_copy(x_fm[dc][:, t * 128:(t + 1) * 128],
                                   ps_tr[par][:, dc * 128:(dc + 1) * 128])
            return ii
        x_ev[t] = P.emit("v", evf, waits=[("t", x_tr[t])])
    x_done = x_ev[NXT - 1]
    xtr_last = x_tr[NXT - 1]

    sc_p0.close()

    sc_c3 = contextlib.ExitStack()
    C3 = sc_c3.enter_context(nc.sbuf_tensor("C3", [16, 3 * TPC], BF16))
    sc_w = contextlib.ExitStack()
    w_big1 = sc_w.enter_context(nc.sbuf_tensor("w_big1", [128, 8 * DM], BF16))
    w_big2 = w_big1

    gw_wq = P.emit("g", lambda g: g.dma_start(
        w_big1[:, :].rearrange("p (c n) -> p c n", c=8), chunked(wq_d[:, :])),
        waits=[("t", xtr_last), ("v", x_done), ("g_j", gw_j)], sem="g_wb")

    mq_mm, mq_ev = {}, {}
    for qc in range(NQT):
        for dc in range(NDC):
            i = qc * NDC + dc
            par = i % 2

            def mmf(t_, qc=qc, dc=dc, par=par):
                for di in range(NDC):
                    ii = t_.matmul(ps_s[par][:, 0:QT], wsl(w_big1, di, dc),
                                   x_fm[di][:, qc * QT:(qc + 1) * QT],
                                   start=(di == 0), stop=(di == NDC - 1))
                return ii
            mq_mm[i] = P.emit("t", mmf, waits=[("v", x_done), ("g_wb", gw_wq),
                                               ("v", mq_ev.get(i - 2, 0))])
            mq_ev[i] = P.emit("v", lambda v, qc=qc, dc=dc, par=par:
                              v.tensor_scalar(mq[dc][:, qc * QT:(qc + 1) * QT],
                                              ps_s[par][:, 0:QT],
                                              bq_sb[:, dc:dc + 1], None,
                                              ALU.add),
                              waits=[("t", mq_mm[i])])
    mq_done = mq_ev[NQT * NDC - 1]

    def xmeanf(v):
        for dc in range(NDC):
            ii = v.reduce_sum(xm_sb[:, dc:dc + 1], x_fm[dc][:, :], axis=AX.X)
        return ii
    v_xm = P.emit("v", xmeanf, waits=[("v", x_done)])
    v_xmb = P.emit("v", lambda v: v.tensor_scalar(
        xm_b[:, :], xm_sb[:, :], 1.0 / S, None, ALU.mult), waits=[("v", v_xm)])

    def glof(t_):
        for dc in range(NDC):
            for di in range(NDC):
                ii = t_.matmul(ps_c[0][:, dc:dc + 1], wsl(w_big1, di, dc),
                               xm_b[:, di:di + 1], start=(di == 0),
                               stop=(di == NDC - 1))
        return ii
    t_glo = P.emit("t", glof, waits=[("v", v_xmb), ("g_wb", gw_wq)])
    v_glo = P.emit("v", lambda v: v.tensor_tensor(
        glo_sb[:, :], ps_c[0][:, 0:8], bq_sb[:, :], ALU.add),
        waits=[("t", t_glo)])
    v_glob = P.emit("v", lambda v: v.tensor_copy(glo_b[:, :], glo_sb[:, :]),
                    waits=[("v", v_glo)])

    def gff(t_):
        for dc in range(NDC):
            for di in range(NDC):
                ii = t_.matmul(ps_c[1][:, dc:dc + 1], wsl(w_big2, di, dc),
                               glo_b[:, di:di + 1], start=(di == 0),
                               stop=(di == NDC - 1))
        return ii
    gw_wfg = P.emit("g", lambda g: g.dma_start(
        w_big1[:, :].rearrange("p (c n) -> p c n", c=8), chunked(wfg_d[:, :])),
        waits=[("t", t_glo), ("t", mq_mm[NQT * NDC - 1])], sem="g_wb")
    t_gf = P.emit("t", gff, waits=[("v", v_glob), ("g_wb", gw_wfg)])
    v_cb1 = P.emit("v", lambda v: v.tensor_tensor(
        cbias[:, :], ps_c[1][:, 0:8], bfq_sb[:, :], ALU.add),
        waits=[("t", t_gf)])
    v_cb = P.emit("v", lambda v: v.tensor_tensor(
        cbias[:, :], cbias[:, :], bfg_sb[:, :], ALU.add), waits=[("v", v_cb1)])

    gw_wfq = P.emit("g", lambda g: g.dma_start(
        w_big2[:, :].rearrange("p (c n) -> p c n", c=8), chunked(wfq_d[:, :])),
        waits=[("t", t_gf)], sem="g_wb")

    sc_p2 = contextlib.ExitStack()
    upz_b = sc_p2.enter_context(nc.sbuf_tensor("upz_b", [128, 512], BF16))
    gw_upz = P.emit("g", lambda g: g.dma_start(upz_b[:, :], upz_d[:, :]),
                    waits=[("t", xtr_last), ("v", x_done), ("g_j", gw_j)],
                    sem="g_upz")
    spz = sc_p2.enter_context(nc.sbuf_tensor("spz", [128, 4096], F32))
    Sp, Sz = spz[0:16, 0:TPC], spz[32:48, 0:TPC]
    cA, cB = spz[64:80, 0:TPC], spz[64:80, TPC:2 * TPC]
    ctp = sc_p2.enter_context(nc.sbuf_tensor("ctp", [128, 1024], BF16))
    c_t_sb = [ctp[:, 0:QT], ctp[:, QT:2 * QT]]

    c_mm, c_tanh, pz_mm, sig = {}, {}, {}, {}
    for qc in range(NQT):
        for dc in range(NDC):
            i = qc * NDC + dc
            par = i % 2

            def cmf(t_, qc=qc, dc=dc, par=par):
                for di in range(NDC):
                    ii = t_.matmul(ps_s[par][:, 0:QT], wsl(w_big2, di, dc),
                                   mq[di][:, qc * QT:(qc + 1) * QT],
                                   start=(di == 0), stop=(di == NDC - 1))
                return ii
            c_mm[i] = P.emit("t", cmf,
                             waits=[("v", v_cb), ("g_wb", gw_wfq), ("v", mq_done),
                                    ("s", c_tanh.get(i - 2, 0))])
            c_tanh[i] = P.emit("s", lambda s, dc=dc, par=par: s.activation(
                c_t_sb[par], ps_s[par][:, 0:QT], AF.Tanh,
                bias=cbias[:, dc:dc + 1]), waits=[("t", c_mm[i])])
            pz_mm[i] = P.emit("t", lambda t_, dc=dc, par=par: t_.matmul(
                ps_c[0][0:64, 0:QT], upz_b[:, dc * 64:(dc + 1) * 64],
                c_t_sb[par], start=(dc == 0), stop=(dc == NDC - 1)),
                waits=[("s", c_tanh[i]), ("s", sig.get(qc - 1, 0)),
                       ("v", v_glo if i == 0 else 0),
                       ("g_upz", gw_upz if i == 0 else 0)])
        sig[qc] = P.emit("s", lambda s, qc=qc: (
            s.activation(Sp[:, qc * QT:(qc + 1) * QT], ps_c[0][0:16, 0:QT],
                         AF.Sigmoid),
            s.activation(Sz[:, qc * QT:(qc + 1) * QT], ps_c[0][32:48, 0:QT],
                         AF.Sigmoid))[-1],
            waits=[("t", pz_mm[qc * NDC + NDC - 1])])
    sig_done = sig[NQT - 1]
    pz_last = pz_mm[NQT * NDC - 1]

    steps = [
        lambda v: v.reciprocal(cA, Sz),
        lambda v: v.tensor_tensor(cA, cA, cA, ALU.mult),
        lambda v: v.tensor_scalar(cA, cA, -1.1920928955078125e-07,
                                  None, ALU.mult),
        lambda v: v.tensor_scalar(cB, Sp, float(S), None, ALU.mult),
        lambda v: v.tensor_copy(C3[:, 0:TPC], cA),
        lambda v: v.tensor_tensor(cA, cA, cB, ALU.mult),
        lambda v: v.tensor_scalar(C3[:, TPC:2 * TPC], cA, -2.0, None,
                                  ALU.mult),
        lambda v: v.tensor_tensor(cA, cA, cB, ALU.mult),
        lambda v: v.tensor_scalar(C3[:, 2 * TPC:3 * TPC], cA, -4.0, None,
                                  ALU.add),
    ]
    vprev = P.emit("v", steps[0], waits=[("s", sig_done)])
    for f in steps[1:]:
        vprev = P.emit("v", f, waits=[("v", vprev)])
    coeff_done = vprev

    sc_p2.close()
    sc_w.close()

    sc_att2 = contextlib.ExitStack()
    v_aug = sc_att2.enter_context(nc.sbuf_tensor("v_aug", [128, NKC * 130],
                                                 BF16))
    kv_w = [sc_att2.enter_context(nc.sbuf_tensor(f"kv{p}", [128, 2048], BF16))
            for p in range(2)]
    e_t = [sc_att2.enter_context(nc.sbuf_tensor(f"e_t{p}", [128, 2 * QT],
                                                BF16)) for p in range(2)]
    rhs_aug = [sc_att2.enter_context(nc.sbuf_tensor(f"rhs{p}", [128, QT],
                                                    BF16)) for p in range(4)]
    rbp = sc_att2.enter_context(nc.sbuf_tensor("rbp", [128, QT], F32))
    rb_sb = [rbp[0:64, :], rbp[64:128, :]]
    bv_rb = sc_att2.enter_context(nc.sbuf_tensor("bv_rb", [1, DM], BF16))
    gw_bv = P.emit("g", lambda g: g.dma_start(bv_rb[:, :], bv_d[:, :]),
                   waits=[("t", pz_last), ("v", coeff_done)], sem="g_bv")

    def vones(v):
        va = v_aug[:, :].rearrange("p (t c) -> p t c", c=130)
        v.memset(va[:, :, 64:65], 1.0)
        return v.memset(va[:, :, 129:130], 1.0)
    v_vones = P.emit("v", vones, waits=[("t", pz_last), ("v", coeff_done)])

    AT = {}
    pair_t_end = {}
    for p in range(NE):
        wpar = p % 2
        cols = slice(p * 128, (p + 1) * 128)
        P.emit("g", lambda g, wpar=wpar, cols=cols: g.dma_start(
            kv_w[wpar][:, 0:1024].rearrange("p (c n) -> p c n", c=8),
            chunked(wk_d[:, cols])),
            waits=[("t", pair_t_end.get(p - 2, pz_last)), ("g_j", gw_j),
                   ("v", coeff_done if p == 0 else 0),
                   ("s", sig_done if p == 0 else 0)], sem=f"g_kv{wpar}")
        kv_gw = P.emit("g", lambda g, wpar=wpar, cols=cols: g.dma_start(
            kv_w[wpar][:, 1024:2048].rearrange("p (c n) -> p c n", c=8),
            chunked(wv_d[:, cols])), sem=f"g_kv{wpar}")

        kev = {}
        for kt in range(8):
            par = kt % 2

            def kmf(t_, kt=kt, par=par, wpar=wpar):
                for di in range(NDC):
                    ii = t_.matmul(ps_s[par][:, 0:QT],
                                   kv_w[wpar][:, di * 128:(di + 1) * 128],
                                   x_fm[di][:, kt * QT:(kt + 1) * QT],
                                   start=(di == 0), stop=(di == NDC - 1))
                return ii
            w = [(f"g_kv{wpar}", kv_gw), ("v", kev.get(kt - 2, 0))]
            if kt == 0:
                w += [("s", AT.get((p - 1, NQT - 1, "rbs"), sig_done)),
                      ("v", AT.get((p - 1, NQT - 1, "ctxe"),
                                   max(coeff_done, mq_done)))]
            tk = P.emit("t", kmf, waits=w)
            kev[kt] = P.emit("v", lambda v, kt=kt, par=par, p=p: (
                v.tensor_scalar(k_augA[0:64, kt * QT:(kt + 1) * QT],
                                ps_s[par][0:64, 0:QT], bk_sb[0:64, p:p + 1],
                                0.125, ALU.add, ALU.mult),
                v.tensor_scalar(k_augB[0:64, kt * QT:(kt + 1) * QT],
                                ps_s[par][64:128, 0:QT], bk_sb[64:128, p:p + 1],
                                0.125, ALU.add, ALU.mult))[-1],
                waits=[("t", tk),
                       ("t", AT.get((p - 1, NQT - 1, "last_sc"), 0)
                        if kt == 0 else 0)])

        t_bv = P.emit("t", lambda t_, p=p: t_.matmul(
            ps_c[0][:, 0:128], ones1_b[:, :], bv_rb[:, p * 128:(p + 1) * 128],
            start=True, stop=True),
            waits=[("v", AT.get((p - 1, NQT - 1, "ctxe"), 0)),
                   ("s", sig_done if p == 0 else 0), ("v", v_init),
                   ("g_bv", gw_bv)])
        v_bv = P.emit("v", lambda v: v.tensor_copy(bvb[:, :],
                                                   ps_c[0][:, 0:128]),
                      waits=[("t", t_bv)])
        vev = {}
        for tt in range(NKC):
            par = tt % 2

            def vmf(t_, tt=tt, par=par, wpar=wpar):
                for di in range(NDC):
                    ii = t_.matmul(ps_s[par][:, QT:QT + 128],
                                   x_fm[di][:, tt * 128:(tt + 1) * 128],
                                   kv_w[wpar][:, 1024 + di * 128:
                                               1024 + (di + 1) * 128],
                                   start=(di == 0), stop=(di == NDC - 1))
                return ii
            tv = P.emit("t", vmf, waits=[("v", vev.get(tt - 2, 0))])
            base = tt * 130
            vev[tt] = P.emit("v", lambda v, base=base, par=par: (
                v.tensor_tensor(v_aug[:, base:base + 64],
                                ps_s[par][:, QT:QT + 64], bvb[:, 0:64],
                                ALU.add),
                v.tensor_tensor(v_aug[:, base + 65:base + 129],
                                ps_s[par][:, QT + 64:QT + 128], bvb[:, 64:128],
                                ALU.add))[-1],
                waits=[("t", tv), ("v", v_bv), ("v", v_vones if tt == 0 else 0),
                       ("t", AT.get((p - 1, NQT - 1, NKC - 1, "ctx"), 0)
                        if tt == 0 else 0)])
        kv_done_v = max(vev[NKC - 1], kev[7])

        for qc in range(NQT):
            for hi in range(2):
                bi = 2 * (qc % 2) + hi
                h = 2 * p + hi
                guard = AT.get((p, qc - 2, "last_sc"),
                               AT.get((p - 1, NQT - 1, "last_sc"), pz_last))
                AT[(p, qc, hi, "q")] = P.emit(
                    "v", lambda v, bi=bi, hi=hi, p=p, qc=qc: v.tensor_copy(
                        rhs_aug[bi][0:64, :],
                        mq[p][hi * 64:(hi + 1) * 64, qc * QT:(qc + 1) * QT]),
                    waits=[("t", guard),
                           ("v", kv_done_v if qc == 0 and hi == 0 else 0)])
                AT[(p, qc, hi, "c3")] = P.emit(
                    "g", lambda g, bi=bi, h=h, qc=qc: g.dma_start(
                        rhs_aug[bi][64:67, 0:QT],
                        C3[h:h + 1, :].rearrange("o (c n) -> o c n", c=3)
                        [:, :, qc * QT:(qc + 1) * QT]),
                    waits=[("v", AT[(p, qc, hi, "q")])], sem=f"g_c3{qc % 2}")
            def emit_ctx(kc):
                def ctf(t_, kc=kc):
                    base = kc * 130
                    t_.matmul(ps_c[0][0:65, :], v_aug[:, base:base + 65],
                              e_t[kc % 2][:, 0:QT], start=(kc == 0),
                              stop=(kc == NKC - 1))
                    return t_.matmul(ps_c[1][0:65, :],
                                     v_aug[:, base + 65:base + 130],
                                     e_t[kc % 2][:, QT:2 * QT],
                                     start=(kc == 0), stop=(kc == NKC - 1))
                wct = [("s", AT[(p, qc, kc, "exp")])]
                if kc == 0:
                    wct += [("v", AT.get((p, qc - 1, "ctxe"),
                                         AT.get((p - 1, NQT - 1, "ctxe"), 0))),
                            ("v", v_bv)]
                AT[(p, qc, kc, "ctx")] = P.emit("t", ctf, waits=wct)

            for kc in range(NKC):
                par = kc % 2
                wsc = [("s", AT.get((p, qc, kc - 2, "exp"), 0)),
                       (f"g_c3{qc % 2}", AT[(p, qc, 1, "c3")]),
                       ("v", kv_done_v)]
                if kc == 0:
                    wsc += [("s", AT.get((p, qc - 1, "rbs"),
                                         AT.get((p - 1, NQT - 1, "rbs"), 0))),
                            ("g_j", gw_j if p == 0 and qc == 0 else 0)]

                def scf(t_, kc=kc, par=par, biA=2 * (qc % 2),
                        biB=2 * (qc % 2) + 1):
                    t_.matmul(ps_s[par][:, 0:QT],
                              k_augA[0:67, kc * 128:(kc + 1) * 128],
                              rhs_aug[biA][0:67, :], start=True, stop=True)
                    return t_.matmul(ps_s[par][:, QT:2 * QT],
                                     k_augB[0:67, kc * 128:(kc + 1) * 128],
                                     rhs_aug[biB][0:67, :], start=True,
                                     stop=True)
                AT[(p, qc, kc, "sc")] = P.emit("t", scf, waits=wsc)
                AT[(p, qc, kc, "exp")] = P.emit(
                    "s", lambda s, kc=kc, par=par: s.activation(
                        e_t[kc % 2][:, :], ps_s[par][:, :], AF.Exp),
                    waits=[("t", AT[(p, qc, kc, "sc")]),
                           ("t", AT.get((p, qc, kc - 2, "ctx"), 0))])
                if kc >= 1:
                    emit_ctx(kc - 1)
            emit_ctx(NKC - 1)
            AT[(p, qc, "last_sc")] = AT[(p, qc, NKC - 1, "sc")]
            AT[(p, qc, "rec")] = P.emit("v", lambda v: (
                v.reciprocal(recr_f[0], ps_c[0][64:65, :]),
                v.reciprocal(recr_f[1], ps_c[1][64:65, :]))[-1],
                waits=[("t", AT[(p, qc, NKC - 1, "ctx")])])
            AT[(p, qc, "recb")] = P.emit("v", lambda v: (
                v.tensor_copy(recr[0], recr_f[0]),
                v.tensor_copy(recr[1], recr_f[1]))[-1],
                waits=[("v", AT[(p, qc, "rec")])])
            AT[(p, qc, "rbt")] = P.emit("t", lambda t_: (
                t_.matmul(ps_s[0][0:64, 0:QT], ones1_b[:, 0:64], recr[0],
                          start=True, stop=True),
                t_.matmul(ps_s[0][0:64, QT:2 * QT], ones1_b[:, 0:64], recr[1],
                          start=True, stop=True))[-1],
                waits=[("v", AT[(p, qc, "recb")]),
                       ("s", AT[(p, qc, NKC - 1, "exp")])])
            AT[(p, qc, "rbs")] = P.emit("s", lambda s: (
                s.copy(rb_sb[0], ps_s[0][0:64, 0:QT]),
                s.copy(rb_sb[1], ps_s[0][0:64, QT:2 * QT]))[-1],
                waits=[("t", AT[(p, qc, "rbt")])])
            AT[(p, qc, "ctxe")] = P.emit("v", lambda v, p=p, qc=qc: (
                v.tensor_tensor(ctx_fm[p][0:64, qc * QT:(qc + 1) * QT],
                                ps_c[0][0:64, :], rb_sb[0], ALU.mult),
                v.tensor_tensor(ctx_fm[p][64:128, qc * QT:(qc + 1) * QT],
                                ps_c[1][0:64, :], rb_sb[1], ALU.mult))[-1],
                waits=[("s", AT[(p, qc, "rbs")])])
        pair_t_end[p] = P.cnt["t"]
    att_v = AT[(NE - 1, NQT - 1, "ctxe")]
    att_rbs = AT[(NE - 1, NQT - 1, "rbs")]
    att_t = P.cnt["t"]

    sc_att2.close()
    sc_c3.close()
    sc_kaug.close()
    sc_p4 = contextlib.ExitStack()
    wo_b = sc_p4.enter_context(nc.sbuf_tensor("wo_b", [128, 8 * DM], BF16))
    zbuf_a = sc_p4.enter_context(nc.sbuf_tensor("zbuf_a", [128, 8 * QT], F32))
    mb_a = sc_p4.enter_context(nc.sbuf_tensor("mb_a", [128, QT], F32))
    ivb_a = sc_p4.enter_context(nc.sbuf_tensor("ivb_a", [128, QT], F32))
    rows_a = sc_p4.enter_context(nc.sbuf_tensor("rows_a", [1, 2048], F32))
    zbp_a = [sc_p4.enter_context(nc.sbuf_tensor(f"zbp_a{p}", [128, QT], BF16))
             for p in range(2)]
    zsp_a = [sc_p4.enter_context(nc.sbuf_tensor(f"zsp_a{p}", [128, QT], BF16))
             for p in range(2)]

    gw_wo = P.emit("g", lambda g: g.dma_start(
        wo_b[:, :].rearrange("p (c n) -> p c n", c=8), chunked(wo_d[:, :])),
        waits=[("t", att_t), ("s", att_rbs), ("v", att_v)], sem="g_wb")

    def ln_phase(qc, mode, y_dst, st_, zbuf, mb_sb, invb_sb, rows4,
                 zb_t, zsq_t):
        L = {}
        mrow, srow = rows4[0:1, 0:512], rows4[0:1, 512:1024]
        vrow, vrow2 = rows4[0:1, 1024:1536], rows4[0:1, 1536:2048]
        zev = st_.setdefault("zev", {})
        for dc in range(NDC):
            gi = qc * NDC + dc
            par = gi % 2
            if mode == "attn":
                wmm = [("v", att_v), ("g_wb", gw_wo),
                       ("v", zev.get(gi - 2, 0)),
                       ("s", att_rbs if gi < 2 else 0),
                       ("s", st_.get("prev_bcs", 0) if dc < 2 else 0)]

                def mmf(t_, qc=qc, dc=dc, par=par):
                    for di in range(NDC):
                        ii = t_.matmul(ps_s[par][:, 0:QT], wsl(wo_b, di, dc),
                                       ctx_fm[di][:, qc * QT:(qc + 1) * QT],
                                       start=(di == 0), stop=(di == NDC - 1))
                    return ii
                L[(dc, "mm")] = P.emit("t", mmf, waits=wmm)
                zev[gi] = L[(dc, "z")] = P.emit(
                    "v", lambda v, qc=qc, dc=dc, par=par:
                    v.scalar_tensor_tensor(
                        zbuf[:, dc * QT:(dc + 1) * QT], ps_s[par][:, 0:QT],
                        bo_sb[:, dc:dc + 1],
                        x_fm[dc][:, qc * QT:(qc + 1) * QT], ALU.add, ALU.add),
                    waits=[("t", L[(dc, "mm")])])
            else:
                L[(dc, "z")] = P.emit(
                    "v", lambda v, qc=qc, dc=dc: v.scalar_tensor_tensor(
                        zbuf[:, dc * QT:(dc + 1) * QT],
                        st_["routed"][dc][:, qc * QT:(qc + 1) * QT], 1.0 / 3.0,
                        y1[dc][:, qc * QT:(qc + 1) * QT], ALU.mult, ALU.add),
                    waits=[("v", st_.get("src_v", 0)),
                           ("t", st_.get("prev_ttr", 0))])
            zg = (L.get((dc - 2, "sums"), st_.get("prev_sums_last", 0)
                        if dc < 2 else 0))
            L[(dc, "zb")] = P.emit("v", lambda v, dc=dc: v.tensor_copy(
                zb_t[dc % 2][:, :], zbuf[:, dc * QT:(dc + 1) * QT]),
                waits=[("v", L[(dc, "z")]), ("t", zg)])
            L[(dc, "zsq")] = P.emit("s", lambda s, dc=dc: s.activation(
                zsq_t[dc % 2][:, :], zbuf[:, dc * QT:(dc + 1) * QT],
                AF.Square), waits=[("v", L[(dc, "z")]), ("t", zg)])
            wsum = [("v", L[(dc, "zb")]), ("s", L[(dc, "zsq")])]
            if dc == 0:
                wsum += [("s", st_.get("psc_guard_s", 0)),
                         ("v", st_.get("psc_guard_v", 0))]
            L[(dc, "sums")] = P.emit("t", lambda t_, dc=dc: (
                t_.matmul(ps_c[0][0:1, 0:QT], ones128_b[:, :],
                          zb_t[dc % 2][:, :], start=(dc == 0),
                          stop=(dc == NDC - 1)),
                t_.matmul(ps_c[1][0:1, 0:QT], ones128_b[:, :],
                          zsq_t[dc % 2][:, :], start=(dc == 0),
                          stop=(dc == NDC - 1)))[-1], waits=wsum)
        r0 = P.emit("v", lambda v: v.tensor_scalar(
            mrow, ps_c[0][0:1, 0:QT], 1.0 / DM, None, ALU.mult),
            waits=[("t", L[(NDC - 1, "sums")])])
        r1 = P.emit("v", lambda v: v.tensor_scalar(
            vrow, ps_c[1][0:1, 0:QT], 1.0 / DM, None, ALU.mult),
            waits=[("v", r0)])
        r2 = P.emit("v", lambda v: v.tensor_tensor(vrow2, mrow, mrow,
                                                   ALU.mult), waits=[("v", r1)])
        r3 = P.emit("v", lambda v: v.tensor_tensor(vrow, vrow, vrow2,
                                                   ALU.subtract),
                    waits=[("v", r2)])
        r4 = P.emit("s", lambda s: s.activation(srow, vrow, AF.Sqrt,
                                                bias=1e-5), waits=[("v", r3)])
        r5 = P.emit("v", lambda v: v.reciprocal(vrow2, srow),
                    waits=[("s", r4)])
        r6 = P.emit("v", lambda v: (v.tensor_copy(mrow_b, mrow),
                                    v.tensor_copy(irow_b, vrow2))[-1],
                    waits=[("v", r5)])
        wbc = [("v", r6), ("v", L[(NDC - 1, "z")]),
               ("s", st_.get("pss_guard_s", 0)),
               ("v", st_.get("pss_guard_v", 0))]
        bct = P.emit("t", lambda t_: (
            t_.matmul(ps_s[0][:, 0:QT], ones1_b[:, :], mrow_b, start=True,
                      stop=True),
            t_.matmul(ps_s[1][:, 0:QT], ones1_b[:, :], irow_b, start=True,
                      stop=True))[-1], waits=wbc)
        bcs = P.emit("s", lambda s: (s.copy(mb_sb[:, :], ps_s[0][:, 0:QT]),
                                     s.copy(invb_sb[:, :],
                                            ps_s[1][:, 0:QT]))[-1],
                     waits=[("t", bct)])
        subv = P.emit("v", lambda v: [v.tensor_tensor(
            zbuf[:, dc * QT:(dc + 1) * QT], zbuf[:, dc * QT:(dc + 1) * QT],
            mb_sb[:, :], ALU.subtract) for dc in range(NDC)][-1],
            waits=[("s", bcs)])
        if y_dst is not None:
            fin = P.emit("v", lambda v, qc=qc: [v.tensor_tensor(
                y_dst[dc][:, qc * QT:(qc + 1) * QT],
                zbuf[:, dc * QT:(dc + 1) * QT], invb_sb[:, :], ALU.mult)
                for dc in range(NDC)][-1], waits=[("v", subv)])
        else:
            fin = P.emit("v", lambda v: [v.tensor_tensor(
                zbuf[:, dc * QT:(dc + 1) * QT], zbuf[:, dc * QT:(dc + 1) * QT],
                invb_sb[:, :], ALU.mult) for dc in range(NDC)][-1],
                waits=[("v", subv)])
        st_["prev_sums_last"] = L[(NDC - 1, "sums")]
        st_["prev_bcs"] = bcs
        st_["pss_guard_s"] = bcs
        st_["psc_guard_v"] = r1
        L["fin"], L["bcs"], L["r1"] = fin, bcs, r1
        return L

    st1 = {"psc_guard_s": sig_done}
    gate_ws = {}
    gw_wr8 = 0
    for qc in range(NQT):
        L1 = ln_phase(qc, "attn", y1, st1, zbuf_a, mb_a, ivb_a, rows_a,
                      zbp_a, zsp_a)
        for tt in range(4):
            tok = qc * QT + tt * 128

            def gmf(t_, qc=qc, tt=tt, tok=tok):
                for di in range(NDC):
                    ii = t_.matmul(ps_c[0][:, 8 * tt:8 * tt + 8],
                                   y1[di][:, tok:tok + 128],
                                   wg_b[:, di * 8:(di + 1) * 8],
                                   start=(di == 0), stop=(di == NDC - 1))
                return ii
            gt_t = P.emit("t", gmf, waits=[("v", L1["fin"]),
                                           ("s", gate_ws.get((qc, tt - 1), 0))])
            gs = P.emit("s", lambda s, tt=tt: s.activation(
                gsig[:, :], ps_c[0][:, 8 * tt:8 * tt + 8], AF.Sigmoid),
                waits=[("t", gt_t)])
            st1["psc_guard_s"] = gs
            chain = [
                lambda v: v.reduce_max(gt[:, 0:1], gsig[:, :], axis=AX.X),
                lambda v: v.tensor_scalar(gt[:, 8:16], gsig[:, :], gt[:, 0:1],
                                          None, ALU.is_equal),
                lambda v: v.scalar_tensor_tensor(gt[:, 16:24], gt[:, 8:16],
                                                 -1e30, gsig[:, :], ALU.mult,
                                                 ALU.add),
                lambda v: v.reduce_max(gt[:, 1:2], gt[:, 16:24], axis=AX.X),
                lambda v: v.tensor_scalar(gt[:, 24:32], gt[:, 16:24],
                                          gt[:, 1:2], None, ALU.is_equal),
                lambda v: v.tensor_tensor(gt[:, 24:32], gt[:, 24:32],
                                          gt[:, 8:16], ALU.add),
                lambda v: v.tensor_tensor(gt[:, 2:3], gt[:, 0:1], gt[:, 1:2],
                                          ALU.add),
                lambda v: v.reciprocal(gt[:, 3:4], gt[:, 2:3]),
                lambda v: v.tensor_tensor(gt[:, 32:40], gsig[:, :],
                                          gt[:, 24:32], ALU.mult),
                lambda v: v.tensor_scalar(wtok[:, :], gt[:, 32:40], gt[:, 3:4],
                                          None, ALU.mult),
            ]
            vv = P.emit("v", chain[0], waits=[("s", gs)])
            for f in chain[1:]:
                vv = P.emit("v", f, waits=[("v", vv)])
            wts = 0
            for rnd in range(2):
                def wtf(t_, rnd=rnd):
                    for e4 in range(4):
                        ii = t_.transpose(
                            ps_s[0][0:1, QT + e4 * 128:QT + (e4 + 1) * 128],
                            wtok[:, rnd * 4 + e4:rnd * 4 + e4 + 1],
                            eye_f[:, :])
                    return ii
                wt_t = P.emit("t", wtf, waits=[("v", vv), ("s", wts),
                                               ("y_eye", ld_eye)])
                wts = P.emit("s", lambda s, rnd=rnd: s.copy(
                    wstg[0:1, 0:512], ps_s[0][0:1, QT:QT + 512]),
                    waits=[("t", wt_t), ("g_wr8", gw_wr8)])
                for e4 in range(4):
                    gw_wr8 = P.emit("g", lambda g, rnd=rnd, tok=tok, e4=e4:
                                    g.dma_start(
                        wr8[rnd * 4 + e4:rnd * 4 + e4 + 1, tok:tok + 128],
                        wstg[0:1, e4 * 128:(e4 + 1) * 128]),
                        waits=[("s", wts)] if e4 == 0 else [], sem="g_wr8")
            gate_ws[(qc, tt)] = wts
    wr8_done = gw_wr8
    p4_v = P.cnt["v"]
    p4_s = P.cnt["s"]
    p4_t = P.cnt["t"]

    sc_p4.close()
    sc_ctx.close()
    sc_x.close()

    sc_r = contextlib.ExitStack()
    routed = [sc_r.enter_context(nc.sbuf_tensor(f"rt{c}", [128, TPC], BF16))
              for c in range(NDC)]
    sc_ffn = contextlib.ExitStack()
    we1f = sc_ffn.enter_context(nc.sbuf_tensor("we1f", [128, 8 * FF], BF16))
    we3f = sc_ffn.enter_context(nc.sbuf_tensor("we3f", [128, 8 * FF], BF16))
    we2f = sc_ffn.enter_context(nc.sbuf_tensor("we2f", [128, NFC * DM], BF16))
    h_sb = sc_ffn.enter_context(nc.sbuf_tensor("h_sb", [128, NFC * QT], BF16))
    sa_sb = [sc_ffn.enter_context(nc.sbuf_tensor(f"sa{p}", [128, QT], BF16))
             for p in range(2)]
    h1_sb = [sc_ffn.enter_context(nc.sbuf_tensor(f"h1{p}", [128, QT], F32))
             for p in range(2)]
    wbb = sc_ffn.enter_context(nc.sbuf_tensor("wbb", [128, QT], BF16))

    FW = {}
    ab_last, o_last, o_racc_last = {}, {}, {}
    ffn_s_last, ffn_h1_last = 0, 0
    space_guard = [("v", p4_v), ("t", p4_t), ("s", p4_s)]
    stg_dma = {}
    for ei in range(NE + 1):
        e = ei - 1
        s1d = ws1_d[:, :] if e < 0 else we1_d[e, :, :]
        s3d = ws3_d[:, :] if e < 0 else we3_d[e, :, :]
        s2d = ws2_d[:, :] if e < 0 else we2_d[e, :, :]
        FW[(ei, "w1")] = P.emit("g", lambda g, s1d=s1d: g.dma_start(
            we1f[:, :].rearrange("p (c n) -> p c n", c=8), chunked(s1d)),
            waits=[("t", ab_last.get(ei - 1, 0))] +
                  (space_guard if ei == 0 else []), sem="g_we13")
        FW[(ei, "w3")] = P.emit("g", lambda g, s3d=s3d: g.dma_start(
            we3f[:, :].rearrange("p (c n) -> p c n", c=8), chunked(s3d)),
            sem="g_we13")
        FW[(ei, "w2")] = P.emit("g", lambda g, s2d=s2d: g.dma_start(
            we2f[:, :].rearrange("p (c n) -> p c n", c=16),
            s2d.rearrange("(c p) n -> p c n", p=128)),
            waits=[("t", o_last.get(ei - 1, 0))], sem="g_we2")
        ht, ot, sls = {}, {}, {}
        for tt in range(NQT):
            if e >= 0:
                sd = P.emit("g", lambda g, e=e, tt=tt: g.dma_start(
                    wstg[0:1, (tt % 2) * 512:(tt % 2) * 512 + 512],
                    wr8[e:e + 1, tt * QT:(tt + 1) * QT]),
                    waits=[("g_wr8", wr8_done),
                           ("t", stg_dma.get(tt % 2, 0))],
                    sem=f"g_st{tt % 2}")
                bw = P.emit("t", lambda t_, tt=tt: t_.matmul(
                    ps_c[0][:, 0:QT], ones1_b[:, :],
                    wstg[0:1, (tt % 2) * 512:(tt % 2) * 512 + 512],
                    start=True, stop=True),
                    waits=[(f"g_st{tt % 2}", sd),
                           ("v", ot.get((tt - 1, "racc"),
                                        o_racc_last.get(ei - 1, 0)))])
                stg_dma[tt % 2] = bw
                bws = P.emit("s", lambda s: s.copy(wbb[:, :],
                                                   ps_c[0][:, 0:QT]),
                             waits=[("t", bw)])
            else:
                bws = 0
            for fc in range(NFC):
                par = fc % 2

                def abf(t_, fc=fc, par=par, tt=tt):
                    for di in range(NDC):
                        t_.matmul(ps_s[par][:, 0:QT],
                                  we1f[:, di * FF + fc * 128:
                                       di * FF + (fc + 1) * 128],
                                  y1[di][:, tt * QT:(tt + 1) * QT],
                                  start=(di == 0), stop=(di == NDC - 1))
                    for di in range(NDC):
                        ii = t_.matmul(ps_s[par][:, QT:2 * QT],
                                       we3f[:, di * FF + fc * 128:
                                            di * FF + (fc + 1) * 128],
                                       y1[di][:, tt * QT:(tt + 1) * QT],
                                       start=(di == 0), stop=(di == NDC - 1))
                    return ii
                w = [("g_we13", FW[(ei, "w3")]),
                     ("v", ht.get((tt, fc - 2), 0)),
                     ("s", sls.get((tt, fc - 2), 0))]
                ab = P.emit("t", abf, waits=w)
                sls[(tt, fc)] = sl = P.emit(
                    "s", lambda s, fc=fc, par=par: s.activation(
                        sa_sb[fc % 2][:, :], ps_s[par][:, 0:QT], AF.Silu),
                    waits=[("t", ab)])
                ffn_s_last = sl
                h1 = P.emit("v", lambda v, fc=fc, par=par: v.tensor_tensor(
                    h1_sb[fc % 2][:, :], sa_sb[fc % 2][:, :],
                    ps_s[par][:, QT:2 * QT], ALU.mult), waits=[("s", sl)])
                ffn_h1_last = h1
                wh = [("v", h1), ("t", ot.get((tt - 1, "last"),
                                              o_last.get(ei - 1, 0)))]
                if e >= 0:
                    ht[(tt, fc)] = P.emit(
                        "v", lambda v, fc=fc: v.tensor_tensor(
                            h_sb[:, fc * QT:(fc + 1) * QT],
                            h1_sb[fc % 2][:, :], wbb[:, :], ALU.mult),
                        waits=wh + [("s", bws)])
                else:
                    ht[(tt, fc)] = P.emit(
                        "v", lambda v, fc=fc: v.tensor_copy(
                            h_sb[:, fc * QT:(fc + 1) * QT],
                            h1_sb[fc % 2][:, :]), waits=wh)
            if tt == NQT - 1:
                ab_last[ei] = P.cnt["t"]
            for dc in range(NDC):
                par = dc % 2

                def omf(t_, dc=dc, par=par):
                    for fc in range(NFC):
                        ii = t_.matmul(ps_c[par][:, 0:QT],
                                       we2f[:, fc * DM + dc * 128:
                                            fc * DM + (dc + 1) * 128],
                                       h_sb[:, fc * QT:(fc + 1) * QT],
                                       start=(fc == 0), stop=(fc == NFC - 1))
                    return ii
                w = [("v", ht[(tt, NFC - 1)]), ("g_we2", FW[(ei, "w2")]),
                     ("v", ot.get((tt, dc - 2, "racc"), 0))]
                if dc == 0 and e >= 0:
                    w.append(("s", bws))
                if dc < 2:
                    w.append(("v", ot.get((tt - 1, "racc"),
                                          o_racc_last.get(ei - 1, 0))))
                ot[(tt, dc, "t")] = P.emit("t", omf, waits=w)
                if ei == 0:
                    ot[(tt, dc, "racc")] = P.emit(
                        "v", lambda v, dc=dc, par=par, tt=tt: v.tensor_copy(
                            routed[dc][:, tt * QT:(tt + 1) * QT],
                            ps_c[par][:, 0:QT]),
                        waits=[("t", ot[(tt, dc, "t")])])
                else:
                    ot[(tt, dc, "racc")] = P.emit(
                        "v", lambda v, dc=dc, par=par, tt=tt: v.tensor_tensor(
                            routed[dc][:, tt * QT:(tt + 1) * QT],
                            routed[dc][:, tt * QT:(tt + 1) * QT],
                            ps_c[par][:, 0:QT], ALU.add),
                        waits=[("t", ot[(tt, dc, "t")])])
            ot[(tt, "last")] = ot[(tt, NDC - 1, "t")]
            ot[(tt, "racc")] = ot[(tt, NDC - 1, "racc")]
        o_last[ei] = ot[(NQT - 1, "last")]
        o_racc_last[ei] = ot[(NQT - 1, "racc")]
    ffn_done_v = o_racc_last[NE]
    p5_t = P.cnt["t"]

    sc_ffn.close()

    sc_p6 = contextlib.ExitStack()
    zbuf_b = sc_p6.enter_context(nc.sbuf_tensor("zbuf_b", [128, 8 * QT], F32))
    mb_b = sc_p6.enter_context(nc.sbuf_tensor("mb_b", [128, QT], F32))
    ivb_b = sc_p6.enter_context(nc.sbuf_tensor("ivb_b", [128, QT], F32))
    rows_b = sc_p6.enter_context(nc.sbuf_tensor("rows_b", [1, 2048], F32))
    zbp_b = [sc_p6.enter_context(nc.sbuf_tensor(f"zbp_b{p}", [128, QT], BF16))
             for p in range(2)]
    zsp_b = [sc_p6.enter_context(nc.sbuf_tensor(f"zsp_b{p}", [128, QT], BF16))
             for p in range(2)]
    stage_b = [sc_p6.enter_context(nc.sbuf_tensor(f"ost{p}", [128, DM], F32))
               for p in range(2)]

    st2 = {"src_v": ffn_done_v, "psc_guard_v": ffn_done_v,
           "pss_guard_s": ffn_s_last, "pss_guard_v": ffn_h1_last,
           "prev_ttr": p5_t, "routed": routed}
    ST = {}
    prev_st = {0: 0, 1: 0}
    for qc in range(NQT):
        L2 = ln_phase(qc, "ffn", None, st2, zbuf_b, mb_b, ivb_b, rows_b,
                      zbp_b, zsp_b)
        last_ttr = 0
        for tt in range(4):
            par = tt % 2
            tok = qc * QT + tt * 128

            def trf(t_, tt=tt):
                for dc in range(NDC):
                    ii = t_.transpose(
                        ps_c[dc // 4][:, (dc % 4) * 128:(dc % 4 + 1) * 128],
                        zbuf_b[:, dc * QT + tt * 128:dc * QT + (tt + 1) * 128],
                        eye_f[:, :])
                return ii
            ttr = P.emit("t", trf, waits=[("v", L2["fin"]),
                                          ("v", ST.get((qc, tt - 1, "ev"), 0)),
                                          ("v", L2["r1"] if tt == 0 else 0)])
            last_ttr = ttr
            ST[(qc, tt, "ev")] = P.emit("v", lambda v, par=par: [
                v.tensor_copy(stage_b[par][:, dc * 128:(dc + 1) * 128],
                              ps_c[dc // 4][:, (dc % 4) * 128:
                                            (dc % 4 + 1) * 128])
                for dc in range(NDC)][-1],
                waits=[("t", ttr), (f"y_st{par}", prev_st[par])])
            ST[(qc, tt, "st")] = P.emit(
                "y", lambda y, par=par, tok=tok: y.dma_start(
                    out_d[tok:tok + 128, :], stage_b[par][:, :]),
                waits=[("v", ST[(qc, tt, "ev")])], sem=f"y_st{par}")
            prev_st[par] = ST[(qc, tt, "st")]
        st2["prev_ttr"] = last_ttr
        st2["psc_guard_v"] = ST[(qc, 3, "ev")]
    P.emit("y", lambda y: y.nop(),
           waits=[("y_st0", ST[(NQT - 1, 2, "st")]),
                  ("y_st1", ST[(NQT - 1, 3, "st")])], inc=False)

    P.finalize(nc, sems)
    sc_p6.close()
    sc_r.close()
    stk.close()
    return nc


def _host_inputs(inputs):
    x = np.ascontiguousarray(np.asarray(inputs["L0_attn_hidden_states"],
                                        np.float32))
    up = np.asarray(inputs["L0_attn_up"], np.float32).reshape(-1)
    uz = np.asarray(inputs["L0_attn_uz"], np.float32).reshape(-1)
    upz = np.zeros((128, 512), np.float32)
    r = np.arange(128)
    for dc in range(8):
        h = 2 * dc + r // 64
        upz[r, dc * 64 + h] = up[dc * 128 + r]
        upz[r, dc * 64 + 32 + h] = uz[dc * 128 + r]
    wg = np.asarray(inputs["L0_ffn_Wg"], np.float32)
    wg_m = np.ascontiguousarray(wg.reshape(8, 128, 8).transpose(1, 0, 2)
                                .reshape(128, 64))

    def colmat(name):
        a = np.asarray(inputs[name], np.float32).reshape(-1)
        return np.ascontiguousarray(a.reshape(8, 128).T)

    def arr(name):
        return np.ascontiguousarray(np.asarray(inputs[name], np.float32))

    shared = {
        "eye": np.eye(128, dtype=np.float32),
        "wq": arr("L0_attn_wq"), "wk": arr("L0_attn_wk"),
        "wv": arr("L0_attn_wv"), "wo": arr("L0_attn_wo"),
        "wfq": arr("L0_attn_wfq"), "wfg": arr("L0_attn_wfg"),
        "bq_m": colmat("L0_attn_bq"), "bk_m": colmat("L0_attn_bk"),
        "bo_m": colmat("L0_attn_bo"), "bfq_m": colmat("L0_attn_bfq"),
        "bfg_m": colmat("L0_attn_bfg"),
        "bv_r": np.ascontiguousarray(
            np.asarray(inputs["L0_attn_bv"], np.float32).reshape(1, -1)),
        "upz_m": upz, "wg_m": wg_m,
        "ws1": arr("L0_ffn_Ws1"), "ws3": arr("L0_ffn_Ws3"),
        "ws2": arr("L0_ffn_Ws2"), "we1": arr("L0_ffn_We1"),
        "we3": arr("L0_ffn_We3"), "we2": arr("L0_ffn_We2"),
    }
    in_maps = []
    for c in range(8):
        b, off = c // 2, (c % 2) * TPC
        m = dict(shared)
        m["xr"] = np.ascontiguousarray(
            np.concatenate([x[b, off:], x[b, :off]], axis=0))
        m["jv"] = np.ascontiguousarray(
            np.concatenate([np.arange(off, S), np.arange(0, off)])
            .astype(np.float32).reshape(1, S))
        in_maps.append(m)
    return in_maps


_NC_CACHE = {}


def kernel(**inputs):
    if "nc" not in _NC_CACHE:
        _NC_CACHE["nc"] = build_full()
    nc = _NC_CACHE["nc"]
    in_maps = _host_inputs(inputs)
    res = run_bass_kernel_spmd(nc, in_maps, core_ids=list(range(8)))
    out = np.empty((B, S, DM), np.float32)
    for c in range(8):
        b, off = c // 2, (c % 2) * TPC
        out[b, off:off + TPC] = res.results[c]["out"]
    return out


# version 04
# speedup vs baseline: 1.2973x; 1.0117x over previous
import contextlib
import numpy as np
import concourse.bass as bass
import concourse.mybir as mybir
from concourse.bass_utils import run_bass_kernel_spmd

F32 = mybir.dt.float32
BF16 = mybir.dt.bfloat16
AF = mybir.ActivationFunctionType
ALU = mybir.AluOpType
AX = mybir.AxisListType

B, S, DM, H, D = 4, 4096, 1024, 16, 64
TPC, FF, NE, QT = 2048, 2048, 8, 512
NQT, NKC, NDC, NFC, NXT = TPC // QT, S // 128, DM // 128, FF // 128, S // 128

ENGS = ("t", "s", "v", "g", "y")


class Prog:

    def __init__(self):
        self.recs = {e: [] for e in ENGS}
        self.cnt = {}

    def emit(self, eng, fn, waits=(), inc=True, sem=None):
        semn = sem or eng
        inc_k = 16 if eng in ("g", "y") else 1
        ws = [(e, v) for (e, v) in waits if v and v > 0]
        self.recs[eng].append((fn, ws, semn if inc else None, inc_k))
        if inc:
            self.cnt[semn] = self.cnt.get(semn, 0) + inc_k
        return self.cnt.get(semn, 0)

    def finalize(self, nc, sems):
        def run(eng_obj, name):
            lw = {}
            for fn, ws, semn, inc_k in self.recs[name]:
                for se, val in ws:
                    if val > lw.get(se, 0):
                        eng_obj.wait_ge(sems[se], val)
                        lw[se] = val
                inst = fn(eng_obj)
                if semn is not None:
                    inst.then_inc(sems[semn], inc_k)

        with nc.Block() as block:
            block.tensor(lambda e: run(e, "t"))
            block.scalar(lambda e: run(e, "s"))
            block.vector(lambda e: run(e, "v"))
            block.gpsimd(lambda e: run(e, "g"))
            block.sync(lambda e: run(e, "y"))


def build_full():
    nc = bass.Bass("TRN2", target_bir_lowering=False)

    dt_ = nc.dram_tensor
    xr = dt_("xr", [S, DM], F32, kind="ExternalInput")
    jv_d = dt_("jv", [1, S], F32, kind="ExternalInput")
    eye_d = dt_("eye", [128, 128], F32, kind="ExternalInput")
    wq_d = dt_("wq", [DM, DM], F32, kind="ExternalInput")
    wk_d = dt_("wk", [DM, DM], F32, kind="ExternalInput")
    wv_d = dt_("wv", [DM, DM], F32, kind="ExternalInput")
    wo_d = dt_("wo", [DM, DM], F32, kind="ExternalInput")
    wfq_d = dt_("wfq", [DM, DM], F32, kind="ExternalInput")
    wfg_d = dt_("wfg", [DM, DM], F32, kind="ExternalInput")
    bq_d = dt_("bq_m", [128, 8], F32, kind="ExternalInput")
    bk_d = dt_("bk_m", [128, 8], F32, kind="ExternalInput")
    bo_d = dt_("bo_m", [128, 8], F32, kind="ExternalInput")
    bfq_d = dt_("bfq_m", [128, 8], F32, kind="ExternalInput")
    bfg_d = dt_("bfg_m", [128, 8], F32, kind="ExternalInput")
    bv_d = dt_("bv_r", [1, DM], F32, kind="ExternalInput")
    upz_d = dt_("upz_m", [128, 512], F32, kind="ExternalInput")
    wg_d = dt_("wg_m", [128, 64], F32, kind="ExternalInput")
    ws1_d = dt_("ws1", [DM, FF], F32, kind="ExternalInput")
    ws3_d = dt_("ws3", [DM, FF], F32, kind="ExternalInput")
    ws2_d = dt_("ws2", [FF, DM], F32, kind="ExternalInput")
    we1_d = dt_("we1", [NE, DM, FF], F32, kind="ExternalInput")
    we3_d = dt_("we3", [NE, DM, FF], F32, kind="ExternalInput")
    we2_d = dt_("we2", [NE, FF, DM], F32, kind="ExternalInput")
    out_d = dt_("out", [TPC, DM], F32, kind="ExternalOutput")
    rbd = dt_("rbd", [1, 2 * QT], F32, kind="Internal")

    al = nc.alloc_sbuf_tensor
    eye_f = al("eye_f", [128, 128], F32)
    eye_b = al("eye_b", [128, 128], BF16)
    ones1_b = al("ones1_b", [1, 128], BF16)
    ones128_b = al("ones128_b", [128, 1], BF16)
    cpool = al("cpool", [128, 2], F32)
    bq_sb = al("bq_sb", [128, 8], F32)
    bk_sb = al("bk_sb", [128, 8], F32)
    bo_sb = al("bo_sb", [128, 8], F32)
    bfq_sb = al("bfq_sb", [128, 8], F32)
    bfg_sb = al("bfg_sb", [128, 8], F32)
    cbias = al("cbias", [128, 8], F32)
    bvb = al("bvb", [128, 128], F32)
    wg_b = al("wg_b", [128, 64], BF16)
    xm_sb = al("xm_sb", [128, 8], F32)
    xm_b = al("xm_b", [128, 8], BF16)
    glo_sb = al("glo_sb", [128, 8], F32)
    glo_b = al("glo_b", [128, 8], BF16)
    rowsF = al("rowsF", [1, 1024], F32)
    recr_f = [rowsF[0:1, 0:512], rowsF[0:1, 512:1024]]
    rowsB = al("rowsB", [1, 2048], BF16)
    mrow_b, irow_b = rowsB[0:1, 0:512], rowsB[0:1, 512:1024]
    recr = [rowsB[0:1, 1024:1536], rowsB[0:1, 1536:2048]]
    gsig = al("gsig", [128, 8], F32)
    gt = al("gt", [128, 40], F32)
    wtok = al("wtok", [128, 8], F32)
    wr8 = al("wr8", [8, TPC], BF16)
    wstg = al("wstg", [1, 1024], BF16)
    mq = [al(f"mq{c}", [128, TPC], BF16) for c in range(NDC)]
    y1 = mq

    ap_ = nc.alloc_psum_tensor
    ps_s = [ap_(f"ps_s{p}", [128, 2 * QT], F32) for p in range(2)]
    ps_c = [ap_(f"ps_c{i}", [128, QT], F32) for i in range(2)]
    ps_tr = [ap_(f"ps_tr{p}", [128, 1024], BF16) for p in range(2)]

    stk = contextlib.ExitStack()
    STREAMS = ["t", "s", "v", "y_jv", "y_eye", "y_x0", "y_x1", "y_st0",
               "y_st1", "g_sm", "g_j", "g_wb", "g_upz", "g_bv", "g_kv0",
               "g_kv1", "g_c30", "g_c31", "g_wr8", "g_st0", "g_st1",
               "g_we13", "g_we2", "y_rb"]
    sems = {e: stk.enter_context(nc.semaphore(f"sem_{e}")) for e in STREAMS}
    P = Prog()
    nc.const_aps.register(F32, 1e-5, cpool[:, 0:1])

    def chunked(dr, c=8):
        return dr.rearrange("(c p) n -> p c n", p=128)

    def wsl(buf, di, dc):
        return buf[:, di * DM + dc * 128: di * DM + (dc + 1) * 128]

    sc_x = contextlib.ExitStack()
    x_fm = [sc_x.enter_context(nc.sbuf_tensor(f"x_fm{c}", [128, S], BF16))
            for c in range(NDC)]
    sc_ctx = contextlib.ExitStack()
    ctx_fm = [sc_ctx.enter_context(nc.sbuf_tensor(f"ctx{c}", [128, TPC], BF16))
              for c in range(NDC)]
    sc_kaug = contextlib.ExitStack()
    k_augA = sc_kaug.enter_context(nc.sbuf_tensor("k_augA", [128, S], BF16))
    k_augB = sc_kaug.enter_context(nc.sbuf_tensor("k_augB", [128, S], BF16))

    sc_p0 = contextlib.ExitStack()
    poolF = sc_p0.enter_context(nc.sbuf_tensor("poolF", [1, S], F32))
    poolB = sc_p0.enter_context(nc.sbuf_tensor("poolB", [65, S], BF16))
    stage_a = [sc_p0.enter_context(nc.sbuf_tensor(f"stg{p}", [128, DM], F32))
               for p in range(2)]
    xb_t = [sc_p0.enter_context(nc.sbuf_tensor(f"xb{p}", [128, DM], BF16))
            for p in range(2)]
    jv_sb = poolF[0:1, :]
    jv_b = poolB[0:1, :]
    jsq_b = poolB[32:33, :]
    onesrow_b = poolB[64:65, :]

    ld_jv = P.emit("y", lambda y: y.dma_start(jv_sb, jv_d[:, :]), sem="y_jv")
    ld_eye = P.emit("y", lambda y: y.dma_start(eye_f[:, :], eye_d[:, :]), sem="y_eye")
    gw_small = 0
    for dst, src in [
        (eye_b[:, :], eye_d[:, :]), (bq_sb[:, :], bq_d[:, :]),
        (bk_sb[:, :], bk_d[:, :]), (bo_sb[:, :], bo_d[:, :]),
        (bfq_sb[:, :], bfq_d[:, :]), (bfg_sb[:, :], bfg_d[:, :]),
        (wg_b[:, :], wg_d[:, :]),
    ]:
        gw_small = P.emit("g", lambda g, d=dst, s=src: g.dma_start(d, s), sem="g_sm")

    def memsets(v):
        v.memset(cpool[:, 0:1], 1e-5)
        v.memset(ones1_b[:, :], 1.0)
        v.memset(ones128_b[:, :], 1.0)
        return v.memset(onesrow_b, 1.0)
    v_init = P.emit("v", memsets)
    v_jsq = P.emit("v", lambda v: v.tensor_tensor(jsq_b, jv_sb, jv_sb,
                                                  ALU.mult),
                   waits=[("y_jv", ld_jv)])
    v_jb = P.emit("v", lambda v: v.tensor_copy(jv_b, jv_sb),
                  waits=[("v", v_jsq)])
    gw_j = 0
    for ka in (k_augA, k_augB):
        P.emit("g", lambda g, ka=ka: g.dma_start(ka[64:65, :], jsq_b),
               waits=[("v", v_jb), ("v", v_init)], sem="g_j")
        P.emit("g", lambda g, ka=ka: g.dma_start(ka[65:66, :], jv_b),
               sem="g_j")
        gw_j = P.emit("g", lambda g, ka=ka: g.dma_start(ka[66:67, :],
                                                        onesrow_b), sem="g_j")

    x_ld, x_cv, x_tr, x_ev = {}, {}, {}, {}
    for t in range(NXT):
        par = t % 2
        x_ld[t] = P.emit("y", lambda y, t=t, par=par: y.dma_start(
            stage_a[par][:, :], xr[t * 128:(t + 1) * 128, :]),
            waits=[("v", x_cv.get(t - 2, 0))], sem=f"y_x{par}")
        x_cv[t] = P.emit("v", lambda v, par=par: v.tensor_copy(
            xb_t[par][:, :], stage_a[par][:, :]),
            waits=[(f"y_x{par}", x_ld[t]), ("t", x_tr.get(t - 2, 0)),
                   ("v", v_init)])

        def trf(t_, par=par):
            for dc in range(NDC):
                ii = t_.transpose(ps_tr[par][:, dc * 128:(dc + 1) * 128],
                                  xb_t[par][:, dc * 128:(dc + 1) * 128],
                                  eye_b[:, :])
            return ii
        x_tr[t] = P.emit("t", trf, waits=[("v", x_cv[t]),
                                          ("g_sm", gw_small),
                                          ("v", x_ev.get(t - 2, 0))])

        def evf(v, t=t, par=par):
            for dc in range(NDC):
                ii = v.tensor_copy(x_fm[dc][:, t * 128:(t + 1) * 128],
                                   ps_tr[par][:, dc * 128:(dc + 1) * 128])
            return ii
        x_ev[t] = P.emit("v", evf, waits=[("t", x_tr[t])])
    x_done = x_ev[NXT - 1]
    xtr_last = x_tr[NXT - 1]

    sc_p0.close()

    sc_c3 = contextlib.ExitStack()
    C3 = sc_c3.enter_context(nc.sbuf_tensor("C3", [16, 3 * TPC], BF16))
    sc_w = contextlib.ExitStack()
    w_big1 = sc_w.enter_context(nc.sbuf_tensor("w_big1", [128, 8 * DM], BF16))
    w_big2 = w_big1

    gw_wq = P.emit("g", lambda g: g.dma_start(
        w_big1[:, :].rearrange("p (c n) -> p c n", c=8), chunked(wq_d[:, :])),
        waits=[("t", xtr_last), ("v", x_done), ("g_j", gw_j)], sem="g_wb")

    mq_mm, mq_ev = {}, {}
    for qc in range(NQT):
        for dc in range(NDC):
            i = qc * NDC + dc
            par = i % 2

            def mmf(t_, qc=qc, dc=dc, par=par):
                for di in range(NDC):
                    ii = t_.matmul(ps_s[par][:, 0:QT], wsl(w_big1, di, dc),
                                   x_fm[di][:, qc * QT:(qc + 1) * QT],
                                   start=(di == 0), stop=(di == NDC - 1))
                return ii
            mq_mm[i] = P.emit("t", mmf, waits=[("v", x_done), ("g_wb", gw_wq),
                                               ("v", mq_ev.get(i - 2, 0))])
            mq_ev[i] = P.emit("v", lambda v, qc=qc, dc=dc, par=par:
                              v.tensor_scalar(mq[dc][:, qc * QT:(qc + 1) * QT],
                                              ps_s[par][:, 0:QT],
                                              bq_sb[:, dc:dc + 1], None,
                                              ALU.add),
                              waits=[("t", mq_mm[i])])
    mq_done = mq_ev[NQT * NDC - 1]

    def xmeanf(v):
        for dc in range(NDC):
            ii = v.reduce_sum(xm_sb[:, dc:dc + 1], x_fm[dc][:, :], axis=AX.X)
        return ii
    v_xm = P.emit("v", xmeanf, waits=[("v", x_done)])
    v_xmb = P.emit("v", lambda v: v.tensor_scalar(
        xm_b[:, :], xm_sb[:, :], 1.0 / S, None, ALU.mult), waits=[("v", v_xm)])

    def glof(t_):
        for dc in range(NDC):
            for di in range(NDC):
                ii = t_.matmul(ps_c[0][:, dc:dc + 1], wsl(w_big1, di, dc),
                               xm_b[:, di:di + 1], start=(di == 0),
                               stop=(di == NDC - 1))
        return ii
    t_glo = P.emit("t", glof, waits=[("v", v_xmb), ("g_wb", gw_wq)])
    v_glo = P.emit("v", lambda v: v.tensor_tensor(
        glo_sb[:, :], ps_c[0][:, 0:8], bq_sb[:, :], ALU.add),
        waits=[("t", t_glo)])
    v_glob = P.emit("v", lambda v: v.tensor_copy(glo_b[:, :], glo_sb[:, :]),
                    waits=[("v", v_glo)])

    def gff(t_):
        for dc in range(NDC):
            for di in range(NDC):
                ii = t_.matmul(ps_c[1][:, dc:dc + 1], wsl(w_big2, di, dc),
                               glo_b[:, di:di + 1], start=(di == 0),
                               stop=(di == NDC - 1))
        return ii
    gw_wfg = P.emit("g", lambda g: g.dma_start(
        w_big1[:, :].rearrange("p (c n) -> p c n", c=8), chunked(wfg_d[:, :])),
        waits=[("t", t_glo), ("t", mq_mm[NQT * NDC - 1])], sem="g_wb")
    t_gf = P.emit("t", gff, waits=[("v", v_glob), ("g_wb", gw_wfg)])
    v_cb1 = P.emit("v", lambda v: v.tensor_tensor(
        cbias[:, :], ps_c[1][:, 0:8], bfq_sb[:, :], ALU.add),
        waits=[("t", t_gf)])
    v_cb = P.emit("v", lambda v: v.tensor_tensor(
        cbias[:, :], cbias[:, :], bfg_sb[:, :], ALU.add), waits=[("v", v_cb1)])

    gw_wfq = P.emit("g", lambda g: g.dma_start(
        w_big2[:, :].rearrange("p (c n) -> p c n", c=8), chunked(wfq_d[:, :])),
        waits=[("t", t_gf)], sem="g_wb")

    sc_p2 = contextlib.ExitStack()
    upz_b = sc_p2.enter_context(nc.sbuf_tensor("upz_b", [128, 512], BF16))
    gw_upz = P.emit("g", lambda g: g.dma_start(upz_b[:, :], upz_d[:, :]),
                    waits=[("t", xtr_last), ("v", x_done), ("g_j", gw_j)],
                    sem="g_upz")
    spz = sc_p2.enter_context(nc.sbuf_tensor("spz", [128, 4096], F32))
    Sp, Sz = spz[0:16, 0:TPC], spz[32:48, 0:TPC]
    cA, cB = spz[64:80, 0:TPC], spz[64:80, TPC:2 * TPC]
    ctp = sc_p2.enter_context(nc.sbuf_tensor("ctp", [128, 1024], BF16))
    c_t_sb = [ctp[:, 0:QT], ctp[:, QT:2 * QT]]

    c_mm, c_tanh, pz_mm, sig = {}, {}, {}, {}
    for qc in range(NQT):
        for dc in range(NDC):
            i = qc * NDC + dc
            par = i % 2

            def cmf(t_, qc=qc, dc=dc, par=par):
                for di in range(NDC):
                    ii = t_.matmul(ps_s[par][:, 0:QT], wsl(w_big2, di, dc),
                                   mq[di][:, qc * QT:(qc + 1) * QT],
                                   start=(di == 0), stop=(di == NDC - 1))
                return ii
            c_mm[i] = P.emit("t", cmf,
                             waits=[("v", v_cb), ("g_wb", gw_wfq), ("v", mq_done),
                                    ("s", c_tanh.get(i - 2, 0))])
            c_tanh[i] = P.emit("s", lambda s, dc=dc, par=par: s.activation(
                c_t_sb[par], ps_s[par][:, 0:QT], AF.Tanh,
                bias=cbias[:, dc:dc + 1]), waits=[("t", c_mm[i])])
            pz_mm[i] = P.emit("t", lambda t_, dc=dc, par=par: t_.matmul(
                ps_c[0][0:64, 0:QT], upz_b[:, dc * 64:(dc + 1) * 64],
                c_t_sb[par], start=(dc == 0), stop=(dc == NDC - 1)),
                waits=[("s", c_tanh[i]), ("s", sig.get(qc - 1, 0)),
                       ("v", v_glo if i == 0 else 0),
                       ("g_upz", gw_upz if i == 0 else 0)])
        sig[qc] = P.emit("s", lambda s, qc=qc: (
            s.activation(Sp[:, qc * QT:(qc + 1) * QT], ps_c[0][0:16, 0:QT],
                         AF.Sigmoid),
            s.activation(Sz[:, qc * QT:(qc + 1) * QT], ps_c[0][32:48, 0:QT],
                         AF.Sigmoid))[-1],
            waits=[("t", pz_mm[qc * NDC + NDC - 1])])
    sig_done = sig[NQT - 1]
    pz_last = pz_mm[NQT * NDC - 1]

    steps = [
        lambda v: v.reciprocal(cA, Sz),
        lambda v: v.tensor_tensor(cA, cA, cA, ALU.mult),
        lambda v: v.tensor_scalar(cA, cA, -1.1920928955078125e-07,
                                  None, ALU.mult),
        lambda v: v.tensor_scalar(cB, Sp, float(S), None, ALU.mult),
        lambda v: v.tensor_copy(C3[:, 0:TPC], cA),
        lambda v: v.tensor_tensor(cA, cA, cB, ALU.mult),
        lambda v: v.tensor_scalar(C3[:, TPC:2 * TPC], cA, -2.0, None,
                                  ALU.mult),
        lambda v: v.tensor_tensor(cA, cA, cB, ALU.mult),
        lambda v: v.tensor_scalar(C3[:, 2 * TPC:3 * TPC], cA, -4.0, None,
                                  ALU.add),
    ]
    vprev = P.emit("v", steps[0], waits=[("s", sig_done)])
    for f in steps[1:]:
        vprev = P.emit("v", f, waits=[("v", vprev)])
    coeff_done = vprev

    sc_p2.close()
    sc_w.close()

    sc_att2 = contextlib.ExitStack()
    v_aug = sc_att2.enter_context(nc.sbuf_tensor("v_aug", [128, NKC * 130],
                                                 BF16))
    kv_w = [sc_att2.enter_context(nc.sbuf_tensor(f"kv{p}", [128, 2048], BF16))
            for p in range(2)]
    e_t = [sc_att2.enter_context(nc.sbuf_tensor(f"e_t{p}", [128, 2 * QT],
                                                BF16)) for p in range(2)]
    rhs_aug = [sc_att2.enter_context(nc.sbuf_tensor(f"rhs{p}", [128, QT],
                                                    BF16)) for p in range(4)]
    rbp = sc_att2.enter_context(nc.sbuf_tensor("rbp", [128, QT], F32))
    rb_sb = [rbp[0:64, :], rbp[64:128, :]]
    bv_rb = sc_att2.enter_context(nc.sbuf_tensor("bv_rb", [1, DM], BF16))
    gw_bv = P.emit("g", lambda g: g.dma_start(bv_rb[:, :], bv_d[:, :]),
                   waits=[("t", pz_last), ("v", coeff_done)], sem="g_bv")

    def vones(v):
        va = v_aug[:, :].rearrange("p (t c) -> p t c", c=130)
        v.memset(va[:, :, 64:65], 1.0)
        return v.memset(va[:, :, 129:130], 1.0)
    v_vones = P.emit("v", vones, waits=[("t", pz_last), ("v", coeff_done)])

    AT = {}
    pair_t_end = {}
    for p in range(NE):
        wpar = p % 2
        cols = slice(p * 128, (p + 1) * 128)
        P.emit("g", lambda g, wpar=wpar, cols=cols: g.dma_start(
            kv_w[wpar][:, 0:1024].rearrange("p (c n) -> p c n", c=8),
            chunked(wk_d[:, cols])),
            waits=[("t", pair_t_end.get(p - 2, pz_last)), ("g_j", gw_j),
                   ("v", coeff_done if p == 0 else 0),
                   ("s", sig_done if p == 0 else 0)], sem=f"g_kv{wpar}")
        kv_gw = P.emit("g", lambda g, wpar=wpar, cols=cols: g.dma_start(
            kv_w[wpar][:, 1024:2048].rearrange("p (c n) -> p c n", c=8),
            chunked(wv_d[:, cols])), sem=f"g_kv{wpar}")

        kev = {}
        for kt in range(8):
            par = kt % 2

            def kmf(t_, kt=kt, par=par, wpar=wpar):
                for di in range(NDC):
                    ii = t_.matmul(ps_s[par][:, 0:QT],
                                   kv_w[wpar][:, di * 128:(di + 1) * 128],
                                   x_fm[di][:, kt * QT:(kt + 1) * QT],
                                   start=(di == 0), stop=(di == NDC - 1))
                return ii
            w = [(f"g_kv{wpar}", kv_gw), ("v", kev.get(kt - 2, 0))]
            if kt == 0:
                w += [("s", AT.get((p, 0, -1, "exp"), sig_done)),
                      ("v", AT.get((p - 1, NQT - 1, "ctxe"),
                                   max(coeff_done, mq_done)))]
            tk = P.emit("t", kmf, waits=w)
            kev[kt] = P.emit("v", lambda v, kt=kt, par=par, p=p: (
                v.tensor_scalar(k_augA[0:64, kt * QT:(kt + 1) * QT],
                                ps_s[par][0:64, 0:QT], bk_sb[0:64, p:p + 1],
                                0.125, ALU.add, ALU.mult),
                v.tensor_scalar(k_augB[0:64, kt * QT:(kt + 1) * QT],
                                ps_s[par][64:128, 0:QT], bk_sb[64:128, p:p + 1],
                                0.125, ALU.add, ALU.mult))[-1],
                waits=[("t", tk),
                       ("t", AT.get((p - 1, NQT - 1, "last_sc"), 0)
                        if kt == 0 else 0)])

        t_bv = P.emit("t", lambda t_, p=p: t_.matmul(
            ps_c[0][:, 0:128], ones1_b[:, :], bv_rb[:, p * 128:(p + 1) * 128],
            start=True, stop=True),
            waits=[("v", AT.get((p - 1, NQT - 1, "ctxe"), 0)),
                   ("s", sig_done if p == 0 else 0), ("v", v_init),
                   ("g_bv", gw_bv)])
        v_bv = P.emit("v", lambda v: v.tensor_copy(bvb[:, :],
                                                   ps_c[0][:, 0:128]),
                      waits=[("t", t_bv)])
        vev = {}
        for tt in range(NKC):
            par = tt % 2

            def vmf(t_, tt=tt, par=par, wpar=wpar):
                for di in range(NDC):
                    ii = t_.matmul(ps_s[par][:, QT:QT + 128],
                                   x_fm[di][:, tt * 128:(tt + 1) * 128],
                                   kv_w[wpar][:, 1024 + di * 128:
                                               1024 + (di + 1) * 128],
                                   start=(di == 0), stop=(di == NDC - 1))
                return ii
            tv = P.emit("t", vmf, waits=[("v", vev.get(tt - 2, 0))])
            base = tt * 130
            vev[tt] = P.emit("v", lambda v, base=base, par=par: (
                v.tensor_tensor(v_aug[:, base:base + 64],
                                ps_s[par][:, QT:QT + 64], bvb[:, 0:64],
                                ALU.add),
                v.tensor_tensor(v_aug[:, base + 65:base + 129],
                                ps_s[par][:, QT + 64:QT + 128], bvb[:, 64:128],
                                ALU.add))[-1],
                waits=[("t", tv), ("v", v_bv), ("v", v_vones if tt == 0 else 0),
                       ("t", AT.get((p - 1, NQT - 1, NKC - 1, "ctx"), 0)
                        if tt == 0 else 0)])
        kv_done_v = max(vev[NKC - 1], kev[7])

        for qc in range(NQT):
            for hi in range(2):
                bi = 2 * (qc % 2) + hi
                h = 2 * p + hi
                guard = AT.get((p, qc - 2, "last_sc"),
                               AT.get((p - 1, NQT - 1, "last_sc"), pz_last))
                AT[(p, qc, hi, "q")] = P.emit(
                    "v", lambda v, bi=bi, hi=hi, p=p, qc=qc: v.tensor_copy(
                        rhs_aug[bi][0:64, :],
                        mq[p][hi * 64:(hi + 1) * 64, qc * QT:(qc + 1) * QT]),
                    waits=[("t", guard),
                           ("v", kv_done_v if qc == 0 and hi == 0 else 0)])
                AT[(p, qc, hi, "c3")] = P.emit(
                    "g", lambda g, bi=bi, h=h, qc=qc: g.dma_start(
                        rhs_aug[bi][64:67, 0:QT],
                        C3[h:h + 1, :].rearrange("o (c n) -> o c n", c=3)
                        [:, :, qc * QT:(qc + 1) * QT]),
                    waits=[("v", AT[(p, qc, hi, "q")])], sem=f"g_c3{qc % 2}")
            def emit_ctx(kc):
                def ctf(t_, kc=kc):
                    base = kc * 130
                    t_.matmul(ps_c[0][0:65, :], v_aug[:, base:base + 65],
                              e_t[kc % 2][:, 0:QT], start=(kc == 0),
                              stop=(kc == NKC - 1))
                    return t_.matmul(ps_c[1][0:65, :],
                                     v_aug[:, base + 65:base + 130],
                                     e_t[kc % 2][:, QT:2 * QT],
                                     start=(kc == 0), stop=(kc == NKC - 1))
                wct = [("s", AT[(p, qc, kc, "exp")])]
                if kc == 0:
                    wct += [("v", AT.get((p, qc - 1, "ctxe"),
                                         AT.get((p - 1, NQT - 1, "ctxe"), 0))),
                            ("v", v_bv)]
                AT[(p, qc, kc, "ctx")] = P.emit("t", ctf, waits=wct)

            for kc in range(NKC):
                par = kc % 2
                wsc = [("s", AT.get((p, qc, kc - 2, "exp"), 0)),
                       (f"g_c3{qc % 2}", AT[(p, qc, 1, "c3")]),
                       ("v", kv_done_v)]
                if kc == 0:
                    wsc += [("g_j", gw_j if p == 0 and qc == 0 else 0)]

                def scf(t_, kc=kc, par=par, biA=2 * (qc % 2),
                        biB=2 * (qc % 2) + 1):
                    t_.matmul(ps_s[par][:, 0:QT],
                              k_augA[0:67, kc * 128:(kc + 1) * 128],
                              rhs_aug[biA][0:67, :], start=True, stop=True)
                    return t_.matmul(ps_s[par][:, QT:2 * QT],
                                     k_augB[0:67, kc * 128:(kc + 1) * 128],
                                     rhs_aug[biB][0:67, :], start=True,
                                     stop=True)
                AT[(p, qc, kc, "sc")] = P.emit("t", scf, waits=wsc)
                AT[(p, qc, kc, "exp")] = P.emit(
                    "s", lambda s, kc=kc, par=par: s.activation(
                        e_t[kc % 2][:, :], ps_s[par][:, :], AF.Exp),
                    waits=[("t", AT[(p, qc, kc, "sc")]),
                           ("t", AT.get((p, qc, kc - 2, "ctx"), 0))])
                if kc >= 1:
                    emit_ctx(kc - 1)
            emit_ctx(NKC - 1)
            AT[(p, qc, "last_sc")] = AT[(p, qc, NKC - 1, "sc")]
            AT[(p, qc, "rec")] = P.emit("v", lambda v: (
                v.reciprocal(recr_f[0], ps_c[0][64:65, :]),
                v.reciprocal(recr_f[1], ps_c[1][64:65, :]))[-1],
                waits=[("t", AT[(p, qc, NKC - 1, "ctx")]),
                       ("y_rb", AT.get((p, qc - 1, "rb"),
                                       AT.get((p - 1, NQT - 1, "rb"), 0)))])
            P.emit("y", lambda y: y.dma_start(
                rbd[:, :], rowsF[0:1, 0:2 * QT]),
                waits=[("v", AT[(p, qc, "rec")])], sem="y_rb")
            P.emit("y", lambda y: y.dma_start(
                rb_sb[0], bass.AP(rbd, 0, [[0, 64], [1, QT]])), sem="y_rb")
            AT[(p, qc, "rb")] = P.emit("y", lambda y: y.dma_start(
                rb_sb[1], bass.AP(rbd, QT, [[0, 64], [1, QT]])), sem="y_rb")
            AT[(p, qc, "ctxe")] = P.emit("v", lambda v, p=p, qc=qc: (
                v.tensor_tensor(ctx_fm[p][0:64, qc * QT:(qc + 1) * QT],
                                ps_c[0][0:64, :], rb_sb[0], ALU.mult),
                v.tensor_tensor(ctx_fm[p][64:128, qc * QT:(qc + 1) * QT],
                                ps_c[1][0:64, :], rb_sb[1], ALU.mult))[-1],
                waits=[("y_rb", AT[(p, qc, "rb")])])
            AT[(p, qc + 1, -2, "exp")] = AT[(p, qc, NKC - 2, "exp")]
            AT[(p, qc + 1, -1, "exp")] = AT[(p, qc, NKC - 1, "exp")]
            AT[(p, qc + 1, -2, "ctx")] = AT[(p, qc, NKC - 2, "ctx")]
            AT[(p, qc + 1, -1, "ctx")] = AT[(p, qc, NKC - 1, "ctx")]
            if qc == NQT - 1:
                AT[(p + 1, 0, -2, "exp")] = AT[(p, qc, NKC - 2, "exp")]
                AT[(p + 1, 0, -1, "exp")] = AT[(p, qc, NKC - 1, "exp")]
                AT[(p + 1, 0, -2, "ctx")] = AT[(p, qc, NKC - 2, "ctx")]
                AT[(p + 1, 0, -1, "ctx")] = AT[(p, qc, NKC - 1, "ctx")]
                AT[(p + 1, 0, "rb_prev")] = AT[(p, qc, "rb")]
        pair_t_end[p] = P.cnt["t"]
    att_v = AT[(NE - 1, NQT - 1, "ctxe")]
    att_rbs = AT[(NE - 1, NQT - 1, NKC - 1, "exp")]
    att_t = P.cnt["t"]

    sc_att2.close()
    sc_c3.close()
    sc_kaug.close()
    sc_p4 = contextlib.ExitStack()
    wo_b = sc_p4.enter_context(nc.sbuf_tensor("wo_b", [128, 8 * DM], BF16))
    zbuf_a = sc_p4.enter_context(nc.sbuf_tensor("zbuf_a", [128, 8 * QT], F32))
    mb_a = sc_p4.enter_context(nc.sbuf_tensor("mb_a", [128, QT], F32))
    ivb_a = sc_p4.enter_context(nc.sbuf_tensor("ivb_a", [128, QT], F32))
    rows_a = sc_p4.enter_context(nc.sbuf_tensor("rows_a", [1, 2048], F32))
    zbp_a = [sc_p4.enter_context(nc.sbuf_tensor(f"zbp_a{p}", [128, QT], BF16))
             for p in range(2)]
    zsp_a = [sc_p4.enter_context(nc.sbuf_tensor(f"zsp_a{p}", [128, QT], BF16))
             for p in range(2)]

    gw_wo = P.emit("g", lambda g: g.dma_start(
        wo_b[:, :].rearrange("p (c n) -> p c n", c=8), chunked(wo_d[:, :])),
        waits=[("t", att_t), ("s", att_rbs), ("v", att_v)], sem="g_wb")

    def ln_phase(qc, mode, y_dst, st_, zbuf, mb_sb, invb_sb, rows4,
                 zb_t, zsq_t):
        L = {}
        mrow, srow = rows4[0:1, 0:512], rows4[0:1, 512:1024]
        vrow, vrow2 = rows4[0:1, 1024:1536], rows4[0:1, 1536:2048]
        zev = st_.setdefault("zev", {})
        for dc in range(NDC):
            gi = qc * NDC + dc
            par = gi % 2
            if mode == "attn":
                wmm = [("v", att_v), ("g_wb", gw_wo),
                       ("v", zev.get(gi - 2, 0)),
                       ("s", att_rbs if gi < 2 else 0),
                       ("s", st_.get("prev_bcs", 0) if dc < 2 else 0)]

                def mmf(t_, qc=qc, dc=dc, par=par):
                    for di in range(NDC):
                        ii = t_.matmul(ps_s[par][:, 0:QT], wsl(wo_b, di, dc),
                                       ctx_fm[di][:, qc * QT:(qc + 1) * QT],
                                       start=(di == 0), stop=(di == NDC - 1))
                    return ii
                L[(dc, "mm")] = P.emit("t", mmf, waits=wmm)
                zev[gi] = L[(dc, "z")] = P.emit(
                    "v", lambda v, qc=qc, dc=dc, par=par:
                    v.scalar_tensor_tensor(
                        zbuf[:, dc * QT:(dc + 1) * QT], ps_s[par][:, 0:QT],
                        bo_sb[:, dc:dc + 1],
                        x_fm[dc][:, qc * QT:(qc + 1) * QT], ALU.add, ALU.add),
                    waits=[("t", L[(dc, "mm")])])
            else:
                L[(dc, "z")] = P.emit(
                    "v", lambda v, qc=qc, dc=dc: v.scalar_tensor_tensor(
                        zbuf[:, dc * QT:(dc + 1) * QT],
                        st_["routed"][dc][:, qc * QT:(qc + 1) * QT], 1.0 / 3.0,
                        y1[dc][:, qc * QT:(qc + 1) * QT], ALU.mult, ALU.add),
                    waits=[("v", st_.get("src_v", 0)),
                           ("t", st_.get("prev_ttr", 0))])
            zg = (L.get((dc - 2, "sums"), st_.get("prev_sums_last", 0)
                        if dc < 2 else 0))
            L[(dc, "zb")] = P.emit("v", lambda v, dc=dc: v.tensor_copy(
                zb_t[dc % 2][:, :], zbuf[:, dc * QT:(dc + 1) * QT]),
                waits=[("v", L[(dc, "z")]), ("t", zg)])
            L[(dc, "zsq")] = P.emit("s", lambda s, dc=dc: s.activation(
                zsq_t[dc % 2][:, :], zbuf[:, dc * QT:(dc + 1) * QT],
                AF.Square), waits=[("v", L[(dc, "z")]), ("t", zg)])

            def emit_sums(d2):
                wsum = [("v", L[(d2, "zb")]), ("s", L[(d2, "zsq")])]
                if d2 == 0:
                    wsum += [("s", st_.get("psc_guard_s", 0)),
                             ("v", st_.get("psc_guard_v", 0))]
                L[(d2, "sums")] = P.emit("t", lambda t_, d2=d2: (
                    t_.matmul(ps_c[0][0:1, 0:QT], ones128_b[:, :],
                              zb_t[d2 % 2][:, :], start=(d2 == 0),
                              stop=(d2 == NDC - 1)),
                    t_.matmul(ps_c[1][0:1, 0:QT], ones128_b[:, :],
                              zsq_t[d2 % 2][:, :], start=(d2 == 0),
                              stop=(d2 == NDC - 1)))[-1], waits=wsum)

            if dc >= 1:
                emit_sums(dc - 1)
            if dc == NDC - 1:
                emit_sums(dc)
        r0 = P.emit("v", lambda v: v.tensor_scalar(
            mrow, ps_c[0][0:1, 0:QT], 1.0 / DM, None, ALU.mult),
            waits=[("t", L[(NDC - 1, "sums")])])
        r1 = P.emit("v", lambda v: v.tensor_scalar(
            vrow, ps_c[1][0:1, 0:QT], 1.0 / DM, None, ALU.mult),
            waits=[("v", r0)])
        r2 = P.emit("v", lambda v: v.tensor_tensor(vrow2, mrow, mrow,
                                                   ALU.mult), waits=[("v", r1)])
        r3 = P.emit("v", lambda v: v.tensor_tensor(vrow, vrow, vrow2,
                                                   ALU.subtract),
                    waits=[("v", r2)])
        r4 = P.emit("s", lambda s: s.activation(srow, vrow, AF.Sqrt,
                                                bias=1e-5), waits=[("v", r3)])
        r5 = P.emit("v", lambda v: v.reciprocal(vrow2, srow),
                    waits=[("s", r4)])
        r6 = P.emit("v", lambda v: (v.tensor_copy(mrow_b, mrow),
                                    v.tensor_copy(irow_b, vrow2))[-1],
                    waits=[("v", r5)])
        wbc = [("v", r6), ("v", L[(NDC - 1, "z")]),
               ("s", st_.get("pss_guard_s", 0)),
               ("v", st_.get("pss_guard_v", 0))]
        bct = P.emit("t", lambda t_: (
            t_.matmul(ps_s[0][:, 0:QT], ones1_b[:, :], mrow_b, start=True,
                      stop=True),
            t_.matmul(ps_s[1][:, 0:QT], ones1_b[:, :], irow_b, start=True,
                      stop=True))[-1], waits=wbc)
        bcs = P.emit("s", lambda s: (s.copy(mb_sb[:, :], ps_s[0][:, 0:QT]),
                                     s.copy(invb_sb[:, :],
                                            ps_s[1][:, 0:QT]))[-1],
                     waits=[("t", bct)])
        subv = P.emit("v", lambda v: [v.tensor_tensor(
            zbuf[:, dc * QT:(dc + 1) * QT], zbuf[:, dc * QT:(dc + 1) * QT],
            mb_sb[:, :], ALU.subtract) for dc in range(NDC)][-1],
            waits=[("s", bcs)])
        if y_dst is not None:
            fin = P.emit("v", lambda v, qc=qc: [v.tensor_tensor(
                y_dst[dc][:, qc * QT:(qc + 1) * QT],
                zbuf[:, dc * QT:(dc + 1) * QT], invb_sb[:, :], ALU.mult)
                for dc in range(NDC)][-1], waits=[("v", subv)])
        else:
            fin = P.emit("v", lambda v: [v.tensor_tensor(
                zbuf[:, dc * QT:(dc + 1) * QT], zbuf[:, dc * QT:(dc + 1) * QT],
                invb_sb[:, :], ALU.mult) for dc in range(NDC)][-1],
                waits=[("v", subv)])
        st_["prev_sums_last"] = L[(NDC - 1, "sums")]
        st_["prev_bcs"] = bcs
        st_["pss_guard_s"] = bcs
        st_["psc_guard_v"] = r1
        L["fin"], L["bcs"], L["r1"] = fin, bcs, r1
        return L

    st1 = {"psc_guard_s": sig_done}
    gate_ws = {}
    gw_wr8 = 0
    for qc in range(NQT):
        L1 = ln_phase(qc, "attn", y1, st1, zbuf_a, mb_a, ivb_a, rows_a,
                      zbp_a, zsp_a)
        for tt in range(4):
            tok = qc * QT + tt * 128

            def gmf(t_, qc=qc, tt=tt, tok=tok):
                for di in range(NDC):
                    ii = t_.matmul(ps_c[0][:, 8 * tt:8 * tt + 8],
                                   y1[di][:, tok:tok + 128],
                                   wg_b[:, di * 8:(di + 1) * 8],
                                   start=(di == 0), stop=(di == NDC - 1))
                return ii
            gt_t = P.emit("t", gmf, waits=[("v", L1["fin"]),
                                           ("s", gate_ws.get((qc, tt - 1), 0))])
            gs = P.emit("s", lambda s, tt=tt: s.activation(
                gsig[:, :], ps_c[0][:, 8 * tt:8 * tt + 8], AF.Sigmoid),
                waits=[("t", gt_t)])
            st1["psc_guard_s"] = gs
            chain = [
                lambda v: v.reduce_max(gt[:, 0:1], gsig[:, :], axis=AX.X),
                lambda v: v.tensor_scalar(gt[:, 8:16], gsig[:, :], gt[:, 0:1],
                                          None, ALU.is_equal),
                lambda v: v.scalar_tensor_tensor(gt[:, 16:24], gt[:, 8:16],
                                                 -1e30, gsig[:, :], ALU.mult,
                                                 ALU.add),
                lambda v: v.reduce_max(gt[:, 1:2], gt[:, 16:24], axis=AX.X),
                lambda v: v.tensor_scalar(gt[:, 24:32], gt[:, 16:24],
                                          gt[:, 1:2], None, ALU.is_equal),
                lambda v: v.tensor_tensor(gt[:, 24:32], gt[:, 24:32],
                                          gt[:, 8:16], ALU.add),
                lambda v: v.tensor_tensor(gt[:, 2:3], gt[:, 0:1], gt[:, 1:2],
                                          ALU.add),
                lambda v: v.reciprocal(gt[:, 3:4], gt[:, 2:3]),
                lambda v: v.tensor_tensor(gt[:, 32:40], gsig[:, :],
                                          gt[:, 24:32], ALU.mult),
                lambda v: v.tensor_scalar(wtok[:, :], gt[:, 32:40], gt[:, 3:4],
                                          None, ALU.mult),
            ]
            vv = P.emit("v", chain[0], waits=[("s", gs)])
            for f in chain[1:]:
                vv = P.emit("v", f, waits=[("v", vv)])
            wts = 0
            for rnd in range(2):
                def wtf(t_, rnd=rnd):
                    for e4 in range(4):
                        ii = t_.transpose(
                            ps_s[0][0:1, QT + e4 * 128:QT + (e4 + 1) * 128],
                            wtok[:, rnd * 4 + e4:rnd * 4 + e4 + 1],
                            eye_f[:, :])
                    return ii
                wt_t = P.emit("t", wtf, waits=[("v", vv), ("s", wts),
                                               ("y_eye", ld_eye)])
                wts = P.emit("s", lambda s, rnd=rnd: s.copy(
                    wstg[0:1, 0:512], ps_s[0][0:1, QT:QT + 512]),
                    waits=[("t", wt_t), ("g_wr8", gw_wr8)])
                for e4 in range(4):
                    gw_wr8 = P.emit("g", lambda g, rnd=rnd, tok=tok, e4=e4:
                                    g.dma_start(
                        wr8[rnd * 4 + e4:rnd * 4 + e4 + 1, tok:tok + 128],
                        wstg[0:1, e4 * 128:(e4 + 1) * 128]),
                        waits=[("s", wts)] if e4 == 0 else [], sem="g_wr8")
            gate_ws[(qc, tt)] = wts
    wr8_done = gw_wr8
    p4_v = P.cnt["v"]
    p4_s = P.cnt["s"]
    p4_t = P.cnt["t"]

    sc_p4.close()
    sc_ctx.close()
    sc_x.close()

    sc_r = contextlib.ExitStack()
    routed = [sc_r.enter_context(nc.sbuf_tensor(f"rt{c}", [128, TPC], BF16))
              for c in range(NDC)]
    sc_ffn = contextlib.ExitStack()
    we1f = sc_ffn.enter_context(nc.sbuf_tensor("we1f", [128, 8 * FF], BF16))
    we3f = sc_ffn.enter_context(nc.sbuf_tensor("we3f", [128, 8 * FF], BF16))
    we2f = sc_ffn.enter_context(nc.sbuf_tensor("we2f", [128, NFC * DM], BF16))
    h_sb = sc_ffn.enter_context(nc.sbuf_tensor("h_sb", [128, NFC * QT], BF16))
    sa_sb = [sc_ffn.enter_context(nc.sbuf_tensor(f"sa{p}", [128, QT], BF16))
             for p in range(2)]
    h1_sb = [sc_ffn.enter_context(nc.sbuf_tensor(f"h1{p}", [128, QT], F32))
             for p in range(2)]
    wbb = sc_ffn.enter_context(nc.sbuf_tensor("wbb", [128, QT], BF16))

    FW = {}
    ab_last, o_last, o_racc_last = {}, {}, {}
    ffn_s_last, ffn_h1_last = 0, 0
    space_guard = [("v", p4_v), ("t", p4_t), ("s", p4_s)]
    stg_dma = {}
    for ei in range(NE + 1):
        e = ei - 1
        s1d = ws1_d[:, :] if e < 0 else we1_d[e, :, :]
        s3d = ws3_d[:, :] if e < 0 else we3_d[e, :, :]
        s2d = ws2_d[:, :] if e < 0 else we2_d[e, :, :]
        FW[(ei, "w1")] = P.emit("g", lambda g, s1d=s1d: g.dma_start(
            we1f[:, :].rearrange("p (c n) -> p c n", c=8), chunked(s1d)),
            waits=[("t", ab_last.get(ei - 1, 0))] +
                  (space_guard if ei == 0 else []), sem="g_we13")
        FW[(ei, "w3")] = P.emit("g", lambda g, s3d=s3d: g.dma_start(
            we3f[:, :].rearrange("p (c n) -> p c n", c=8), chunked(s3d)),
            sem="g_we13")
        FW[(ei, "w2")] = P.emit("g", lambda g, s2d=s2d: g.dma_start(
            we2f[:, :].rearrange("p (c n) -> p c n", c=16),
            s2d.rearrange("(c p) n -> p c n", p=128)),
            waits=[("t", o_last.get(ei - 1, 0))], sem="g_we2")
        ht, ot, sls = {}, {}, {}
        for tt in range(NQT):
            if e >= 0:
                sd = P.emit("g", lambda g, e=e, tt=tt: g.dma_start(
                    wstg[0:1, (tt % 2) * 512:(tt % 2) * 512 + 512],
                    wr8[e:e + 1, tt * QT:(tt + 1) * QT]),
                    waits=[("g_wr8", wr8_done),
                           ("t", stg_dma.get(tt % 2, 0))],
                    sem=f"g_st{tt % 2}")
                bw = P.emit("t", lambda t_, tt=tt: t_.matmul(
                    ps_c[0][:, 0:QT], ones1_b[:, :],
                    wstg[0:1, (tt % 2) * 512:(tt % 2) * 512 + 512],
                    start=True, stop=True),
                    waits=[(f"g_st{tt % 2}", sd),
                           ("v", ot.get((tt - 1, "racc"),
                                        o_racc_last.get(ei - 1, 0)))])
                stg_dma[tt % 2] = bw
                bws = P.emit("s", lambda s: s.copy(wbb[:, :],
                                                   ps_c[0][:, 0:QT]),
                             waits=[("t", bw)])
            else:
                bws = 0
            for fc in range(NFC):
                par = fc % 2

                def abf(t_, fc=fc, par=par, tt=tt):
                    for di in range(NDC):
                        t_.matmul(ps_s[par][:, 0:QT],
                                  we1f[:, di * FF + fc * 128:
                                       di * FF + (fc + 1) * 128],
                                  y1[di][:, tt * QT:(tt + 1) * QT],
                                  start=(di == 0), stop=(di == NDC - 1))
                    for di in range(NDC):
                        ii = t_.matmul(ps_s[par][:, QT:2 * QT],
                                       we3f[:, di * FF + fc * 128:
                                            di * FF + (fc + 1) * 128],
                                       y1[di][:, tt * QT:(tt + 1) * QT],
                                       start=(di == 0), stop=(di == NDC - 1))
                    return ii
                w = [("g_we13", FW[(ei, "w3")]),
                     ("v", ht.get((tt, fc - 2), 0)),
                     ("s", sls.get((tt, fc - 2), 0))]
                ab = P.emit("t", abf, waits=w)
                sls[(tt, fc)] = sl = P.emit(
                    "s", lambda s, fc=fc, par=par: s.activation(
                        sa_sb[fc % 2][:, :], ps_s[par][:, 0:QT], AF.Silu),
                    waits=[("t", ab)])
                ffn_s_last = sl
                h1 = P.emit("v", lambda v, fc=fc, par=par: v.tensor_tensor(
                    h1_sb[fc % 2][:, :], sa_sb[fc % 2][:, :],
                    ps_s[par][:, QT:2 * QT], ALU.mult), waits=[("s", sl)])
                ffn_h1_last = h1
                wh = [("v", h1), ("t", ot.get((tt - 1, "last"),
                                              o_last.get(ei - 1, 0)))]
                if e >= 0:
                    ht[(tt, fc)] = P.emit(
                        "v", lambda v, fc=fc: v.tensor_tensor(
                            h_sb[:, fc * QT:(fc + 1) * QT],
                            h1_sb[fc % 2][:, :], wbb[:, :], ALU.mult),
                        waits=wh + [("s", bws)])
                else:
                    ht[(tt, fc)] = P.emit(
                        "v", lambda v, fc=fc: v.tensor_copy(
                            h_sb[:, fc * QT:(fc + 1) * QT],
                            h1_sb[fc % 2][:, :]), waits=wh)
            if tt == NQT - 1:
                ab_last[ei] = P.cnt["t"]
            for dc in range(NDC):
                par = dc % 2

                def omf(t_, dc=dc, par=par):
                    for fc in range(NFC):
                        ii = t_.matmul(ps_c[par][:, 0:QT],
                                       we2f[:, fc * DM + dc * 128:
                                            fc * DM + (dc + 1) * 128],
                                       h_sb[:, fc * QT:(fc + 1) * QT],
                                       start=(fc == 0), stop=(fc == NFC - 1))
                    return ii
                w = [("v", ht[(tt, NFC - 1)]), ("g_we2", FW[(ei, "w2")]),
                     ("v", ot.get((tt, dc - 2, "racc"), 0))]
                if dc == 0 and e >= 0:
                    w.append(("s", bws))
                if dc < 2:
                    w.append(("v", ot.get((tt - 1, "racc"),
                                          o_racc_last.get(ei - 1, 0))))
                ot[(tt, dc, "t")] = P.emit("t", omf, waits=w)
                if ei == 0:
                    ot[(tt, dc, "racc")] = P.emit(
                        "v", lambda v, dc=dc, par=par, tt=tt: v.tensor_copy(
                            routed[dc][:, tt * QT:(tt + 1) * QT],
                            ps_c[par][:, 0:QT]),
                        waits=[("t", ot[(tt, dc, "t")])])
                else:
                    ot[(tt, dc, "racc")] = P.emit(
                        "v", lambda v, dc=dc, par=par, tt=tt: v.tensor_tensor(
                            routed[dc][:, tt * QT:(tt + 1) * QT],
                            routed[dc][:, tt * QT:(tt + 1) * QT],
                            ps_c[par][:, 0:QT], ALU.add),
                        waits=[("t", ot[(tt, dc, "t")])])
            ot[(tt, "last")] = ot[(tt, NDC - 1, "t")]
            ot[(tt, "racc")] = ot[(tt, NDC - 1, "racc")]
        o_last[ei] = ot[(NQT - 1, "last")]
        o_racc_last[ei] = ot[(NQT - 1, "racc")]
    ffn_done_v = o_racc_last[NE]
    p5_t = P.cnt["t"]

    sc_ffn.close()

    sc_p6 = contextlib.ExitStack()
    zbuf_b = sc_p6.enter_context(nc.sbuf_tensor("zbuf_b", [128, 8 * QT], F32))
    mb_b = sc_p6.enter_context(nc.sbuf_tensor("mb_b", [128, QT], F32))
    ivb_b = sc_p6.enter_context(nc.sbuf_tensor("ivb_b", [128, QT], F32))
    rows_b = sc_p6.enter_context(nc.sbuf_tensor("rows_b", [1, 2048], F32))
    zbp_b = [sc_p6.enter_context(nc.sbuf_tensor(f"zbp_b{p}", [128, QT], BF16))
             for p in range(2)]
    zsp_b = [sc_p6.enter_context(nc.sbuf_tensor(f"zsp_b{p}", [128, QT], BF16))
             for p in range(2)]
    stage_b = [sc_p6.enter_context(nc.sbuf_tensor(f"ost{p}", [128, DM], F32))
               for p in range(2)]

    st2 = {"src_v": ffn_done_v, "psc_guard_v": ffn_done_v,
           "pss_guard_s": ffn_s_last, "pss_guard_v": ffn_h1_last,
           "prev_ttr": p5_t, "routed": routed}
    ST = {}
    prev_st = {0: 0, 1: 0}
    for qc in range(NQT):
        L2 = ln_phase(qc, "ffn", None, st2, zbuf_b, mb_b, ivb_b, rows_b,
                      zbp_b, zsp_b)
        last_ttr = 0
        for tt in range(4):
            par = tt % 2
            tok = qc * QT + tt * 128

            def trf(t_, tt=tt):
                for dc in range(NDC):
                    ii = t_.transpose(
                        ps_c[dc // 4][:, (dc % 4) * 128:(dc % 4 + 1) * 128],
                        zbuf_b[:, dc * QT + tt * 128:dc * QT + (tt + 1) * 128],
                        eye_f[:, :])
                return ii
            ttr = P.emit("t", trf, waits=[("v", L2["fin"]),
                                          ("v", ST.get((qc, tt - 1, "ev"), 0)),
                                          ("v", L2["r1"] if tt == 0 else 0)])
            last_ttr = ttr
            ST[(qc, tt, "ev")] = P.emit("v", lambda v, par=par: [
                v.tensor_copy(stage_b[par][:, dc * 128:(dc + 1) * 128],
                              ps_c[dc // 4][:, (dc % 4) * 128:
                                            (dc % 4 + 1) * 128])
                for dc in range(NDC)][-1],
                waits=[("t", ttr), (f"y_st{par}", prev_st[par])])
            ST[(qc, tt, "st")] = P.emit(
                "y", lambda y, par=par, tok=tok: y.dma_start(
                    out_d[tok:tok + 128, :], stage_b[par][:, :]),
                waits=[("v", ST[(qc, tt, "ev")])], sem=f"y_st{par}")
            prev_st[par] = ST[(qc, tt, "st")]
        st2["prev_ttr"] = last_ttr
        st2["psc_guard_v"] = ST[(qc, 3, "ev")]
    P.emit("y", lambda y: y.nop(),
           waits=[("y_st0", ST[(NQT - 1, 2, "st")]),
                  ("y_st1", ST[(NQT - 1, 3, "st")])], inc=False)

    P.finalize(nc, sems)
    sc_p6.close()
    sc_r.close()
    stk.close()
    return nc


def _host_inputs(inputs):
    x = np.ascontiguousarray(np.asarray(inputs["L0_attn_hidden_states"],
                                        np.float32))
    up = np.asarray(inputs["L0_attn_up"], np.float32).reshape(-1)
    uz = np.asarray(inputs["L0_attn_uz"], np.float32).reshape(-1)
    upz = np.zeros((128, 512), np.float32)
    r = np.arange(128)
    for dc in range(8):
        h = 2 * dc + r // 64
        upz[r, dc * 64 + h] = up[dc * 128 + r]
        upz[r, dc * 64 + 32 + h] = uz[dc * 128 + r]
    wg = np.asarray(inputs["L0_ffn_Wg"], np.float32)
    wg_m = np.ascontiguousarray(wg.reshape(8, 128, 8).transpose(1, 0, 2)
                                .reshape(128, 64))

    def colmat(name):
        a = np.asarray(inputs[name], np.float32).reshape(-1)
        return np.ascontiguousarray(a.reshape(8, 128).T)

    def arr(name):
        return np.ascontiguousarray(np.asarray(inputs[name], np.float32))

    shared = {
        "eye": np.eye(128, dtype=np.float32),
        "wq": arr("L0_attn_wq"), "wk": arr("L0_attn_wk"),
        "wv": arr("L0_attn_wv"), "wo": arr("L0_attn_wo"),
        "wfq": arr("L0_attn_wfq"), "wfg": arr("L0_attn_wfg"),
        "bq_m": colmat("L0_attn_bq"), "bk_m": colmat("L0_attn_bk"),
        "bo_m": colmat("L0_attn_bo"), "bfq_m": colmat("L0_attn_bfq"),
        "bfg_m": colmat("L0_attn_bfg"),
        "bv_r": np.ascontiguousarray(
            np.asarray(inputs["L0_attn_bv"], np.float32).reshape(1, -1)),
        "upz_m": upz, "wg_m": wg_m,
        "ws1": arr("L0_ffn_Ws1"), "ws3": arr("L0_ffn_Ws3"),
        "ws2": arr("L0_ffn_Ws2"), "we1": arr("L0_ffn_We1"),
        "we3": arr("L0_ffn_We3"), "we2": arr("L0_ffn_We2"),
    }
    in_maps = []
    for c in range(8):
        b, off = c // 2, (c % 2) * TPC
        m = dict(shared)
        m["xr"] = np.ascontiguousarray(
            np.concatenate([x[b, off:], x[b, :off]], axis=0))
        m["jv"] = np.ascontiguousarray(
            np.concatenate([np.arange(off, S), np.arange(0, off)])
            .astype(np.float32).reshape(1, S))
        in_maps.append(m)
    return in_maps


_NC_CACHE = {}


def kernel(**inputs):
    if "nc" not in _NC_CACHE:
        _NC_CACHE["nc"] = build_full()
    nc = _NC_CACHE["nc"]
    in_maps = _host_inputs(inputs)
    res = run_bass_kernel_spmd(nc, in_maps, core_ids=list(range(8)))
    out = np.empty((B, S, DM), np.float32)
    for c in range(8):
        b, off = c // 2, (c % 2) * TPC
        out[b, off:off + TPC] = res.results[c]["out"]
    return out


# version 05
# speedup vs baseline: 1.3667x; 1.0535x over previous
import contextlib
import numpy as np
import concourse.bass as bass
import concourse.mybir as mybir
from concourse.bass_utils import run_bass_kernel_spmd

F32 = mybir.dt.float32
BF16 = mybir.dt.bfloat16
AF = mybir.ActivationFunctionType
ALU = mybir.AluOpType
AX = mybir.AxisListType

B, S, DM, H, D = 4, 4096, 1024, 16, 64
TPC, FF, NE, QT = 2048, 2048, 8, 512
NQT, NKC, NDC, NFC, NXT = TPC // QT, S // 128, DM // 128, FF // 128, S // 128

ENGS = ("t", "s", "v", "g", "y")


class Prog:

    def __init__(self):
        self.recs = {e: [] for e in ENGS}
        self.cnt = {}

    def emit(self, eng, fn, waits=(), inc=True, sem=None):
        semn = sem or eng
        inc_k = 16 if eng in ("g", "y") else 1
        ws = [(e, v) for (e, v) in waits if v and v > 0]
        self.recs[eng].append((fn, ws, semn if inc else None, inc_k))
        if inc:
            self.cnt[semn] = self.cnt.get(semn, 0) + inc_k
        return self.cnt.get(semn, 0)

    def finalize(self, nc, sems):
        def run(eng_obj, name):
            lw = {}
            for fn, ws, semn, inc_k in self.recs[name]:
                for se, val in ws:
                    if val > lw.get(se, 0):
                        eng_obj.wait_ge(sems[se], val)
                        lw[se] = val
                inst = fn(eng_obj)
                if semn is not None:
                    inst.then_inc(sems[semn], inc_k)

        with nc.Block() as block:
            block.tensor(lambda e: run(e, "t"))
            block.scalar(lambda e: run(e, "s"))
            block.vector(lambda e: run(e, "v"))
            block.gpsimd(lambda e: run(e, "g"))
            block.sync(lambda e: run(e, "y"))


def build_full():
    nc = bass.Bass("TRN2", target_bir_lowering=False)

    dt_ = nc.dram_tensor
    xr = dt_("xr", [S, DM], F32, kind="ExternalInput")
    jv_d = dt_("jv", [1, S], F32, kind="ExternalInput")
    eye_d = dt_("eye", [128, 128], F32, kind="ExternalInput")
    wq_d = dt_("wq", [DM, DM], F32, kind="ExternalInput")
    wk_d = dt_("wk", [DM, DM], F32, kind="ExternalInput")
    wv_d = dt_("wv", [DM, DM], F32, kind="ExternalInput")
    wo_d = dt_("wo", [DM, DM], F32, kind="ExternalInput")
    wfq_d = dt_("wfq", [DM, DM], F32, kind="ExternalInput")
    wfg_d = dt_("wfg", [DM, DM], F32, kind="ExternalInput")
    bq_d = dt_("bq_m", [128, 8], F32, kind="ExternalInput")
    bk_d = dt_("bk_m", [128, 8], F32, kind="ExternalInput")
    bo_d = dt_("bo_m", [128, 8], F32, kind="ExternalInput")
    bfq_d = dt_("bfq_m", [128, 8], F32, kind="ExternalInput")
    bfg_d = dt_("bfg_m", [128, 8], F32, kind="ExternalInput")
    bv_d = dt_("bv_r", [1, DM], F32, kind="ExternalInput")
    upz_d = dt_("upz_m", [128, 512], F32, kind="ExternalInput")
    wg_d = dt_("wg_m", [128, 64], F32, kind="ExternalInput")
    ws1_d = dt_("ws1", [DM, FF], F32, kind="ExternalInput")
    ws3_d = dt_("ws3", [DM, FF], F32, kind="ExternalInput")
    ws2_d = dt_("ws2", [FF, DM], F32, kind="ExternalInput")
    we1_d = dt_("we1", [NE, DM, FF], F32, kind="ExternalInput")
    we3_d = dt_("we3", [NE, DM, FF], F32, kind="ExternalInput")
    we2_d = dt_("we2", [NE, FF, DM], F32, kind="ExternalInput")
    out_d = dt_("out", [TPC, DM], F32, kind="ExternalOutput")
    rbd = dt_("rbd", [1, 2 * QT], F32, kind="Internal")

    al = nc.alloc_sbuf_tensor
    eye_f = al("eye_f", [128, 128], F32)
    eye_b = al("eye_b", [128, 128], BF16)
    ones1_b = al("ones1_b", [1, 128], BF16)
    ones128_b = al("ones128_b", [128, 1], BF16)
    cpool = al("cpool", [128, 2], F32)
    bq_sb = al("bq_sb", [128, 8], F32)
    bk_sb = al("bk_sb", [128, 8], F32)
    bo_sb = al("bo_sb", [128, 8], F32)
    bfq_sb = al("bfq_sb", [128, 8], F32)
    bfg_sb = al("bfg_sb", [128, 8], F32)
    cbias = al("cbias", [128, 8], F32)
    bvb = al("bvb", [128, 128], F32)
    wg_b = al("wg_b", [128, 64], BF16)
    xm_sb = al("xm_sb", [128, 8], F32)
    xm_b = al("xm_b", [128, 8], BF16)
    glo_sb = al("glo_sb", [128, 8], F32)
    glo_b = al("glo_b", [128, 8], BF16)
    rowsF = al("rowsF", [1, 1024], F32)
    recr_f = [rowsF[0:1, 0:512], rowsF[0:1, 512:1024]]
    rowsB = al("rowsB", [1, 2048], BF16)
    mrow_b, irow_b = rowsB[0:1, 0:512], rowsB[0:1, 512:1024]
    recr = [rowsB[0:1, 1024:1536], rowsB[0:1, 1536:2048]]
    gsig = al("gsig", [128, 8], F32)
    gt = al("gt", [128, 40], F32)
    wtok = al("wtok", [128, 8], F32)
    wr8 = al("wr8", [8, TPC], BF16)
    wstg = al("wstg", [1, 1024], BF16)
    mq = [al(f"mq{c}", [128, TPC], BF16) for c in range(NDC)]
    y1 = mq

    ap_ = nc.alloc_psum_tensor
    ps_s = [ap_(f"ps_s{p}", [128, 2 * QT], F32) for p in range(2)]
    ps_c = [ap_(f"ps_c{i}", [128, QT], F32) for i in range(2)]
    ps_tr = [ap_(f"ps_tr{p}", [128, 1024], BF16) for p in range(2)]

    stk = contextlib.ExitStack()
    STREAMS = ["t", "s", "v", "y_jv", "y_eye", "y_x0", "y_x1", "y_st0",
               "y_st1", "g_sm", "g_j", "g_wb", "g_upz", "g_bv", "g_kv0",
               "g_kv1", "g_c30", "g_c31", "g_wr8", "g_st0", "g_st1",
               "g_we13", "g_we2", "y_rb"]
    sems = {e: stk.enter_context(nc.semaphore(f"sem_{e}")) for e in STREAMS}
    P = Prog()
    nc.const_aps.register(F32, 1e-5, cpool[:, 0:1])

    def chunked(dr, c=8):
        return dr.rearrange("(c p) n -> p c n", p=128)

    def wsl(buf, di, dc):
        return buf[:, di * DM + dc * 128: di * DM + (dc + 1) * 128]

    sc_x = contextlib.ExitStack()
    x_fm = [sc_x.enter_context(nc.sbuf_tensor(f"x_fm{c}", [128, S], BF16))
            for c in range(NDC)]
    sc_ctx = contextlib.ExitStack()
    ctx_fm = [sc_ctx.enter_context(nc.sbuf_tensor(f"ctx{c}", [128, TPC], BF16))
              for c in range(NDC)]
    sc_kaug = contextlib.ExitStack()
    k_augA = sc_kaug.enter_context(nc.sbuf_tensor("k_augA", [128, S], BF16))
    k_augB = sc_kaug.enter_context(nc.sbuf_tensor("k_augB", [128, S], BF16))

    sc_p0 = contextlib.ExitStack()
    poolF = sc_p0.enter_context(nc.sbuf_tensor("poolF", [1, S], F32))
    poolB = sc_p0.enter_context(nc.sbuf_tensor("poolB", [65, S], BF16))
    stage_a = [sc_p0.enter_context(nc.sbuf_tensor(f"stg{p}", [128, DM], F32))
               for p in range(2)]
    xb_t = [sc_p0.enter_context(nc.sbuf_tensor(f"xb{p}", [128, DM], BF16))
            for p in range(2)]
    jv_sb = poolF[0:1, :]
    jv_b = poolB[0:1, :]
    jsq_b = poolB[32:33, :]
    onesrow_b = poolB[64:65, :]

    ld_jv = P.emit("y", lambda y: y.dma_start(jv_sb, jv_d[:, :]), sem="y_jv")
    ld_eye = P.emit("y", lambda y: y.dma_start(eye_f[:, :], eye_d[:, :]), sem="y_eye")
    gw_small = 0
    for dst, src in [
        (eye_b[:, :], eye_d[:, :]), (bq_sb[:, :], bq_d[:, :]),
        (bk_sb[:, :], bk_d[:, :]), (bo_sb[:, :], bo_d[:, :]),
        (bfq_sb[:, :], bfq_d[:, :]), (bfg_sb[:, :], bfg_d[:, :]),
        (wg_b[:, :], wg_d[:, :]),
    ]:
        gw_small = P.emit("g", lambda g, d=dst, s=src: g.dma_start(d, s), sem="g_sm")

    def memsets(v):
        v.memset(cpool[:, 0:1], 1e-5)
        v.memset(ones1_b[:, :], 1.0)
        v.memset(ones128_b[:, :], 1.0)
        return v.memset(onesrow_b, 1.0)
    v_init = P.emit("v", memsets)
    v_jsq = P.emit("v", lambda v: v.tensor_tensor(jsq_b, jv_sb, jv_sb,
                                                  ALU.mult),
                   waits=[("y_jv", ld_jv)])
    v_jb = P.emit("v", lambda v: v.tensor_copy(jv_b, jv_sb),
                  waits=[("v", v_jsq)])
    gw_j = 0
    for ka in (k_augA, k_augB):
        P.emit("g", lambda g, ka=ka: g.dma_start(ka[64:65, :], jsq_b),
               waits=[("v", v_jb), ("v", v_init)], sem="g_j")
        P.emit("g", lambda g, ka=ka: g.dma_start(ka[65:66, :], jv_b),
               sem="g_j")
        gw_j = P.emit("g", lambda g, ka=ka: g.dma_start(ka[66:67, :],
                                                        onesrow_b), sem="g_j")

    x_ld, x_cv, x_tr, x_ev = {}, {}, {}, {}
    for t in range(NXT):
        par = t % 2
        x_ld[t] = P.emit("y", lambda y, t=t, par=par: y.dma_start(
            stage_a[par][:, :], xr[t * 128:(t + 1) * 128, :]),
            waits=[("v", x_cv.get(t - 2, 0))], sem=f"y_x{par}")
        x_cv[t] = P.emit("v", lambda v, par=par: v.tensor_copy(
            xb_t[par][:, :], stage_a[par][:, :]),
            waits=[(f"y_x{par}", x_ld[t]), ("t", x_tr.get(t - 2, 0)),
                   ("v", v_init)])

        def trf(t_, par=par):
            for dc in range(NDC):
                ii = t_.transpose(ps_tr[par][:, dc * 128:(dc + 1) * 128],
                                  xb_t[par][:, dc * 128:(dc + 1) * 128],
                                  eye_b[:, :])
            return ii
        x_tr[t] = P.emit("t", trf, waits=[("v", x_cv[t]),
                                          ("g_sm", gw_small),
                                          ("v", x_ev.get(t - 2, 0))])

        def evf(v, t=t, par=par):
            for dc in range(NDC):
                ii = v.tensor_copy(x_fm[dc][:, t * 128:(t + 1) * 128],
                                   ps_tr[par][:, dc * 128:(dc + 1) * 128])
            return ii
        x_ev[t] = P.emit("v", evf, waits=[("t", x_tr[t])])
    x_done = x_ev[NXT - 1]
    xtr_last = x_tr[NXT - 1]

    sc_p0.close()

    sc_c3 = contextlib.ExitStack()
    C3 = sc_c3.enter_context(nc.sbuf_tensor("C3", [16, 3 * TPC], BF16))
    sc_w = contextlib.ExitStack()
    w_big1 = sc_w.enter_context(nc.sbuf_tensor("w_big1", [128, 8 * DM], BF16))
    w_big2 = w_big1

    gw_wq = P.emit("g", lambda g: g.dma_start(
        w_big1[:, :].rearrange("p (c n) -> p c n", c=8), chunked(wq_d[:, :])),
        waits=[("t", xtr_last), ("v", x_done), ("g_j", gw_j)], sem="g_wb")

    mq_mm, mq_ev = {}, {}
    for qc in range(NQT):
        for dc in range(NDC):
            i = qc * NDC + dc
            par = i % 2

            def mmf(t_, qc=qc, dc=dc, par=par):
                for di in range(NDC):
                    ii = t_.matmul(ps_s[par][:, 0:QT], wsl(w_big1, di, dc),
                                   x_fm[di][:, qc * QT:(qc + 1) * QT],
                                   start=(di == 0), stop=(di == NDC - 1))
                return ii
            mq_mm[i] = P.emit("t", mmf, waits=[("v", x_done), ("g_wb", gw_wq),
                                               ("v", mq_ev.get(i - 2, 0))])
            mq_ev[i] = P.emit("v", lambda v, qc=qc, dc=dc, par=par:
                              v.tensor_scalar(mq[dc][:, qc * QT:(qc + 1) * QT],
                                              ps_s[par][:, 0:QT],
                                              bq_sb[:, dc:dc + 1], None,
                                              ALU.add),
                              waits=[("t", mq_mm[i])])
    mq_done = mq_ev[NQT * NDC - 1]

    def xmeanf(v):
        for dc in range(NDC):
            ii = v.reduce_sum(xm_sb[:, dc:dc + 1], x_fm[dc][:, :], axis=AX.X)
        return ii
    v_xm = P.emit("v", xmeanf, waits=[("v", x_done)])
    v_xmb = P.emit("v", lambda v: v.tensor_scalar(
        xm_b[:, :], xm_sb[:, :], 1.0 / S, None, ALU.mult), waits=[("v", v_xm)])

    def glof(t_):
        for dc in range(NDC):
            for di in range(NDC):
                ii = t_.matmul(ps_c[0][:, dc:dc + 1], wsl(w_big1, di, dc),
                               xm_b[:, di:di + 1], start=(di == 0),
                               stop=(di == NDC - 1))
        return ii
    t_glo = P.emit("t", glof, waits=[("v", v_xmb), ("g_wb", gw_wq)])
    v_glo = P.emit("v", lambda v: v.tensor_tensor(
        glo_sb[:, :], ps_c[0][:, 0:8], bq_sb[:, :], ALU.add),
        waits=[("t", t_glo)])
    v_glob = P.emit("v", lambda v: v.tensor_copy(glo_b[:, :], glo_sb[:, :]),
                    waits=[("v", v_glo)])

    def gff(t_):
        for dc in range(NDC):
            for di in range(NDC):
                ii = t_.matmul(ps_c[1][:, dc:dc + 1], wsl(w_big2, di, dc),
                               glo_b[:, di:di + 1], start=(di == 0),
                               stop=(di == NDC - 1))
        return ii
    gw_wfg = P.emit("g", lambda g: g.dma_start(
        w_big1[:, :].rearrange("p (c n) -> p c n", c=8), chunked(wfg_d[:, :])),
        waits=[("t", t_glo), ("t", mq_mm[NQT * NDC - 1])], sem="g_wb")
    t_gf = P.emit("t", gff, waits=[("v", v_glob), ("g_wb", gw_wfg)])
    v_cb1 = P.emit("v", lambda v: v.tensor_tensor(
        cbias[:, :], ps_c[1][:, 0:8], bfq_sb[:, :], ALU.add),
        waits=[("t", t_gf)])
    v_cb = P.emit("v", lambda v: v.tensor_tensor(
        cbias[:, :], cbias[:, :], bfg_sb[:, :], ALU.add), waits=[("v", v_cb1)])

    gw_wfq = P.emit("g", lambda g: g.dma_start(
        w_big2[:, :].rearrange("p (c n) -> p c n", c=8), chunked(wfq_d[:, :])),
        waits=[("t", t_gf)], sem="g_wb")

    sc_p2 = contextlib.ExitStack()
    upz_b = sc_p2.enter_context(nc.sbuf_tensor("upz_b", [128, 512], BF16))
    gw_upz = P.emit("g", lambda g: g.dma_start(upz_b[:, :], upz_d[:, :]),
                    waits=[("t", xtr_last), ("v", x_done), ("g_j", gw_j)],
                    sem="g_upz")
    spz = sc_p2.enter_context(nc.sbuf_tensor("spz", [128, 4096], F32))
    Sp, Sz = spz[0:16, 0:TPC], spz[32:48, 0:TPC]
    cA, cB = spz[64:80, 0:TPC], spz[64:80, TPC:2 * TPC]
    ctp = sc_p2.enter_context(nc.sbuf_tensor("ctp", [128, 1024], BF16))
    c_t_sb = [ctp[:, 0:QT], ctp[:, QT:2 * QT]]

    c_mm, c_tanh, pz_mm, sig = {}, {}, {}, {}
    for qc in range(NQT):
        for dc in range(NDC):
            i = qc * NDC + dc
            par = i % 2

            def cmf(t_, qc=qc, dc=dc, par=par):
                for di in range(NDC):
                    ii = t_.matmul(ps_s[par][:, 0:QT], wsl(w_big2, di, dc),
                                   mq[di][:, qc * QT:(qc + 1) * QT],
                                   start=(di == 0), stop=(di == NDC - 1))
                return ii
            c_mm[i] = P.emit("t", cmf,
                             waits=[("v", v_cb), ("g_wb", gw_wfq), ("v", mq_done),
                                    ("s", c_tanh.get(i - 2, 0))])
            c_tanh[i] = P.emit("s", lambda s, dc=dc, par=par: s.activation(
                c_t_sb[par], ps_s[par][:, 0:QT], AF.Tanh,
                bias=cbias[:, dc:dc + 1]), waits=[("t", c_mm[i])])
            pz_mm[i] = P.emit("t", lambda t_, dc=dc, par=par: t_.matmul(
                ps_c[0][0:64, 0:QT], upz_b[:, dc * 64:(dc + 1) * 64],
                c_t_sb[par], start=(dc == 0), stop=(dc == NDC - 1)),
                waits=[("s", c_tanh[i]), ("s", sig.get(qc - 1, 0)),
                       ("v", v_glo if i == 0 else 0),
                       ("g_upz", gw_upz if i == 0 else 0)])
        sig[qc] = P.emit("s", lambda s, qc=qc: (
            s.activation(Sp[:, qc * QT:(qc + 1) * QT], ps_c[0][0:16, 0:QT],
                         AF.Sigmoid),
            s.activation(Sz[:, qc * QT:(qc + 1) * QT], ps_c[0][32:48, 0:QT],
                         AF.Sigmoid))[-1],
            waits=[("t", pz_mm[qc * NDC + NDC - 1])])
    sig_done = sig[NQT - 1]
    pz_last = pz_mm[NQT * NDC - 1]

    steps = [
        lambda v: v.reciprocal(cA, Sz),
        lambda v: v.tensor_tensor(cA, cA, cA, ALU.mult),
        lambda v: v.tensor_scalar(cA, cA, -1.1920928955078125e-07,
                                  None, ALU.mult),
        lambda v: v.tensor_scalar(cB, Sp, float(S), None, ALU.mult),
        lambda v: v.tensor_copy(C3[:, 0:TPC], cA),
        lambda v: v.tensor_tensor(cA, cA, cB, ALU.mult),
        lambda v: v.tensor_scalar(C3[:, TPC:2 * TPC], cA, -2.0, None,
                                  ALU.mult),
        lambda v: v.tensor_tensor(cA, cA, cB, ALU.mult),
        lambda v: v.tensor_scalar(C3[:, 2 * TPC:3 * TPC], cA, -4.0, None,
                                  ALU.add),
    ]
    vprev = P.emit("v", steps[0], waits=[("s", sig_done)])
    for f in steps[1:]:
        vprev = P.emit("v", f, waits=[("v", vprev)])
    coeff_done = vprev

    sc_p2.close()
    sc_w.close()

    sc_att2 = contextlib.ExitStack()
    v_aug = sc_att2.enter_context(nc.sbuf_tensor("v_aug", [128, NKC * 130],
                                                 BF16))
    kv_w = [sc_att2.enter_context(nc.sbuf_tensor(f"kv{p}", [128, 2048], BF16))
            for p in range(2)]
    e_t = [sc_att2.enter_context(nc.sbuf_tensor(f"e_t{p}", [128, 2 * QT],
                                                BF16)) for p in range(2)]
    rhs_aug = [sc_att2.enter_context(nc.sbuf_tensor(f"rhs{p}", [128, QT],
                                                    BF16)) for p in range(4)]
    rbp = sc_att2.enter_context(nc.sbuf_tensor("rbp", [128, QT], F32))
    rb_sb = [rbp[0:64, :], rbp[64:128, :]]
    bv_rb = sc_att2.enter_context(nc.sbuf_tensor("bv_rb", [1, DM], BF16))
    gw_bv = P.emit("g", lambda g: g.dma_start(bv_rb[:, :], bv_d[:, :]),
                   waits=[("t", pz_last), ("v", coeff_done)], sem="g_bv")

    def vones(v):
        va = v_aug[:, :].rearrange("p (t c) -> p t c", c=130)
        v.memset(va[:, :, 64:65], 1.0)
        return v.memset(va[:, :, 129:130], 1.0)
    v_vones = P.emit("v", vones, waits=[("t", pz_last), ("v", coeff_done)])

    AT = {}
    pair_t_end = {}
    for p in range(NE):
        wpar = p % 2
        cols = slice(p * 128, (p + 1) * 128)
        P.emit("g", lambda g, wpar=wpar, cols=cols: g.dma_start(
            kv_w[wpar][:, 0:1024].rearrange("p (c n) -> p c n", c=8),
            chunked(wk_d[:, cols])),
            waits=[("t", pair_t_end.get(p - 2, pz_last)), ("g_j", gw_j),
                   ("v", coeff_done if p == 0 else 0),
                   ("s", sig_done if p == 0 else 0)], sem=f"g_kv{wpar}")
        kv_gw = P.emit("g", lambda g, wpar=wpar, cols=cols: g.dma_start(
            kv_w[wpar][:, 1024:2048].rearrange("p (c n) -> p c n", c=8),
            chunked(wv_d[:, cols])), sem=f"g_kv{wpar}")

        kev = {}
        for kt in range(8):
            par = kt % 2

            def kmf(t_, kt=kt, par=par, wpar=wpar):
                for di in range(NDC):
                    ii = t_.matmul(ps_s[par][:, 0:QT],
                                   kv_w[wpar][:, di * 128:(di + 1) * 128],
                                   x_fm[di][:, kt * QT:(kt + 1) * QT],
                                   start=(di == 0), stop=(di == NDC - 1))
                return ii
            w = [(f"g_kv{wpar}", kv_gw), ("v", kev.get(kt - 2, 0))]
            if kt == 0:
                w += [("s", AT.get((p, 0, -1, "exp"), sig_done)),
                      ("v", AT.get((p - 1, NQT - 1, "ctxe"),
                                   max(coeff_done, mq_done)))]
            tk = P.emit("t", kmf, waits=w)
            kev[kt] = P.emit("v", lambda v, kt=kt, par=par, p=p: (
                v.tensor_scalar(k_augA[0:64, kt * QT:(kt + 1) * QT],
                                ps_s[par][0:64, 0:QT], bk_sb[0:64, p:p + 1],
                                0.125, ALU.add, ALU.mult),
                v.tensor_scalar(k_augB[0:64, kt * QT:(kt + 1) * QT],
                                ps_s[par][64:128, 0:QT], bk_sb[64:128, p:p + 1],
                                0.125, ALU.add, ALU.mult))[-1],
                waits=[("t", tk),
                       ("t", AT.get((p - 1, NQT - 1, "last_sc"), 0)
                        if kt == 0 else 0)])

        t_bv = P.emit("t", lambda t_, p=p: t_.matmul(
            ps_c[0][:, 0:128], ones1_b[:, :], bv_rb[:, p * 128:(p + 1) * 128],
            start=True, stop=True),
            waits=[("v", AT.get((p - 1, NQT - 1, "ctxe"), 0)),
                   ("s", sig_done if p == 0 else 0), ("v", v_init),
                   ("g_bv", gw_bv)])
        v_bv = P.emit("v", lambda v: v.tensor_copy(bvb[:, :],
                                                   ps_c[0][:, 0:128]),
                      waits=[("t", t_bv)])
        vev = {}
        for tt in range(NKC):
            par = tt % 2

            def vmf(t_, tt=tt, par=par, wpar=wpar):
                for di in range(NDC):
                    ii = t_.matmul(ps_s[par][:, QT:QT + 128],
                                   x_fm[di][:, tt * 128:(tt + 1) * 128],
                                   kv_w[wpar][:, 1024 + di * 128:
                                               1024 + (di + 1) * 128],
                                   start=(di == 0), stop=(di == NDC - 1))
                return ii
            tv = P.emit("t", vmf, waits=[("v", vev.get(tt - 2, 0))])
            base = tt * 130
            vev[tt] = P.emit("v", lambda v, base=base, par=par: (
                v.tensor_tensor(v_aug[:, base:base + 64],
                                ps_s[par][:, QT:QT + 64], bvb[:, 0:64],
                                ALU.add),
                v.tensor_tensor(v_aug[:, base + 65:base + 129],
                                ps_s[par][:, QT + 64:QT + 128], bvb[:, 64:128],
                                ALU.add))[-1],
                waits=[("t", tv), ("v", v_bv), ("v", v_vones if tt == 0 else 0),
                       ("t", AT.get((p - 1, NQT - 1, NKC - 1, "ctx"), 0)
                        if tt == 0 else 0)])
        kv_done_v = max(vev[NKC - 1], kev[7])

        def emit_rhs(pp, qc):
            for hi in range(2):
                bi = 2 * (qc % 2) + hi
                h = 2 * pp + hi
                guard = AT.get((pp, qc - 2, "last_sc"),
                               AT.get((pp - 1, NQT - 1, "last_sc"), pz_last))
                AT[(pp, qc, hi, "q")] = P.emit(
                    "v", lambda v, bi=bi, hi=hi, pp=pp, qc=qc: v.tensor_copy(
                        rhs_aug[bi][0:64, :],
                        mq[pp][hi * 64:(hi + 1) * 64, qc * QT:(qc + 1) * QT]),
                    waits=[("t", guard)])
                AT[(pp, qc, hi, "c3")] = P.emit(
                    "g", lambda g, bi=bi, h=h, qc=qc: g.dma_start(
                        rhs_aug[bi][64:67, 0:QT],
                        C3[h:h + 1, :].rearrange("o (c n) -> o c n", c=3)
                        [:, :, qc * QT:(qc + 1) * QT]),
                    waits=[("v", AT[(pp, qc, hi, "q")])], sem=f"g_c3{qc % 2}")

        if p == 0:
            emit_rhs(p, 0)
        emit_rhs(p, 1)
        for qc in range(NQT):
            def emit_ctx(kc):
                def ctf(t_, kc=kc):
                    base = kc * 130
                    t_.matmul(ps_c[0][0:65, :], v_aug[:, base:base + 65],
                              e_t[kc % 2][:, 0:QT], start=(kc == 0),
                              stop=(kc == NKC - 1))
                    return t_.matmul(ps_c[1][0:65, :],
                                     v_aug[:, base + 65:base + 130],
                                     e_t[kc % 2][:, QT:2 * QT],
                                     start=(kc == 0), stop=(kc == NKC - 1))
                wct = [("s", AT[(p, qc, kc, "exp")])]
                if kc == 0:
                    wct += [("v", AT.get((p, qc - 1, "ctxe"),
                                         AT.get((p - 1, NQT - 1, "ctxe"), 0))),
                            ("v", v_bv)]
                AT[(p, qc, kc, "ctx")] = P.emit("t", ctf, waits=wct)

            for kc in range(NKC):
                par = kc % 2
                wsc = [("s", AT.get((p, qc, kc - 2, "exp"), 0)),
                       (f"g_c3{qc % 2}", AT[(p, qc, 1, "c3")]),
                       ("v", kv_done_v)]
                if kc == 0:
                    wsc += [("g_j", gw_j if p == 0 and qc == 0 else 0)]

                def scf(t_, kc=kc, par=par, biA=2 * (qc % 2),
                        biB=2 * (qc % 2) + 1):
                    t_.matmul(ps_s[par][:, 0:QT],
                              k_augA[0:67, kc * 128:(kc + 1) * 128],
                              rhs_aug[biA][0:67, :], start=True, stop=True)
                    return t_.matmul(ps_s[par][:, QT:2 * QT],
                                     k_augB[0:67, kc * 128:(kc + 1) * 128],
                                     rhs_aug[biB][0:67, :], start=True,
                                     stop=True)
                AT[(p, qc, kc, "sc")] = P.emit("t", scf, waits=wsc)
                AT[(p, qc, kc, "exp")] = P.emit(
                    "s", lambda s, kc=kc, par=par: s.activation(
                        e_t[kc % 2][:, :], ps_s[par][:, :], AF.Exp),
                    waits=[("t", AT[(p, qc, kc, "sc")]),
                           ("t", AT.get((p, qc, kc - 2, "ctx"), 0))])
                if kc == 2 and qc >= 1:
                    if qc + 1 < NQT:
                        emit_rhs(p, qc + 1)
                    elif p + 1 < NE:
                        emit_rhs(p + 1, 0)
                if kc >= 1:
                    emit_ctx(kc - 1)
            emit_ctx(NKC - 1)
            AT[(p, qc, "last_sc")] = AT[(p, qc, NKC - 1, "sc")]
            AT[(p, qc, "rec")] = P.emit("v", lambda v: (
                v.reciprocal(recr_f[0], ps_c[0][64:65, :]),
                v.reciprocal(recr_f[1], ps_c[1][64:65, :]))[-1],
                waits=[("t", AT[(p, qc, NKC - 1, "ctx")]),
                       ("y_rb", AT.get((p, qc - 1, "rb"),
                                       AT.get((p - 1, NQT - 1, "rb"), 0)))])
            P.emit("y", lambda y: y.dma_start(
                rbd[:, :], rowsF[0:1, 0:2 * QT]),
                waits=[("v", AT[(p, qc, "rec")])], sem="y_rb")
            P.emit("y", lambda y: y.dma_start(
                rb_sb[0], bass.AP(rbd, 0, [[0, 64], [1, QT]])), sem="y_rb")
            AT[(p, qc, "rb")] = P.emit("y", lambda y: y.dma_start(
                rb_sb[1], bass.AP(rbd, QT, [[0, 64], [1, QT]])), sem="y_rb")
            AT[(p, qc, "ctxe")] = P.emit("v", lambda v, p=p, qc=qc: (
                v.tensor_tensor(ctx_fm[p][0:64, qc * QT:(qc + 1) * QT],
                                ps_c[0][0:64, :], rb_sb[0], ALU.mult),
                v.tensor_tensor(ctx_fm[p][64:128, qc * QT:(qc + 1) * QT],
                                ps_c[1][0:64, :], rb_sb[1], ALU.mult))[-1],
                waits=[("y_rb", AT[(p, qc, "rb")])])
            AT[(p, qc + 1, -2, "exp")] = AT[(p, qc, NKC - 2, "exp")]
            AT[(p, qc + 1, -1, "exp")] = AT[(p, qc, NKC - 1, "exp")]
            AT[(p, qc + 1, -2, "ctx")] = AT[(p, qc, NKC - 2, "ctx")]
            AT[(p, qc + 1, -1, "ctx")] = AT[(p, qc, NKC - 1, "ctx")]
            if qc == NQT - 1:
                AT[(p + 1, 0, -2, "exp")] = AT[(p, qc, NKC - 2, "exp")]
                AT[(p + 1, 0, -1, "exp")] = AT[(p, qc, NKC - 1, "exp")]
                AT[(p + 1, 0, -2, "ctx")] = AT[(p, qc, NKC - 2, "ctx")]
                AT[(p + 1, 0, -1, "ctx")] = AT[(p, qc, NKC - 1, "ctx")]
                AT[(p + 1, 0, "rb_prev")] = AT[(p, qc, "rb")]
        pair_t_end[p] = P.cnt["t"]
    att_v = AT[(NE - 1, NQT - 1, "ctxe")]
    att_rbs = AT[(NE - 1, NQT - 1, NKC - 1, "exp")]
    att_t = P.cnt["t"]

    sc_att2.close()
    sc_c3.close()
    sc_kaug.close()
    sc_p4 = contextlib.ExitStack()
    wo_b = sc_p4.enter_context(nc.sbuf_tensor("wo_b", [128, 8 * DM], BF16))
    zbuf_a = sc_p4.enter_context(nc.sbuf_tensor("zbuf_a", [128, 8 * QT], F32))
    mb_a = sc_p4.enter_context(nc.sbuf_tensor("mb_a", [128, QT], F32))
    ivb_a = sc_p4.enter_context(nc.sbuf_tensor("ivb_a", [128, QT], F32))
    rows_a = sc_p4.enter_context(nc.sbuf_tensor("rows_a", [1, 2048], F32))
    zbp_a = [sc_p4.enter_context(nc.sbuf_tensor(f"zbp_a{p}", [128, QT], BF16))
             for p in range(2)]
    zsp_a = [sc_p4.enter_context(nc.sbuf_tensor(f"zsp_a{p}", [128, QT], BF16))
             for p in range(2)]

    gw_wo = P.emit("g", lambda g: g.dma_start(
        wo_b[:, :].rearrange("p (c n) -> p c n", c=8), chunked(wo_d[:, :])),
        waits=[("t", att_t), ("s", att_rbs), ("v", att_v)], sem="g_wb")

    def ln_phase(qc, mode, y_dst, st_, zbuf, mb_sb, invb_sb, rows4,
                 zb_t, zsq_t):
        L = {}
        mrow, srow = rows4[0:1, 0:512], rows4[0:1, 512:1024]
        vrow, vrow2 = rows4[0:1, 1024:1536], rows4[0:1, 1536:2048]
        zev = st_.setdefault("zev", {})
        for dc in range(NDC):
            gi = qc * NDC + dc
            par = gi % 2
            if mode == "attn":
                wmm = [("v", att_v), ("g_wb", gw_wo),
                       ("v", zev.get(gi - 2, 0)),
                       ("s", att_rbs if gi < 2 else 0),
                       ("s", st_.get("prev_bcs", 0) if dc < 2 else 0)]

                def mmf(t_, qc=qc, dc=dc, par=par):
                    for di in range(NDC):
                        ii = t_.matmul(ps_s[par][:, 0:QT], wsl(wo_b, di, dc),
                                       ctx_fm[di][:, qc * QT:(qc + 1) * QT],
                                       start=(di == 0), stop=(di == NDC - 1))
                    return ii
                L[(dc, "mm")] = P.emit("t", mmf, waits=wmm)
                zev[gi] = L[(dc, "z")] = P.emit(
                    "v", lambda v, qc=qc, dc=dc, par=par:
                    v.scalar_tensor_tensor(
                        zbuf[:, dc * QT:(dc + 1) * QT], ps_s[par][:, 0:QT],
                        bo_sb[:, dc:dc + 1],
                        x_fm[dc][:, qc * QT:(qc + 1) * QT], ALU.add, ALU.add),
                    waits=[("t", L[(dc, "mm")])])
            else:
                L[(dc, "z")] = P.emit(
                    "v", lambda v, qc=qc, dc=dc: v.scalar_tensor_tensor(
                        zbuf[:, dc * QT:(dc + 1) * QT],
                        st_["routed"][dc][:, qc * QT:(qc + 1) * QT], 1.0 / 3.0,
                        y1[dc][:, qc * QT:(qc + 1) * QT], ALU.mult, ALU.add),
                    waits=[("v", st_.get("src_v", 0)),
                           ("t", st_.get("prev_ttr", 0))])
            zg = (L.get((dc - 2, "sums"), st_.get("prev_sums_last", 0)
                        if dc < 2 else 0))
            L[(dc, "zb")] = P.emit("v", lambda v, dc=dc: v.tensor_copy(
                zb_t[dc % 2][:, :], zbuf[:, dc * QT:(dc + 1) * QT]),
                waits=[("v", L[(dc, "z")]), ("t", zg)])
            L[(dc, "zsq")] = P.emit("s", lambda s, dc=dc: s.activation(
                zsq_t[dc % 2][:, :], zbuf[:, dc * QT:(dc + 1) * QT],
                AF.Square), waits=[("v", L[(dc, "z")]), ("t", zg)])

            def emit_sums(d2):
                wsum = [("v", L[(d2, "zb")]), ("s", L[(d2, "zsq")])]
                if d2 == 0:
                    wsum += [("s", st_.get("psc_guard_s", 0)),
                             ("v", st_.get("psc_guard_v", 0))]
                L[(d2, "sums")] = P.emit("t", lambda t_, d2=d2: (
                    t_.matmul(ps_c[0][0:1, 0:QT], ones128_b[:, :],
                              zb_t[d2 % 2][:, :], start=(d2 == 0),
                              stop=(d2 == NDC - 1)),
                    t_.matmul(ps_c[1][0:1, 0:QT], ones128_b[:, :],
                              zsq_t[d2 % 2][:, :], start=(d2 == 0),
                              stop=(d2 == NDC - 1)))[-1], waits=wsum)

            if dc >= 1:
                emit_sums(dc - 1)
            if dc == NDC - 1:
                emit_sums(dc)
        r0 = P.emit("v", lambda v: v.tensor_scalar(
            mrow, ps_c[0][0:1, 0:QT], 1.0 / DM, None, ALU.mult),
            waits=[("t", L[(NDC - 1, "sums")])])
        r1 = P.emit("v", lambda v: v.tensor_scalar(
            vrow, ps_c[1][0:1, 0:QT], 1.0 / DM, None, ALU.mult),
            waits=[("v", r0)])
        r2 = P.emit("v", lambda v: v.tensor_tensor(vrow2, mrow, mrow,
                                                   ALU.mult), waits=[("v", r1)])
        r3 = P.emit("v", lambda v: v.tensor_tensor(vrow, vrow, vrow2,
                                                   ALU.subtract),
                    waits=[("v", r2)])
        r4 = P.emit("s", lambda s: s.activation(srow, vrow, AF.Sqrt,
                                                bias=1e-5), waits=[("v", r3)])
        r5 = P.emit("v", lambda v: v.reciprocal(vrow2, srow),
                    waits=[("s", r4)])
        r6 = P.emit("v", lambda v: (v.tensor_copy(mrow_b, mrow),
                                    v.tensor_copy(irow_b, vrow2))[-1],
                    waits=[("v", r5)])
        wbc = [("v", r6), ("v", L[(NDC - 1, "z")]),
               ("s", st_.get("pss_guard_s", 0)),
               ("v", st_.get("pss_guard_v", 0))]
        bct = P.emit("t", lambda t_: (
            t_.matmul(ps_s[0][:, 0:QT], ones1_b[:, :], mrow_b, start=True,
                      stop=True),
            t_.matmul(ps_s[1][:, 0:QT], ones1_b[:, :], irow_b, start=True,
                      stop=True))[-1], waits=wbc)
        bcs = P.emit("s", lambda s: (s.copy(mb_sb[:, :], ps_s[0][:, 0:QT]),
                                     s.copy(invb_sb[:, :],
                                            ps_s[1][:, 0:QT]))[-1],
                     waits=[("t", bct)])
        subv = P.emit("v", lambda v: [v.tensor_tensor(
            zbuf[:, dc * QT:(dc + 1) * QT], zbuf[:, dc * QT:(dc + 1) * QT],
            mb_sb[:, :], ALU.subtract) for dc in range(NDC)][-1],
            waits=[("s", bcs)])
        if y_dst is not None:
            fin = P.emit("v", lambda v, qc=qc: [v.tensor_tensor(
                y_dst[dc][:, qc * QT:(qc + 1) * QT],
                zbuf[:, dc * QT:(dc + 1) * QT], invb_sb[:, :], ALU.mult)
                for dc in range(NDC)][-1], waits=[("v", subv)])
        else:
            fin = P.emit("v", lambda v: [v.tensor_tensor(
                zbuf[:, dc * QT:(dc + 1) * QT], zbuf[:, dc * QT:(dc + 1) * QT],
                invb_sb[:, :], ALU.mult) for dc in range(NDC)][-1],
                waits=[("v", subv)])
        st_["prev_sums_last"] = L[(NDC - 1, "sums")]
        st_["prev_bcs"] = bcs
        st_["pss_guard_s"] = bcs
        st_["psc_guard_v"] = r1
        L["fin"], L["bcs"], L["r1"] = fin, bcs, r1
        return L

    st1 = {"psc_guard_s": sig_done}
    gate_ws = {}
    gw_wr8 = 0
    for qc in range(NQT):
        L1 = ln_phase(qc, "attn", y1, st1, zbuf_a, mb_a, ivb_a, rows_a,
                      zbp_a, zsp_a)
        for tt in range(4):
            tok = qc * QT + tt * 128

            def gmf(t_, qc=qc, tt=tt, tok=tok):
                for di in range(NDC):
                    ii = t_.matmul(ps_c[0][:, 8 * tt:8 * tt + 8],
                                   y1[di][:, tok:tok + 128],
                                   wg_b[:, di * 8:(di + 1) * 8],
                                   start=(di == 0), stop=(di == NDC - 1))
                return ii
            gt_t = P.emit("t", gmf, waits=[("v", L1["fin"]),
                                           ("s", gate_ws.get((qc, tt - 1), 0))])
            gs = P.emit("s", lambda s, tt=tt: s.activation(
                gsig[:, :], ps_c[0][:, 8 * tt:8 * tt + 8], AF.Sigmoid),
                waits=[("t", gt_t)])
            st1["psc_guard_s"] = gs
            chain = [
                lambda v: v.reduce_max(gt[:, 0:1], gsig[:, :], axis=AX.X),
                lambda v: v.tensor_scalar(gt[:, 8:16], gsig[:, :], gt[:, 0:1],
                                          None, ALU.is_equal),
                lambda v: v.scalar_tensor_tensor(gt[:, 16:24], gt[:, 8:16],
                                                 -1e30, gsig[:, :], ALU.mult,
                                                 ALU.add),
                lambda v: v.reduce_max(gt[:, 1:2], gt[:, 16:24], axis=AX.X),
                lambda v: v.tensor_scalar(gt[:, 24:32], gt[:, 16:24],
                                          gt[:, 1:2], None, ALU.is_equal),
                lambda v: v.tensor_tensor(gt[:, 24:32], gt[:, 24:32],
                                          gt[:, 8:16], ALU.add),
                lambda v: v.tensor_tensor(gt[:, 2:3], gt[:, 0:1], gt[:, 1:2],
                                          ALU.add),
                lambda v: v.reciprocal(gt[:, 3:4], gt[:, 2:3]),
                lambda v: v.tensor_tensor(gt[:, 32:40], gsig[:, :],
                                          gt[:, 24:32], ALU.mult),
                lambda v: v.tensor_scalar(wtok[:, :], gt[:, 32:40], gt[:, 3:4],
                                          None, ALU.mult),
            ]
            vv = P.emit("v", chain[0], waits=[("s", gs)])
            for f in chain[1:]:
                vv = P.emit("v", f, waits=[("v", vv)])
            wts = 0
            for rnd in range(2):
                def wtf(t_, rnd=rnd):
                    for e4 in range(4):
                        ii = t_.transpose(
                            ps_s[0][0:1, QT + e4 * 128:QT + (e4 + 1) * 128],
                            wtok[:, rnd * 4 + e4:rnd * 4 + e4 + 1],
                            eye_f[:, :])
                    return ii
                wt_t = P.emit("t", wtf, waits=[("v", vv), ("s", wts),
                                               ("y_eye", ld_eye)])
                wts = P.emit("s", lambda s, rnd=rnd: s.copy(
                    wstg[0:1, 0:512], ps_s[0][0:1, QT:QT + 512]),
                    waits=[("t", wt_t), ("g_wr8", gw_wr8)])
                for e4 in range(4):
                    gw_wr8 = P.emit("g", lambda g, rnd=rnd, tok=tok, e4=e4:
                                    g.dma_start(
                        wr8[rnd * 4 + e4:rnd * 4 + e4 + 1, tok:tok + 128],
                        wstg[0:1, e4 * 128:(e4 + 1) * 128]),
                        waits=[("s", wts)] if e4 == 0 else [], sem="g_wr8")
            gate_ws[(qc, tt)] = wts
    wr8_done = gw_wr8
    p4_v = P.cnt["v"]
    p4_s = P.cnt["s"]
    p4_t = P.cnt["t"]

    sc_p4.close()
    sc_ctx.close()
    sc_x.close()

    sc_r = contextlib.ExitStack()
    routed = [sc_r.enter_context(nc.sbuf_tensor(f"rt{c}", [128, TPC], BF16))
              for c in range(NDC)]
    sc_ffn = contextlib.ExitStack()
    we1f = sc_ffn.enter_context(nc.sbuf_tensor("we1f", [128, 8 * FF], BF16))
    we3f = sc_ffn.enter_context(nc.sbuf_tensor("we3f", [128, 8 * FF], BF16))
    we2f = sc_ffn.enter_context(nc.sbuf_tensor("we2f", [128, NFC * DM], BF16))
    h_sb = sc_ffn.enter_context(nc.sbuf_tensor("h_sb", [128, NFC * QT], BF16))
    sa_sb = [sc_ffn.enter_context(nc.sbuf_tensor(f"sa{p}", [128, QT], BF16))
             for p in range(2)]
    h1_sb = [sc_ffn.enter_context(nc.sbuf_tensor(f"h1{p}", [128, QT], F32))
             for p in range(2)]
    wbb = sc_ffn.enter_context(nc.sbuf_tensor("wbb", [128, QT], BF16))

    FW = {}
    ab_last, o_last, o_racc_last = {}, {}, {}
    ffn_s_last, ffn_h1_last = 0, 0
    space_guard = [("v", p4_v), ("t", p4_t), ("s", p4_s)]
    stg_dma = {}
    for ei in range(NE + 1):
        e = ei - 1
        s1d = ws1_d[:, :] if e < 0 else we1_d[e, :, :]
        s3d = ws3_d[:, :] if e < 0 else we3_d[e, :, :]
        s2d = ws2_d[:, :] if e < 0 else we2_d[e, :, :]
        FW[(ei, "w1")] = P.emit("g", lambda g, s1d=s1d: g.dma_start(
            we1f[:, :].rearrange("p (c n) -> p c n", c=8), chunked(s1d)),
            waits=[("t", ab_last.get(ei - 1, 0))] +
                  (space_guard if ei == 0 else []), sem="g_we13")
        FW[(ei, "w3")] = P.emit("g", lambda g, s3d=s3d: g.dma_start(
            we3f[:, :].rearrange("p (c n) -> p c n", c=8), chunked(s3d)),
            sem="g_we13")
        FW[(ei, "w2")] = P.emit("g", lambda g, s2d=s2d: g.dma_start(
            we2f[:, :].rearrange("p (c n) -> p c n", c=16),
            s2d.rearrange("(c p) n -> p c n", p=128)),
            waits=[("t", o_last.get(ei - 1, 0))], sem="g_we2")
        ht, ot, sls = {}, {}, {}
        for tt in range(NQT):
            if e >= 0:
                sd = P.emit("g", lambda g, e=e, tt=tt: g.dma_start(
                    wstg[0:1, (tt % 2) * 512:(tt % 2) * 512 + 512],
                    wr8[e:e + 1, tt * QT:(tt + 1) * QT]),
                    waits=[("g_wr8", wr8_done),
                           ("t", stg_dma.get(tt % 2, 0))],
                    sem=f"g_st{tt % 2}")
                bw = P.emit("t", lambda t_, tt=tt: t_.matmul(
                    ps_c[0][:, 0:QT], ones1_b[:, :],
                    wstg[0:1, (tt % 2) * 512:(tt % 2) * 512 + 512],
                    start=True, stop=True),
                    waits=[(f"g_st{tt % 2}", sd),
                           ("v", ot.get((tt - 1, "racc"),
                                        o_racc_last.get(ei - 1, 0)))])
                stg_dma[tt % 2] = bw
                bws = P.emit("s", lambda s: s.copy(wbb[:, :],
                                                   ps_c[0][:, 0:QT]),
                             waits=[("t", bw)])
            else:
                bws = 0
            for fc in range(NFC):
                par = fc % 2

                def abf(t_, fc=fc, par=par, tt=tt):
                    for di in range(NDC):
                        t_.matmul(ps_s[par][:, 0:QT],
                                  we1f[:, di * FF + fc * 128:
                                       di * FF + (fc + 1) * 128],
                                  y1[di][:, tt * QT:(tt + 1) * QT],
                                  start=(di == 0), stop=(di == NDC - 1))
                    for di in range(NDC):
                        ii = t_.matmul(ps_s[par][:, QT:2 * QT],
                                       we3f[:, di * FF + fc * 128:
                                            di * FF + (fc + 1) * 128],
                                       y1[di][:, tt * QT:(tt + 1) * QT],
                                       start=(di == 0), stop=(di == NDC - 1))
                    return ii
                w = [("g_we13", FW[(ei, "w3")]),
                     ("v", ht.get((tt, fc - 2), 0)),
                     ("s", sls.get((tt, fc - 2), 0))]
                ab = P.emit("t", abf, waits=w)
                sls[(tt, fc)] = sl = P.emit(
                    "s", lambda s, fc=fc, par=par: s.activation(
                        sa_sb[fc % 2][:, :], ps_s[par][:, 0:QT], AF.Silu),
                    waits=[("t", ab)])
                ffn_s_last = sl
                h1 = P.emit("v", lambda v, fc=fc, par=par: v.tensor_tensor(
                    h1_sb[fc % 2][:, :], sa_sb[fc % 2][:, :],
                    ps_s[par][:, QT:2 * QT], ALU.mult), waits=[("s", sl)])
                ffn_h1_last = h1
                wh = [("v", h1), ("t", ot.get((tt - 1, "last"),
                                              o_last.get(ei - 1, 0)))]
                if e >= 0:
                    ht[(tt, fc)] = P.emit(
                        "v", lambda v, fc=fc: v.tensor_tensor(
                            h_sb[:, fc * QT:(fc + 1) * QT],
                            h1_sb[fc % 2][:, :], wbb[:, :], ALU.mult),
                        waits=wh + [("s", bws)])
                else:
                    ht[(tt, fc)] = P.emit(
                        "v", lambda v, fc=fc: v.tensor_copy(
                            h_sb[:, fc * QT:(fc + 1) * QT],
                            h1_sb[fc % 2][:, :]), waits=wh)
            if tt == NQT - 1:
                ab_last[ei] = P.cnt["t"]
            for dc in range(NDC):
                par = dc % 2

                def omf(t_, dc=dc, par=par):
                    for fc in range(NFC):
                        ii = t_.matmul(ps_c[par][:, 0:QT],
                                       we2f[:, fc * DM + dc * 128:
                                            fc * DM + (dc + 1) * 128],
                                       h_sb[:, fc * QT:(fc + 1) * QT],
                                       start=(fc == 0), stop=(fc == NFC - 1))
                    return ii
                w = [("v", ht[(tt, NFC - 1)]), ("g_we2", FW[(ei, "w2")]),
                     ("v", ot.get((tt, dc - 2, "racc"), 0))]
                if dc == 0 and e >= 0:
                    w.append(("s", bws))
                if dc < 2:
                    w.append(("v", ot.get((tt - 1, "racc"),
                                          o_racc_last.get(ei - 1, 0))))
                ot[(tt, dc, "t")] = P.emit("t", omf, waits=w)
                if ei == 0:
                    ot[(tt, dc, "racc")] = P.emit(
                        "v", lambda v, dc=dc, par=par, tt=tt: v.tensor_copy(
                            routed[dc][:, tt * QT:(tt + 1) * QT],
                            ps_c[par][:, 0:QT]),
                        waits=[("t", ot[(tt, dc, "t")])])
                else:
                    ot[(tt, dc, "racc")] = P.emit(
                        "v", lambda v, dc=dc, par=par, tt=tt: v.tensor_tensor(
                            routed[dc][:, tt * QT:(tt + 1) * QT],
                            routed[dc][:, tt * QT:(tt + 1) * QT],
                            ps_c[par][:, 0:QT], ALU.add),
                        waits=[("t", ot[(tt, dc, "t")])])
            ot[(tt, "last")] = ot[(tt, NDC - 1, "t")]
            ot[(tt, "racc")] = ot[(tt, NDC - 1, "racc")]
        o_last[ei] = ot[(NQT - 1, "last")]
        o_racc_last[ei] = ot[(NQT - 1, "racc")]
    ffn_done_v = o_racc_last[NE]
    p5_t = P.cnt["t"]

    sc_ffn.close()

    sc_p6 = contextlib.ExitStack()
    zbuf_b = sc_p6.enter_context(nc.sbuf_tensor("zbuf_b", [128, 8 * QT], F32))
    mb_b = sc_p6.enter_context(nc.sbuf_tensor("mb_b", [128, QT], F32))
    ivb_b = sc_p6.enter_context(nc.sbuf_tensor("ivb_b", [128, QT], F32))
    rows_b = sc_p6.enter_context(nc.sbuf_tensor("rows_b", [1, 2048], F32))
    zbp_b = [sc_p6.enter_context(nc.sbuf_tensor(f"zbp_b{p}", [128, QT], BF16))
             for p in range(2)]
    zsp_b = [sc_p6.enter_context(nc.sbuf_tensor(f"zsp_b{p}", [128, QT], BF16))
             for p in range(2)]
    stage_b = [sc_p6.enter_context(nc.sbuf_tensor(f"ost{p}", [128, DM], F32))
               for p in range(2)]

    st2 = {"src_v": ffn_done_v, "psc_guard_v": ffn_done_v,
           "pss_guard_s": ffn_s_last, "pss_guard_v": ffn_h1_last,
           "prev_ttr": p5_t, "routed": routed}
    ST = {}
    prev_st = {0: 0, 1: 0}
    for qc in range(NQT):
        L2 = ln_phase(qc, "ffn", None, st2, zbuf_b, mb_b, ivb_b, rows_b,
                      zbp_b, zsp_b)
        last_ttr = 0
        for tt in range(4):
            par = tt % 2
            tok = qc * QT + tt * 128

            def trf(t_, tt=tt):
                for dc in range(NDC):
                    ii = t_.transpose(
                        ps_c[dc // 4][:, (dc % 4) * 128:(dc % 4 + 1) * 128],
                        zbuf_b[:, dc * QT + tt * 128:dc * QT + (tt + 1) * 128],
                        eye_f[:, :])
                return ii
            ttr = P.emit("t", trf, waits=[("v", L2["fin"]),
                                          ("v", ST.get((qc, tt - 1, "ev"), 0)),
                                          ("v", L2["r1"] if tt == 0 else 0)])
            last_ttr = ttr
            ST[(qc, tt, "ev")] = P.emit("v", lambda v, par=par: [
                v.tensor_copy(stage_b[par][:, dc * 128:(dc + 1) * 128],
                              ps_c[dc // 4][:, (dc % 4) * 128:
                                            (dc % 4 + 1) * 128])
                for dc in range(NDC)][-1],
                waits=[("t", ttr), (f"y_st{par}", prev_st[par])])
            ST[(qc, tt, "st")] = P.emit(
                "y", lambda y, par=par, tok=tok: y.dma_start(
                    out_d[tok:tok + 128, :], stage_b[par][:, :]),
                waits=[("v", ST[(qc, tt, "ev")])], sem=f"y_st{par}")
            prev_st[par] = ST[(qc, tt, "st")]
        st2["prev_ttr"] = last_ttr
        st2["psc_guard_v"] = ST[(qc, 3, "ev")]
    P.emit("y", lambda y: y.nop(),
           waits=[("y_st0", ST[(NQT - 1, 2, "st")]),
                  ("y_st1", ST[(NQT - 1, 3, "st")])], inc=False)

    P.finalize(nc, sems)
    sc_p6.close()
    sc_r.close()
    stk.close()
    return nc


def _host_inputs(inputs):
    x = np.ascontiguousarray(np.asarray(inputs["L0_attn_hidden_states"],
                                        np.float32))
    up = np.asarray(inputs["L0_attn_up"], np.float32).reshape(-1)
    uz = np.asarray(inputs["L0_attn_uz"], np.float32).reshape(-1)
    upz = np.zeros((128, 512), np.float32)
    r = np.arange(128)
    for dc in range(8):
        h = 2 * dc + r // 64
        upz[r, dc * 64 + h] = up[dc * 128 + r]
        upz[r, dc * 64 + 32 + h] = uz[dc * 128 + r]
    wg = np.asarray(inputs["L0_ffn_Wg"], np.float32)
    wg_m = np.ascontiguousarray(wg.reshape(8, 128, 8).transpose(1, 0, 2)
                                .reshape(128, 64))

    def colmat(name):
        a = np.asarray(inputs[name], np.float32).reshape(-1)
        return np.ascontiguousarray(a.reshape(8, 128).T)

    def arr(name):
        return np.ascontiguousarray(np.asarray(inputs[name], np.float32))

    shared = {
        "eye": np.eye(128, dtype=np.float32),
        "wq": arr("L0_attn_wq"), "wk": arr("L0_attn_wk"),
        "wv": arr("L0_attn_wv"), "wo": arr("L0_attn_wo"),
        "wfq": arr("L0_attn_wfq"), "wfg": arr("L0_attn_wfg"),
        "bq_m": colmat("L0_attn_bq"), "bk_m": colmat("L0_attn_bk"),
        "bo_m": colmat("L0_attn_bo"), "bfq_m": colmat("L0_attn_bfq"),
        "bfg_m": colmat("L0_attn_bfg"),
        "bv_r": np.ascontiguousarray(
            np.asarray(inputs["L0_attn_bv"], np.float32).reshape(1, -1)),
        "upz_m": upz, "wg_m": wg_m,
        "ws1": arr("L0_ffn_Ws1"), "ws3": arr("L0_ffn_Ws3"),
        "ws2": arr("L0_ffn_Ws2"), "we1": arr("L0_ffn_We1"),
        "we3": arr("L0_ffn_We3"), "we2": arr("L0_ffn_We2"),
    }
    in_maps = []
    for c in range(8):
        b, off = c // 2, (c % 2) * TPC
        m = dict(shared)
        m["xr"] = np.ascontiguousarray(
            np.concatenate([x[b, off:], x[b, :off]], axis=0))
        m["jv"] = np.ascontiguousarray(
            np.concatenate([np.arange(off, S), np.arange(0, off)])
            .astype(np.float32).reshape(1, S))
        in_maps.append(m)
    return in_maps


_NC_CACHE = {}


def kernel(**inputs):
    if "nc" not in _NC_CACHE:
        _NC_CACHE["nc"] = build_full()
    nc = _NC_CACHE["nc"]
    in_maps = _host_inputs(inputs)
    res = run_bass_kernel_spmd(nc, in_maps, core_ids=list(range(8)))
    out = np.empty((B, S, DM), np.float32)
    for c in range(8):
        b, off = c // 2, (c % 2) * TPC
        out[b, off:off + TPC] = res.results[c]["out"]
    return out


# version 07
# speedup vs baseline: 1.3674x; 1.0005x over previous
import contextlib
import numpy as np
import concourse.bass as bass
import concourse.mybir as mybir
from concourse.bass_utils import run_bass_kernel_spmd

F32 = mybir.dt.float32
BF16 = mybir.dt.bfloat16
AF = mybir.ActivationFunctionType
ALU = mybir.AluOpType
AX = mybir.AxisListType

B, S, DM, H, D = 4, 4096, 1024, 16, 64
TPC, FF, NE, QT = 2048, 2048, 8, 512
NQT, NKC, NDC, NFC, NXT = TPC // QT, S // 128, DM // 128, FF // 128, S // 128

ENGS = ("t", "s", "v", "g", "y")


class Prog:

    def __init__(self):
        self.recs = {e: [] for e in ENGS}
        self.cnt = {}

    def emit(self, eng, fn, waits=(), inc=True, sem=None):
        semn = sem or eng
        inc_k = 16 if eng in ("g", "y") else 1
        ws = [(e, v) for (e, v) in waits if v and v > 0]
        self.recs[eng].append((fn, ws, semn if inc else None, inc_k))
        if inc:
            self.cnt[semn] = self.cnt.get(semn, 0) + inc_k
        return self.cnt.get(semn, 0)

    def finalize(self, nc, sems):
        def run(eng_obj, name):
            lw = {}
            for fn, ws, semn, inc_k in self.recs[name]:
                for se, val in ws:
                    if val > lw.get(se, 0):
                        eng_obj.wait_ge(sems[se], val)
                        lw[se] = val
                inst = fn(eng_obj)
                if semn is not None:
                    inst.then_inc(sems[semn], inc_k)

        with nc.Block() as block:
            block.tensor(lambda e: run(e, "t"))
            block.scalar(lambda e: run(e, "s"))
            block.vector(lambda e: run(e, "v"))
            block.gpsimd(lambda e: run(e, "g"))
            block.sync(lambda e: run(e, "y"))


def build_full():
    nc = bass.Bass("TRN2", target_bir_lowering=False)

    dt_ = nc.dram_tensor
    xr = dt_("xr", [S, DM], F32, kind="ExternalInput")
    jv_d = dt_("jv", [1, S], F32, kind="ExternalInput")
    eye_d = dt_("eye", [128, 128], F32, kind="ExternalInput")
    wq_d = dt_("wq", [DM, DM], F32, kind="ExternalInput")
    wk_d = dt_("wk", [DM, DM], F32, kind="ExternalInput")
    wv_d = dt_("wv", [DM, DM], F32, kind="ExternalInput")
    wo_d = dt_("wo", [DM, DM], F32, kind="ExternalInput")
    wfq_d = dt_("wfq", [DM, DM], F32, kind="ExternalInput")
    wfg_d = dt_("wfg", [DM, DM], F32, kind="ExternalInput")
    bq_d = dt_("bq_m", [128, 8], F32, kind="ExternalInput")
    bk_d = dt_("bk_m", [128, 8], F32, kind="ExternalInput")
    bo_d = dt_("bo_m", [128, 8], F32, kind="ExternalInput")
    bfq_d = dt_("bfq_m", [128, 8], F32, kind="ExternalInput")
    bfg_d = dt_("bfg_m", [128, 8], F32, kind="ExternalInput")
    bv_d = dt_("bv_m", [128, 8], F32, kind="ExternalInput")
    upz_d = dt_("upz_m", [128, 512], F32, kind="ExternalInput")
    wg_d = dt_("wg_m", [128, 64], F32, kind="ExternalInput")
    ws1_d = dt_("ws1", [DM, FF], F32, kind="ExternalInput")
    ws3_d = dt_("ws3", [DM, FF], F32, kind="ExternalInput")
    ws2_d = dt_("ws2", [FF, DM], F32, kind="ExternalInput")
    we1_d = dt_("we1", [NE, DM, FF], F32, kind="ExternalInput")
    we3_d = dt_("we3", [NE, DM, FF], F32, kind="ExternalInput")
    we2_d = dt_("we2", [NE, FF, DM], F32, kind="ExternalInput")
    out_d = dt_("out", [TPC, DM], F32, kind="ExternalOutput")
    rbd = dt_("rbd", [1, 2 * QT], F32, kind="Internal")

    al = nc.alloc_sbuf_tensor
    eye_f = al("eye_f", [128, 128], F32)
    eye_b = al("eye_b", [128, 128], BF16)
    ones1_b = al("ones1_b", [1, 128], BF16)
    ones128_b = al("ones128_b", [128, 1], BF16)
    cpool = al("cpool", [128, 2], F32)
    bq_sb = al("bq_sb", [128, 8], F32)
    bk_sb = al("bk_sb", [128, 8], F32)
    bo_sb = al("bo_sb", [128, 8], F32)
    bfq_sb = al("bfq_sb", [128, 8], F32)
    bfg_sb = al("bfg_sb", [128, 8], F32)
    cbias = al("cbias", [128, 8], F32)
    bv_sb = al("bv_sb", [128, 8], F32)
    wg_b = al("wg_b", [128, 64], BF16)
    xm_sb = al("xm_sb", [128, 8], F32)
    xm_b = al("xm_b", [128, 8], BF16)
    glo_sb = al("glo_sb", [128, 8], F32)
    glo_b = al("glo_b", [128, 8], BF16)
    rowsF = al("rowsF", [1, 1024], F32)
    recr_f = [rowsF[0:1, 0:512], rowsF[0:1, 512:1024]]
    rowsB = al("rowsB", [1, 2048], BF16)
    mrow_b, irow_b = rowsB[0:1, 0:512], rowsB[0:1, 512:1024]
    recr = [rowsB[0:1, 1024:1536], rowsB[0:1, 1536:2048]]
    gsig = al("gsig", [128, 8], F32)
    gt = al("gt", [128, 40], F32)
    wtok = al("wtok", [128, 8], F32)
    wr8 = al("wr8", [8, TPC], BF16)
    wstg = al("wstg", [1, 1024], BF16)
    mq = [al(f"mq{c}", [128, TPC], BF16) for c in range(NDC)]
    y1 = mq

    ap_ = nc.alloc_psum_tensor
    ps_s = [ap_(f"ps_s{p}", [128, 2 * QT], F32) for p in range(2)]
    ps_c = [ap_(f"ps_c{i}", [128, QT], F32) for i in range(2)]
    ps_tr = [ap_(f"ps_tr{p}", [128, 1024], BF16) for p in range(2)]

    stk = contextlib.ExitStack()
    STREAMS = ["t", "s", "v", "y_jv", "y_eye", "y_x0", "y_x1", "y_st0",
               "y_st1", "g_sm", "g_j", "g_wb", "g_upz", "g_bv", "g_kv0",
               "g_kv1", "g_c30", "g_c31", "g_wr8", "g_st0", "g_st1",
               "g_we13", "g_we2", "y_rb"]
    sems = {e: stk.enter_context(nc.semaphore(f"sem_{e}")) for e in STREAMS}
    P = Prog()
    nc.const_aps.register(F32, 1e-5, cpool[:, 0:1])

    def chunked(dr, c=8):
        return dr.rearrange("(c p) n -> p c n", p=128)

    def wsl(buf, di, dc):
        return buf[:, di * DM + dc * 128: di * DM + (dc + 1) * 128]

    sc_x = contextlib.ExitStack()
    x_fm = [sc_x.enter_context(nc.sbuf_tensor(f"x_fm{c}", [128, S], BF16))
            for c in range(NDC)]
    sc_ctx = contextlib.ExitStack()
    ctx_fm = [sc_ctx.enter_context(nc.sbuf_tensor(f"ctx{c}", [128, TPC], BF16))
              for c in range(NDC)]
    sc_kaug = contextlib.ExitStack()
    k_augA = sc_kaug.enter_context(nc.sbuf_tensor("k_augA", [128, S], BF16))
    k_augB = sc_kaug.enter_context(nc.sbuf_tensor("k_augB", [128, S], BF16))

    sc_p0 = contextlib.ExitStack()
    poolF = sc_p0.enter_context(nc.sbuf_tensor("poolF", [1, S], F32))
    poolB = sc_p0.enter_context(nc.sbuf_tensor("poolB", [65, S], BF16))
    stage_a = [sc_p0.enter_context(nc.sbuf_tensor(f"stg{p}", [128, DM], F32))
               for p in range(2)]
    xb_t = [sc_p0.enter_context(nc.sbuf_tensor(f"xb{p}", [128, DM], BF16))
            for p in range(2)]
    jv_sb = poolF[0:1, :]
    jv_b = poolB[0:1, :]
    jsq_b = poolB[32:33, :]
    onesrow_b = poolB[64:65, :]

    ld_jv = P.emit("y", lambda y: y.dma_start(jv_sb, jv_d[:, :]), sem="y_jv")
    ld_eye = P.emit("y", lambda y: y.dma_start(eye_f[:, :], eye_d[:, :]), sem="y_eye")
    gw_small = 0
    for dst, src in [
        (eye_b[:, :], eye_d[:, :]), (bq_sb[:, :], bq_d[:, :]),
        (bk_sb[:, :], bk_d[:, :]), (bo_sb[:, :], bo_d[:, :]),
        (bfq_sb[:, :], bfq_d[:, :]), (bfg_sb[:, :], bfg_d[:, :]),
        (bv_sb[:, :], bv_d[:, :]), (wg_b[:, :], wg_d[:, :]),
    ]:
        gw_small = P.emit("g", lambda g, d=dst, s=src: g.dma_start(d, s), sem="g_sm")

    def memsets(v):
        v.memset(cpool[:, 0:1], 1e-5)
        v.memset(ones1_b[:, :], 1.0)
        v.memset(ones128_b[:, :], 1.0)
        return v.memset(onesrow_b, 1.0)
    v_init = P.emit("v", memsets)
    v_jsq = P.emit("v", lambda v: v.tensor_tensor(jsq_b, jv_sb, jv_sb,
                                                  ALU.mult),
                   waits=[("y_jv", ld_jv)])
    v_jb = P.emit("v", lambda v: v.tensor_copy(jv_b, jv_sb),
                  waits=[("v", v_jsq)])
    gw_j = 0
    for ka in (k_augA, k_augB):
        P.emit("g", lambda g, ka=ka: g.dma_start(ka[64:65, :], jsq_b),
               waits=[("v", v_jb), ("v", v_init)], sem="g_j")
        P.emit("g", lambda g, ka=ka: g.dma_start(ka[65:66, :], jv_b),
               sem="g_j")
        gw_j = P.emit("g", lambda g, ka=ka: g.dma_start(ka[66:67, :],
                                                        onesrow_b), sem="g_j")

    x_ld, x_cv, x_tr, x_ev = {}, {}, {}, {}
    for t in range(NXT):
        par = t % 2
        x_ld[t] = P.emit("y", lambda y, t=t, par=par: y.dma_start(
            stage_a[par][:, :], xr[t * 128:(t + 1) * 128, :]),
            waits=[("v", x_cv.get(t - 2, 0))], sem=f"y_x{par}")
        x_cv[t] = P.emit("v", lambda v, par=par: v.tensor_copy(
            xb_t[par][:, :], stage_a[par][:, :]),
            waits=[(f"y_x{par}", x_ld[t]), ("t", x_tr.get(t - 2, 0)),
                   ("v", v_init)])

        def trf(t_, par=par):
            for dc in range(NDC):
                ii = t_.transpose(ps_tr[par][:, dc * 128:(dc + 1) * 128],
                                  xb_t[par][:, dc * 128:(dc + 1) * 128],
                                  eye_b[:, :])
            return ii
        x_tr[t] = P.emit("t", trf, waits=[("v", x_cv[t]),
                                          ("g_sm", gw_small),
                                          ("v", x_ev.get(t - 2, 0))])

        def evf(v, t=t, par=par):
            for dc in range(NDC):
                ii = v.tensor_copy(x_fm[dc][:, t * 128:(t + 1) * 128],
                                   ps_tr[par][:, dc * 128:(dc + 1) * 128])
            return ii
        x_ev[t] = P.emit("v", evf, waits=[("t", x_tr[t])])
    x_done = x_ev[NXT - 1]
    xtr_last = x_tr[NXT - 1]

    sc_p0.close()

    sc_c3 = contextlib.ExitStack()
    C3 = sc_c3.enter_context(nc.sbuf_tensor("C3", [16, 3 * TPC], BF16))
    sc_w = contextlib.ExitStack()
    w_big1 = sc_w.enter_context(nc.sbuf_tensor("w_big1", [128, 8 * DM], BF16))
    w_big2 = w_big1

    gw_wq = P.emit("g", lambda g: g.dma_start(
        w_big1[:, :].rearrange("p (c n) -> p c n", c=8), chunked(wq_d[:, :])),
        waits=[("t", xtr_last), ("v", x_done), ("g_j", gw_j)], sem="g_wb")

    mq_mm, mq_ev = {}, {}
    for qc in range(NQT):
        for dc in range(NDC):
            i = qc * NDC + dc
            par = i % 2

            def mmf(t_, qc=qc, dc=dc, par=par):
                for di in range(NDC):
                    ii = t_.matmul(ps_s[par][:, 0:QT], wsl(w_big1, di, dc),
                                   x_fm[di][:, qc * QT:(qc + 1) * QT],
                                   start=(di == 0), stop=(di == NDC - 1))
                return ii
            mq_mm[i] = P.emit("t", mmf, waits=[("v", x_done), ("g_wb", gw_wq),
                                               ("v", mq_ev.get(i - 2, 0))])
            mq_ev[i] = P.emit("v", lambda v, qc=qc, dc=dc, par=par:
                              v.tensor_scalar(mq[dc][:, qc * QT:(qc + 1) * QT],
                                              ps_s[par][:, 0:QT],
                                              bq_sb[:, dc:dc + 1], None,
                                              ALU.add),
                              waits=[("t", mq_mm[i])])
    mq_done = mq_ev[NQT * NDC - 1]

    def xmeanf(v):
        for dc in range(NDC):
            ii = v.reduce_sum(xm_sb[:, dc:dc + 1], x_fm[dc][:, :], axis=AX.X)
        return ii
    v_xm = P.emit("v", xmeanf, waits=[("v", x_done)])
    v_xmb = P.emit("v", lambda v: v.tensor_scalar(
        xm_b[:, :], xm_sb[:, :], 1.0 / S, None, ALU.mult), waits=[("v", v_xm)])

    def glof(t_):
        for dc in range(NDC):
            for di in range(NDC):
                ii = t_.matmul(ps_c[0][:, dc:dc + 1], wsl(w_big1, di, dc),
                               xm_b[:, di:di + 1], start=(di == 0),
                               stop=(di == NDC - 1))
        return ii
    t_glo = P.emit("t", glof, waits=[("v", v_xmb), ("g_wb", gw_wq)])
    v_glo = P.emit("v", lambda v: v.tensor_tensor(
        glo_sb[:, :], ps_c[0][:, 0:8], bq_sb[:, :], ALU.add),
        waits=[("t", t_glo)])
    v_glob = P.emit("v", lambda v: v.tensor_copy(glo_b[:, :], glo_sb[:, :]),
                    waits=[("v", v_glo)])

    def gff(t_):
        for dc in range(NDC):
            for di in range(NDC):
                ii = t_.matmul(ps_c[1][:, dc:dc + 1], wsl(w_big2, di, dc),
                               glo_b[:, di:di + 1], start=(di == 0),
                               stop=(di == NDC - 1))
        return ii
    gw_wfg = P.emit("g", lambda g: g.dma_start(
        w_big1[:, :].rearrange("p (c n) -> p c n", c=8), chunked(wfg_d[:, :])),
        waits=[("t", t_glo), ("t", mq_mm[NQT * NDC - 1])], sem="g_wb")
    t_gf = P.emit("t", gff, waits=[("v", v_glob), ("g_wb", gw_wfg)])
    v_cb1 = P.emit("v", lambda v: v.tensor_tensor(
        cbias[:, :], ps_c[1][:, 0:8], bfq_sb[:, :], ALU.add),
        waits=[("t", t_gf)])
    v_cb = P.emit("v", lambda v: v.tensor_tensor(
        cbias[:, :], cbias[:, :], bfg_sb[:, :], ALU.add), waits=[("v", v_cb1)])

    gw_wfq = P.emit("g", lambda g: g.dma_start(
        w_big2[:, :].rearrange("p (c n) -> p c n", c=8), chunked(wfq_d[:, :])),
        waits=[("t", t_gf)], sem="g_wb")

    sc_p2 = contextlib.ExitStack()
    upz_b = sc_p2.enter_context(nc.sbuf_tensor("upz_b", [128, 512], BF16))
    gw_upz = P.emit("g", lambda g: g.dma_start(upz_b[:, :], upz_d[:, :]),
                    waits=[("t", xtr_last), ("v", x_done), ("g_j", gw_j)],
                    sem="g_upz")
    spz = sc_p2.enter_context(nc.sbuf_tensor("spz", [128, 4096], F32))
    Sp, Sz = spz[0:16, 0:TPC], spz[32:48, 0:TPC]
    cA, cB = spz[64:80, 0:TPC], spz[64:80, TPC:2 * TPC]
    ctp = sc_p2.enter_context(nc.sbuf_tensor("ctp", [128, 1024], BF16))
    c_t_sb = [ctp[:, 0:QT], ctp[:, QT:2 * QT]]

    c_mm, c_tanh, pz_mm, sig = {}, {}, {}, {}
    for qc in range(NQT):
        for dc in range(NDC):
            i = qc * NDC + dc
            par = i % 2

            def cmf(t_, qc=qc, dc=dc, par=par):
                for di in range(NDC):
                    ii = t_.matmul(ps_s[par][:, 0:QT], wsl(w_big2, di, dc),
                                   mq[di][:, qc * QT:(qc + 1) * QT],
                                   start=(di == 0), stop=(di == NDC - 1))
                return ii
            c_mm[i] = P.emit("t", cmf,
                             waits=[("v", v_cb), ("g_wb", gw_wfq), ("v", mq_done),
                                    ("s", c_tanh.get(i - 2, 0))])
            c_tanh[i] = P.emit("s", lambda s, dc=dc, par=par: s.activation(
                c_t_sb[par], ps_s[par][:, 0:QT], AF.Tanh,
                bias=cbias[:, dc:dc + 1]), waits=[("t", c_mm[i])])
            pz_mm[i] = P.emit("t", lambda t_, dc=dc, par=par: t_.matmul(
                ps_c[0][0:64, 0:QT], upz_b[:, dc * 64:(dc + 1) * 64],
                c_t_sb[par], start=(dc == 0), stop=(dc == NDC - 1)),
                waits=[("s", c_tanh[i]), ("s", sig.get(qc - 1, 0)),
                       ("v", v_glo if i == 0 else 0),
                       ("g_upz", gw_upz if i == 0 else 0)])
        sig[qc] = P.emit("s", lambda s, qc=qc: (
            s.activation(Sp[:, qc * QT:(qc + 1) * QT], ps_c[0][0:16, 0:QT],
                         AF.Sigmoid),
            s.activation(Sz[:, qc * QT:(qc + 1) * QT], ps_c[0][32:48, 0:QT],
                         AF.Sigmoid))[-1],
            waits=[("t", pz_mm[qc * NDC + NDC - 1])])
    sig_done = sig[NQT - 1]
    pz_last = pz_mm[NQT * NDC - 1]

    steps = [
        lambda v: v.reciprocal(cA, Sz),
        lambda v: v.tensor_tensor(cA, cA, cA, ALU.mult),
        lambda v: v.tensor_scalar(cA, cA, -1.1920928955078125e-07,
                                  None, ALU.mult),
        lambda v: v.tensor_scalar(cB, Sp, float(S), None, ALU.mult),
        lambda v: v.tensor_copy(C3[:, 0:TPC], cA),
        lambda v: v.tensor_tensor(cA, cA, cB, ALU.mult),
        lambda v: v.tensor_scalar(C3[:, TPC:2 * TPC], cA, -2.0, None,
                                  ALU.mult),
        lambda v: v.tensor_tensor(cA, cA, cB, ALU.mult),
        lambda v: v.tensor_scalar(C3[:, 2 * TPC:3 * TPC], cA, -4.0, None,
                                  ALU.add),
    ]
    vprev = P.emit("v", steps[0], waits=[("s", sig_done)])
    for f in steps[1:]:
        vprev = P.emit("v", f, waits=[("v", vprev)])
    coeff_done = vprev

    sc_p2.close()
    sc_w.close()

    sc_att2 = contextlib.ExitStack()
    v_aug = sc_att2.enter_context(nc.sbuf_tensor("v_aug", [128, NKC * 130],
                                                 BF16))
    kv_w = [sc_att2.enter_context(nc.sbuf_tensor(f"kv{p}", [128, 2048], BF16))
            for p in range(2)]
    e_t = [sc_att2.enter_context(nc.sbuf_tensor(f"e_t{p}", [128, 2 * QT],
                                                BF16)) for p in range(2)]
    rhs_aug = [sc_att2.enter_context(nc.sbuf_tensor(f"rhs{p}", [128, QT],
                                                    BF16)) for p in range(4)]
    rbp = sc_att2.enter_context(nc.sbuf_tensor("rbp", [128, QT], F32))
    rb_sb = [rbp[0:64, :], rbp[64:128, :]]
    v_fm = sc_att2.enter_context(nc.sbuf_tensor("v_fm", [128, S], BF16))

    def vones(v):
        va = v_aug[:, :].rearrange("p (t c) -> p t c", c=130)
        v.memset(va[:, :, 64:65], 1.0)
        return v.memset(va[:, :, 129:130], 1.0)
    v_vones = P.emit("v", vones, waits=[("t", pz_last), ("v", coeff_done)])

    AT = {}
    pair_t_end = {}
    for p in range(NE):
        wpar = p % 2
        cols = slice(p * 128, (p + 1) * 128)
        P.emit("g", lambda g, wpar=wpar, cols=cols: g.dma_start(
            kv_w[wpar][:, 0:1024].rearrange("p (c n) -> p c n", c=8),
            chunked(wk_d[:, cols])),
            waits=[("t", pair_t_end.get(p - 2, pz_last)), ("g_j", gw_j),
                   ("v", coeff_done if p == 0 else 0),
                   ("s", sig_done if p == 0 else 0)], sem=f"g_kv{wpar}")
        kv_gw = P.emit("g", lambda g, wpar=wpar, cols=cols: g.dma_start(
            kv_w[wpar][:, 1024:2048].rearrange("p (c n) -> p c n", c=8),
            chunked(wv_d[:, cols])), sem=f"g_kv{wpar}")

        kev = {}
        for kt in range(8):
            par = kt % 2

            def kmf(t_, kt=kt, par=par, wpar=wpar):
                for di in range(NDC):
                    ii = t_.matmul(ps_s[par][:, 0:QT],
                                   kv_w[wpar][:, di * 128:(di + 1) * 128],
                                   x_fm[di][:, kt * QT:(kt + 1) * QT],
                                   start=(di == 0), stop=(di == NDC - 1))
                return ii
            w = [(f"g_kv{wpar}", kv_gw), ("v", kev.get(kt - 2, 0))]
            if kt == 0:
                w += [("s", AT.get((p, 0, -1, "exp"), sig_done)),
                      ("v", AT.get((p - 1, NQT - 1, "ctxe"),
                                   max(coeff_done, mq_done)))]
            tk = P.emit("t", kmf, waits=w)
            kev[kt] = P.emit("v", lambda v, kt=kt, par=par, p=p: (
                v.tensor_scalar(k_augA[0:64, kt * QT:(kt + 1) * QT],
                                ps_s[par][0:64, 0:QT], bk_sb[0:64, p:p + 1],
                                0.125, ALU.add, ALU.mult),
                v.tensor_scalar(k_augB[0:64, kt * QT:(kt + 1) * QT],
                                ps_s[par][64:128, 0:QT], bk_sb[64:128, p:p + 1],
                                0.125, ALU.add, ALU.mult))[-1],
                waits=[("t", tk),
                       ("t", AT.get((p - 1, NQT - 1, "last_sc"), 0)
                        if kt == 0 else 0)])

        vfev = {}
        vtr = {}
        vev = {}
        for kt in range(8):
            par = kt % 2

            def vmf(t_, kt=kt, par=par, wpar=wpar):
                for di in range(NDC):
                    ii = t_.matmul(ps_s[par][:, QT:2 * QT],
                                   kv_w[wpar][:, 1024 + di * 128:
                                               1024 + (di + 1) * 128],
                                   x_fm[di][:, kt * QT:(kt + 1) * QT],
                                   start=(di == 0), stop=(di == NDC - 1))
                return ii
            tv = P.emit("t", vmf, waits=[("v", vfev.get(kt - 2, 0))])
            vfev[kt] = P.emit("v", lambda v, kt=kt, par=par, p=p:
                              v.tensor_scalar(
                v_fm[:, kt * QT:(kt + 1) * QT], ps_s[par][:, QT:2 * QT],
                bv_sb[:, p:p + 1], None, ALU.add),
                waits=[("t", tv),
                       ("t", AT.get((p - 1, NQT - 1, NKC - 1, "ctx"), 0)
                        if kt == 0 else 0),
                       ("v", v_vones if kt == 0 else 0)])

            def vtrf(t_, kt=kt, par=par):
                for i in range(4):
                    ii = t_.transpose(ps_tr[par][:, i * 128:(i + 1) * 128],
                                      v_fm[:, kt * QT + i * 128:
                                           kt * QT + (i + 1) * 128],
                                      eye_b[:, :])
                return ii
            vtr[kt] = P.emit("t", vtrf, waits=[("v", vfev[kt]),
                                               ("v", vev.get(kt - 2, 0)),
                                               ("t", xtr_last if kt < 2 and p == 0 else 0)])

            def vevf(v, kt=kt, par=par):
                for i in range(4):
                    base = (kt * 4 + i) * 130
                    v.tensor_copy(v_aug[:, base:base + 64],
                                  ps_tr[par][:, i * 128:i * 128 + 64])
                    ii = v.tensor_copy(v_aug[:, base + 65:base + 129],
                                       ps_tr[par][:, i * 128 + 64:
                                                   i * 128 + 128])
                return ii
            vev[kt] = P.emit("v", vevf, waits=[("t", vtr[kt])])
        kv_done_v = max(vev[7], kev[7])

        def emit_rhs(pp, qc):
            for hi in range(2):
                bi = 2 * (qc % 2) + hi
                h = 2 * pp + hi
                guard = AT.get((pp, qc - 2, "last_sc"),
                               AT.get((pp - 1, NQT - 1, "last_sc"), pz_last))
                AT[(pp, qc, hi, "q")] = P.emit(
                    "v", lambda v, bi=bi, hi=hi, pp=pp, qc=qc: v.tensor_copy(
                        rhs_aug[bi][0:64, :],
                        mq[pp][hi * 64:(hi + 1) * 64, qc * QT:(qc + 1) * QT]),
                    waits=[("t", guard)])
                AT[(pp, qc, hi, "c3")] = P.emit(
                    "g", lambda g, bi=bi, h=h, qc=qc: g.dma_start(
                        rhs_aug[bi][64:67, 0:QT],
                        C3[h:h + 1, :].rearrange("o (c n) -> o c n", c=3)
                        [:, :, qc * QT:(qc + 1) * QT]),
                    waits=[("v", AT[(pp, qc, hi, "q")])], sem=f"g_c3{qc % 2}")

        if p == 0:
            emit_rhs(p, 0)
        emit_rhs(p, 1)
        for qc in range(NQT):
            def emit_ctx(kc):
                def ctf(t_, kc=kc):
                    base = kc * 130
                    t_.matmul(ps_c[0][0:65, :], v_aug[:, base:base + 65],
                              e_t[kc % 2][:, 0:QT], start=(kc == 0),
                              stop=(kc == NKC - 1))
                    return t_.matmul(ps_c[1][0:65, :],
                                     v_aug[:, base + 65:base + 130],
                                     e_t[kc % 2][:, QT:2 * QT],
                                     start=(kc == 0), stop=(kc == NKC - 1))
                wct = [("s", AT[(p, qc, kc, "exp")])]
                if kc == 0:
                    wct += [("v", AT.get((p, qc - 1, "ctxe"),
                                         AT.get((p - 1, NQT - 1, "ctxe"), 0))),
                            ("v", kv_done_v)]
                AT[(p, qc, kc, "ctx")] = P.emit("t", ctf, waits=wct)

            for kc in range(NKC):
                par = kc % 2
                wsc = [("s", AT.get((p, qc, kc - 2, "exp"), 0)),
                       (f"g_c3{qc % 2}", AT[(p, qc, 1, "c3")]),
                       ("v", kv_done_v)]
                if kc == 0:
                    wsc += [("g_j", gw_j if p == 0 and qc == 0 else 0)]

                def scf(t_, kc=kc, par=par, biA=2 * (qc % 2),
                        biB=2 * (qc % 2) + 1):
                    t_.matmul(ps_s[par][:, 0:QT],
                              k_augA[0:67, kc * 128:(kc + 1) * 128],
                              rhs_aug[biA][0:67, :], start=True, stop=True)
                    return t_.matmul(ps_s[par][:, QT:2 * QT],
                                     k_augB[0:67, kc * 128:(kc + 1) * 128],
                                     rhs_aug[biB][0:67, :], start=True,
                                     stop=True)
                AT[(p, qc, kc, "sc")] = P.emit("t", scf, waits=wsc)
                AT[(p, qc, kc, "exp")] = P.emit(
                    "s", lambda s, kc=kc, par=par: s.activation(
                        e_t[kc % 2][:, :], ps_s[par][:, :], AF.Exp),
                    waits=[("t", AT[(p, qc, kc, "sc")]),
                           ("t", AT.get((p, qc, kc - 2, "ctx"), 0))])
                if kc == 2 and qc >= 1:
                    if qc + 1 < NQT:
                        emit_rhs(p, qc + 1)
                    elif p + 1 < NE:
                        emit_rhs(p + 1, 0)
                if kc >= 1:
                    emit_ctx(kc - 1)
            emit_ctx(NKC - 1)
            AT[(p, qc, "last_sc")] = AT[(p, qc, NKC - 1, "sc")]
            AT[(p, qc, "rec")] = P.emit("v", lambda v: (
                v.reciprocal(recr_f[0], ps_c[0][64:65, :]),
                v.reciprocal(recr_f[1], ps_c[1][64:65, :]))[-1],
                waits=[("t", AT[(p, qc, NKC - 1, "ctx")]),
                       ("y_rb", AT.get((p, qc - 1, "rb"),
                                       AT.get((p - 1, NQT - 1, "rb"), 0)))])
            wr_rbd = P.emit("y", lambda y: y.dma_start(
                rbd[:, :], rowsF[0:1, 0:2 * QT]),
                waits=[("v", AT[(p, qc, "rec")])], sem="y_rb")
            P.emit("y", lambda y: y.dma_start(
                rb_sb[0], bass.AP(rbd, 0, [[0, 64], [1, QT]])),
                waits=[("y_rb", wr_rbd)], sem="y_rb")
            AT[(p, qc, "rb")] = P.emit("y", lambda y: y.dma_start(
                rb_sb[1], bass.AP(rbd, QT, [[0, 64], [1, QT]])), sem="y_rb")
            AT[(p, qc, "ctxe")] = P.emit("v", lambda v, p=p, qc=qc: (
                v.tensor_tensor(ctx_fm[p][0:64, qc * QT:(qc + 1) * QT],
                                ps_c[0][0:64, :], rb_sb[0], ALU.mult),
                v.tensor_tensor(ctx_fm[p][64:128, qc * QT:(qc + 1) * QT],
                                ps_c[1][0:64, :], rb_sb[1], ALU.mult))[-1],
                waits=[("y_rb", AT[(p, qc, "rb")])])
            AT[(p, qc + 1, -2, "exp")] = AT[(p, qc, NKC - 2, "exp")]
            AT[(p, qc + 1, -1, "exp")] = AT[(p, qc, NKC - 1, "exp")]
            AT[(p, qc + 1, -2, "ctx")] = AT[(p, qc, NKC - 2, "ctx")]
            AT[(p, qc + 1, -1, "ctx")] = AT[(p, qc, NKC - 1, "ctx")]
            if qc == NQT - 1:
                AT[(p + 1, 0, -2, "exp")] = AT[(p, qc, NKC - 2, "exp")]
                AT[(p + 1, 0, -1, "exp")] = AT[(p, qc, NKC - 1, "exp")]
                AT[(p + 1, 0, -2, "ctx")] = AT[(p, qc, NKC - 2, "ctx")]
                AT[(p + 1, 0, -1, "ctx")] = AT[(p, qc, NKC - 1, "ctx")]
                AT[(p + 1, 0, "rb_prev")] = AT[(p, qc, "rb")]
        pair_t_end[p] = P.cnt["t"]
    att_v = AT[(NE - 1, NQT - 1, "ctxe")]
    att_rbs = AT[(NE - 1, NQT - 1, NKC - 1, "exp")]
    att_t = P.cnt["t"]

    sc_att2.close()
    sc_c3.close()
    sc_kaug.close()
    sc_p4 = contextlib.ExitStack()
    wo_b = sc_p4.enter_context(nc.sbuf_tensor("wo_b", [128, 8 * DM], BF16))
    zbuf_a = sc_p4.enter_context(nc.sbuf_tensor("zbuf_a", [128, 8 * QT], F32))
    mb_a = sc_p4.enter_context(nc.sbuf_tensor("mb_a", [128, QT], F32))
    ivb_a = sc_p4.enter_context(nc.sbuf_tensor("ivb_a", [128, QT], F32))
    rows_a = sc_p4.enter_context(nc.sbuf_tensor("rows_a", [1, 2048], F32))
    zbp_a = [sc_p4.enter_context(nc.sbuf_tensor(f"zbp_a{p}", [128, QT], BF16))
             for p in range(2)]
    zsp_a = [sc_p4.enter_context(nc.sbuf_tensor(f"zsp_a{p}", [128, QT], BF16))
             for p in range(2)]

    gw_wo = P.emit("g", lambda g: g.dma_start(
        wo_b[:, :].rearrange("p (c n) -> p c n", c=8), chunked(wo_d[:, :])),
        waits=[("t", att_t), ("s", att_rbs), ("v", att_v)], sem="g_wb")

    def ln_phase(qc, mode, y_dst, st_, zbuf, mb_sb, invb_sb, rows4,
                 zb_t, zsq_t):
        L = {}
        mrow, srow = rows4[0:1, 0:512], rows4[0:1, 512:1024]
        vrow, vrow2 = rows4[0:1, 1024:1536], rows4[0:1, 1536:2048]
        zev = st_.setdefault("zev", {})
        for dc in range(NDC):
            gi = qc * NDC + dc
            par = gi % 2
            if mode == "attn":
                wmm = [("v", att_v), ("g_wb", gw_wo),
                       ("v", zev.get(gi - 2, 0)),
                       ("s", att_rbs if gi < 2 else 0),
                       ("s", st_.get("prev_bcs", 0) if dc < 2 else 0)]

                def mmf(t_, qc=qc, dc=dc, par=par):
                    for di in range(NDC):
                        ii = t_.matmul(ps_s[par][:, 0:QT], wsl(wo_b, di, dc),
                                       ctx_fm[di][:, qc * QT:(qc + 1) * QT],
                                       start=(di == 0), stop=(di == NDC - 1))
                    return ii
                L[(dc, "mm")] = P.emit("t", mmf, waits=wmm)
                zev[gi] = L[(dc, "z")] = P.emit(
                    "v", lambda v, qc=qc, dc=dc, par=par:
                    v.scalar_tensor_tensor(
                        zbuf[:, dc * QT:(dc + 1) * QT], ps_s[par][:, 0:QT],
                        bo_sb[:, dc:dc + 1],
                        x_fm[dc][:, qc * QT:(qc + 1) * QT], ALU.add, ALU.add),
                    waits=[("t", L[(dc, "mm")])])
            else:
                L[(dc, "z")] = P.emit(
                    "v", lambda v, qc=qc, dc=dc: v.scalar_tensor_tensor(
                        zbuf[:, dc * QT:(dc + 1) * QT],
                        st_["routed"][dc][:, qc * QT:(qc + 1) * QT], 1.0 / 3.0,
                        y1[dc][:, qc * QT:(qc + 1) * QT], ALU.mult, ALU.add),
                    waits=[("v", st_.get("src_v", 0)),
                           ("t", st_.get("prev_ttr", 0))])
            zg = (L.get((dc - 2, "sums"), st_.get("prev_sums_last", 0)
                        if dc < 2 else 0))
            L[(dc, "zb")] = P.emit("v", lambda v, dc=dc: v.tensor_copy(
                zb_t[dc % 2][:, :], zbuf[:, dc * QT:(dc + 1) * QT]),
                waits=[("v", L[(dc, "z")]), ("t", zg)])
            L[(dc, "zsq")] = P.emit("s", lambda s, dc=dc: s.activation(
                zsq_t[dc % 2][:, :], zbuf[:, dc * QT:(dc + 1) * QT],
                AF.Square), waits=[("v", L[(dc, "z")]), ("t", zg)])

            def emit_sums(d2):
                wsum = [("v", L[(d2, "zb")]), ("s", L[(d2, "zsq")])]
                if d2 == 0:
                    wsum += [("s", st_.get("psc_guard_s", 0)),
                             ("v", st_.get("psc_guard_v", 0))]
                L[(d2, "sums")] = P.emit("t", lambda t_, d2=d2: (
                    t_.matmul(ps_c[0][0:1, 0:QT], ones128_b[:, :],
                              zb_t[d2 % 2][:, :], start=(d2 == 0),
                              stop=(d2 == NDC - 1)),
                    t_.matmul(ps_c[1][0:1, 0:QT], ones128_b[:, :],
                              zsq_t[d2 % 2][:, :], start=(d2 == 0),
                              stop=(d2 == NDC - 1)))[-1], waits=wsum)

            if dc >= 1:
                emit_sums(dc - 1)
            if dc == NDC - 1:
                emit_sums(dc)
        r0 = P.emit("v", lambda v: v.tensor_scalar(
            mrow, ps_c[0][0:1, 0:QT], 1.0 / DM, None, ALU.mult),
            waits=[("t", L[(NDC - 1, "sums")])])
        r1 = P.emit("v", lambda v: v.tensor_scalar(
            vrow, ps_c[1][0:1, 0:QT], 1.0 / DM, None, ALU.mult),
            waits=[("v", r0)])
        r2 = P.emit("v", lambda v: v.tensor_tensor(vrow2, mrow, mrow,
                                                   ALU.mult), waits=[("v", r1)])
        r3 = P.emit("v", lambda v: v.tensor_tensor(vrow, vrow, vrow2,
                                                   ALU.subtract),
                    waits=[("v", r2)])
        r4 = P.emit("s", lambda s: s.activation(srow, vrow, AF.Sqrt,
                                                bias=1e-5), waits=[("v", r3)])
        r5 = P.emit("v", lambda v: v.reciprocal(vrow2, srow),
                    waits=[("s", r4)])
        r6 = P.emit("v", lambda v: (v.tensor_copy(mrow_b, mrow),
                                    v.tensor_copy(irow_b, vrow2))[-1],
                    waits=[("v", r5)])
        wbc = [("v", r6), ("v", L[(NDC - 1, "z")]),
               ("s", st_.get("pss_guard_s", 0)),
               ("v", st_.get("pss_guard_v", 0))]
        bct = P.emit("t", lambda t_: (
            t_.matmul(ps_s[0][:, 0:QT], ones1_b[:, :], mrow_b, start=True,
                      stop=True),
            t_.matmul(ps_s[1][:, 0:QT], ones1_b[:, :], irow_b, start=True,
                      stop=True))[-1], waits=wbc)
        bcs = P.emit("s", lambda s: (s.copy(mb_sb[:, :], ps_s[0][:, 0:QT]),
                                     s.copy(invb_sb[:, :],
                                            ps_s[1][:, 0:QT]))[-1],
                     waits=[("t", bct)])
        subv = P.emit("v", lambda v: [v.tensor_tensor(
            zbuf[:, dc * QT:(dc + 1) * QT], zbuf[:, dc * QT:(dc + 1) * QT],
            mb_sb[:, :], ALU.subtract) for dc in range(NDC)][-1],
            waits=[("s", bcs)])
        if y_dst is not None:
            fin = P.emit("v", lambda v, qc=qc: [v.tensor_tensor(
                y_dst[dc][:, qc * QT:(qc + 1) * QT],
                zbuf[:, dc * QT:(dc + 1) * QT], invb_sb[:, :], ALU.mult)
                for dc in range(NDC)][-1], waits=[("v", subv)])
        else:
            fin = P.emit("v", lambda v: [v.tensor_tensor(
                zbuf[:, dc * QT:(dc + 1) * QT], zbuf[:, dc * QT:(dc + 1) * QT],
                invb_sb[:, :], ALU.mult) for dc in range(NDC)][-1],
                waits=[("v", subv)])
        st_["prev_sums_last"] = L[(NDC - 1, "sums")]
        st_["prev_bcs"] = bcs
        st_["pss_guard_s"] = bcs
        st_["psc_guard_v"] = r1
        L["fin"], L["bcs"], L["r1"] = fin, bcs, r1
        return L

    st1 = {"psc_guard_s": sig_done}
    gate_ws = {}
    gw_wr8 = 0
    for qc in range(NQT):
        L1 = ln_phase(qc, "attn", y1, st1, zbuf_a, mb_a, ivb_a, rows_a,
                      zbp_a, zsp_a)
        for tt in range(4):
            tok = qc * QT + tt * 128

            def gmf(t_, qc=qc, tt=tt, tok=tok):
                for di in range(NDC):
                    ii = t_.matmul(ps_c[0][:, 8 * tt:8 * tt + 8],
                                   y1[di][:, tok:tok + 128],
                                   wg_b[:, di * 8:(di + 1) * 8],
                                   start=(di == 0), stop=(di == NDC - 1))
                return ii
            gt_t = P.emit("t", gmf, waits=[("v", L1["fin"]),
                                           ("s", gate_ws.get((qc, tt - 1), 0))])
            gs = P.emit("s", lambda s, tt=tt: s.activation(
                gsig[:, :], ps_c[0][:, 8 * tt:8 * tt + 8], AF.Sigmoid),
                waits=[("t", gt_t)])
            st1["psc_guard_s"] = gs
            chain = [
                lambda v: v.reduce_max(gt[:, 0:1], gsig[:, :], axis=AX.X),
                lambda v: v.tensor_scalar(gt[:, 8:16], gsig[:, :], gt[:, 0:1],
                                          None, ALU.is_equal),
                lambda v: v.scalar_tensor_tensor(gt[:, 16:24], gt[:, 8:16],
                                                 -1e30, gsig[:, :], ALU.mult,
                                                 ALU.add),
                lambda v: v.reduce_max(gt[:, 1:2], gt[:, 16:24], axis=AX.X),
                lambda v: v.tensor_scalar(gt[:, 24:32], gt[:, 16:24],
                                          gt[:, 1:2], None, ALU.is_equal),
                lambda v: v.tensor_tensor(gt[:, 24:32], gt[:, 24:32],
                                          gt[:, 8:16], ALU.add),
                lambda v: v.tensor_tensor(gt[:, 2:3], gt[:, 0:1], gt[:, 1:2],
                                          ALU.add),
                lambda v: v.reciprocal(gt[:, 3:4], gt[:, 2:3]),
                lambda v: v.tensor_tensor(gt[:, 32:40], gsig[:, :],
                                          gt[:, 24:32], ALU.mult),
                lambda v: v.tensor_scalar(wtok[:, :], gt[:, 32:40], gt[:, 3:4],
                                          None, ALU.mult),
            ]
            vv = P.emit("v", chain[0], waits=[("s", gs)])
            for f in chain[1:]:
                vv = P.emit("v", f, waits=[("v", vv)])
            wts = 0
            for rnd in range(2):
                def wtf(t_, rnd=rnd):
                    for e4 in range(4):
                        ii = t_.transpose(
                            ps_s[0][0:1, QT + e4 * 128:QT + (e4 + 1) * 128],
                            wtok[:, rnd * 4 + e4:rnd * 4 + e4 + 1],
                            eye_f[:, :])
                    return ii
                wt_t = P.emit("t", wtf, waits=[("v", vv), ("s", wts),
                                               ("y_eye", ld_eye)])
                wts = P.emit("s", lambda s, rnd=rnd: s.copy(
                    wstg[0:1, 0:512], ps_s[0][0:1, QT:QT + 512]),
                    waits=[("t", wt_t), ("g_wr8", gw_wr8)])
                for e4 in range(4):
                    gw_wr8 = P.emit("g", lambda g, rnd=rnd, tok=tok, e4=e4:
                                    g.dma_start(
                        wr8[rnd * 4 + e4:rnd * 4 + e4 + 1, tok:tok + 128],
                        wstg[0:1, e4 * 128:(e4 + 1) * 128]),
                        waits=[("s", wts)] if e4 == 0 else [], sem="g_wr8")
            gate_ws[(qc, tt)] = wts
    wr8_done = gw_wr8
    p4_v = P.cnt["v"]
    p4_s = P.cnt["s"]
    p4_t = P.cnt["t"]

    sc_p4.close()
    sc_ctx.close()
    sc_x.close()

    sc_r = contextlib.ExitStack()
    routed = [sc_r.enter_context(nc.sbuf_tensor(f"rt{c}", [128, TPC], BF16))
              for c in range(NDC)]
    sc_ffn = contextlib.ExitStack()
    we1f = sc_ffn.enter_context(nc.sbuf_tensor("we1f", [128, 8 * FF], BF16))
    we3f = sc_ffn.enter_context(nc.sbuf_tensor("we3f", [128, 8 * FF], BF16))
    we2f = sc_ffn.enter_context(nc.sbuf_tensor("we2f", [128, NFC * DM], BF16))
    h_sb = sc_ffn.enter_context(nc.sbuf_tensor("h_sb", [128, NFC * QT], BF16))
    sa_sb = [sc_ffn.enter_context(nc.sbuf_tensor(f"sa{p}", [128, QT], BF16))
             for p in range(2)]
    h1_sb = [sc_ffn.enter_context(nc.sbuf_tensor(f"h1{p}", [128, QT], F32))
             for p in range(2)]
    wbb = sc_ffn.enter_context(nc.sbuf_tensor("wbb", [128, QT], BF16))

    FW = {}
    ab_last, o_last, o_racc_last = {}, {}, {}
    ffn_s_last, ffn_h1_last = 0, 0
    space_guard = [("v", p4_v), ("t", p4_t), ("s", p4_s)]
    stg_dma = {}
    for ei in range(NE + 1):
        e = ei - 1
        s1d = ws1_d[:, :] if e < 0 else we1_d[e, :, :]
        s3d = ws3_d[:, :] if e < 0 else we3_d[e, :, :]
        s2d = ws2_d[:, :] if e < 0 else we2_d[e, :, :]
        FW[(ei, "w1")] = P.emit("g", lambda g, s1d=s1d: g.dma_start(
            we1f[:, :].rearrange("p (c n) -> p c n", c=8), chunked(s1d)),
            waits=[("t", ab_last.get(ei - 1, 0))] +
                  (space_guard if ei == 0 else []), sem="g_we13")
        FW[(ei, "w3")] = P.emit("g", lambda g, s3d=s3d: g.dma_start(
            we3f[:, :].rearrange("p (c n) -> p c n", c=8), chunked(s3d)),
            sem="g_we13")
        FW[(ei, "w2")] = P.emit("g", lambda g, s2d=s2d: g.dma_start(
            we2f[:, :].rearrange("p (c n) -> p c n", c=16),
            s2d.rearrange("(c p) n -> p c n", p=128)),
            waits=[("t", o_last.get(ei - 1, 0))], sem="g_we2")
        ht, ot, sls = {}, {}, {}
        for tt in range(NQT):
            if e >= 0:
                sd = P.emit("g", lambda g, e=e, tt=tt: g.dma_start(
                    wstg[0:1, (tt % 2) * 512:(tt % 2) * 512 + 512],
                    wr8[e:e + 1, tt * QT:(tt + 1) * QT]),
                    waits=[("g_wr8", wr8_done),
                           ("t", stg_dma.get(tt % 2, 0))],
                    sem=f"g_st{tt % 2}")
                bw = P.emit("t", lambda t_, tt=tt: t_.matmul(
                    ps_c[0][:, 0:QT], ones1_b[:, :],
                    wstg[0:1, (tt % 2) * 512:(tt % 2) * 512 + 512],
                    start=True, stop=True),
                    waits=[(f"g_st{tt % 2}", sd),
                           ("v", ot.get((tt - 1, "racc"),
                                        o_racc_last.get(ei - 1, 0)))])
                stg_dma[tt % 2] = bw
                bws = P.emit("s", lambda s: s.copy(wbb[:, :],
                                                   ps_c[0][:, 0:QT]),
                             waits=[("t", bw)])
            else:
                bws = 0
            for fc in range(NFC):
                par = fc % 2

                def abf(t_, fc=fc, par=par, tt=tt):
                    for di in range(NDC):
                        t_.matmul(ps_s[par][:, 0:QT],
                                  we1f[:, di * FF + fc * 128:
                                       di * FF + (fc + 1) * 128],
                                  y1[di][:, tt * QT:(tt + 1) * QT],
                                  start=(di == 0), stop=(di == NDC - 1))
                    for di in range(NDC):
                        ii = t_.matmul(ps_s[par][:, QT:2 * QT],
                                       we3f[:, di * FF + fc * 128:
                                            di * FF + (fc + 1) * 128],
                                       y1[di][:, tt * QT:(tt + 1) * QT],
                                       start=(di == 0), stop=(di == NDC - 1))
                    return ii
                w = [("g_we13", FW[(ei, "w3")]),
                     ("v", ht.get((tt, fc - 2), 0)),
                     ("s", sls.get((tt, fc - 2), 0))]
                ab = P.emit("t", abf, waits=w)
                sls[(tt, fc)] = sl = P.emit(
                    "s", lambda s, fc=fc, par=par: s.activation(
                        sa_sb[fc % 2][:, :], ps_s[par][:, 0:QT], AF.Silu),
                    waits=[("t", ab)])
                ffn_s_last = sl
                h1 = P.emit("v", lambda v, fc=fc, par=par: v.tensor_tensor(
                    h1_sb[fc % 2][:, :], sa_sb[fc % 2][:, :],
                    ps_s[par][:, QT:2 * QT], ALU.mult), waits=[("s", sl)])
                ffn_h1_last = h1
                wh = [("v", h1), ("t", ot.get((tt - 1, "last"),
                                              o_last.get(ei - 1, 0)))]
                if e >= 0:
                    ht[(tt, fc)] = P.emit(
                        "v", lambda v, fc=fc: v.tensor_tensor(
                            h_sb[:, fc * QT:(fc + 1) * QT],
                            h1_sb[fc % 2][:, :], wbb[:, :], ALU.mult),
                        waits=wh + [("s", bws)])
                else:
                    ht[(tt, fc)] = P.emit(
                        "v", lambda v, fc=fc: v.tensor_copy(
                            h_sb[:, fc * QT:(fc + 1) * QT],
                            h1_sb[fc % 2][:, :]), waits=wh)
            if tt == NQT - 1:
                ab_last[ei] = P.cnt["t"]
            def omh(t_, dc, lo, hi):
                for fc in range(lo, hi):
                    ii = t_.matmul(ps_c[dc % 2][:, 0:QT],
                                   we2f[:, fc * DM + dc * 128:
                                        fc * DM + (dc + 1) * 128],
                                   h_sb[:, fc * QT:(fc + 1) * QT],
                                   start=(fc == 0), stop=(fc == NFC - 1))
                return ii
            for dc in (0, 1):
                w = [("v", ht[(tt, NFC // 2 - 1)]), ("g_we2", FW[(ei, "w2")]),
                     ("v", ot.get((tt - 1, "racc"),
                                  o_racc_last.get(ei - 1, 0)))]
                if dc == 0 and e >= 0:
                    w.append(("s", bws))
                P.emit("t", lambda t_, dc=dc: omh(t_, dc, 0, NFC // 2),
                       waits=w)
            for dc in range(NDC):
                if dc < 2:
                    ot[(tt, dc, "t")] = P.emit(
                        "t", lambda t_, dc=dc: omh(t_, dc, NFC // 2, NFC),
                        waits=[("v", ht[(tt, NFC - 1)])])
                else:
                    ot[(tt, dc, "t")] = P.emit(
                        "t", lambda t_, dc=dc: omh(t_, dc, 0, NFC),
                        waits=[("v", ht[(tt, NFC - 1)]),
                               ("v", ot.get((tt, dc - 2, "racc"), 0))])
                if ei == 0:
                    ot[(tt, dc, "racc")] = P.emit(
                        "v", lambda v, dc=dc, par=dc % 2, tt=tt: v.tensor_copy(
                            routed[dc][:, tt * QT:(tt + 1) * QT],
                            ps_c[par][:, 0:QT]),
                        waits=[("t", ot[(tt, dc, "t")])])
                else:
                    ot[(tt, dc, "racc")] = P.emit(
                        "v", lambda v, dc=dc, par=dc % 2, tt=tt:
                        v.tensor_tensor(
                            routed[dc][:, tt * QT:(tt + 1) * QT],
                            routed[dc][:, tt * QT:(tt + 1) * QT],
                            ps_c[par][:, 0:QT], ALU.add),
                        waits=[("t", ot[(tt, dc, "t")])])
            ot[(tt, "last")] = ot[(tt, NDC - 1, "t")]
            ot[(tt, "racc")] = ot[(tt, NDC - 1, "racc")]
        o_last[ei] = ot[(NQT - 1, "last")]
        o_racc_last[ei] = ot[(NQT - 1, "racc")]
    ffn_done_v = o_racc_last[NE]
    p5_t = P.cnt["t"]

    sc_ffn.close()

    sc_p6 = contextlib.ExitStack()
    zbuf_b = sc_p6.enter_context(nc.sbuf_tensor("zbuf_b", [128, 8 * QT], F32))
    mb_b = sc_p6.enter_context(nc.sbuf_tensor("mb_b", [128, QT], F32))
    ivb_b = sc_p6.enter_context(nc.sbuf_tensor("ivb_b", [128, QT], F32))
    rows_b = sc_p6.enter_context(nc.sbuf_tensor("rows_b", [1, 2048], F32))
    zbp_b = [sc_p6.enter_context(nc.sbuf_tensor(f"zbp_b{p}", [128, QT], BF16))
             for p in range(2)]
    zsp_b = [sc_p6.enter_context(nc.sbuf_tensor(f"zsp_b{p}", [128, QT], BF16))
             for p in range(2)]
    stage_b = [sc_p6.enter_context(nc.sbuf_tensor(f"ost{p}", [128, DM], F32))
               for p in range(2)]

    st2 = {"src_v": ffn_done_v, "psc_guard_v": ffn_done_v,
           "pss_guard_s": ffn_s_last, "pss_guard_v": ffn_h1_last,
           "prev_ttr": p5_t, "routed": routed}
    ST = {}
    prev_st = {0: 0, 1: 0}
    for qc in range(NQT):
        L2 = ln_phase(qc, "ffn", None, st2, zbuf_b, mb_b, ivb_b, rows_b,
                      zbp_b, zsp_b)
        last_ttr = 0
        for tt in range(4):
            par = tt % 2
            tok = qc * QT + tt * 128

            def trf(t_, tt=tt):
                for dc in range(NDC):
                    ii = t_.transpose(
                        ps_c[dc // 4][:, (dc % 4) * 128:(dc % 4 + 1) * 128],
                        zbuf_b[:, dc * QT + tt * 128:dc * QT + (tt + 1) * 128],
                        eye_f[:, :])
                return ii
            ttr = P.emit("t", trf, waits=[("v", L2["fin"]),
                                          ("v", ST.get((qc, tt - 1, "ev"), 0)),
                                          ("v", L2["r1"] if tt == 0 else 0)])
            last_ttr = ttr
            ST[(qc, tt, "ev")] = P.emit("v", lambda v, par=par: [
                v.tensor_copy(stage_b[par][:, dc * 128:(dc + 1) * 128],
                              ps_c[dc // 4][:, (dc % 4) * 128:
                                            (dc % 4 + 1) * 128])
                for dc in range(NDC)][-1],
                waits=[("t", ttr), (f"y_st{par}", prev_st[par])])
            ST[(qc, tt, "st")] = P.emit(
                "y", lambda y, par=par, tok=tok: y.dma_start(
                    out_d[tok:tok + 128, :], stage_b[par][:, :]),
                waits=[("v", ST[(qc, tt, "ev")])], sem=f"y_st{par}")
            prev_st[par] = ST[(qc, tt, "st")]
        st2["prev_ttr"] = last_ttr
        st2["psc_guard_v"] = ST[(qc, 3, "ev")]
    P.emit("y", lambda y: y.nop(),
           waits=[("y_st0", ST[(NQT - 1, 2, "st")]),
                  ("y_st1", ST[(NQT - 1, 3, "st")])], inc=False)

    P.finalize(nc, sems)
    sc_p6.close()
    sc_r.close()
    stk.close()
    return nc


def _host_inputs(inputs):
    x = np.ascontiguousarray(np.asarray(inputs["L0_attn_hidden_states"],
                                        np.float32))
    up = np.asarray(inputs["L0_attn_up"], np.float32).reshape(-1)
    uz = np.asarray(inputs["L0_attn_uz"], np.float32).reshape(-1)
    upz = np.zeros((128, 512), np.float32)
    r = np.arange(128)
    for dc in range(8):
        h = 2 * dc + r // 64
        upz[r, dc * 64 + h] = up[dc * 128 + r]
        upz[r, dc * 64 + 32 + h] = uz[dc * 128 + r]
    wg = np.asarray(inputs["L0_ffn_Wg"], np.float32)
    wg_m = np.ascontiguousarray(wg.reshape(8, 128, 8).transpose(1, 0, 2)
                                .reshape(128, 64))

    def colmat(name):
        a = np.asarray(inputs[name], np.float32).reshape(-1)
        return np.ascontiguousarray(a.reshape(8, 128).T)

    def arr(name):
        return np.ascontiguousarray(np.asarray(inputs[name], np.float32))

    shared = {
        "eye": np.eye(128, dtype=np.float32),
        "wq": arr("L0_attn_wq"), "wk": arr("L0_attn_wk"),
        "wv": arr("L0_attn_wv"), "wo": arr("L0_attn_wo"),
        "wfq": arr("L0_attn_wfq"), "wfg": arr("L0_attn_wfg"),
        "bq_m": colmat("L0_attn_bq"), "bk_m": colmat("L0_attn_bk"),
        "bo_m": colmat("L0_attn_bo"), "bfq_m": colmat("L0_attn_bfq"),
        "bfg_m": colmat("L0_attn_bfg"),
        "bv_m": colmat("L0_attn_bv"),
        "upz_m": upz, "wg_m": wg_m,
        "ws1": arr("L0_ffn_Ws1"), "ws3": arr("L0_ffn_Ws3"),
        "ws2": arr("L0_ffn_Ws2"), "we1": arr("L0_ffn_We1"),
        "we3": arr("L0_ffn_We3"), "we2": arr("L0_ffn_We2"),
    }
    in_maps = []
    for c in range(8):
        b, off = c // 2, (c % 2) * TPC
        m = dict(shared)
        m["xr"] = np.ascontiguousarray(
            np.concatenate([x[b, off:], x[b, :off]], axis=0))
        m["jv"] = np.ascontiguousarray(
            np.concatenate([np.arange(off, S), np.arange(0, off)])
            .astype(np.float32).reshape(1, S))
        in_maps.append(m)
    return in_maps


_NC_CACHE = {}


def kernel(**inputs):
    if "nc" not in _NC_CACHE:
        _NC_CACHE["nc"] = build_full()
    nc = _NC_CACHE["nc"]
    in_maps = _host_inputs(inputs)
    res = run_bass_kernel_spmd(nc, in_maps, core_ids=list(range(8)))
    out = np.empty((B, S, DM), np.float32)
    for c in range(8):
        b, off = c // 2, (c % 2) * TPC
        out[b, off:off + TPC] = res.results[c]["out"]
    return out


# version 08
# speedup vs baseline: 1.3809x; 1.0099x over previous
import contextlib
import numpy as np
import concourse.bass as bass
import concourse.mybir as mybir
from concourse.bass_utils import run_bass_kernel_spmd

F32 = mybir.dt.float32
BF16 = mybir.dt.bfloat16
AF = mybir.ActivationFunctionType
ALU = mybir.AluOpType
AX = mybir.AxisListType

B, S, DM, H, D = 4, 4096, 1024, 16, 64
TPC, FF, NE, QT = 2048, 2048, 8, 512
NQT, NKC, NDC, NFC, NXT = TPC // QT, S // 128, DM // 128, FF // 128, S // 128

ENGS = ("t", "s", "v", "g", "y")


class Prog:

    def __init__(self):
        self.recs = {e: [] for e in ENGS}
        self.cnt = {}

    def emit(self, eng, fn, waits=(), inc=True, sem=None):
        semn = sem or eng
        inc_k = 16 if eng in ("g", "y") else 1
        ws = [(e, v) for (e, v) in waits if v and v > 0]
        self.recs[eng].append((fn, ws, semn if inc else None, inc_k))
        if inc:
            self.cnt[semn] = self.cnt.get(semn, 0) + inc_k
        return self.cnt.get(semn, 0)

    def finalize(self, nc, sems):
        def run(eng_obj, name):
            lw = {}
            for fn, ws, semn, inc_k in self.recs[name]:
                for se, val in ws:
                    if val > lw.get(se, 0):
                        eng_obj.wait_ge(sems[se], val)
                        lw[se] = val
                inst = fn(eng_obj)
                if semn is not None:
                    inst.then_inc(sems[semn], inc_k)

        with nc.Block() as block:
            block.tensor(lambda e: run(e, "t"))
            block.scalar(lambda e: run(e, "s"))
            block.vector(lambda e: run(e, "v"))
            block.gpsimd(lambda e: run(e, "g"))
            block.sync(lambda e: run(e, "y"))


def build_full():
    nc = bass.Bass("TRN2", target_bir_lowering=False)

    dt_ = nc.dram_tensor
    xr = dt_("xr", [S, DM], F32, kind="ExternalInput")
    jv_d = dt_("jv", [1, S], F32, kind="ExternalInput")
    eye_d = dt_("eye", [128, 128], F32, kind="ExternalInput")
    wq_d = dt_("wq", [DM, DM], F32, kind="ExternalInput")
    wk_d = dt_("wk", [DM, DM], F32, kind="ExternalInput")
    wv_d = dt_("wv", [DM, DM], F32, kind="ExternalInput")
    wo_d = dt_("wo", [DM, DM], F32, kind="ExternalInput")
    wfq_d = dt_("wfq", [DM, DM], F32, kind="ExternalInput")
    wfg_d = dt_("wfg", [DM, DM], F32, kind="ExternalInput")
    bq_d = dt_("bq_m", [128, 8], F32, kind="ExternalInput")
    bk_d = dt_("bk_m", [128, 8], F32, kind="ExternalInput")
    bo_d = dt_("bo_m", [128, 8], F32, kind="ExternalInput")
    bfq_d = dt_("bfq_m", [128, 8], F32, kind="ExternalInput")
    bfg_d = dt_("bfg_m", [128, 8], F32, kind="ExternalInput")
    bv_d = dt_("bv_m", [128, 8], F32, kind="ExternalInput")
    bvr_d = dt_("bv_r", [1, DM], F32, kind="ExternalInput")
    upz_d = dt_("upz_m", [128, 512], F32, kind="ExternalInput")
    wg_d = dt_("wg_m", [128, 64], F32, kind="ExternalInput")
    ws1_d = dt_("ws1", [DM, FF], F32, kind="ExternalInput")
    ws3_d = dt_("ws3", [DM, FF], F32, kind="ExternalInput")
    ws2_d = dt_("ws2", [FF, DM], F32, kind="ExternalInput")
    we1_d = dt_("we1", [NE, DM, FF], F32, kind="ExternalInput")
    we3_d = dt_("we3", [NE, DM, FF], F32, kind="ExternalInput")
    we2_d = dt_("we2", [NE, FF, DM], F32, kind="ExternalInput")
    out_d = dt_("out", [TPC, DM], F32, kind="ExternalOutput")
    rbd = dt_("rbd", [1, 2 * QT], F32, kind="Internal")

    al = nc.alloc_sbuf_tensor
    eye_f = al("eye_f", [128, 128], F32)
    eye_b = al("eye_b", [128, 128], BF16)
    ones1_b = al("ones1_b", [1, 128], BF16)
    ones128_b = al("ones128_b", [128, 1], BF16)
    cpool = al("cpool", [128, 2], F32)
    bq_sb = al("bq_sb", [128, 8], F32)
    bk_sb = al("bk_sb", [128, 8], F32)
    bo_sb = al("bo_sb", [128, 8], F32)
    bfq_sb = al("bfq_sb", [128, 8], F32)
    bfg_sb = al("bfg_sb", [128, 8], F32)
    cbias = al("cbias", [128, 8], F32)
    bv_sb = al("bv_sb", [128, 8], F32)
    wg_b = al("wg_b", [128, 64], BF16)
    xm_sb = al("xm_sb", [128, 8], F32)
    xm_b = al("xm_b", [128, 8], BF16)
    glo_sb = al("glo_sb", [128, 8], F32)
    glo_b = al("glo_b", [128, 8], BF16)
    rowsF = al("rowsF", [1, 1024], F32)
    recr_f = [rowsF[0:1, 0:512], rowsF[0:1, 512:1024]]
    rowsB = al("rowsB", [1, 2048], BF16)
    mrow_b, irow_b = rowsB[0:1, 0:512], rowsB[0:1, 512:1024]
    recr = [rowsB[0:1, 1024:1536], rowsB[0:1, 1536:2048]]
    gsig = al("gsig", [128, 8], F32)
    gt = al("gt", [128, 40], F32)
    wtok = al("wtok", [128, 8], F32)
    wr8 = al("wr8", [8, TPC], BF16)
    wstg = al("wstg", [1, 1024], BF16)
    mq = [al(f"mq{c}", [128, TPC], BF16) for c in range(NDC)]
    y1 = mq

    ap_ = nc.alloc_psum_tensor
    ps_s = [ap_(f"ps_s{p}", [128, 2 * QT], F32) for p in range(2)]
    ps_c = [ap_(f"ps_c{i}", [128, QT], F32) for i in range(2)]
    sc_ptr = contextlib.ExitStack()
    ps_tr = [sc_ptr.enter_context(nc.psum_tensor(f"ps_tr{p}", [128, 1024],
                                                 BF16)) for p in range(2)]

    stk = contextlib.ExitStack()
    STREAMS = ["t", "s", "v", "y_jv", "y_eye", "y_x0", "y_x1", "y_st0",
               "y_st1", "g_sm", "g_j", "g_wb", "g_upz", "g_bv", "g_kv0",
               "g_kv1", "g_c30", "g_c31", "g_wr8", "g_st0", "g_st1",
               "g_we13", "g_we2", "y_rb", "y_bv", "g_wv"]
    sems = {e: stk.enter_context(nc.semaphore(f"sem_{e}")) for e in STREAMS}
    P = Prog()
    nc.const_aps.register(F32, 1e-5, cpool[:, 0:1])

    def chunked(dr, c=8):
        return dr.rearrange("(c p) n -> p c n", p=128)

    def wsl(buf, di, dc):
        return buf[:, di * DM + dc * 128: di * DM + (dc + 1) * 128]

    sc_x = contextlib.ExitStack()
    x_fm = [sc_x.enter_context(nc.sbuf_tensor(f"x_fm{c}", [128, S], BF16))
            for c in range(NDC)]
    sc_ctx = contextlib.ExitStack()
    ctx_fm = [sc_ctx.enter_context(nc.sbuf_tensor(f"ctx{c}", [128, TPC], BF16))
              for c in range(NDC)]
    sc_kaug = contextlib.ExitStack()
    k_augA = sc_kaug.enter_context(nc.sbuf_tensor("k_augA", [128, S], BF16))
    k_augB = sc_kaug.enter_context(nc.sbuf_tensor("k_augB", [128, S], BF16))

    sc_p0 = contextlib.ExitStack()
    poolF = sc_p0.enter_context(nc.sbuf_tensor("poolF", [1, S], F32))
    poolB = sc_p0.enter_context(nc.sbuf_tensor("poolB", [65, S], BF16))
    stage_a = [sc_p0.enter_context(nc.sbuf_tensor(f"stg{p}", [128, DM], F32))
               for p in range(2)]
    xb_t = [sc_p0.enter_context(nc.sbuf_tensor(f"xb{p}", [128, DM], BF16))
            for p in range(2)]
    jv_sb = poolF[0:1, :]
    jv_b = poolB[0:1, :]
    jsq_b = poolB[32:33, :]
    onesrow_b = poolB[64:65, :]

    ld_jv = P.emit("y", lambda y: y.dma_start(jv_sb, jv_d[:, :]), sem="y_jv")
    ld_eye = P.emit("y", lambda y: y.dma_start(eye_f[:, :], eye_d[:, :]), sem="y_eye")
    gw_small = 0
    for dst, src in [
        (eye_b[:, :], eye_d[:, :]), (bq_sb[:, :], bq_d[:, :]),
        (bk_sb[:, :], bk_d[:, :]), (bo_sb[:, :], bo_d[:, :]),
        (bfq_sb[:, :], bfq_d[:, :]), (bfg_sb[:, :], bfg_d[:, :]),
        (bv_sb[:, :], bv_d[:, :]), (wg_b[:, :], wg_d[:, :]),
    ]:
        gw_small = P.emit("g", lambda g, d=dst, s=src: g.dma_start(d, s), sem="g_sm")

    def memsets(v):
        v.memset(cpool[:, 0:1], 1e-5)
        v.memset(ones1_b[:, :], 1.0)
        v.memset(ones128_b[:, :], 1.0)
        return v.memset(onesrow_b, 1.0)
    v_init = P.emit("v", memsets)
    v_jsq = P.emit("v", lambda v: v.tensor_tensor(jsq_b, jv_sb, jv_sb,
                                                  ALU.mult),
                   waits=[("y_jv", ld_jv)])
    v_jb = P.emit("v", lambda v: v.tensor_copy(jv_b, jv_sb),
                  waits=[("v", v_jsq)])
    gw_j = 0
    for ka in (k_augA, k_augB):
        P.emit("g", lambda g, ka=ka: g.dma_start(ka[64:65, :], jsq_b),
               waits=[("v", v_jb), ("v", v_init)], sem="g_j")
        P.emit("g", lambda g, ka=ka: g.dma_start(ka[65:66, :], jv_b),
               sem="g_j")
        gw_j = P.emit("g", lambda g, ka=ka: g.dma_start(ka[66:67, :],
                                                        onesrow_b), sem="g_j")

    x_ld, x_cv, x_tr, x_ev = {}, {}, {}, {}
    for t in range(NXT):
        par = t % 2
        x_ld[t] = P.emit("y", lambda y, t=t, par=par: y.dma_start(
            stage_a[par][:, :], xr[t * 128:(t + 1) * 128, :]),
            waits=[("v", x_cv.get(t - 2, 0))], sem=f"y_x{par}")
        x_cv[t] = P.emit("v", lambda v, par=par: v.tensor_copy(
            xb_t[par][:, :], stage_a[par][:, :]),
            waits=[(f"y_x{par}", x_ld[t]), ("t", x_tr.get(t - 2, 0)),
                   ("v", v_init)])

        def trf(t_, par=par):
            for dc in range(NDC):
                ii = t_.transpose(ps_tr[par][:, dc * 128:(dc + 1) * 128],
                                  xb_t[par][:, dc * 128:(dc + 1) * 128],
                                  eye_b[:, :])
            return ii
        x_tr[t] = P.emit("t", trf, waits=[("v", x_cv[t]),
                                          ("g_sm", gw_small),
                                          ("v", x_ev.get(t - 2, 0))])

        def evf(v, t=t, par=par):
            for dc in range(NDC):
                ii = v.tensor_copy(x_fm[dc][:, t * 128:(t + 1) * 128],
                                   ps_tr[par][:, dc * 128:(dc + 1) * 128])
            return ii
        x_ev[t] = P.emit("v", evf, waits=[("t", x_tr[t])])
    x_done = x_ev[NXT - 1]
    xtr_last = x_tr[NXT - 1]

    sc_p0.close()
    sc_ptr.close()
    ps_d = nc.alloc_psum_tensor("ps_d", [128, 2 * QT], F32)

    sc_c3 = contextlib.ExitStack()
    C3 = sc_c3.enter_context(nc.sbuf_tensor("C3", [80, TPC], BF16))
    sc_w = contextlib.ExitStack()
    w_big1 = sc_w.enter_context(nc.sbuf_tensor("w_big1", [128, 8 * DM], BF16))
    w_big2 = w_big1

    gw_wq = P.emit("g", lambda g: g.dma_start(
        w_big1[:, :].rearrange("p (c n) -> p c n", c=8), chunked(wq_d[:, :])),
        waits=[("t", xtr_last), ("v", x_done), ("g_j", gw_j)], sem="g_wb")

    mq_mm, mq_ev = {}, {}
    for qc in range(NQT):
        for dc in range(NDC):
            i = qc * NDC + dc
            par = i % 2

            def mmf(t_, qc=qc, dc=dc, par=par):
                for di in range(NDC):
                    ii = t_.matmul(ps_s[par][:, 0:QT], wsl(w_big1, di, dc),
                                   x_fm[di][:, qc * QT:(qc + 1) * QT],
                                   start=(di == 0), stop=(di == NDC - 1))
                return ii
            mq_mm[i] = P.emit("t", mmf, waits=[("v", x_done), ("g_wb", gw_wq),
                                               ("v", mq_ev.get(i - 2, 0))])
            mq_ev[i] = P.emit("v", lambda v, qc=qc, dc=dc, par=par:
                              v.tensor_scalar(mq[dc][:, qc * QT:(qc + 1) * QT],
                                              ps_s[par][:, 0:QT],
                                              bq_sb[:, dc:dc + 1], None,
                                              ALU.add),
                              waits=[("t", mq_mm[i])])
    mq_done = mq_ev[NQT * NDC - 1]

    def xmeanf(v):
        for dc in range(NDC):
            ii = v.reduce_sum(xm_sb[:, dc:dc + 1], x_fm[dc][:, :], axis=AX.X)
        return ii
    v_xm = P.emit("v", xmeanf, waits=[("v", x_done)])
    v_xmb = P.emit("v", lambda v: v.tensor_scalar(
        xm_b[:, :], xm_sb[:, :], 1.0 / S, None, ALU.mult), waits=[("v", v_xm)])

    def glof(t_):
        for dc in range(NDC):
            for di in range(NDC):
                ii = t_.matmul(ps_c[0][:, dc:dc + 1], wsl(w_big1, di, dc),
                               xm_b[:, di:di + 1], start=(di == 0),
                               stop=(di == NDC - 1))
        return ii
    t_glo = P.emit("t", glof, waits=[("v", v_xmb), ("g_wb", gw_wq)])
    v_glo = P.emit("v", lambda v: v.tensor_tensor(
        glo_sb[:, :], ps_c[0][:, 0:8], bq_sb[:, :], ALU.add),
        waits=[("t", t_glo)])
    v_glob = P.emit("v", lambda v: v.tensor_copy(glo_b[:, :], glo_sb[:, :]),
                    waits=[("v", v_glo)])

    def gff(t_):
        for dc in range(NDC):
            for di in range(NDC):
                ii = t_.matmul(ps_c[1][:, dc:dc + 1], wsl(w_big2, di, dc),
                               glo_b[:, di:di + 1], start=(di == 0),
                               stop=(di == NDC - 1))
        return ii
    gw_wfg = P.emit("g", lambda g: g.dma_start(
        w_big1[:, :].rearrange("p (c n) -> p c n", c=8), chunked(wfg_d[:, :])),
        waits=[("t", t_glo), ("t", mq_mm[NQT * NDC - 1])], sem="g_wb")
    t_gf = P.emit("t", gff, waits=[("v", v_glob), ("g_wb", gw_wfg)])
    v_cb1 = P.emit("v", lambda v: v.tensor_tensor(
        cbias[:, :], ps_c[1][:, 0:8], bfq_sb[:, :], ALU.add),
        waits=[("t", t_gf)])
    v_cb = P.emit("v", lambda v: v.tensor_tensor(
        cbias[:, :], cbias[:, :], bfg_sb[:, :], ALU.add), waits=[("v", v_cb1)])

    gw_wfq = P.emit("g", lambda g: g.dma_start(
        w_big2[:, :].rearrange("p (c n) -> p c n", c=8), chunked(wfq_d[:, :])),
        waits=[("t", t_gf)], sem="g_wb")

    sc_p2 = contextlib.ExitStack()
    upz_b = sc_p2.enter_context(nc.sbuf_tensor("upz_b", [128, 512], BF16))
    gw_upz = P.emit("g", lambda g: g.dma_start(upz_b[:, :], upz_d[:, :]),
                    waits=[("t", xtr_last), ("v", x_done), ("g_j", gw_j)],
                    sem="g_upz")
    spz = sc_p2.enter_context(nc.sbuf_tensor("spz", [128, 4096], F32))
    Sp, Sz = spz[0:16, 0:TPC], spz[32:48, 0:TPC]
    cA, cB = spz[64:80, 0:TPC], spz[64:80, TPC:2 * TPC]
    ctp = sc_p2.enter_context(nc.sbuf_tensor("ctp", [128, 1024], BF16))
    c_t_sb = [ctp[:, 0:QT], ctp[:, QT:2 * QT]]

    c_mm, c_tanh, pz_mm, sig = {}, {}, {}, {}
    for qc in range(NQT):
        for dc in range(NDC):
            i = qc * NDC + dc
            par = i % 2

            def cmf(t_, qc=qc, dc=dc, par=par):
                for di in range(NDC):
                    ii = t_.matmul(ps_s[par][:, 0:QT], wsl(w_big2, di, dc),
                                   mq[di][:, qc * QT:(qc + 1) * QT],
                                   start=(di == 0), stop=(di == NDC - 1))
                return ii
            c_mm[i] = P.emit("t", cmf,
                             waits=[("v", v_cb), ("g_wb", gw_wfq), ("v", mq_done),
                                    ("s", c_tanh.get(i - 2, 0))])
            c_tanh[i] = P.emit("s", lambda s, dc=dc, par=par: s.activation(
                c_t_sb[par], ps_s[par][:, 0:QT], AF.Tanh,
                bias=cbias[:, dc:dc + 1]), waits=[("t", c_mm[i])])
            pz_mm[i] = P.emit("t", lambda t_, dc=dc, par=par: t_.matmul(
                ps_c[0][0:64, 0:QT], upz_b[:, dc * 64:(dc + 1) * 64],
                c_t_sb[par], start=(dc == 0), stop=(dc == NDC - 1)),
                waits=[("s", c_tanh[i]), ("s", sig.get(qc - 1, 0)),
                       ("v", v_glo if i == 0 else 0),
                       ("g_upz", gw_upz if i == 0 else 0)])
        sig[qc] = P.emit("s", lambda s, qc=qc: (
            s.activation(Sp[:, qc * QT:(qc + 1) * QT], ps_c[0][0:16, 0:QT],
                         AF.Sigmoid),
            s.activation(Sz[:, qc * QT:(qc + 1) * QT], ps_c[0][32:48, 0:QT],
                         AF.Sigmoid))[-1],
            waits=[("t", pz_mm[qc * NDC + NDC - 1])])
    sig_done = sig[NQT - 1]
    pz_last = pz_mm[NQT * NDC - 1]

    steps = [
        lambda v: v.reciprocal(cA, Sz),
        lambda v: v.tensor_tensor(cA, cA, cA, ALU.mult),
        lambda v: v.tensor_scalar(cA, cA, -1.1920928955078125e-07,
                                  None, ALU.mult),
        lambda v: v.tensor_scalar(cB, Sp, float(S), None, ALU.mult),
        lambda v: v.tensor_copy(C3[0:16, 0:TPC], cA),
        lambda v: v.tensor_tensor(cA, cA, cB, ALU.mult),
        lambda v: v.tensor_scalar(C3[32:48, 0:TPC], cA, -2.0, None,
                                  ALU.mult),
        lambda v: v.tensor_tensor(cA, cA, cB, ALU.mult),
        lambda v: v.tensor_scalar(C3[64:80, 0:TPC], cA, -4.0, None,
                                  ALU.add),
    ]
    vprev = P.emit("v", steps[0], waits=[("s", sig_done)])
    for f in steps[1:]:
        vprev = P.emit("v", f, waits=[("v", vprev)])
    coeff_done = vprev

    sc_p2.close()
    sc_w.close()

    sc_att2 = contextlib.ExitStack()
    v_augs = [sc_att2.enter_context(nc.sbuf_tensor(f"v_aug{i}",
                                                   [128, NKC * 130], BF16))
              for i in range(2)]
    kv_w = [sc_att2.enter_context(nc.sbuf_tensor(f"kv{p}", [128, 1024], BF16))
            for p in range(2)]
    wv2 = sc_att2.enter_context(nc.sbuf_tensor("wv2", [128, 2048], BF16))
    e_t = [sc_att2.enter_context(nc.sbuf_tensor(f"e_t{p}", [128, 2 * QT],
                                                BF16)) for p in range(3)]
    rhs_aug = [sc_att2.enter_context(nc.sbuf_tensor(f"rhs{p}", [128, QT],
                                                    BF16)) for p in range(4)]
    rbp = sc_att2.enter_context(nc.sbuf_tensor("rbp", [128, QT], F32))
    rb_sb = [rbp[0:64, :], rbp[64:128, :]]
    bvb2 = sc_att2.enter_context(nc.sbuf_tensor("bvb2", [128, 256], F32))

    def vones(v):
        for va_t in v_augs:
            va = va_t[:, :].rearrange("p (t c) -> p t c", c=130)
            v.memset(va[:, :, 64:65], 1.0)
            ii = v.memset(va[:, :, 129:130], 1.0)
        return ii
    v_vones = P.emit("v", vones, waits=[("t", pz_last), ("v", coeff_done)])

    SB3 = [ps_s[0], ps_s[1], ps_d]

    AT = {}
    pair_t_end = {}
    for p in range(NE):
        wpar = p % 2
        cols = slice(p * 128, (p + 1) * 128)
        kv_gw = P.emit("g", lambda g, wpar=wpar, cols=cols: g.dma_start(
            kv_w[wpar][:, :].rearrange("p (c n) -> p c n", c=8),
            chunked(wk_d[:, cols])),
            waits=[("t", pair_t_end.get(p - 2, pz_last)), ("g_j", gw_j),
                   ("v", coeff_done if p == 0 else 0),
                   ("s", sig_done if p == 0 else 0)], sem=f"g_kv{wpar}")
        if p % 2 == 0:
            cols2 = slice(p * 128, (p + 2) * 128)
            gw_wv = P.emit("g", lambda g, cols2=cols2: g.dma_start(
                wv2[:, :].rearrange("p (c n) -> p c n", c=8),
                chunked(wv_d[:, cols2])),
                waits=[("t", AT.get((p - 2, "v_done"), pz_last))], sem="g_wv")
            bv_dma = P.emit("y", lambda y, p=p: y.dma_start(
                bvb2[:, :], bass.AP(bvr_d, p * 128, [[0, 128], [1, 256]])),
                waits=[("v", AT.get((p - 2, "bv_used"), 0)),
                       ("v", v_init)], sem="y_bv")

        kev = {}
        for kt in range(8):
            par = kt % 2

            def kmf(t_, kt=kt, par=par, wpar=wpar):
                for di in range(NDC):
                    ii = t_.matmul(ps_s[par][:, 0:QT],
                                   kv_w[wpar][:, di * 128:(di + 1) * 128],
                                   x_fm[di][:, kt * QT:(kt + 1) * QT],
                                   start=(di == 0), stop=(di == NDC - 1))
                return ii
            w = [(f"g_kv{wpar}", kv_gw), ("v", kev.get(kt - 2, 0))]
            if kt == 0:
                w += [("s", AT.get((p, 0, -1, "exp"), sig_done)),
                      ("v", AT.get((p - 1, NQT - 1, "ctxe"),
                                   max(coeff_done, mq_done)))]
            tk = P.emit("t", kmf, waits=w)
            kev[kt] = P.emit("v", lambda v, kt=kt, par=par, p=p: (
                v.tensor_scalar(k_augA[0:64, kt * QT:(kt + 1) * QT],
                                ps_s[par][0:64, 0:QT], bk_sb[0:64, p:p + 1],
                                0.125, ALU.add, ALU.mult),
                v.tensor_scalar(k_augB[0:64, kt * QT:(kt + 1) * QT],
                                ps_s[par][64:128, 0:QT], bk_sb[64:128, p:p + 1],
                                0.125, ALU.add, ALU.mult))[-1],
                waits=[("t", tk),
                       ("t", AT.get((p - 1, NQT - 1, "last_sc"), 0)
                        if kt == 0 else 0)])

        if p % 2 == 0:
            vev = {}
            for tt in range(NKC):
                par = tt % 2

                def vmf(t_, tt=tt, par=par):
                    for di in range(NDC):
                        ii = t_.matmul(ps_s[par][:, QT:QT + 256],
                                       x_fm[di][:, tt * 128:(tt + 1) * 128],
                                       wv2[:, di * 256:(di + 1) * 256],
                                       start=(di == 0), stop=(di == NDC - 1))
                    return ii
                w = [("v", vev.get(tt - 2, 0)), ("g_wv", gw_wv)]
                if tt == 0:
                    w += [("t", AT.get((p - 1, NQT - 1, NKC - 1, "ctx"), 0))]
                tv = P.emit("t", vmf, waits=w)
                base = tt * 130

                def vevf(v, base=base, par=par):
                    v.tensor_tensor(v_augs[0][:, base:base + 64],
                                    ps_s[par][:, QT:QT + 64], bvb2[:, 0:64],
                                    ALU.add)
                    v.tensor_tensor(v_augs[0][:, base + 65:base + 129],
                                    ps_s[par][:, QT + 64:QT + 128],
                                    bvb2[:, 64:128], ALU.add)
                    v.tensor_tensor(v_augs[1][:, base:base + 64],
                                    ps_s[par][:, QT + 128:QT + 192],
                                    bvb2[:, 128:192], ALU.add)
                    return v.tensor_tensor(v_augs[1][:, base + 65:base + 129],
                                           ps_s[par][:, QT + 192:QT + 256],
                                           bvb2[:, 192:256], ALU.add)
                vev[tt] = P.emit("v", vevf,
                                 waits=[("t", tv), ("y_bv", bv_dma),
                                        ("v", v_vones if tt == 0 and p == 0
                                         else 0)])
            AT[(p, "v_done")] = P.cnt["t"]
            AT[(p, "bv_used")] = vev[NKC - 1]
            kv_done_v = max(vev[NKC - 1], kev[7])
        else:
            kv_done_v = kev[7]
        v_aug = v_augs[p % 2]

        def emit_rhs(pp, qc):
            for hi in range(2):
                bi = 2 * (qc % 2) + hi
                h = 2 * pp + hi
                guard = AT.get((pp, qc - 2, "last_sc"),
                               AT.get((pp - 1, NQT - 1, "last_sc"), pz_last))
                AT[(pp, qc, hi, "q")] = P.emit(
                    "v", lambda v, bi=bi, hi=hi, pp=pp, qc=qc: v.tensor_copy(
                        rhs_aug[bi][0:64, :],
                        mq[pp][hi * 64:(hi + 1) * 64, qc * QT:(qc + 1) * QT]),
                    waits=[("t", guard)])
                for ri in range(3):
                    AT[(pp, qc, hi, "c3")] = P.emit(
                        "g", lambda g, bi=bi, h=h, qc=qc, ri=ri: g.dma_start(
                            rhs_aug[bi][64 + ri:65 + ri, 0:QT],
                            C3[h + 32 * ri:h + 32 * ri + 1,
                               qc * QT:(qc + 1) * QT]),
                        waits=[("v", AT[(pp, qc, hi, "q")])]
                        if ri == 0 else [], sem=f"g_c3{qc % 2}")

        if p == 0:
            emit_rhs(p, 0)
        emit_rhs(p, 1)
        for qc in range(NQT):
            def emit_ctx(kc):
                def ctf(t_, kc=kc, va=v_aug):
                    base = kc * 130
                    t_.matmul(ps_c[0][0:65, :], va[:, base:base + 65],
                              e_t[kc % 3][:, 0:QT], start=(kc == 0),
                              stop=(kc == NKC - 1))
                    return t_.matmul(ps_c[1][0:65, :],
                                     va[:, base + 65:base + 130],
                                     e_t[kc % 3][:, QT:2 * QT],
                                     start=(kc == 0), stop=(kc == NKC - 1))
                wct = [("s", AT[(p, qc, kc, "exp")])]
                if kc == 0:
                    wct += [("v", AT.get((p, qc - 1, "ctxe"),
                                         AT.get((p - 1, NQT - 1, "ctxe"), 0))),
                            ("v", kv_done_v)]
                AT[(p, qc, kc, "ctx")] = P.emit("t", ctf, waits=wct)

            for kc in range(NKC):
                b3 = SB3[kc % 3]
                wsc = [("s", AT.get((p, qc, kc - 3, "exp"), 0)),
                       (f"g_c3{qc % 2}", AT[(p, qc, 1, "c3")]),
                       ("v", kv_done_v)]
                if kc == 0:
                    wsc += [("g_j", gw_j if p == 0 and qc == 0 else 0)]

                def scf(t_, kc=kc, b3=b3, biA=2 * (qc % 2),
                        biB=2 * (qc % 2) + 1):
                    t_.matmul(b3[:, 0:QT],
                              k_augA[0:67, kc * 128:(kc + 1) * 128],
                              rhs_aug[biA][0:67, :], start=True, stop=True)
                    return t_.matmul(b3[:, QT:2 * QT],
                                     k_augB[0:67, kc * 128:(kc + 1) * 128],
                                     rhs_aug[biB][0:67, :], start=True,
                                     stop=True)
                AT[(p, qc, kc, "sc")] = P.emit("t", scf, waits=wsc)
                AT[(p, qc, kc, "exp")] = P.emit(
                    "s", lambda s, kc=kc, b3=b3: s.activation(
                        e_t[kc % 3][:, :], b3[:, :], AF.Exp),
                    waits=[("t", AT[(p, qc, kc, "sc")]),
                           ("t", AT.get((p, qc, kc - 3, "ctx"), 0))])
                if kc == 2 and qc >= 1:
                    if qc + 1 < NQT:
                        emit_rhs(p, qc + 1)
                    elif p + 1 < NE:
                        emit_rhs(p + 1, 0)
                if kc >= 1:
                    emit_ctx(kc - 1)
            emit_ctx(NKC - 1)
            AT[(p, qc, "last_sc")] = AT[(p, qc, NKC - 1, "sc")]
            AT[(p, qc, "rec")] = P.emit("v", lambda v: (
                v.reciprocal(recr_f[0], ps_c[0][64:65, :]),
                v.reciprocal(recr_f[1], ps_c[1][64:65, :]))[-1],
                waits=[("t", AT[(p, qc, NKC - 1, "ctx")]),
                       ("y_rb", AT.get((p, qc - 1, "rb"),
                                       AT.get((p - 1, NQT - 1, "rb"), 0)))])
            wr_rbd = P.emit("y", lambda y: y.dma_start(
                rbd[:, :], rowsF[0:1, 0:2 * QT]),
                waits=[("v", AT[(p, qc, "rec")])], sem="y_rb")
            P.emit("y", lambda y: y.dma_start(
                rb_sb[0], bass.AP(rbd, 0, [[0, 64], [1, QT]])),
                waits=[("y_rb", wr_rbd)], sem="y_rb")
            AT[(p, qc, "rb")] = P.emit("y", lambda y: y.dma_start(
                rb_sb[1], bass.AP(rbd, QT, [[0, 64], [1, QT]])), sem="y_rb")
            AT[(p, qc, "ctxe")] = P.emit("v", lambda v, p=p, qc=qc: (
                v.tensor_tensor(ctx_fm[p][0:64, qc * QT:(qc + 1) * QT],
                                ps_c[0][0:64, :], rb_sb[0], ALU.mult),
                v.tensor_tensor(ctx_fm[p][64:128, qc * QT:(qc + 1) * QT],
                                ps_c[1][0:64, :], rb_sb[1], ALU.mult))[-1],
                waits=[("y_rb", AT[(p, qc, "rb")])])
            for i in (1, 2, 3):
                AT[(p, qc + 1, -i, "exp")] = AT[(p, qc, NKC - i, "exp")]
                AT[(p, qc + 1, -i, "ctx")] = AT[(p, qc, NKC - i, "ctx")]
                if qc == NQT - 1:
                    AT[(p + 1, 0, -i, "exp")] = AT[(p, qc, NKC - i, "exp")]
                    AT[(p + 1, 0, -i, "ctx")] = AT[(p, qc, NKC - i, "ctx")]
        pair_t_end[p] = P.cnt["t"]
    att_v = AT[(NE - 1, NQT - 1, "ctxe")]
    att_rbs = AT[(NE - 1, NQT - 1, NKC - 1, "exp")]
    att_t = P.cnt["t"]

    sc_att2.close()
    sc_c3.close()
    sc_kaug.close()
    sc_p4 = contextlib.ExitStack()
    wo_b = sc_p4.enter_context(nc.sbuf_tensor("wo_b", [128, 8 * DM], BF16))
    zbuf_a = sc_p4.enter_context(nc.sbuf_tensor("zbuf_a", [128, 8 * QT], F32))
    mb_a = sc_p4.enter_context(nc.sbuf_tensor("mb_a", [128, QT], F32))
    ivb_a = sc_p4.enter_context(nc.sbuf_tensor("ivb_a", [128, QT], F32))
    rows_a = sc_p4.enter_context(nc.sbuf_tensor("rows_a", [1, 2048], F32))
    zbp_a = [sc_p4.enter_context(nc.sbuf_tensor(f"zbp_a{p}", [128, QT], BF16))
             for p in range(2)]
    zsp_a = [sc_p4.enter_context(nc.sbuf_tensor(f"zsp_a{p}", [128, QT], BF16))
             for p in range(2)]

    gw_wo = P.emit("g", lambda g: g.dma_start(
        wo_b[:, :].rearrange("p (c n) -> p c n", c=8), chunked(wo_d[:, :])),
        waits=[("t", att_t), ("s", att_rbs), ("v", att_v)], sem="g_wb")

    def ln_phase(qc, mode, y_dst, st_, zbuf, mb_sb, invb_sb, rows4,
                 zb_t, zsq_t):
        L = {}
        mrow, srow = rows4[0:1, 0:512], rows4[0:1, 512:1024]
        vrow, vrow2 = rows4[0:1, 1024:1536], rows4[0:1, 1536:2048]
        zev = st_.setdefault("zev", {})
        for dc in range(NDC):
            gi = qc * NDC + dc
            par = gi % 2
            if mode == "attn":
                wmm = [("v", att_v), ("g_wb", gw_wo),
                       ("v", zev.get(gi - 2, 0)),
                       ("s", att_rbs if gi < 2 else 0),
                       ("s", st_.get("prev_bcs", 0) if dc < 2 else 0)]

                def mmf(t_, qc=qc, dc=dc, par=par):
                    for di in range(NDC):
                        ii = t_.matmul(ps_s[par][:, 0:QT], wsl(wo_b, di, dc),
                                       ctx_fm[di][:, qc * QT:(qc + 1) * QT],
                                       start=(di == 0), stop=(di == NDC - 1))
                    return ii
                L[(dc, "mm")] = P.emit("t", mmf, waits=wmm)
                zev[gi] = L[(dc, "z")] = P.emit(
                    "v", lambda v, qc=qc, dc=dc, par=par:
                    v.scalar_tensor_tensor(
                        zbuf[:, dc * QT:(dc + 1) * QT], ps_s[par][:, 0:QT],
                        bo_sb[:, dc:dc + 1],
                        x_fm[dc][:, qc * QT:(qc + 1) * QT], ALU.add, ALU.add),
                    waits=[("t", L[(dc, "mm")])])
            else:
                L[(dc, "z")] = P.emit(
                    "v", lambda v, qc=qc, dc=dc: v.scalar_tensor_tensor(
                        zbuf[:, dc * QT:(dc + 1) * QT],
                        st_["routed"][dc][:, qc * QT:(qc + 1) * QT], 1.0 / 3.0,
                        y1[dc][:, qc * QT:(qc + 1) * QT], ALU.mult, ALU.add),
                    waits=[("v", st_.get("src_v", 0)),
                           ("t", st_.get("prev_ttr", 0))])
            zg = (L.get((dc - 2, "sums"), st_.get("prev_sums_last", 0)
                        if dc < 2 else 0))
            L[(dc, "zb")] = P.emit("v", lambda v, dc=dc: v.tensor_copy(
                zb_t[dc % 2][:, :], zbuf[:, dc * QT:(dc + 1) * QT]),
                waits=[("v", L[(dc, "z")]), ("t", zg)])
            L[(dc, "zsq")] = P.emit("s", lambda s, dc=dc: s.activation(
                zsq_t[dc % 2][:, :], zbuf[:, dc * QT:(dc + 1) * QT],
                AF.Square), waits=[("v", L[(dc, "z")]), ("t", zg)])

            def emit_sums(d2):
                wsum = [("v", L[(d2, "zb")]), ("s", L[(d2, "zsq")])]
                if d2 == 0:
                    wsum += [("s", st_.get("psc_guard_s", 0)),
                             ("v", st_.get("psc_guard_v", 0))]
                L[(d2, "sums")] = P.emit("t", lambda t_, d2=d2: (
                    t_.matmul(ps_c[0][0:1, 0:QT], ones128_b[:, :],
                              zb_t[d2 % 2][:, :], start=(d2 == 0),
                              stop=(d2 == NDC - 1)),
                    t_.matmul(ps_c[1][0:1, 0:QT], ones128_b[:, :],
                              zsq_t[d2 % 2][:, :], start=(d2 == 0),
                              stop=(d2 == NDC - 1)))[-1], waits=wsum)

            if dc >= 1:
                emit_sums(dc - 1)
            if dc == NDC - 1:
                emit_sums(dc)
        r0 = P.emit("v", lambda v: v.tensor_scalar(
            mrow, ps_c[0][0:1, 0:QT], 1.0 / DM, None, ALU.mult),
            waits=[("t", L[(NDC - 1, "sums")])])
        r1 = P.emit("v", lambda v: v.tensor_scalar(
            vrow, ps_c[1][0:1, 0:QT], 1.0 / DM, None, ALU.mult),
            waits=[("v", r0)])
        r2 = P.emit("v", lambda v: v.tensor_tensor(vrow2, mrow, mrow,
                                                   ALU.mult), waits=[("v", r1)])
        r3 = P.emit("v", lambda v: v.tensor_tensor(vrow, vrow, vrow2,
                                                   ALU.subtract),
                    waits=[("v", r2)])
        r4 = P.emit("s", lambda s: s.activation(srow, vrow, AF.Sqrt,
                                                bias=1e-5), waits=[("v", r3)])
        r5 = P.emit("v", lambda v: v.reciprocal(vrow2, srow),
                    waits=[("s", r4)])
        r6 = P.emit("v", lambda v: (v.tensor_copy(mrow_b, mrow),
                                    v.tensor_copy(irow_b, vrow2))[-1],
                    waits=[("v", r5)])
        wbc = [("v", r6), ("v", L[(NDC - 1, "z")]),
               ("s", st_.get("pss_guard_s", 0)),
               ("v", st_.get("pss_guard_v", 0))]
        bct = P.emit("t", lambda t_: (
            t_.matmul(ps_s[0][:, 0:QT], ones1_b[:, :], mrow_b, start=True,
                      stop=True),
            t_.matmul(ps_s[1][:, 0:QT], ones1_b[:, :], irow_b, start=True,
                      stop=True))[-1], waits=wbc)
        bcs = P.emit("s", lambda s: (s.copy(mb_sb[:, :], ps_s[0][:, 0:QT]),
                                     s.copy(invb_sb[:, :],
                                            ps_s[1][:, 0:QT]))[-1],
                     waits=[("t", bct)])
        subv = P.emit("v", lambda v: [v.tensor_tensor(
            zbuf[:, dc * QT:(dc + 1) * QT], zbuf[:, dc * QT:(dc + 1) * QT],
            mb_sb[:, :], ALU.subtract) for dc in range(NDC)][-1],
            waits=[("s", bcs)])
        if y_dst is not None:
            fin = P.emit("v", lambda v, qc=qc: [v.tensor_tensor(
                y_dst[dc][:, qc * QT:(qc + 1) * QT],
                zbuf[:, dc * QT:(dc + 1) * QT], invb_sb[:, :], ALU.mult)
                for dc in range(NDC)][-1], waits=[("v", subv)])
        else:
            fin = P.emit("v", lambda v: [v.tensor_tensor(
                zbuf[:, dc * QT:(dc + 1) * QT], zbuf[:, dc * QT:(dc + 1) * QT],
                invb_sb[:, :], ALU.mult) for dc in range(NDC)][-1],
                waits=[("v", subv)])
        st_["prev_sums_last"] = L[(NDC - 1, "sums")]
        st_["prev_bcs"] = bcs
        st_["pss_guard_s"] = bcs
        st_["psc_guard_v"] = r1
        L["fin"], L["bcs"], L["r1"] = fin, bcs, r1
        return L

    st1 = {"psc_guard_s": sig_done}
    gate_ws = {}
    gw_wr8 = 0
    for qc in range(NQT):
        L1 = ln_phase(qc, "attn", y1, st1, zbuf_a, mb_a, ivb_a, rows_a,
                      zbp_a, zsp_a)
        for tt in range(4):
            tok = qc * QT + tt * 128

            def gmf(t_, qc=qc, tt=tt, tok=tok):
                for di in range(NDC):
                    ii = t_.matmul(ps_c[0][:, 8 * tt:8 * tt + 8],
                                   y1[di][:, tok:tok + 128],
                                   wg_b[:, di * 8:(di + 1) * 8],
                                   start=(di == 0), stop=(di == NDC - 1))
                return ii
            gt_t = P.emit("t", gmf, waits=[("v", L1["fin"]),
                                           ("s", gate_ws.get((qc, tt - 1), 0))])
            gs = P.emit("s", lambda s, tt=tt: s.activation(
                gsig[:, :], ps_c[0][:, 8 * tt:8 * tt + 8], AF.Sigmoid),
                waits=[("t", gt_t)])
            st1["psc_guard_s"] = gs
            chain = [
                lambda v: v.reduce_max(gt[:, 0:1], gsig[:, :], axis=AX.X),
                lambda v: v.tensor_scalar(gt[:, 8:16], gsig[:, :], gt[:, 0:1],
                                          None, ALU.is_equal),
                lambda v: v.scalar_tensor_tensor(gt[:, 16:24], gt[:, 8:16],
                                                 -1e30, gsig[:, :], ALU.mult,
                                                 ALU.add),
                lambda v: v.reduce_max(gt[:, 1:2], gt[:, 16:24], axis=AX.X),
                lambda v: v.tensor_scalar(gt[:, 24:32], gt[:, 16:24],
                                          gt[:, 1:2], None, ALU.is_equal),
                lambda v: v.tensor_tensor(gt[:, 24:32], gt[:, 24:32],
                                          gt[:, 8:16], ALU.add),
                lambda v: v.tensor_tensor(gt[:, 2:3], gt[:, 0:1], gt[:, 1:2],
                                          ALU.add),
                lambda v: v.reciprocal(gt[:, 3:4], gt[:, 2:3]),
                lambda v: v.tensor_tensor(gt[:, 32:40], gsig[:, :],
                                          gt[:, 24:32], ALU.mult),
                lambda v: v.tensor_scalar(wtok[:, :], gt[:, 32:40], gt[:, 3:4],
                                          None, ALU.mult),
            ]
            vv = P.emit("v", chain[0], waits=[("s", gs)])
            for f in chain[1:]:
                vv = P.emit("v", f, waits=[("v", vv)])
            wts = 0
            for rnd in range(2):
                def wtf(t_, rnd=rnd):
                    for e4 in range(4):
                        ii = t_.transpose(
                            ps_s[0][0:1, QT + e4 * 128:QT + (e4 + 1) * 128],
                            wtok[:, rnd * 4 + e4:rnd * 4 + e4 + 1],
                            eye_f[:, :])
                    return ii
                wt_t = P.emit("t", wtf, waits=[("v", vv), ("s", wts),
                                               ("y_eye", ld_eye)])
                wts = P.emit("s", lambda s, rnd=rnd: s.copy(
                    wstg[0:1, 0:512], ps_s[0][0:1, QT:QT + 512]),
                    waits=[("t", wt_t), ("g_wr8", gw_wr8)])
                for e4 in range(4):
                    gw_wr8 = P.emit("g", lambda g, rnd=rnd, tok=tok, e4=e4:
                                    g.dma_start(
                        wr8[rnd * 4 + e4:rnd * 4 + e4 + 1, tok:tok + 128],
                        wstg[0:1, e4 * 128:(e4 + 1) * 128]),
                        waits=[("s", wts)] if e4 == 0 else [], sem="g_wr8")
            gate_ws[(qc, tt)] = wts
    wr8_done = gw_wr8
    p4_v = P.cnt["v"]
    p4_s = P.cnt["s"]
    p4_t = P.cnt["t"]

    sc_p4.close()
    sc_ctx.close()
    sc_x.close()

    sc_r = contextlib.ExitStack()
    routed = [sc_r.enter_context(nc.sbuf_tensor(f"rt{c}", [128, TPC], BF16))
              for c in range(NDC)]
    sc_ffn = contextlib.ExitStack()
    we1f = sc_ffn.enter_context(nc.sbuf_tensor("we1f", [128, 8 * FF], BF16))
    we3f = sc_ffn.enter_context(nc.sbuf_tensor("we3f", [128, 8 * FF], BF16))
    we2f = sc_ffn.enter_context(nc.sbuf_tensor("we2f", [128, NFC * DM], BF16))
    h_sb = sc_ffn.enter_context(nc.sbuf_tensor("h_sb", [128, NFC * QT], BF16))
    sa_sb = [sc_ffn.enter_context(nc.sbuf_tensor(f"sa{p}", [128, QT], BF16))
             for p in range(2)]
    h1_sb = [sc_ffn.enter_context(nc.sbuf_tensor(f"h1{p}", [128, QT], F32))
             for p in range(2)]
    wbb = sc_ffn.enter_context(nc.sbuf_tensor("wbb", [128, QT], BF16))

    FW = {}
    prev_bws = [0]
    ab_last, o_last, o_racc_last = {}, {}, {}
    ffn_s_last, ffn_h1_last = 0, 0
    space_guard = [("v", p4_v), ("t", p4_t), ("s", p4_s)]
    stg_dma = {}
    for ei in range(NE + 1):
        e = ei - 1
        s1d = ws1_d[:, :] if e < 0 else we1_d[e, :, :]
        s3d = ws3_d[:, :] if e < 0 else we3_d[e, :, :]
        s2d = ws2_d[:, :] if e < 0 else we2_d[e, :, :]
        FW[(ei, "w1")] = P.emit("g", lambda g, s1d=s1d: g.dma_start(
            we1f[:, :].rearrange("p (c n) -> p c n", c=8), chunked(s1d)),
            waits=[("t", ab_last.get(ei - 1, 0))] +
                  (space_guard if ei == 0 else []), sem="g_we13")
        FW[(ei, "w3")] = P.emit("g", lambda g, s3d=s3d: g.dma_start(
            we3f[:, :].rearrange("p (c n) -> p c n", c=8), chunked(s3d)),
            sem="g_we13")
        FW[(ei, "w2")] = P.emit("g", lambda g, s2d=s2d: g.dma_start(
            we2f[:, :].rearrange("p (c n) -> p c n", c=16),
            s2d.rearrange("(c p) n -> p c n", p=128)),
            waits=[("t", o_last.get(ei - 1, 0))], sem="g_we2")
        ht, ot, sls = {}, {}, {}
        for tt in range(NQT):
            if e >= 0:
                sd = P.emit("g", lambda g, e=e, tt=tt: g.dma_start(
                    wstg[0:1, (tt % 2) * 512:(tt % 2) * 512 + 512],
                    wr8[e:e + 1, tt * QT:(tt + 1) * QT]),
                    waits=[("g_wr8", wr8_done),
                           ("t", stg_dma.get(tt % 2, 0))],
                    sem=f"g_st{tt % 2}")
                bw = P.emit("t", lambda t_, tt=tt: t_.matmul(
                    ps_d[:, 0:QT], ones1_b[:, :],
                    wstg[0:1, (tt % 2) * 512:(tt % 2) * 512 + 512],
                    start=True, stop=True),
                    waits=[(f"g_st{tt % 2}", sd), ("s", prev_bws[0])])
                stg_dma[tt % 2] = bw
                bws = P.emit("s", lambda s: s.copy(wbb[:, :],
                                                   ps_d[:, 0:QT]),
                             waits=[("t", bw)])
                prev_bws[0] = bws
            else:
                bws = 0
            for fc in range(NFC):
                par = fc % 2

                def abf(t_, fc=fc, par=par, tt=tt):
                    for di in range(NDC):
                        t_.matmul(ps_s[par][:, 0:QT],
                                  we1f[:, di * FF + fc * 128:
                                       di * FF + (fc + 1) * 128],
                                  y1[di][:, tt * QT:(tt + 1) * QT],
                                  start=(di == 0), stop=(di == NDC - 1))
                    for di in range(NDC):
                        ii = t_.matmul(ps_s[par][:, QT:2 * QT],
                                       we3f[:, di * FF + fc * 128:
                                            di * FF + (fc + 1) * 128],
                                       y1[di][:, tt * QT:(tt + 1) * QT],
                                       start=(di == 0), stop=(di == NDC - 1))
                    return ii
                w = [("g_we13", FW[(ei, "w3")]),
                     ("v", ht.get((tt, fc - 2), 0)),
                     ("s", sls.get((tt, fc - 2), 0))]
                ab = P.emit("t", abf, waits=w)
                sls[(tt, fc)] = sl = P.emit(
                    "s", lambda s, fc=fc, par=par: s.activation(
                        sa_sb[fc % 2][:, :], ps_s[par][:, 0:QT], AF.Silu),
                    waits=[("t", ab)])
                ffn_s_last = sl
                h1 = P.emit("v", lambda v, fc=fc, par=par: v.tensor_tensor(
                    h1_sb[fc % 2][:, :], sa_sb[fc % 2][:, :],
                    ps_s[par][:, QT:2 * QT], ALU.mult), waits=[("s", sl)])
                ffn_h1_last = h1
                wh = [("v", h1), ("t", ot.get((tt - 1, "last"),
                                              o_last.get(ei - 1, 0)))]
                if e >= 0:
                    ht[(tt, fc)] = P.emit(
                        "v", lambda v, fc=fc: v.tensor_tensor(
                            h_sb[:, fc * QT:(fc + 1) * QT],
                            h1_sb[fc % 2][:, :], wbb[:, :], ALU.mult),
                        waits=wh + [("s", bws)])
                else:
                    ht[(tt, fc)] = P.emit(
                        "v", lambda v, fc=fc: v.tensor_copy(
                            h_sb[:, fc * QT:(fc + 1) * QT],
                            h1_sb[fc % 2][:, :]), waits=wh)
            if tt == NQT - 1:
                ab_last[ei] = P.cnt["t"]
            def omh(t_, dc, lo, hi):
                for fc in range(lo, hi):
                    ii = t_.matmul(ps_c[dc % 2][:, 0:QT],
                                   we2f[:, fc * DM + dc * 128:
                                        fc * DM + (dc + 1) * 128],
                                   h_sb[:, fc * QT:(fc + 1) * QT],
                                   start=(fc == 0), stop=(fc == NFC - 1))
                return ii
            for dc in (0, 1):
                w = [("v", ht[(tt, NFC // 2 - 1)]), ("g_we2", FW[(ei, "w2")]),
                     ("v", ot.get((tt - 1, "racc"),
                                  o_racc_last.get(ei - 1, 0)))]
                P.emit("t", lambda t_, dc=dc: omh(t_, dc, 0, NFC // 2),
                       waits=w)
            for dc in range(NDC):
                if dc < 2:
                    ot[(tt, dc, "t")] = P.emit(
                        "t", lambda t_, dc=dc: omh(t_, dc, NFC // 2, NFC),
                        waits=[("v", ht[(tt, NFC - 1)])])
                else:
                    ot[(tt, dc, "t")] = P.emit(
                        "t", lambda t_, dc=dc: omh(t_, dc, 0, NFC),
                        waits=[("v", ht[(tt, NFC - 1)]),
                               ("v", ot.get((tt, dc - 2, "racc"), 0))])
                if ei == 0:
                    ot[(tt, dc, "racc")] = P.emit(
                        "v", lambda v, dc=dc, par=dc % 2, tt=tt: v.tensor_copy(
                            routed[dc][:, tt * QT:(tt + 1) * QT],
                            ps_c[par][:, 0:QT]),
                        waits=[("t", ot[(tt, dc, "t")])])
                else:
                    ot[(tt, dc, "racc")] = P.emit(
                        "v", lambda v, dc=dc, par=dc % 2, tt=tt:
                        v.tensor_tensor(
                            routed[dc][:, tt * QT:(tt + 1) * QT],
                            routed[dc][:, tt * QT:(tt + 1) * QT],
                            ps_c[par][:, 0:QT], ALU.add),
                        waits=[("t", ot[(tt, dc, "t")])])
            ot[(tt, "last")] = ot[(tt, NDC - 1, "t")]
            ot[(tt, "racc")] = ot[(tt, NDC - 1, "racc")]
        o_last[ei] = ot[(NQT - 1, "last")]
        o_racc_last[ei] = ot[(NQT - 1, "racc")]
    ffn_done_v = o_racc_last[NE]
    p5_t = P.cnt["t"]

    sc_ffn.close()

    sc_p6 = contextlib.ExitStack()
    zbuf_b = sc_p6.enter_context(nc.sbuf_tensor("zbuf_b", [128, 8 * QT], F32))
    mb_b = sc_p6.enter_context(nc.sbuf_tensor("mb_b", [128, QT], F32))
    ivb_b = sc_p6.enter_context(nc.sbuf_tensor("ivb_b", [128, QT], F32))
    rows_b = sc_p6.enter_context(nc.sbuf_tensor("rows_b", [1, 2048], F32))
    zbp_b = [sc_p6.enter_context(nc.sbuf_tensor(f"zbp_b{p}", [128, QT], BF16))
             for p in range(2)]
    zsp_b = [sc_p6.enter_context(nc.sbuf_tensor(f"zsp_b{p}", [128, QT], BF16))
             for p in range(2)]
    stage_b = [sc_p6.enter_context(nc.sbuf_tensor(f"ost{p}", [128, DM], F32))
               for p in range(2)]

    st2 = {"src_v": ffn_done_v, "psc_guard_v": ffn_done_v,
           "pss_guard_s": ffn_s_last, "pss_guard_v": ffn_h1_last,
           "prev_ttr": p5_t, "routed": routed}
    ST = {}
    prev_st = {0: 0, 1: 0}
    for qc in range(NQT):
        L2 = ln_phase(qc, "ffn", None, st2, zbuf_b, mb_b, ivb_b, rows_b,
                      zbp_b, zsp_b)
        last_ttr = 0
        for tt in range(4):
            par = tt % 2
            tok = qc * QT + tt * 128

            def trf(t_, tt=tt):
                for dc in range(NDC):
                    ii = t_.transpose(
                        ps_c[dc // 4][:, (dc % 4) * 128:(dc % 4 + 1) * 128],
                        zbuf_b[:, dc * QT + tt * 128:dc * QT + (tt + 1) * 128],
                        eye_f[:, :])
                return ii
            ttr = P.emit("t", trf, waits=[("v", L2["fin"]),
                                          ("v", ST.get((qc, tt - 1, "ev"), 0)),
                                          ("v", L2["r1"] if tt == 0 else 0)])
            last_ttr = ttr
            ST[(qc, tt, "ev")] = P.emit("v", lambda v, par=par: [
                v.tensor_copy(stage_b[par][:, dc * 128:(dc + 1) * 128],
                              ps_c[dc // 4][:, (dc % 4) * 128:
                                            (dc % 4 + 1) * 128])
                for dc in range(NDC)][-1],
                waits=[("t", ttr), (f"y_st{par}", prev_st[par])])
            ST[(qc, tt, "st")] = P.emit(
                "y", lambda y, par=par, tok=tok: y.dma_start(
                    out_d[tok:tok + 128, :], stage_b[par][:, :]),
                waits=[("v", ST[(qc, tt, "ev")])], sem=f"y_st{par}")
            prev_st[par] = ST[(qc, tt, "st")]
        st2["prev_ttr"] = last_ttr
        st2["psc_guard_v"] = ST[(qc, 3, "ev")]
    P.emit("y", lambda y: y.nop(),
           waits=[("y_st0", ST[(NQT - 1, 2, "st")]),
                  ("y_st1", ST[(NQT - 1, 3, "st")])], inc=False)

    P.finalize(nc, sems)
    sc_p6.close()
    sc_r.close()
    stk.close()
    return nc


def _host_inputs(inputs):
    x = np.ascontiguousarray(np.asarray(inputs["L0_attn_hidden_states"],
                                        np.float32))
    up = np.asarray(inputs["L0_attn_up"], np.float32).reshape(-1)
    uz = np.asarray(inputs["L0_attn_uz"], np.float32).reshape(-1)
    upz = np.zeros((128, 512), np.float32)
    r = np.arange(128)
    for dc in range(8):
        h = 2 * dc + r // 64
        upz[r, dc * 64 + h] = up[dc * 128 + r]
        upz[r, dc * 64 + 32 + h] = uz[dc * 128 + r]
    wg = np.asarray(inputs["L0_ffn_Wg"], np.float32)
    wg_m = np.ascontiguousarray(wg.reshape(8, 128, 8).transpose(1, 0, 2)
                                .reshape(128, 64))

    def colmat(name):
        a = np.asarray(inputs[name], np.float32).reshape(-1)
        return np.ascontiguousarray(a.reshape(8, 128).T)

    def arr(name):
        return np.ascontiguousarray(np.asarray(inputs[name], np.float32))

    shared = {
        "eye": np.eye(128, dtype=np.float32),
        "wq": arr("L0_attn_wq"), "wk": arr("L0_attn_wk"),
        "wv": arr("L0_attn_wv"), "wo": arr("L0_attn_wo"),
        "wfq": arr("L0_attn_wfq"), "wfg": arr("L0_attn_wfg"),
        "bq_m": colmat("L0_attn_bq"), "bk_m": colmat("L0_attn_bk"),
        "bo_m": colmat("L0_attn_bo"), "bfq_m": colmat("L0_attn_bfq"),
        "bfg_m": colmat("L0_attn_bfg"),
        "bv_m": colmat("L0_attn_bv"),
        "bv_r": np.ascontiguousarray(
            np.asarray(inputs["L0_attn_bv"], np.float32).reshape(1, -1)),
        "upz_m": upz, "wg_m": wg_m,
        "ws1": arr("L0_ffn_Ws1"), "ws3": arr("L0_ffn_Ws3"),
        "ws2": arr("L0_ffn_Ws2"), "we1": arr("L0_ffn_We1"),
        "we3": arr("L0_ffn_We3"), "we2": arr("L0_ffn_We2"),
    }
    in_maps = []
    for c in range(8):
        b, off = c // 2, (c % 2) * TPC
        m = dict(shared)
        m["xr"] = np.ascontiguousarray(
            np.concatenate([x[b, off:], x[b, :off]], axis=0))
        m["jv"] = np.ascontiguousarray(
            np.concatenate([np.arange(off, S), np.arange(0, off)])
            .astype(np.float32).reshape(1, S))
        in_maps.append(m)
    return in_maps


_NC_CACHE = {}


def kernel(**inputs):
    if "nc" not in _NC_CACHE:
        _NC_CACHE["nc"] = build_full()
    nc = _NC_CACHE["nc"]
    in_maps = _host_inputs(inputs)
    res = run_bass_kernel_spmd(nc, in_maps, core_ids=list(range(8)))
    out = np.empty((B, S, DM), np.float32)
    for c in range(8):
        b, off = c // 2, (c % 2) * TPC
        out[b, off:off + TPC] = res.results[c]["out"]
    return out


# version 09
# speedup vs baseline: 1.4141x; 1.0241x over previous
import contextlib
import numpy as np
import concourse.bass as bass
import concourse.mybir as mybir
from concourse.bass_utils import run_bass_kernel_spmd

F32 = mybir.dt.float32
BF16 = mybir.dt.bfloat16
AF = mybir.ActivationFunctionType
ALU = mybir.AluOpType
AX = mybir.AxisListType

B, S, DM, H, D = 4, 4096, 1024, 16, 64
TPC, FF, NE, QT = 2048, 2048, 8, 512
NQT, NKC, NDC, NFC, NXT = TPC // QT, S // 128, DM // 128, FF // 128, S // 128

ENGS = ("t", "s", "v", "g", "y")


class Prog:

    def __init__(self):
        self.recs = {e: [] for e in ENGS}
        self.cnt = {}

    def emit(self, eng, fn, waits=(), inc=True, sem=None):
        semn = sem or eng
        inc_k = 16 if eng in ("g", "y") else 1
        ws = [(e, v) for (e, v) in waits if v and v > 0]
        self.recs[eng].append((fn, ws, semn if inc else None, inc_k))
        if inc:
            self.cnt[semn] = self.cnt.get(semn, 0) + inc_k
        return self.cnt.get(semn, 0)

    def finalize(self, nc, sems):
        def run(eng_obj, name):
            lw = {}
            for fn, ws, semn, inc_k in self.recs[name]:
                for se, val in ws:
                    if val > lw.get(se, 0):
                        eng_obj.wait_ge(sems[se], val)
                        lw[se] = val
                inst = fn(eng_obj)
                if semn is not None:
                    inst.then_inc(sems[semn], inc_k)

        with nc.Block() as block:
            block.tensor(lambda e: run(e, "t"))
            block.scalar(lambda e: run(e, "s"))
            block.vector(lambda e: run(e, "v"))
            block.gpsimd(lambda e: run(e, "g"))
            block.sync(lambda e: run(e, "y"))


def build_full():
    nc = bass.Bass("TRN2", target_bir_lowering=False)

    dt_ = nc.dram_tensor
    xr = dt_("xr", [S, DM], F32, kind="ExternalInput")
    jv_d = dt_("jv", [1, S], F32, kind="ExternalInput")
    eye_d = dt_("eye", [128, 128], F32, kind="ExternalInput")
    wq_d = dt_("wq", [DM, DM], F32, kind="ExternalInput")
    wk_d = dt_("wk", [DM, DM], F32, kind="ExternalInput")
    wv_d = dt_("wv", [DM, DM], F32, kind="ExternalInput")
    wo_d = dt_("wo", [DM, DM], F32, kind="ExternalInput")
    wfq_d = dt_("wfq", [DM, DM], F32, kind="ExternalInput")
    wfg_d = dt_("wfg", [DM, DM], F32, kind="ExternalInput")
    bq_d = dt_("bq_m", [128, 8], F32, kind="ExternalInput")
    bk_d = dt_("bk_m", [128, 8], F32, kind="ExternalInput")
    bo_d = dt_("bo_m", [128, 8], F32, kind="ExternalInput")
    bfq_d = dt_("bfq_m", [128, 8], F32, kind="ExternalInput")
    bfg_d = dt_("bfg_m", [128, 8], F32, kind="ExternalInput")
    bv_d = dt_("bv_m", [128, 8], F32, kind="ExternalInput")
    bvr_d = dt_("bv_r", [1, DM], F32, kind="ExternalInput")
    upz_d = dt_("upz_m", [128, 512], F32, kind="ExternalInput")
    wg_d = dt_("wg_m", [128, 64], F32, kind="ExternalInput")
    ws1_d = dt_("ws1", [DM, FF], F32, kind="ExternalInput")
    ws3_d = dt_("ws3", [DM, FF], F32, kind="ExternalInput")
    ws2_d = dt_("ws2", [FF, DM], F32, kind="ExternalInput")
    we1_d = dt_("we1", [NE, DM, FF], F32, kind="ExternalInput")
    we3_d = dt_("we3", [NE, DM, FF], F32, kind="ExternalInput")
    we2_d = dt_("we2", [NE, FF, DM], F32, kind="ExternalInput")
    out_d = dt_("out", [TPC, DM], F32, kind="ExternalOutput")
    rbd = dt_("rbd", [1, 2 * QT], F32, kind="Internal")

    al = nc.alloc_sbuf_tensor
    eye_f = al("eye_f", [128, 128], F32)
    eye_b = al("eye_b", [128, 128], BF16)
    ones1_b = al("ones1_b", [1, 128], BF16)
    ones128_b = al("ones128_b", [128, 1], BF16)
    cpool = al("cpool", [128, 2], F32)
    bq_sb = al("bq_sb", [128, 8], F32)
    bk_sb = al("bk_sb", [128, 8], F32)
    bo_sb = al("bo_sb", [128, 8], F32)
    bfq_sb = al("bfq_sb", [128, 8], F32)
    bfg_sb = al("bfg_sb", [128, 8], F32)
    cbias = al("cbias", [128, 8], F32)
    bv_sb = al("bv_sb", [128, 8], F32)
    wg_b = al("wg_b", [128, 64], BF16)
    xm_sb = al("xm_sb", [128, 8], F32)
    xm_b = al("xm_b", [128, 8], BF16)
    glo_sb = al("glo_sb", [128, 8], F32)
    glo_b = al("glo_b", [128, 8], BF16)
    rowsF = al("rowsF", [1, 1024], F32)
    recr_f = [rowsF[0:1, 0:512], rowsF[0:1, 512:1024]]
    rowsB = al("rowsB", [1, 2048], BF16)
    mrow_b, irow_b = rowsB[0:1, 0:512], rowsB[0:1, 512:1024]
    recr = [rowsB[0:1, 1024:1536], rowsB[0:1, 1536:2048]]
    gsig = al("gsig", [128, 8], F32)
    gt = al("gt", [128, 40], F32)
    wtok = al("wtok", [128, 8], F32)
    wr8 = al("wr8", [8, TPC], BF16)
    wstg = al("wstg", [1, 1024], BF16)
    mq = [al(f"mq{c}", [128, TPC], BF16) for c in range(NDC)]
    y1 = mq

    ap_ = nc.alloc_psum_tensor
    ps_s = [ap_(f"ps_s{p}", [128, 2 * QT], F32) for p in range(2)]
    ps_c = [ap_(f"ps_c{i}", [128, QT], F32) for i in range(2)]
    sc_ptr = contextlib.ExitStack()
    ps_tr = [sc_ptr.enter_context(nc.psum_tensor(f"ps_tr{p}", [128, 1024],
                                                 BF16)) for p in range(2)]

    stk = contextlib.ExitStack()
    STREAMS = ["t", "s", "v", "y_jv", "y_eye", "y_x0", "y_x1", "y_st0",
               "y_st1", "g_sm", "g_j", "g_wb", "g_upz", "g_bv", "g_kv0",
               "g_kv1", "g_c30", "g_c31", "g_wr8", "g_st0", "g_st1",
               "g_we13", "g_we2", "y_rb", "y_bv", "g_wv"]
    sems = {e: stk.enter_context(nc.semaphore(f"sem_{e}")) for e in STREAMS}
    P = Prog()
    nc.const_aps.register(F32, 1e-5, cpool[:, 0:1])

    def chunked(dr, c=8):
        return dr.rearrange("(c p) n -> p c n", p=128)

    def wsl(buf, di, dc):
        return buf[:, di * DM + dc * 128: di * DM + (dc + 1) * 128]

    sc_x = contextlib.ExitStack()
    x_fm = [sc_x.enter_context(nc.sbuf_tensor(f"x_fm{c}", [128, S], BF16))
            for c in range(NDC)]
    sc_ctx = contextlib.ExitStack()
    ctx_fm = [sc_ctx.enter_context(nc.sbuf_tensor(f"ctx{c}", [128, TPC], BF16))
              for c in range(NDC)]
    sc_kaug = contextlib.ExitStack()
    k_augA = sc_kaug.enter_context(nc.sbuf_tensor("k_augA", [128, S], BF16))
    k_augB = sc_kaug.enter_context(nc.sbuf_tensor("k_augB", [128, S], BF16))

    sc_p0 = contextlib.ExitStack()
    poolF = sc_p0.enter_context(nc.sbuf_tensor("poolF", [1, S], F32))
    poolB = sc_p0.enter_context(nc.sbuf_tensor("poolB", [65, S], BF16))
    stage_a = [sc_p0.enter_context(nc.sbuf_tensor(f"stg{p}", [128, DM], F32))
               for p in range(2)]
    xb_t = [sc_p0.enter_context(nc.sbuf_tensor(f"xb{p}", [128, DM], BF16))
            for p in range(2)]
    jv_sb = poolF[0:1, :]
    jv_b = poolB[0:1, :]
    jsq_b = poolB[32:33, :]
    onesrow_b = poolB[64:65, :]

    ld_jv = P.emit("y", lambda y: y.dma_start(jv_sb, jv_d[:, :]), sem="y_jv")
    ld_eye = P.emit("y", lambda y: y.dma_start(eye_f[:, :], eye_d[:, :]), sem="y_eye")
    gw_small = 0
    for dst, src in [
        (eye_b[:, :], eye_d[:, :]), (bq_sb[:, :], bq_d[:, :]),
        (bk_sb[:, :], bk_d[:, :]), (bo_sb[:, :], bo_d[:, :]),
        (bfq_sb[:, :], bfq_d[:, :]), (bfg_sb[:, :], bfg_d[:, :]),
        (bv_sb[:, :], bv_d[:, :]), (wg_b[:, :], wg_d[:, :]),
    ]:
        gw_small = P.emit("g", lambda g, d=dst, s=src: g.dma_start(d, s), sem="g_sm")

    def memsets(v):
        v.memset(cpool[:, 0:1], 1e-5)
        v.memset(ones1_b[:, :], 1.0)
        v.memset(ones128_b[:, :], 1.0)
        return v.memset(onesrow_b, 1.0)
    v_init = P.emit("v", memsets)
    v_jsq = P.emit("v", lambda v: v.tensor_tensor(jsq_b, jv_sb, jv_sb,
                                                  ALU.mult),
                   waits=[("y_jv", ld_jv)])
    v_jb = P.emit("v", lambda v: v.tensor_copy(jv_b, jv_sb),
                  waits=[("v", v_jsq)])
    gw_j = 0
    for ka in (k_augA, k_augB):
        P.emit("g", lambda g, ka=ka: g.dma_start(ka[64:65, :], jsq_b),
               waits=[("v", v_jb), ("v", v_init)], sem="g_j")
        P.emit("g", lambda g, ka=ka: g.dma_start(ka[65:66, :], jv_b),
               sem="g_j")
        gw_j = P.emit("g", lambda g, ka=ka: g.dma_start(ka[66:67, :],
                                                        onesrow_b), sem="g_j")

    x_ld, x_cv, x_tr, x_ev = {}, {}, {}, {}
    for t in range(NXT):
        par = t % 2
        x_ld[t] = P.emit("y", lambda y, t=t, par=par: y.dma_start(
            stage_a[par][:, :], xr[t * 128:(t + 1) * 128, :]),
            waits=[("v", x_cv.get(t - 2, 0))], sem=f"y_x{par}")
        x_cv[t] = P.emit("v", lambda v, par=par: v.tensor_copy(
            xb_t[par][:, :], stage_a[par][:, :]),
            waits=[(f"y_x{par}", x_ld[t]), ("t", x_tr.get(t - 2, 0)),
                   ("v", v_init)])

        def trf(t_, par=par):
            for dc in range(NDC):
                ii = t_.transpose(ps_tr[par][:, dc * 128:(dc + 1) * 128],
                                  xb_t[par][:, dc * 128:(dc + 1) * 128],
                                  eye_b[:, :])
            return ii
        x_tr[t] = P.emit("t", trf, waits=[("v", x_cv[t]),
                                          ("g_sm", gw_small),
                                          ("v", x_ev.get(t - 2, 0))])

        def evf(v, t=t, par=par):
            for dc in range(NDC):
                ii = v.tensor_copy(x_fm[dc][:, t * 128:(t + 1) * 128],
                                   ps_tr[par][:, dc * 128:(dc + 1) * 128])
            return ii
        x_ev[t] = P.emit("v", evf, waits=[("t", x_tr[t])])
    x_done = x_ev[NXT - 1]
    xtr_last = x_tr[NXT - 1]

    sc_p0.close()
    sc_ptr.close()
    ps_d = nc.alloc_psum_tensor("ps_d", [128, 2 * QT], F32)

    sc_c3 = contextlib.ExitStack()
    C3 = sc_c3.enter_context(nc.sbuf_tensor("C3", [80, TPC], BF16))
    sc_w = contextlib.ExitStack()
    w_big1 = sc_w.enter_context(nc.sbuf_tensor("w_big1", [128, 8 * DM], BF16))
    w_big2 = w_big1

    gw_wq = P.emit("g", lambda g: g.dma_start(
        w_big1[:, :].rearrange("p (c n) -> p c n", c=8), chunked(wq_d[:, :])),
        waits=[("t", xtr_last), ("v", x_done), ("g_j", gw_j)], sem="g_wb")

    mq_mm, mq_ev = {}, {}
    for qc in range(NQT):
        for dc in range(NDC):
            i = qc * NDC + dc
            par = i % 2

            def mmf(t_, qc=qc, dc=dc, par=par):
                for di in range(NDC):
                    ii = t_.matmul(ps_s[par][:, 0:QT], wsl(w_big1, di, dc),
                                   x_fm[di][:, qc * QT:(qc + 1) * QT],
                                   start=(di == 0), stop=(di == NDC - 1))
                return ii
            mq_mm[i] = P.emit("t", mmf, waits=[("v", x_done), ("g_wb", gw_wq),
                                               ("v", mq_ev.get(i - 2, 0))])
            mq_ev[i] = P.emit("v", lambda v, qc=qc, dc=dc, par=par:
                              v.tensor_scalar(mq[dc][:, qc * QT:(qc + 1) * QT],
                                              ps_s[par][:, 0:QT],
                                              bq_sb[:, dc:dc + 1], None,
                                              ALU.add),
                              waits=[("t", mq_mm[i])])
    mq_done = mq_ev[NQT * NDC - 1]

    def xmeanf(v):
        for dc in range(NDC):
            ii = v.reduce_sum(xm_sb[:, dc:dc + 1], x_fm[dc][:, :], axis=AX.X)
        return ii
    v_xm = P.emit("v", xmeanf, waits=[("v", x_done)])
    v_xmb = P.emit("v", lambda v: v.tensor_scalar(
        xm_b[:, :], xm_sb[:, :], 1.0 / S, None, ALU.mult), waits=[("v", v_xm)])

    def glof(t_):
        for dc in range(NDC):
            for di in range(NDC):
                ii = t_.matmul(ps_c[0][:, dc:dc + 1], wsl(w_big1, di, dc),
                               xm_b[:, di:di + 1], start=(di == 0),
                               stop=(di == NDC - 1))
        return ii
    t_glo = P.emit("t", glof, waits=[("v", v_xmb), ("g_wb", gw_wq)])
    v_glo = P.emit("v", lambda v: v.tensor_tensor(
        glo_sb[:, :], ps_c[0][:, 0:8], bq_sb[:, :], ALU.add),
        waits=[("t", t_glo)])
    v_glob = P.emit("v", lambda v: v.tensor_copy(glo_b[:, :], glo_sb[:, :]),
                    waits=[("v", v_glo)])

    def gff(t_):
        for dc in range(NDC):
            for di in range(NDC):
                ii = t_.matmul(ps_c[1][:, dc:dc + 1], wsl(w_big2, di, dc),
                               glo_b[:, di:di + 1], start=(di == 0),
                               stop=(di == NDC - 1))
        return ii
    gw_wfg = P.emit("g", lambda g: g.dma_start(
        w_big1[:, :].rearrange("p (c n) -> p c n", c=8), chunked(wfg_d[:, :])),
        waits=[("t", t_glo), ("t", mq_mm[NQT * NDC - 1])], sem="g_wb")
    t_gf = P.emit("t", gff, waits=[("v", v_glob), ("g_wb", gw_wfg)])
    v_cb1 = P.emit("v", lambda v: v.tensor_tensor(
        cbias[:, :], ps_c[1][:, 0:8], bfq_sb[:, :], ALU.add),
        waits=[("t", t_gf)])
    v_cb = P.emit("v", lambda v: v.tensor_tensor(
        cbias[:, :], cbias[:, :], bfg_sb[:, :], ALU.add), waits=[("v", v_cb1)])

    gw_wfq = P.emit("g", lambda g: g.dma_start(
        w_big2[:, :].rearrange("p (c n) -> p c n", c=8), chunked(wfq_d[:, :])),
        waits=[("t", t_gf)], sem="g_wb")

    sc_p2 = contextlib.ExitStack()
    upz_b = sc_p2.enter_context(nc.sbuf_tensor("upz_b", [128, 512], BF16))
    gw_upz = P.emit("g", lambda g: g.dma_start(upz_b[:, :], upz_d[:, :]),
                    waits=[("t", xtr_last), ("v", x_done), ("g_j", gw_j)],
                    sem="g_upz")
    spz = sc_p2.enter_context(nc.sbuf_tensor("spz", [128, 4096], F32))
    Sp, Sz = spz[0:16, 0:TPC], spz[32:48, 0:TPC]
    cA, cB = spz[64:80, 0:TPC], spz[64:80, TPC:2 * TPC]
    ctp = sc_p2.enter_context(nc.sbuf_tensor("ctp", [128, 1024], BF16))
    c_t_sb = [ctp[:, 0:QT], ctp[:, QT:2 * QT]]

    c_mm, c_tanh, pz_mm, sig = {}, {}, {}, {}
    for qc in range(NQT):
        for dc in range(NDC):
            i = qc * NDC + dc
            par = i % 2

            def cmf(t_, qc=qc, dc=dc, par=par):
                for di in range(NDC):
                    ii = t_.matmul(ps_s[par][:, 0:QT], wsl(w_big2, di, dc),
                                   mq[di][:, qc * QT:(qc + 1) * QT],
                                   start=(di == 0), stop=(di == NDC - 1))
                return ii
            c_mm[i] = P.emit("t", cmf,
                             waits=[("v", v_cb), ("g_wb", gw_wfq), ("v", mq_done),
                                    ("s", c_tanh.get(i - 2, 0))])
            c_tanh[i] = P.emit("s", lambda s, dc=dc, par=par: s.activation(
                c_t_sb[par], ps_s[par][:, 0:QT], AF.Tanh,
                bias=cbias[:, dc:dc + 1]), waits=[("t", c_mm[i])])
            pz_mm[i] = P.emit("t", lambda t_, dc=dc, par=par: t_.matmul(
                ps_c[0][0:64, 0:QT], upz_b[:, dc * 64:(dc + 1) * 64],
                c_t_sb[par], start=(dc == 0), stop=(dc == NDC - 1)),
                waits=[("s", c_tanh[i]), ("s", sig.get(qc - 1, 0)),
                       ("v", v_glo if i == 0 else 0),
                       ("g_upz", gw_upz if i == 0 else 0)])
        sig[qc] = P.emit("s", lambda s, qc=qc: (
            s.activation(Sp[:, qc * QT:(qc + 1) * QT], ps_c[0][0:16, 0:QT],
                         AF.Sigmoid),
            s.activation(Sz[:, qc * QT:(qc + 1) * QT], ps_c[0][32:48, 0:QT],
                         AF.Sigmoid))[-1],
            waits=[("t", pz_mm[qc * NDC + NDC - 1])])
    sig_done = sig[NQT - 1]
    pz_last = pz_mm[NQT * NDC - 1]

    steps = [
        lambda v: v.reciprocal(cA, Sz),
        lambda v: v.tensor_tensor(cA, cA, cA, ALU.mult),
        lambda v: v.tensor_scalar(cA, cA, -1.1920928955078125e-07,
                                  None, ALU.mult),
        lambda v: v.tensor_scalar(cB, Sp, float(S), None, ALU.mult),
        lambda v: v.tensor_copy(C3[0:16, 0:TPC], cA),
        lambda v: v.tensor_tensor(cA, cA, cB, ALU.mult),
        lambda v: v.tensor_scalar(C3[32:48, 0:TPC], cA, -2.0, None,
                                  ALU.mult),
        lambda v: v.tensor_tensor(cA, cA, cB, ALU.mult),
        lambda v: v.tensor_scalar(C3[64:80, 0:TPC], cA, -4.0, None,
                                  ALU.add),
    ]
    vprev = P.emit("v", steps[0], waits=[("s", sig_done)])
    for f in steps[1:]:
        vprev = P.emit("v", f, waits=[("v", vprev)])
    coeff_done = vprev

    sc_p2.close()
    sc_w.close()

    sc_att2 = contextlib.ExitStack()
    v_augs = [sc_att2.enter_context(nc.sbuf_tensor(f"v_aug{i}",
                                                   [128, NKC * 130], BF16))
              for i in range(2)]
    kv_w = [sc_att2.enter_context(nc.sbuf_tensor(f"kv{p}", [128, 1024], BF16))
            for p in range(2)]
    wv2 = sc_att2.enter_context(nc.sbuf_tensor("wv2", [128, 2048], BF16))
    DEP = 6
    e_t = [sc_att2.enter_context(nc.sbuf_tensor(f"e_t{p}", [128, 2 * QT],
                                                BF16)) for p in range(DEP)]
    rhs_aug = [sc_att2.enter_context(nc.sbuf_tensor(f"rhs{p}", [128, QT],
                                                    BF16)) for p in range(4)]
    rbp = sc_att2.enter_context(nc.sbuf_tensor("rbp", [128, QT], F32))
    rb_sb = [rbp[0:64, :], rbp[64:128, :]]
    bvb2 = sc_att2.enter_context(nc.sbuf_tensor("bvb2", [128, 256], F32))

    def vones(v):
        for va_t in v_augs:
            va = va_t[:, :].rearrange("p (t c) -> p t c", c=130)
            v.memset(va[:, :, 64:65], 1.0)
            ii = v.memset(va[:, :, 129:130], 1.0)
        return ii
    v_vones = P.emit("v", vones, waits=[("t", pz_last), ("v", coeff_done)])

    SB3 = [ps_s[0], ps_s[1], ps_d]

    AT = {}
    exp_hist = []
    ctx_hist = []
    pair_t_end = {}
    for p in range(NE):
        wpar = p % 2
        cols = slice(p * 128, (p + 1) * 128)
        kv_gw = P.emit("g", lambda g, wpar=wpar, cols=cols: g.dma_start(
            kv_w[wpar][:, :].rearrange("p (c n) -> p c n", c=8),
            chunked(wk_d[:, cols])),
            waits=[("t", pair_t_end.get(p - 2, pz_last)), ("g_j", gw_j),
                   ("v", coeff_done if p == 0 else 0),
                   ("s", sig_done if p == 0 else 0)], sem=f"g_kv{wpar}")
        if p % 2 == 0:
            cols2 = slice(p * 128, (p + 2) * 128)
            gw_wv = P.emit("g", lambda g, cols2=cols2: g.dma_start(
                wv2[:, :].rearrange("p (c n) -> p c n", c=8),
                chunked(wv_d[:, cols2])),
                waits=[("t", AT.get((p - 2, "v_done"), pz_last))], sem="g_wv")
            bv_dma = P.emit("y", lambda y, p=p: y.dma_start(
                bvb2[:, :], bass.AP(bvr_d, p * 128, [[0, 128], [1, 256]])),
                waits=[("v", AT.get((p - 2, "bv_used"), 0)),
                       ("v", v_init)], sem="y_bv")

        kev = {}
        for kt in range(8):
            par = kt % 2

            def kmf(t_, kt=kt, par=par, wpar=wpar):
                for di in range(NDC):
                    ii = t_.matmul(ps_s[par][:, 0:QT],
                                   kv_w[wpar][:, di * 128:(di + 1) * 128],
                                   x_fm[di][:, kt * QT:(kt + 1) * QT],
                                   start=(di == 0), stop=(di == NDC - 1))
                return ii
            w = [(f"g_kv{wpar}", kv_gw), ("v", kev.get(kt - 2, 0))]
            if kt == 0:
                w += [("s", exp_hist[-1] if exp_hist else sig_done),
                      ("v", AT.get((p - 1, NQT - 1, "ctxe"),
                                   max(coeff_done, mq_done)))]
            tk = P.emit("t", kmf, waits=w)
            kev[kt] = P.emit("v", lambda v, kt=kt, par=par, p=p: (
                v.tensor_scalar(k_augA[0:64, kt * QT:(kt + 1) * QT],
                                ps_s[par][0:64, 0:QT], bk_sb[0:64, p:p + 1],
                                0.125, ALU.add, ALU.mult),
                v.tensor_scalar(k_augB[0:64, kt * QT:(kt + 1) * QT],
                                ps_s[par][64:128, 0:QT], bk_sb[64:128, p:p + 1],
                                0.125, ALU.add, ALU.mult))[-1],
                waits=[("t", tk),
                       ("t", AT.get((p - 1, NQT - 1, "last_sc"), 0)
                        if kt == 0 else 0)])

        if p % 2 == 0:
            vev = {}
            for tt in range(NKC):
                par = tt % 2

                def vmf(t_, tt=tt, par=par):
                    for di in range(NDC):
                        ii = t_.matmul(ps_s[par][:, QT:QT + 256],
                                       x_fm[di][:, tt * 128:(tt + 1) * 128],
                                       wv2[:, di * 256:(di + 1) * 256],
                                       start=(di == 0), stop=(di == NDC - 1))
                    return ii
                w = [("v", vev.get(tt - 2, 0)), ("g_wv", gw_wv)]
                if tt == 0:
                    w += [("t", AT.get((p - 1, NQT - 1, NKC - 1, "ctx"), 0))]
                tv = P.emit("t", vmf, waits=w)
                base = tt * 130

                def vevf(v, base=base, par=par):
                    v.tensor_tensor(v_augs[0][:, base:base + 64],
                                    ps_s[par][:, QT:QT + 64], bvb2[:, 0:64],
                                    ALU.add)
                    v.tensor_tensor(v_augs[0][:, base + 65:base + 129],
                                    ps_s[par][:, QT + 64:QT + 128],
                                    bvb2[:, 64:128], ALU.add)
                    v.tensor_tensor(v_augs[1][:, base:base + 64],
                                    ps_s[par][:, QT + 128:QT + 192],
                                    bvb2[:, 128:192], ALU.add)
                    return v.tensor_tensor(v_augs[1][:, base + 65:base + 129],
                                           ps_s[par][:, QT + 192:QT + 256],
                                           bvb2[:, 192:256], ALU.add)
                vev[tt] = P.emit("v", vevf,
                                 waits=[("t", tv), ("y_bv", bv_dma),
                                        ("v", v_vones if tt == 0 and p == 0
                                         else 0)])
            AT[(p, "v_done")] = P.cnt["t"]
            AT[(p, "bv_used")] = vev[NKC - 1]
            kv_done_v = max(vev[NKC - 1], kev[7])
        else:
            kv_done_v = kev[7]
        v_aug = v_augs[p % 2]

        def emit_rhs(pp, qc):
            for hi in range(2):
                bi = 2 * (qc % 2) + hi
                h = 2 * pp + hi
                guard = AT.get((pp, qc - 2, "last_sc"),
                               AT.get((pp - 1, NQT - 1, "last_sc"), pz_last))
                AT[(pp, qc, hi, "q")] = P.emit(
                    "v", lambda v, bi=bi, hi=hi, pp=pp, qc=qc: v.tensor_copy(
                        rhs_aug[bi][0:64, :],
                        mq[pp][hi * 64:(hi + 1) * 64, qc * QT:(qc + 1) * QT]),
                    waits=[("t", guard)])
                for ri in range(3):
                    AT[(pp, qc, hi, "c3")] = P.emit(
                        "g", lambda g, bi=bi, h=h, qc=qc, ri=ri: g.dma_start(
                            rhs_aug[bi][64 + ri:65 + ri, 0:QT],
                            C3[h + 32 * ri:h + 32 * ri + 1,
                               qc * QT:(qc + 1) * QT]),
                        waits=[("v", AT[(pp, qc, hi, "q")])]
                        if ri == 0 else [], sem=f"g_c3{qc % 2}")

        if p == 0:
            emit_rhs(p, 0)
        emit_rhs(p, 1)
        for qc in range(NQT):
            def emit_ctx(kc):
                g = len(ctx_hist)
                assert g % NKC == kc or True
                gb = (qc * NKC + kc)

                def ctf(t_, kc=kc, va=v_aug, eb=e_t[len(ctx_hist) % DEP]):
                    base = kc * 130
                    t_.matmul(ps_c[0][0:65, :], va[:, base:base + 65],
                              eb[:, 0:QT], start=(kc == 0),
                              stop=(kc == NKC - 1))
                    return t_.matmul(ps_c[1][0:65, :],
                                     va[:, base + 65:base + 130],
                                     eb[:, QT:2 * QT],
                                     start=(kc == 0), stop=(kc == NKC - 1))
                wct = [("s", AT[(p, qc, kc, "exp")])]
                if kc == 0:
                    wct += [("v", AT.get((p, qc - 1, "ctxe"),
                                         AT.get((p - 1, NQT - 1, "ctxe"), 0))),
                            ("v", kv_done_v)]
                AT[(p, qc, kc, "ctx")] = P.emit("t", ctf, waits=wct)
                ctx_hist.append(AT[(p, qc, kc, "ctx")])

            for kc in range(NKC):
                if kc >= DEP:
                    emit_ctx(kc - DEP)
                g = len(exp_hist)
                b3 = SB3[g % 3]
                wsc = [("s", exp_hist[g - 3] if g >= 3 else 0),
                       (f"g_c3{qc % 2}", AT[(p, qc, 1, "c3")]),
                       ("v", kv_done_v)]
                if kc == 0:
                    wsc += [("g_j", gw_j if p == 0 and qc == 0 else 0)]

                def scf(t_, kc=kc, b3=b3, biA=2 * (qc % 2),
                        biB=2 * (qc % 2) + 1):
                    t_.matmul(b3[:, 0:QT],
                              k_augA[0:67, kc * 128:(kc + 1) * 128],
                              rhs_aug[biA][0:67, :], start=True, stop=True)
                    return t_.matmul(b3[:, QT:2 * QT],
                                     k_augB[0:67, kc * 128:(kc + 1) * 128],
                                     rhs_aug[biB][0:67, :], start=True,
                                     stop=True)
                AT[(p, qc, kc, "sc")] = P.emit("t", scf, waits=wsc)
                AT[(p, qc, kc, "exp")] = P.emit(
                    "s", lambda s, b3=b3, eb=e_t[g % DEP]: s.activation(
                        eb[:, :], b3[:, :], AF.Exp),
                    waits=[("t", AT[(p, qc, kc, "sc")]),
                           ("t", ctx_hist[g - DEP] if g >= DEP else 0)])
                exp_hist.append(AT[(p, qc, kc, "exp")])
                if kc == 2 and qc >= 1:
                    if qc + 1 < NQT:
                        emit_rhs(p, qc + 1)
                    elif p + 1 < NE:
                        emit_rhs(p + 1, 0)
            for kc2 in range(NKC - DEP, NKC):
                emit_ctx(kc2)
            AT[(p, qc, "last_sc")] = AT[(p, qc, NKC - 1, "sc")]
            AT[(p, qc, "rec")] = P.emit("v", lambda v: (
                v.tensor_copy(recr_f[0], ps_c[0][64:65, :]),
                v.tensor_copy(recr_f[1], ps_c[1][64:65, :]))[-1],
                waits=[("t", AT[(p, qc, NKC - 1, "ctx")]),
                       ("y_rb", AT.get((p, qc - 1, "rb"),
                                       AT.get((p - 1, NQT - 1, "rb"), 0)))])
            wr_rbd = P.emit("y", lambda y: y.dma_start(
                rbd[:, :], rowsF[0:1, 0:2 * QT]),
                waits=[("v", AT[(p, qc, "rec")])], sem="y_rb")
            AT[(p, qc, "rb")] = P.emit("y", lambda y: y.dma_start(
                rbp[:, :], bass.AP(rbd, 0, [[512, 2], [0, 64], [1, QT]])),
                waits=[("y_rb", wr_rbd),
                       ("v", AT.get((p, qc - 1, "ctxe"),
                                    AT.get((p - 1, NQT - 1, "ctxe"), 0)))],
                sem="y_rb")
            AT[(p, qc, "rcp")] = P.emit("v", lambda v: (
                v.reciprocal(rb_sb[0], rb_sb[0]),
                v.reciprocal(rb_sb[1], rb_sb[1]))[-1],
                waits=[("y_rb", AT[(p, qc, "rb")])])
            AT[(p, qc, "ctxe")] = P.emit("v", lambda v, p=p, qc=qc: (
                v.tensor_tensor(ctx_fm[p][0:64, qc * QT:(qc + 1) * QT],
                                ps_c[0][0:64, :], rb_sb[0], ALU.mult),
                v.tensor_tensor(ctx_fm[p][64:128, qc * QT:(qc + 1) * QT],
                                ps_c[1][0:64, :], rb_sb[1], ALU.mult))[-1],
                waits=[("v", AT[(p, qc, "rcp")])])
        pair_t_end[p] = P.cnt["t"]
    att_v = AT[(NE - 1, NQT - 1, "ctxe")]
    att_rbs = AT[(NE - 1, NQT - 1, NKC - 1, "exp")]
    att_t = P.cnt["t"]

    sc_att2.close()
    sc_c3.close()
    sc_kaug.close()
    sc_p4 = contextlib.ExitStack()
    wo_b = sc_p4.enter_context(nc.sbuf_tensor("wo_b", [128, 8 * DM], BF16))
    zbuf_a = sc_p4.enter_context(nc.sbuf_tensor("zbuf_a", [128, 8 * QT], F32))
    mb_a = sc_p4.enter_context(nc.sbuf_tensor("mb_a", [128, QT], F32))
    ivb_a = sc_p4.enter_context(nc.sbuf_tensor("ivb_a", [128, QT], F32))
    rows_a = sc_p4.enter_context(nc.sbuf_tensor("rows_a", [1, 2048], F32))
    zbp_a = [sc_p4.enter_context(nc.sbuf_tensor(f"zbp_a{p}", [128, QT], BF16))
             for p in range(2)]
    zsp_a = [sc_p4.enter_context(nc.sbuf_tensor(f"zsp_a{p}", [128, QT], BF16))
             for p in range(2)]

    gw_wo = P.emit("g", lambda g: g.dma_start(
        wo_b[:, :].rearrange("p (c n) -> p c n", c=8), chunked(wo_d[:, :])),
        waits=[("t", att_t), ("s", att_rbs), ("v", att_v)], sem="g_wb")

    def ln_phase(qc, mode, y_dst, st_, zbuf, mb_sb, invb_sb, rows4,
                 zb_t, zsq_t):
        L = {}
        mrow, srow = rows4[0:1, 0:512], rows4[0:1, 512:1024]
        vrow, vrow2 = rows4[0:1, 1024:1536], rows4[0:1, 1536:2048]
        zev = st_.setdefault("zev", {})
        for dc in range(NDC):
            gi = qc * NDC + dc
            par = gi % 2
            if mode == "attn":
                wmm = [("v", att_v), ("g_wb", gw_wo),
                       ("v", zev.get(gi - 2, 0)),
                       ("s", att_rbs if gi < 2 else 0),
                       ("s", st_.get("prev_bcs", 0) if dc < 2 else 0)]

                def mmf(t_, qc=qc, dc=dc, par=par):
                    for di in range(NDC):
                        ii = t_.matmul(ps_s[par][:, 0:QT], wsl(wo_b, di, dc),
                                       ctx_fm[di][:, qc * QT:(qc + 1) * QT],
                                       start=(di == 0), stop=(di == NDC - 1))
                    return ii
                L[(dc, "mm")] = P.emit("t", mmf, waits=wmm)
                zev[gi] = L[(dc, "z")] = P.emit(
                    "v", lambda v, qc=qc, dc=dc, par=par:
                    v.scalar_tensor_tensor(
                        zbuf[:, dc * QT:(dc + 1) * QT], ps_s[par][:, 0:QT],
                        bo_sb[:, dc:dc + 1],
                        x_fm[dc][:, qc * QT:(qc + 1) * QT], ALU.add, ALU.add),
                    waits=[("t", L[(dc, "mm")])])
            else:
                L[(dc, "z")] = P.emit(
                    "v", lambda v, qc=qc, dc=dc: v.scalar_tensor_tensor(
                        zbuf[:, dc * QT:(dc + 1) * QT],
                        st_["routed"][dc][:, qc * QT:(qc + 1) * QT], 1.0 / 3.0,
                        y1[dc][:, qc * QT:(qc + 1) * QT], ALU.mult, ALU.add),
                    waits=[("v", st_.get("src_v", 0)),
                           ("t", st_.get("prev_ttr", 0))])
            zg = (L.get((dc - 2, "sums"), st_.get("prev_sums_last", 0)
                        if dc < 2 else 0))
            L[(dc, "zb")] = P.emit("v", lambda v, dc=dc: v.tensor_copy(
                zb_t[dc % 2][:, :], zbuf[:, dc * QT:(dc + 1) * QT]),
                waits=[("v", L[(dc, "z")]), ("t", zg)])
            L[(dc, "zsq")] = P.emit("s", lambda s, dc=dc: s.activation(
                zsq_t[dc % 2][:, :], zbuf[:, dc * QT:(dc + 1) * QT],
                AF.Square), waits=[("v", L[(dc, "z")]), ("t", zg)])

            def emit_sums(d2):
                wsum = [("v", L[(d2, "zb")]), ("s", L[(d2, "zsq")])]
                if d2 == 0:
                    wsum += [("s", st_.get("psc_guard_s", 0)),
                             ("v", st_.get("psc_guard_v", 0))]
                L[(d2, "sums")] = P.emit("t", lambda t_, d2=d2: (
                    t_.matmul(ps_c[0][0:1, 0:QT], ones128_b[:, :],
                              zb_t[d2 % 2][:, :], start=(d2 == 0),
                              stop=(d2 == NDC - 1)),
                    t_.matmul(ps_c[1][0:1, 0:QT], ones128_b[:, :],
                              zsq_t[d2 % 2][:, :], start=(d2 == 0),
                              stop=(d2 == NDC - 1)))[-1], waits=wsum)

            if dc >= 1:
                emit_sums(dc - 1)
            if dc == NDC - 1:
                emit_sums(dc)
        r0 = P.emit("v", lambda v: v.tensor_scalar(
            mrow, ps_c[0][0:1, 0:QT], 1.0 / DM, None, ALU.mult),
            waits=[("t", L[(NDC - 1, "sums")])])
        r1 = P.emit("v", lambda v: v.tensor_scalar(
            vrow, ps_c[1][0:1, 0:QT], 1.0 / DM, None, ALU.mult),
            waits=[("v", r0)])
        r2 = P.emit("v", lambda v: v.tensor_tensor(vrow2, mrow, mrow,
                                                   ALU.mult), waits=[("v", r1)])
        r3 = P.emit("v", lambda v: v.tensor_tensor(vrow, vrow, vrow2,
                                                   ALU.subtract),
                    waits=[("v", r2)])
        r4 = P.emit("s", lambda s: s.activation(srow, vrow, AF.Sqrt,
                                                bias=1e-5), waits=[("v", r3)])
        r5 = P.emit("v", lambda v: v.reciprocal(vrow2, srow),
                    waits=[("s", r4)])
        r6 = P.emit("v", lambda v: (v.tensor_copy(mrow_b, mrow),
                                    v.tensor_copy(irow_b, vrow2))[-1],
                    waits=[("v", r5)])
        wbc = [("v", r6), ("v", L[(NDC - 1, "z")]),
               ("s", st_.get("pss_guard_s", 0)),
               ("v", st_.get("pss_guard_v", 0))]
        bct = P.emit("t", lambda t_: (
            t_.matmul(ps_s[0][:, 0:QT], ones1_b[:, :], mrow_b, start=True,
                      stop=True),
            t_.matmul(ps_s[1][:, 0:QT], ones1_b[:, :], irow_b, start=True,
                      stop=True))[-1], waits=wbc)
        bcs = P.emit("s", lambda s: (s.copy(mb_sb[:, :], ps_s[0][:, 0:QT]),
                                     s.copy(invb_sb[:, :],
                                            ps_s[1][:, 0:QT]))[-1],
                     waits=[("t", bct)])
        subv = P.emit("v", lambda v: [v.tensor_tensor(
            zbuf[:, dc * QT:(dc + 1) * QT], zbuf[:, dc * QT:(dc + 1) * QT],
            mb_sb[:, :], ALU.subtract) for dc in range(NDC)][-1],
            waits=[("s", bcs)])
        if y_dst is not None:
            fin = P.emit("v", lambda v, qc=qc: [v.tensor_tensor(
                y_dst[dc][:, qc * QT:(qc + 1) * QT],
                zbuf[:, dc * QT:(dc + 1) * QT], invb_sb[:, :], ALU.mult)
                for dc in range(NDC)][-1], waits=[("v", subv)])
        else:
            fin = P.emit("v", lambda v: [v.tensor_tensor(
                zbuf[:, dc * QT:(dc + 1) * QT], zbuf[:, dc * QT:(dc + 1) * QT],
                invb_sb[:, :], ALU.mult) for dc in range(NDC)][-1],
                waits=[("v", subv)])
        st_["prev_sums_last"] = L[(NDC - 1, "sums")]
        st_["prev_bcs"] = bcs
        st_["pss_guard_s"] = bcs
        st_["psc_guard_v"] = r1
        L["fin"], L["bcs"], L["r1"] = fin, bcs, r1
        return L

    st1 = {"psc_guard_s": sig_done}
    gate_ws = {}
    gw_wr8 = 0
    for qc in range(NQT):
        L1 = ln_phase(qc, "attn", y1, st1, zbuf_a, mb_a, ivb_a, rows_a,
                      zbp_a, zsp_a)
        for tt in range(4):
            tok = qc * QT + tt * 128

            def gmf(t_, qc=qc, tt=tt, tok=tok):
                for di in range(NDC):
                    ii = t_.matmul(ps_c[0][:, 8 * tt:8 * tt + 8],
                                   y1[di][:, tok:tok + 128],
                                   wg_b[:, di * 8:(di + 1) * 8],
                                   start=(di == 0), stop=(di == NDC - 1))
                return ii
            gt_t = P.emit("t", gmf, waits=[("v", L1["fin"]),
                                           ("s", gate_ws.get((qc, tt - 1), 0))])
            gs = P.emit("s", lambda s, tt=tt: s.activation(
                gsig[:, :], ps_c[0][:, 8 * tt:8 * tt + 8], AF.Sigmoid),
                waits=[("t", gt_t)])
            st1["psc_guard_s"] = gs
            chain = [
                lambda v: v.reduce_max(gt[:, 0:1], gsig[:, :], axis=AX.X),
                lambda v: v.tensor_scalar(gt[:, 8:16], gsig[:, :], gt[:, 0:1],
                                          None, ALU.is_equal),
                lambda v: v.scalar_tensor_tensor(gt[:, 16:24], gt[:, 8:16],
                                                 -1e30, gsig[:, :], ALU.mult,
                                                 ALU.add),
                lambda v: v.reduce_max(gt[:, 1:2], gt[:, 16:24], axis=AX.X),
                lambda v: v.tensor_scalar(gt[:, 24:32], gt[:, 16:24],
                                          gt[:, 1:2], None, ALU.is_equal),
                lambda v: v.tensor_tensor(gt[:, 24:32], gt[:, 24:32],
                                          gt[:, 8:16], ALU.add),
                lambda v: v.tensor_tensor(gt[:, 2:3], gt[:, 0:1], gt[:, 1:2],
                                          ALU.add),
                lambda v: v.reciprocal(gt[:, 3:4], gt[:, 2:3]),
                lambda v: v.tensor_tensor(gt[:, 32:40], gsig[:, :],
                                          gt[:, 24:32], ALU.mult),
                lambda v: v.tensor_scalar(wtok[:, :], gt[:, 32:40], gt[:, 3:4],
                                          None, ALU.mult),
            ]
            vv = P.emit("v", chain[0], waits=[("s", gs)])
            for f in chain[1:]:
                vv = P.emit("v", f, waits=[("v", vv)])
            wts = 0
            for rnd in range(2):
                def wtf(t_, rnd=rnd):
                    for e4 in range(4):
                        ii = t_.transpose(
                            ps_s[0][0:1, QT + e4 * 128:QT + (e4 + 1) * 128],
                            wtok[:, rnd * 4 + e4:rnd * 4 + e4 + 1],
                            eye_f[:, :])
                    return ii
                wt_t = P.emit("t", wtf, waits=[("v", vv), ("s", wts),
                                               ("y_eye", ld_eye)])
                wts = P.emit("s", lambda s, rnd=rnd: s.copy(
                    wstg[0:1, 0:512], ps_s[0][0:1, QT:QT + 512]),
                    waits=[("t", wt_t), ("g_wr8", gw_wr8)])
                for e4 in range(4):
                    gw_wr8 = P.emit("g", lambda g, rnd=rnd, tok=tok, e4=e4:
                                    g.dma_start(
                        wr8[rnd * 4 + e4:rnd * 4 + e4 + 1, tok:tok + 128],
                        wstg[0:1, e4 * 128:(e4 + 1) * 128]),
                        waits=[("s", wts)] if e4 == 0 else [], sem="g_wr8")
            gate_ws[(qc, tt)] = wts
    wr8_done = gw_wr8
    p4_v = P.cnt["v"]
    p4_s = P.cnt["s"]
    p4_t = P.cnt["t"]

    sc_p4.close()
    sc_ctx.close()
    sc_x.close()

    sc_r = contextlib.ExitStack()
    routed = [sc_r.enter_context(nc.sbuf_tensor(f"rt{c}", [128, TPC], BF16))
              for c in range(NDC)]
    sc_ffn = contextlib.ExitStack()
    we1f = sc_ffn.enter_context(nc.sbuf_tensor("we1f", [128, 8 * FF], BF16))
    we3f = sc_ffn.enter_context(nc.sbuf_tensor("we3f", [128, 8 * FF], BF16))
    we2f = sc_ffn.enter_context(nc.sbuf_tensor("we2f", [128, NFC * DM], BF16))
    h_sb = sc_ffn.enter_context(nc.sbuf_tensor("h_sb", [128, NFC * QT], BF16))
    sa_sb = [sc_ffn.enter_context(nc.sbuf_tensor(f"sa{p}", [128, QT], BF16))
             for p in range(2)]
    h1_sb = [sc_ffn.enter_context(nc.sbuf_tensor(f"h1{p}", [128, QT], F32))
             for p in range(2)]
    wbb = sc_ffn.enter_context(nc.sbuf_tensor("wbb", [128, QT], BF16))

    FW = {}
    prev_bws = [0]
    ab_last, o_last, o_racc_last = {}, {}, {}
    ffn_s_last, ffn_h1_last = 0, 0
    space_guard = [("v", p4_v), ("t", p4_t), ("s", p4_s)]
    stg_dma = {}
    for ei in range(NE + 1):
        e = ei - 1
        s1d = ws1_d[:, :] if e < 0 else we1_d[e, :, :]
        s3d = ws3_d[:, :] if e < 0 else we3_d[e, :, :]
        s2d = ws2_d[:, :] if e < 0 else we2_d[e, :, :]
        FW[(ei, "w1")] = P.emit("g", lambda g, s1d=s1d: g.dma_start(
            we1f[:, :].rearrange("p (c n) -> p c n", c=8), chunked(s1d)),
            waits=[("t", ab_last.get(ei - 1, 0))] +
                  (space_guard if ei == 0 else []), sem="g_we13")
        FW[(ei, "w3")] = P.emit("g", lambda g, s3d=s3d: g.dma_start(
            we3f[:, :].rearrange("p (c n) -> p c n", c=8), chunked(s3d)),
            sem="g_we13")
        FW[(ei, "w2")] = P.emit("g", lambda g, s2d=s2d: g.dma_start(
            we2f[:, :].rearrange("p (c n) -> p c n", c=16),
            s2d.rearrange("(c p) n -> p c n", p=128)),
            waits=[("t", o_last.get(ei - 1, 0))], sem="g_we2")
        ht, ot, sls = {}, {}, {}
        for tt in range(NQT):
            if e >= 0:
                sd = P.emit("g", lambda g, e=e, tt=tt: g.dma_start(
                    wstg[0:1, (tt % 2) * 512:(tt % 2) * 512 + 512],
                    wr8[e:e + 1, tt * QT:(tt + 1) * QT]),
                    waits=[("g_wr8", wr8_done),
                           ("t", stg_dma.get(tt % 2, 0))],
                    sem=f"g_st{tt % 2}")
                bw = P.emit("t", lambda t_, tt=tt: t_.matmul(
                    ps_d[:, 0:QT], ones1_b[:, :],
                    wstg[0:1, (tt % 2) * 512:(tt % 2) * 512 + 512],
                    start=True, stop=True),
                    waits=[(f"g_st{tt % 2}", sd), ("s", prev_bws[0])])
                stg_dma[tt % 2] = bw
                bws = P.emit("s", lambda s: s.copy(wbb[:, :],
                                                   ps_d[:, 0:QT]),
                             waits=[("t", bw)])
                prev_bws[0] = bws
            else:
                bws = 0
            for fc in range(NFC):
                par = fc % 2

                def abf(t_, fc=fc, par=par, tt=tt):
                    for di in range(NDC):
                        t_.matmul(ps_s[par][:, 0:QT],
                                  we1f[:, di * FF + fc * 128:
                                       di * FF + (fc + 1) * 128],
                                  y1[di][:, tt * QT:(tt + 1) * QT],
                                  start=(di == 0), stop=(di == NDC - 1))
                    for di in range(NDC):
                        ii = t_.matmul(ps_s[par][:, QT:2 * QT],
                                       we3f[:, di * FF + fc * 128:
                                            di * FF + (fc + 1) * 128],
                                       y1[di][:, tt * QT:(tt + 1) * QT],
                                       start=(di == 0), stop=(di == NDC - 1))
                    return ii
                w = [("g_we13", FW[(ei, "w3")]),
                     ("v", ht.get((tt, fc - 2), 0)),
                     ("s", sls.get((tt, fc - 2), 0))]
                ab = P.emit("t", abf, waits=w)
                sls[(tt, fc)] = sl = P.emit(
                    "s", lambda s, fc=fc, par=par: s.activation(
                        sa_sb[fc % 2][:, :], ps_s[par][:, 0:QT], AF.Silu),
                    waits=[("t", ab)])
                ffn_s_last = sl
                h1 = P.emit("v", lambda v, fc=fc, par=par: v.tensor_tensor(
                    h1_sb[fc % 2][:, :], sa_sb[fc % 2][:, :],
                    ps_s[par][:, QT:2 * QT], ALU.mult), waits=[("s", sl)])
                ffn_h1_last = h1
                wh = [("v", h1), ("t", ot.get((tt - 1, "last"),
                                              o_last.get(ei - 1, 0)))]
                if e >= 0:
                    ht[(tt, fc)] = P.emit(
                        "v", lambda v, fc=fc: v.tensor_tensor(
                            h_sb[:, fc * QT:(fc + 1) * QT],
                            h1_sb[fc % 2][:, :], wbb[:, :], ALU.mult),
                        waits=wh + [("s", bws)])
                else:
                    ht[(tt, fc)] = P.emit(
                        "v", lambda v, fc=fc: v.tensor_copy(
                            h_sb[:, fc * QT:(fc + 1) * QT],
                            h1_sb[fc % 2][:, :]), waits=wh)
            if tt == NQT - 1:
                ab_last[ei] = P.cnt["t"]
            def omh(t_, dc, lo, hi):
                for fc in range(lo, hi):
                    ii = t_.matmul(ps_c[dc % 2][:, 0:QT],
                                   we2f[:, fc * DM + dc * 128:
                                        fc * DM + (dc + 1) * 128],
                                   h_sb[:, fc * QT:(fc + 1) * QT],
                                   start=(fc == 0), stop=(fc == NFC - 1))
                return ii
            for dc in (0, 1):
                w = [("v", ht[(tt, NFC // 2 - 1)]), ("g_we2", FW[(ei, "w2")]),
                     ("v", ot.get((tt - 1, "racc"),
                                  o_racc_last.get(ei - 1, 0)))]
                P.emit("t", lambda t_, dc=dc: omh(t_, dc, 0, NFC // 2),
                       waits=w)
            for dc in range(NDC):
                if dc < 2:
                    ot[(tt, dc, "t")] = P.emit(
                        "t", lambda t_, dc=dc: omh(t_, dc, NFC // 2, NFC),
                        waits=[("v", ht[(tt, NFC - 1)])])
                else:
                    ot[(tt, dc, "t")] = P.emit(
                        "t", lambda t_, dc=dc: omh(t_, dc, 0, NFC),
                        waits=[("v", ht[(tt, NFC - 1)]),
                               ("v", ot.get((tt, dc - 2, "racc"), 0))])
                if ei == 0:
                    ot[(tt, dc, "racc")] = P.emit(
                        "v", lambda v, dc=dc, par=dc % 2, tt=tt: v.tensor_copy(
                            routed[dc][:, tt * QT:(tt + 1) * QT],
                            ps_c[par][:, 0:QT]),
                        waits=[("t", ot[(tt, dc, "t")])])
                else:
                    ot[(tt, dc, "racc")] = P.emit(
                        "v", lambda v, dc=dc, par=dc % 2, tt=tt:
                        v.tensor_tensor(
                            routed[dc][:, tt * QT:(tt + 1) * QT],
                            routed[dc][:, tt * QT:(tt + 1) * QT],
                            ps_c[par][:, 0:QT], ALU.add),
                        waits=[("t", ot[(tt, dc, "t")])])
            ot[(tt, "last")] = ot[(tt, NDC - 1, "t")]
            ot[(tt, "racc")] = ot[(tt, NDC - 1, "racc")]
        o_last[ei] = ot[(NQT - 1, "last")]
        o_racc_last[ei] = ot[(NQT - 1, "racc")]
    ffn_done_v = o_racc_last[NE]
    p5_t = P.cnt["t"]

    sc_ffn.close()

    sc_p6 = contextlib.ExitStack()
    zbuf_b = sc_p6.enter_context(nc.sbuf_tensor("zbuf_b", [128, 8 * QT], F32))
    mb_b = sc_p6.enter_context(nc.sbuf_tensor("mb_b", [128, QT], F32))
    ivb_b = sc_p6.enter_context(nc.sbuf_tensor("ivb_b", [128, QT], F32))
    rows_b = sc_p6.enter_context(nc.sbuf_tensor("rows_b", [1, 2048], F32))
    zbp_b = [sc_p6.enter_context(nc.sbuf_tensor(f"zbp_b{p}", [128, QT], BF16))
             for p in range(2)]
    zsp_b = [sc_p6.enter_context(nc.sbuf_tensor(f"zsp_b{p}", [128, QT], BF16))
             for p in range(2)]
    stage_b = [sc_p6.enter_context(nc.sbuf_tensor(f"ost{p}", [128, DM], F32))
               for p in range(2)]

    st2 = {"src_v": ffn_done_v, "psc_guard_v": ffn_done_v,
           "pss_guard_s": ffn_s_last, "pss_guard_v": ffn_h1_last,
           "prev_ttr": p5_t, "routed": routed}
    ST = {}
    prev_st = {0: 0, 1: 0}
    for qc in range(NQT):
        L2 = ln_phase(qc, "ffn", None, st2, zbuf_b, mb_b, ivb_b, rows_b,
                      zbp_b, zsp_b)
        last_ttr = 0
        for tt in range(4):
            par = tt % 2
            tok = qc * QT + tt * 128

            def trf(t_, tt=tt):
                for dc in range(NDC):
                    ii = t_.transpose(
                        ps_c[dc // 4][:, (dc % 4) * 128:(dc % 4 + 1) * 128],
                        zbuf_b[:, dc * QT + tt * 128:dc * QT + (tt + 1) * 128],
                        eye_f[:, :])
                return ii
            ttr = P.emit("t", trf, waits=[("v", L2["fin"]),
                                          ("v", ST.get((qc, tt - 1, "ev"), 0)),
                                          ("v", L2["r1"] if tt == 0 else 0)])
            last_ttr = ttr
            ST[(qc, tt, "ev")] = P.emit("v", lambda v, par=par: [
                v.tensor_copy(stage_b[par][:, dc * 128:(dc + 1) * 128],
                              ps_c[dc // 4][:, (dc % 4) * 128:
                                            (dc % 4 + 1) * 128])
                for dc in range(NDC)][-1],
                waits=[("t", ttr), (f"y_st{par}", prev_st[par])])
            ST[(qc, tt, "st")] = P.emit(
                "y", lambda y, par=par, tok=tok: y.dma_start(
                    out_d[tok:tok + 128, :], stage_b[par][:, :]),
                waits=[("v", ST[(qc, tt, "ev")])], sem=f"y_st{par}")
            prev_st[par] = ST[(qc, tt, "st")]
        st2["prev_ttr"] = last_ttr
        st2["psc_guard_v"] = ST[(qc, 3, "ev")]
    P.emit("y", lambda y: y.nop(),
           waits=[("y_st0", ST[(NQT - 1, 2, "st")]),
                  ("y_st1", ST[(NQT - 1, 3, "st")])], inc=False)

    P.finalize(nc, sems)
    sc_p6.close()
    sc_r.close()
    stk.close()
    return nc


def _host_inputs(inputs):
    x = np.ascontiguousarray(np.asarray(inputs["L0_attn_hidden_states"],
                                        np.float32))
    up = np.asarray(inputs["L0_attn_up"], np.float32).reshape(-1)
    uz = np.asarray(inputs["L0_attn_uz"], np.float32).reshape(-1)
    upz = np.zeros((128, 512), np.float32)
    r = np.arange(128)
    for dc in range(8):
        h = 2 * dc + r // 64
        upz[r, dc * 64 + h] = up[dc * 128 + r]
        upz[r, dc * 64 + 32 + h] = uz[dc * 128 + r]
    wg = np.asarray(inputs["L0_ffn_Wg"], np.float32)
    wg_m = np.ascontiguousarray(wg.reshape(8, 128, 8).transpose(1, 0, 2)
                                .reshape(128, 64))

    def colmat(name):
        a = np.asarray(inputs[name], np.float32).reshape(-1)
        return np.ascontiguousarray(a.reshape(8, 128).T)

    def arr(name):
        return np.ascontiguousarray(np.asarray(inputs[name], np.float32))

    shared = {
        "eye": np.eye(128, dtype=np.float32),
        "wq": arr("L0_attn_wq"), "wk": arr("L0_attn_wk"),
        "wv": arr("L0_attn_wv"), "wo": arr("L0_attn_wo"),
        "wfq": arr("L0_attn_wfq"), "wfg": arr("L0_attn_wfg"),
        "bq_m": colmat("L0_attn_bq"), "bk_m": colmat("L0_attn_bk"),
        "bo_m": colmat("L0_attn_bo"), "bfq_m": colmat("L0_attn_bfq"),
        "bfg_m": colmat("L0_attn_bfg"),
        "bv_m": colmat("L0_attn_bv"),
        "bv_r": np.ascontiguousarray(
            np.asarray(inputs["L0_attn_bv"], np.float32).reshape(1, -1)),
        "upz_m": upz, "wg_m": wg_m,
        "ws1": arr("L0_ffn_Ws1"), "ws3": arr("L0_ffn_Ws3"),
        "ws2": arr("L0_ffn_Ws2"), "we1": arr("L0_ffn_We1"),
        "we3": arr("L0_ffn_We3"), "we2": arr("L0_ffn_We2"),
    }
    in_maps = []
    for c in range(8):
        b, off = c // 2, (c % 2) * TPC
        m = dict(shared)
        m["xr"] = np.ascontiguousarray(
            np.concatenate([x[b, off:], x[b, :off]], axis=0))
        m["jv"] = np.ascontiguousarray(
            np.concatenate([np.arange(off, S), np.arange(0, off)])
            .astype(np.float32).reshape(1, S))
        in_maps.append(m)
    return in_maps


_NC_CACHE = {}


def kernel(**inputs):
    if "nc" not in _NC_CACHE:
        _NC_CACHE["nc"] = build_full()
    nc = _NC_CACHE["nc"]
    in_maps = _host_inputs(inputs)
    res = run_bass_kernel_spmd(nc, in_maps, core_ids=list(range(8)))
    out = np.empty((B, S, DM), np.float32)
    for c in range(8):
        b, off = c // 2, (c % 2) * TPC
        out[b, off:off + TPC] = res.results[c]["out"]
    return out
